# version 11
# baseline (speedup 1.0000x reference)
import contextlib
import numpy as np
import ml_dtypes
import concourse.bass as bass
import concourse.mybir as mybir
from concourse.bass_utils import run_bass_kernel_spmd

F32 = mybir.dt.float32
BF16 = mybir.dt.bfloat16
F16 = mybir.dt.float16
AF = mybir.ActivationFunctionType
ALU = mybir.AluOpType
AX = mybir.AxisListType

MAGIC = float(np.float32(1.5 * 2 ** 23))
LN_EPS = 1e-5

B, L, D = 2, 4096, 2048
NTOK = B * L
NC = 8
TPC = NTOK // NC
DIN = 4096
DPC = DIN // 4
DST = 16
DTR = 128
HEADS, HD = 32, 64
INNER = 2048
DFF = 8192
NT_D = D // 128
NT_DPC = DPC // 128


class Px:

    def __init__(self, nc, E):
        self.nc = nc
        self.sem = {}
        self.cnt = {}
        self.hw = {}
        for s in ("qs", "qg", "pe", "act", "dve", "gpc", "gp"):
            self.sem[s] = E(nc.semaphore("S_" + s))
            self.cnt[s] = 0

    def tok(self, s):
        return (s, self.cnt[s])

    def t(self, s, inst, amt=None):
        if amt is None:
            amt = 16 if s in ("qs", "qg") else 1
        self.cnt[s] += amt
        inst.then_inc(self.sem[s], amt)
        return (s, self.cnt[s])

    def w(self, eng, engname, toks):
        for (s, v) in toks:
            if v <= 0:
                continue
            key = (engname, s)
            if self.hw.get(key, -1) >= v:
                continue
            eng.wait_ge(self.sem[s], v)
            self.hw[key] = v

    def bar(self):
        return [(s, self.cnt[s]) for s in self.cnt]



def MM(p, te, out, lhsT, rhs, start, stop, nmax=512):
    n = rhs.shape[-1]
    inst = None
    for o in range(0, n, nmax):
        e = min(n, o + nmax)
        inst = te.matmul(out[:, o:e], lhsT, rhs[:, o:e], start=start, stop=stop)
    return inst


def _ein(nc, name, shape, dtype):
    return nc.dram_tensor(name, shape, dtype, kind="ExternalInput").ap()


def _eout(nc, name, shape, dtype):
    return nc.dram_tensor(name, shape, dtype, kind="ExternalOutput").ap()


def emit_ln(p, blk, tag, x_sb, tc, xh_sb, rows, ps_r1, ps_r2, ps_bc,
            scr_bf, bci, bcn, ones_bf, ones_row, eps_col, deps, xh32_sb=None,
            nt=16, scrT=None):

    def tile(t, i):
        return t[:, i * tc:(i + 1) * tc]

    def f_scalar(sc):
        p.w(sc, "act" + tag, deps)
        for i in range(nt):
            p.t("act", sc.copy(tile(scr_bf, i), tile(x_sb, i)))
    blk.scalar(f_scalar)
    cast_done = p.tok("act")

    def f_pe1(te):
        p.w(te, "pe" + tag, [cast_done])
        for i in range(nt):
            inst = MM(p, te, ps_r1[0:1, 0:tc], ones_bf[:], tile(scr_bf, i),
                      (i == 0), (i == nt - 1))
        p.t("pe", inst)
    blk.tensor(f_pe1)
    sum_done = p.tok("pe")

    def f_s2(sc):
        p.w(sc, "act" + tag, [sum_done])
        for i in range(nt):
            p.t("act", sc.activation(tile(scr_bf, i), tile(x_sb, i), AF.Square))
    blk.scalar(f_s2)
    sq_done = p.tok("act")

    def f_pe2(te):
        p.w(te, "pe" + tag, [sq_done])
        for i in range(nt):
            inst = MM(p, te, ps_r2[0:1, 0:tc], ones_bf[:], tile(scr_bf, i),
                      (i == 0), (i == nt - 1))
        p.t("pe", inst)
    blk.tensor(f_pe2)
    sq_sum = p.tok("pe")

    Dn = nt * 128

    def f_v2(v):
        p.w(v, "dve" + tag, [sum_done, sq_sum])
        p.t("dve", v.tensor_scalar_mul(rows[:, 0:tc], ps_r1[0:1, 0:tc], 1.0 / Dn))
        p.t("dve", v.tensor_scalar_mul(rows[:, tc:2 * tc], ps_r2[0:1, 0:tc],
                                       1.0 / Dn))
        p.w(v, "dve" + tag, [p.tok("dve")])
        p.t("dve", v.tensor_mul(rows[:, 2 * tc:3 * tc], rows[:, 0:tc],
                                rows[:, 0:tc]))
        p.w(v, "dve" + tag, [p.tok("dve")])
        p.t("dve", v.tensor_sub(rows[:, tc:2 * tc], rows[:, tc:2 * tc],
                                rows[:, 2 * tc:3 * tc]))
    blk.vector(f_v2)
    var_done = p.tok("dve")

    def f_s3(sc):
        p.w(sc, "act" + tag, [var_done])
        p.t("act", sc.activation(rows[:, 2 * tc:3 * tc], rows[:, tc:2 * tc],
                                 AF.Ln, bias=eps_col[0:1, 0:1]))
        p.w(sc, "act" + tag, [p.tok("act")])
        p.t("act", sc.activation(rows[:, 2 * tc:3 * tc], rows[:, 2 * tc:3 * tc],
                                 AF.Exp, scale=-0.5))
    blk.scalar(f_s3)
    inv_done = p.tok("act")

    def f_v3(v):
        p.w(v, "dve" + tag, [inv_done])
        p.t("dve", v.tensor_mul(rows[:, 3 * tc:4 * tc], rows[:, 0:tc],
                                rows[:, 2 * tc:3 * tc]))
        p.w(v, "dve" + tag, [p.tok("dve")])
        p.t("dve", v.tensor_scalar_mul(rows[:, 3 * tc:4 * tc],
                                       rows[:, 3 * tc:4 * tc], -1.0))
    blk.vector(f_v3)
    nmi_done = p.tok("dve")

    def f_pe3(te):
        p.w(te, "pe" + tag, [inv_done])
        p.t("pe", MM(p, te, ps_bc[:, 0:tc], ones_row[:],
                     rows[0:1, 2 * tc:3 * tc], True, True))
    blk.tensor(f_pe3)
    bc1 = p.tok("pe")

    def f_sc1(sc):
        p.w(sc, "act" + tag, [bc1])
        p.t("act", sc.copy(bci, ps_bc[:, 0:tc]))
    blk.scalar(f_sc1)
    ev1 = p.tok("act")

    def f_pe4(te):
        p.w(te, "pe" + tag, [nmi_done, ev1])
        p.t("pe", MM(p, te, ps_bc[:, 0:tc], ones_row[:],
                     rows[0:1, 3 * tc:4 * tc], True, True))
    blk.tensor(f_pe4)
    bc2 = p.tok("pe")

    def f_sc2(sc):
        p.w(sc, "act" + tag, [bc2])
        p.t("act", sc.copy(bcn, ps_bc[:, 0:tc]))
    blk.scalar(f_sc2)
    ev2 = p.tok("act")

    def f_v5(v):
        p.w(v, "dve" + tag, [ev1, ev2])
        for i in range(nt):
            if scrT is None:
                p.t("dve", v.tensor_mul(tile(x_sb, i), tile(x_sb, i), bci))
                p.w(v, "dve" + tag, [p.tok("dve")])
                p.t("dve", v.tensor_add(tile(xh_sb, i), tile(x_sb, i), bcn))
                if xh32_sb is not None:
                    p.t("dve", v.tensor_add(tile(xh32_sb, i), tile(x_sb, i),
                                            bcn))
            else:
                p.t("dve", v.tensor_mul(scrT[:, 0:tc], tile(x_sb, i), bci))
                p.w(v, "dve" + tag, [p.tok("dve")])
                p.t("dve", v.tensor_add(tile(xh_sb, i), scrT[:, 0:tc], bcn))
                if xh32_sb is not None:
                    p.t("dve", v.tensor_add(tile(xh32_sb, i), scrT[:, 0:tc],
                                            bcn))
                p.w(v, "dve" + tag, [p.tok("dve")])
    blk.vector(f_v5)
    return [p.tok("dve")]


def emit_ln1(nc, p, blk, IO):
    xT = IO["xT"]
    w1T = IO["w1T"]
    w2T = IO["w2T"]
    ones_bf_i = IO["ones_h"]
    ones_row_i = IO["ones_row"]
    ones_f_i = IO["ones_f"]
    eps_i = IO["eps_c"]
    xhT = IO["xhT"]
    wabs = IO["wabs"]

    with contextlib.ExitStack() as st_:
        E = st_.enter_context
        ones_bf = E(nc.sbuf_tensor("c_ones_h", [128, 1], F16))
        ones_row = E(nc.sbuf_tensor("c_ones_row", [1, 128], F32))
        ones_f = E(nc.sbuf_tensor("c_ones_f", [128, 1], F32))
        eps_c = E(nc.sbuf_tensor("c_eps", [128, 1], F32))
        x_sb = E(nc.sbuf_tensor("x_sb", [128, NT_D * TPC], F32))
        xh_sb = E(nc.sbuf_tensor("xh_sb", [128, NT_D * TPC], F16))
        scr_bf = E(nc.sbuf_tensor("scr_bf", [128, NT_D * TPC], F16))
        bc_sb = E(nc.sbuf_tensor("bc_sb", [128, 2 * TPC], F32))
        rows = E(nc.sbuf_tensor("rows", [1, 4 * TPC], F32))
        wbuf = E(nc.sbuf_tensor("wbuf", [128, DFF], F32))
        red = E(nc.sbuf_tensor("red", [128, 16], F32))
        red2 = E(nc.sbuf_tensor("red2", [128, 64], F32))
        wsum = E(nc.sbuf_tensor("wsum", [2, 1], F32))
        ps_r1 = E(nc.psum_tensor("ps_r1", [1, TPC], F32))
        ps_r2 = E(nc.psum_tensor("ps_r2", [1, TPC], F32))
        ps_bc = E(nc.psum_tensor("ps_bc", [128, TPC], F32))
        ps_w = E(nc.psum_tensor("ps_w", [2, 1], F32))

        def f_ld(sy):
            for sb, ap in [(ones_bf, ones_bf_i), (ones_row, ones_row_i),
                           (ones_f, ones_f_i), (eps_c, eps_i)]:
                p.t("qs", sy.dma_start(out=sb[:], in_=ap))
            for i in range(NT_D):
                p.t("qs", sy.dma_start(out=x_sb[:, i * TPC:(i + 1) * TPC],
                                       in_=xT[i * 128:(i + 1) * 128, :]))
        blk.sync(f_ld)
        ld_done = p.tok("qs")

        def wpass(src, nrow, width, colbase):
            for i in range(nrow):
                def f_g(g, i=i):
                    p.w(g, "qg", [p.tok("dve")])
                    p.t("qg", g.dma_start(out=wbuf[:, 0:width],
                                          in_=src[i * 128:(i + 1) * 128, :]))
                blk.gpsimd(f_g)
                ldt = p.tok("qg")

                def f_v(v, i=i):
                    p.w(v, "dve", [ldt])
                    p.t("dve", v.tensor_reduce(
                        red2[:, 0:width // 128],
                        wbuf[:, 0:width].rearrange("p (a b) -> p a b", b=128),
                        AX.X, ALU.add, apply_absolute_value=True))
                    p.w(v, "dve", [p.tok("dve")])
                    p.t("dve", v.tensor_reduce(
                        red[:, colbase + i:colbase + i + 1],
                        red2[:, 0:width // 128].rearrange("p (o a) -> p o a", o=1),
                        AX.X, ALU.add))
                blk.vector(f_v)

        wpass(w1T, 2, DFF, 0)
        wpass(w2T, 8, D, 2)

        def f_vsum(v):
            p.w(v, "dve", [p.tok("dve")])
            p.t("dve", v.tensor_reduce(
                red[:, 10:11], red[:, 0:2].rearrange("p (a b) -> p a b", a=1),
                AX.X, ALU.add))
            p.t("dve", v.tensor_reduce(
                red[:, 11:12], red[:, 2:10].rearrange("p (a b) -> p a b", a=1),
                AX.X, ALU.add))
        blk.vector(f_vsum)
        red_done = p.tok("dve")

        def f_pw(te):
            p.w(te, "pe", [red_done, ld_done])
            p.t("pe", te.matmul(ps_w[:], red[:, 10:12], ones_f[:],
                                start=True, stop=True))
        blk.tensor(f_pw)
        pw_done = p.tok("pe")

        def f_vw(v):
            p.w(v, "dve", [pw_done])
            p.t("dve", v.tensor_copy(wsum[:], ps_w[:]))
        blk.vector(f_vw)
        wsum_done = p.tok("dve")

        ln_done = emit_ln(p, blk, "", x_sb, TPC, xh_sb, rows, ps_r1, ps_r2,
                          ps_bc, scr_bf, bc_sb[:, 0:TPC], bc_sb[:, TPC:2 * TPC],
                          ones_bf, ones_row, eps_c[:, :], [ld_done])

        def f_st(sy):
            p.w(sy, "qs", ln_done + [wsum_done])
            for i in range(NT_D):
                p.t("qs", sy.dma_start(out=xhT[i * 128:(i + 1) * 128, :],
                                       in_=xh_sb[:, i * TPC:(i + 1) * TPC]))
            p.t("qs", sy.dma_start(out=wabs, in_=wsum[:]))
            sy.wait_ge(p.sem["qs"], p.cnt["qs"])
        blk.sync(f_st)



def emit_front(nc, p, blk, IO):
    TC = 512
    NCH = NTOK // TC
    xhTb = IO["xhTb"]
    winT = IO["winT"]
    convw = IO["convw"]
    convb = IO["convb"]
    wxT = IO["wxT"]
    w1T = IO["w1T"]
    w2T = IO["w2T"]
    wtot = IO["wtot"]
    ones_row_i = IO["ones_row"]
    u16o = IO["u16"]
    zs16o = IO["zs16"]
    ssmpo = IO["ssmp"]
    w1qo = IO["w1q"]
    w2qo = IO["w2q"]

    with contextlib.ExitStack() as st_:
        E = st_.enter_context
        ones_row = E(nc.sbuf_tensor("c_ones_row", [1, 128], F32))
        winbf = E(nc.sbuf_tensor("winbf", [128, 16 * 1024], F16))
        wx16 = E(nc.sbuf_tensor("wx16", [128, 4 * 160], F16))
        wld = E(nc.sbuf_tensor("wld", [128, 2048], F32))
        wx32 = E(nc.sbuf_tensor("wx32", [128, 4 * 160], F32))
        cw = E(nc.sbuf_tensor("cw", [128, 4 * 4], F32))
        cb = E(nc.sbuf_tensor("cb", [128, 4], F32))
        xh2 = E(nc.sbuf_tensor("xh2", [128, 2 * 16 * TC], F16))
        xzx = E(nc.sbuf_tensor("xzx", [128, 4 * (TC + 3)], F32))
        acc = E(nc.sbuf_tensor("acc", [128, 2 * TC], F32))
        ucv = E(nc.sbuf_tensor("ucv", [128, 8 * TC], F16))
        zst = E(nc.sbuf_tensor("zst", [128, 4 * TC], F16))
        s_out = E(nc.sbuf_tensor("s_out", [128, TC], F32))
        s_out2 = E(nc.sbuf_tensor("s_out2", [32, TC], F32))
        ps_x = E(nc.psum_tensor("ps_x", [128, TC], F32))
        ps_x2 = E(nc.psum_tensor("ps_x2", [128, TC], F32))
        ps_s1 = E(nc.psum_tensor("ps_s1", [128, TC], F32))
        ps_s2 = E(nc.psum_tensor("ps_s2", [32, TC], F32))
        ps_b = E(nc.psum_tensor("ps_b", [128, 2], F32))
        wqi = E(nc.sbuf_tensor("wqi", [128, DFF], F32))
        wqt = E(nc.sbuf_tensor("wqt", [128, DFF], F32))
        wqo_sb = E(nc.sbuf_tensor("wqo_sb", [128, DFF], BF16))
        srow = E(nc.sbuf_tensor("srow", [1, 4], F32))
        sbc = E(nc.sbuf_tensor("sbc", [128, 2], F32))
        wab = E(nc.sbuf_tensor("wab", [1, 2], F32))

        def f_ld(sy):
            p.t("qs", sy.dma_start(out=ones_row[:], in_=ones_row_i))
            p.t("qs", sy.dma_start(out=cw[:], in_=convw))
            p.t("qs", sy.dma_start(out=cb[:], in_=convb))
            p.t("qs", sy.dma_start(out=wx32[:], in_=wxT))
            p.t("qs", sy.dma_start(out=wab[:, 0:1], in_=wtot[0:1, :]))
            p.t("qs", sy.dma_start(out=wab[:, 1:2], in_=wtot[1:2, :]))
        blk.sync(f_ld)
        ld0 = p.tok("qs")

        def f_cast0(sc):
            p.w(sc, "act", [ld0])
            p.t("act", sc.copy(wx16[:], wx32[:]))
        blk.scalar(f_cast0)

        for i in range(16):
            def f_ldi(sy, i=i):
                p.w(sy, "qs", [p.tok("act")])
                p.t("qs", sy.dma_start(out=wld[:, 0:1024],
                                       in_=winT[i * 128:(i + 1) * 128, :]))
            blk.sync(f_ldi)
            ldt = p.tok("qs")

            def f_ci(sc, i=i, ldt=ldt):
                p.w(sc, "act", [ldt])
                p.t("act", sc.copy(winbf[:, i * 1024:(i + 1) * 1024],
                                   wld[:, 0:1024]))
            blk.scalar(f_ci)
        wcast = p.tok("act")

        def f_vq0(v):
            p.w(v, "dve", [ld0])
            p.t("dve", v.tensor_scalar_mul(srow[:, 0:2], wab[0:1, 0:2],
                                           1.0 / (DFF * D)))
            p.w(v, "dve", [p.tok("dve")])
            p.t("dve", v.reciprocal(srow[:, 2:4], srow[:, 0:2]))
        blk.vector(f_vq0)
        sc_done = p.tok("dve")

        def f_pq(te):
            p.w(te, "pe", [sc_done])
            p.t("pe", te.matmul(ps_b[:], ones_row[:], srow[0:1, 2:4],
                                start=True, stop=True))
        blk.tensor(f_pq)
        pb_done = p.tok("pe")

        def f_vq1(v):
            p.w(v, "dve", [pb_done])
            p.t("dve", v.tensor_copy(sbc[:], ps_b[:]))
        blk.vector(f_vq1)
        sbc_done = p.tok("dve")

        def quantpass(src, dst, nrow, width, scol):
            for i in range(nrow):
                def f_g(g, i=i):
                    p.w(g, "qg", [p.tok("dve"), p.tok("qg")])
                    p.t("qg", g.dma_start(out=wqi[:, 0:width],
                                          in_=src[i * 128:(i + 1) * 128, :]))
                blk.gpsimd(f_g)
                ldt = p.tok("qg")

                def f_v(v, i=i, ldt=ldt):
                    p.w(v, "dve", [ldt, sbc_done])
                    p.t("dve", v.tensor_scalar_mul(wqt[:, 0:width],
                                                   wqi[:, 0:width],
                                                   sbc[:, scol:scol + 1]))
                    p.w(v, "dve", [p.tok("dve")])
                    p.t("dve", v.tensor_scalar(wqt[:, 0:width], wqt[:, 0:width],
                                               MAGIC, MAGIC, ALU.add,
                                               ALU.subtract))
                    p.w(v, "dve", [p.tok("dve")])
                    p.t("dve", v.tensor_scalar(wqo_sb[:, 0:width],
                                               wqt[:, 0:width], 1.0, -1.0,
                                               ALU.min, ALU.max))
                blk.vector(f_v)
                qt = p.tok("dve")

                def f_s(sy, i=i, qt=qt):
                    p.w(sy, "qg", [qt])
                    p.t("qg", sy.dma_start(out=dst[i * 128:(i + 1) * 128, :],
                                           in_=wqo_sb[:, 0:width]))
                blk.gpsimd(f_s)

        quantpass(w1T, w1qo, 2, DFF, 0)
        quantpass(w2T, w2qo, 8, D, 1)
        quant_done = p.tok("qg")

        def tl(t, i, w=TC):
            return t[:, i * w:(i + 1) * w]

        def f_hist0(v):
            for mt in range(4):
                p.t("dve", v.memset(xzx[:, mt * (TC + 3):mt * (TC + 3) + 3], 0.0))
        blk.vector(f_hist0)

        for ch in range(NCH):
            xsl = (ch % 2) * 16 * TC
            if ch == NCH // 2:
                def f_histr(v):
                    p.w(v, "dve", [("act", p.cnt["act"]), p.tok("dve")])
                    for mt in range(4):
                        p.t("dve", v.memset(
                            xzx[:, mt * (TC + 3):mt * (TC + 3) + 3], 0.0))
                blk.vector(f_histr)

            pe_c = p.cnt["pe"]
            a_base = p.cnt["act"]
            v_base = p.cnt["dve"]
            qs_c = p.cnt["qs"]

            def f_ldc(sy, ch=ch, xsl=xsl, pe_c=pe_c):
                if ch >= 2:
                    p.w(sy, "qs", [("pe", pe_c - 9)])
                for i in range(16):
                    p.t("qs", sy.dma_start(
                        out=xh2[:, xsl + i * TC:xsl + (i + 1) * TC],
                        in_=xhTb[((ch * TC) // 1024) * 2048 + i * 128:
                                 ((ch * TC) // 1024) * 2048 + (i + 1) * 128,
                                 (ch * TC) % 1024:(ch * TC) % 1024 + TC]))
            blk.sync(f_ldc)
            ldc = p.tok("qs")

            def f_pe(te, ch=ch, xsl=xsl, a_base=a_base, ldc=ldc):
                p.w(te, "pe", [ldc, wcast])
                for mt in range(8):
                    ps = ps_x if mt % 2 == 0 else ps_x2
                    if mt >= 2:
                        p.w(te, "pe", [("act", a_base + mt - 1)])
                    elif ch > 0:
                        p.w(te, "pe", [("act", a_base)])
                    for kt in range(16):
                        inst = te.matmul(
                            ps[:],
                            winbf[:, kt * 1024 + mt * 128:kt * 1024 + (mt + 1) * 128],
                            xh2[:, xsl + kt * TC:xsl + (kt + 1) * TC],
                            start=(kt == 0), stop=(kt == 15))
                    p.t("pe", inst)
                p.w(te, "pe", [("act", a_base + 12)])
                for kt in range(4):
                    i1 = te.matmul(ps_s1[:], wx16[:, kt * 160:kt * 160 + 128],
                                   tl(ucv, kt), start=(kt == 0), stop=(kt == 3))
                for kt in range(4):
                    i2 = te.matmul(ps_s2[:], wx16[:, kt * 160 + 128:kt * 160 + 160],
                                   tl(ucv, kt), start=(kt == 0), stop=(kt == 3))
                p.t("pe", i2)
            blk.tensor(f_pe)

            def f_sc(sc, ch=ch, pe_c=pe_c, v_base=v_base, qs_c=qs_c):
                for mt in range(8):
                    ps = ps_x if mt % 2 == 0 else ps_x2
                    p.w(sc, "act", [("pe", pe_c + mt + 1)])
                    if mt < 4:
                        xb = mt * (TC + 3)
                        if ch > 0:
                            p.w(sc, "act", [("dve", v_base)])
                        p.t("act", sc.copy(xzx[:, xb + 3:xb + 3 + TC], ps[:]))
                    else:
                        if ch >= 1:
                            p.w(sc, "act", [("qs", qs_c - 16 * 10 + 16 * (mt - 3))])
                        p.t("act", sc.activation(
                            zst[:, (mt - 4) * TC:(mt - 3) * TC], ps[:], AF.Silu))
                for mt in range(4):
                    p.w(sc, "act", [("dve", v_base + 5 * (mt + 1))])
                    if ch >= 1:
                        p.w(sc, "act", [("qs", qs_c - 16 * 6 + 16 * (mt + 1))])
                    p.w(sc, "act", [("pe", pe_c)])
                    p.t("act", sc.activation(tl(ucv, mt),
                                             acc[:, (mt % 2) * TC:(mt % 2 + 1) * TC],
                                             AF.Silu, bias=cb[:, mt:mt + 1]))
            blk.scalar(f_sc)

            def f_cv(v, ch=ch, a_base=a_base):
                for mt in range(4):
                    xb = mt * (TC + 3)
                    asl = (mt % 2) * TC
                    p.w(v, "dve", [("act", a_base + mt + 1), p.tok("dve")])
                    if mt >= 2:
                        p.w(v, "dve", [("act", a_base + 8 + mt - 1)])
                    elif ch > 0:
                        p.w(v, "dve", [("act", a_base)])
                    p.t("dve", v.tensor_scalar_mul(acc[:, asl:asl + TC],
                                                   xzx[:, xb:xb + TC],
                                                   cw[:, mt * 4:mt * 4 + 1]))
                    for k in range(1, 4):
                        p.w(v, "dve", [p.tok("dve")])
                        p.t("dve", v.scalar_tensor_tensor(
                            acc[:, asl:asl + TC], xzx[:, xb + k:xb + k + TC],
                            cw[:, mt * 4 + k:mt * 4 + k + 1],
                            acc[:, asl:asl + TC], ALU.mult, ALU.add))
                    p.w(v, "dve", [p.tok("dve"),
                                   ("act", a_base + 8 + mt)])
                    p.t("dve", v.tensor_copy(xzx[:, xb:xb + 3],
                                             xzx[:, xb + TC:xb + TC + 3]))
            blk.vector(f_cv)

            def f_sev(v, pe_c=pe_c):
                p.w(v, "dve", [("pe", pe_c + 9), ("qs", p.cnt["qs"])])
                p.t("dve", v.tensor_copy(s_out[:], ps_s1[:]))
                p.w(v, "dve", [p.tok("dve")])
                p.t("dve", v.tensor_copy(s_out2[:], ps_s2[:]))
            blk.vector(f_sev)

            def f_st(sy, ch=ch, a_base=a_base):
                for mt in range(4):
                    p.w(sy, "qs", [("act", a_base + 4 + mt + 1)])
                    p.t("qs", sy.dma_start(
                        out=zs16o[mt * 128:(mt + 1) * 128, ch * TC:(ch + 1) * TC],
                        in_=zst[:, mt * TC:(mt + 1) * TC]))
                for mt in range(4):
                    p.w(sy, "qs", [("act", a_base + 8 + mt + 1)])
                    p.t("qs", sy.dma_start(
                        out=u16o[mt * 128:(mt + 1) * 128, ch * TC:(ch + 1) * TC],
                        in_=tl(ucv, mt)))
                p.w(sy, "qs", [("dve", p.cnt["dve"])])
                p.t("qs", sy.dma_start(out=ssmpo[0:128, ch * TC:(ch + 1) * TC],
                                       in_=s_out[:]))
                p.t("qs", sy.dma_start(out=ssmpo[128:160, ch * TC:(ch + 1) * TC],
                                       in_=s_out2[:]))
            blk.sync(f_st)

        def f_fin(sy):
            p.w(sy, "qs", [quant_done])
            sy.wait_ge(p.sem["qs"], p.cnt["qs"])
            sy.wait_ge(p.sem["qg"], p.cnt["qg"])
        blk.sync(f_fin)



def emit_scan(nc, p, blk, IO):
    TCS = 1024
    NCH = NTOK // TCS
    ssms = IO["ssms"]
    u16 = IO["u16"]
    zs16 = IO["zs16"]
    wdtT = IO["wdtT"]
    dtb = IO["dtb"]
    acol = IO["acol"]
    dcol = IO["dcol"]
    yszo = IO["ysz_a2a"]

    with contextlib.ExitStack() as st_:
        E = st_.enter_context
        dtr = E(nc.sbuf_tensor("dtr", [128, NTOK], F32))
        dtrb = E(nc.sbuf_tensor("dtrb", [128, NTOK], F16))
        wdt = E(nc.sbuf_tensor("wdt", [128, 512], F16))
        wdt32 = E(nc.sbuf_tensor("wdt32", [128, 512], F32))
        dtb_sb = E(nc.sbuf_tensor("dtb_sb", [128, 4], F32))
        acol_sb = E(nc.sbuf_tensor("acol_sb", [128, 64], F32))
        dcol_sb = E(nc.sbuf_tensor("dcol_sb", [128, 4], F32))
        delta = E(nc.sbuf_tensor("delta", [128, 4 * TCS], F32))
        spr = E(nc.sbuf_tensor("spr", [128, 4 * TCS], F32))
        u_sb = E(nc.sbuf_tensor("u_sb", [128, 4 * TCS], F16))
        zs_sb = E(nc.sbuf_tensor("zs_sb", [128, 4 * TCS], F16))
        wmt = E(nc.sbuf_tensor("wmt", [128, TCS], F32))
        a_sb = E(nc.sbuf_tensor("a_sb", [128, 2 * TCS], F32))
        b_sb = E(nc.sbuf_tensor("b_sb", [128, TCS], F32))
        h_sb = E(nc.sbuf_tensor("h_sb", [128, 2 * TCS], F32))
        hy_sb = E(nc.sbuf_tensor("hy_sb", [128, TCS], F32))
        y15_sb = E(nc.sbuf_tensor("y15_sb", [128, TCS], F32))
        y_sb = E(nc.sbuf_tensor("y_sb", [128, TCS], F32))
        bb_sb = E(nc.sbuf_tensor("bb_sb", [128, 2 * TCS], F32))
        cc_sb = E(nc.sbuf_tensor("cc_sb", [128, 2 * TCS], F32))
        t1_sb = E(nc.sbuf_tensor("t1_sb", [128, TCS], F32))
        yz_sb = E(nc.sbuf_tensor("yz_sb", [128, 2 * TCS], F16))
        state = E(nc.sbuf_tensor("state", [128, 64], F32))
        sp1 = E(nc.psum_tensor("sp1", [128, 1024], F32))
        sp2 = E(nc.psum_tensor("sp2", [128, 1024], F32))

        def f_ld(sy):
            for sb, ap in [(wdt32, wdtT), (dtb_sb, dtb), (acol_sb, acol),
                           (dcol_sb, dcol)]:
                p.t("qs", sy.dma_start(out=sb[:], in_=ap))
            p.t("qs", sy.dma_start(out=dtr[:], in_=ssms[0:128, :]))
        blk.sync(f_ld)
        ld0 = p.tok("qs")

        def f_cast(sc):
            p.w(sc, "act", [ld0])
            p.t("act", sc.copy(dtrb[:], dtr[:]))
            p.t("act", sc.copy(wdt[:], wdt32[:]))
        blk.scalar(f_cast)
        cast_done = p.tok("act")

        def f_st0(v):
            p.t("dve", v.memset(state[:], 0.0))
        blk.vector(f_st0)

        prev_ymul62 = 0
        for ch in range(NCH):
            reset = (ch == NCH // 2)
            v0 = p.cnt["dve"]
            g0 = p.cnt["gp"]
            vc = v0 + 4 + (1 if reset else 0)
            bmul_tok = {}
            scan_tok = {}
            y15_tok = {}
            fin_tok = {}
            for mt in range(4):
                vc += 1
                for n in range(16):
                    k = mt * 16 + n
                    vc += 1
                    bmul_tok[k] = vc
                    vc += 1
                    scan_tok[k] = vc
                    vc += 1
                    if n == 15:
                        vc += 1
                        y15_tok[mt] = vc
                vc += 3
                fin_tok[mt] = vc
            gc = g0
            ymul_tok = {}
            yadd_tok = {}
            for mt in range(4):
                for n in range(15):
                    k = mt * 16 + n
                    gc += 1
                    ymul_tok[k] = gc
                    if n > 0:
                        gc += 1
                        yadd_tok[k] = gc

            def f_lu(sy, ch=ch, v0=v0):
                p.w(sy, "qs", [("dve", v0)])
                for mt in range(4):
                    p.t("qs", sy.dma_start(
                        out=u_sb[:, mt * TCS:(mt + 1) * TCS],
                        in_=u16[mt * 128:(mt + 1) * 128, ch * TCS:(ch + 1) * TCS]))
                    p.t("qs", sy.dma_start(
                        out=zs_sb[:, mt * TCS:(mt + 1) * TCS],
                        in_=zs16[mt * 128:(mt + 1) * 128, ch * TCS:(ch + 1) * TCS]))
            blk.sync(f_lu)
            lu_done = p.tok("qs")
            qs_after_lu = p.cnt["qs"]

            bct_bb = {}
            bct_cc = {}

            def f_gp(g, ch=ch, v0=v0, prev_ymul62=prev_ymul62):
                def issue_bb(k):
                    n = k % 16
                    sl = (k % 2) * TCS
                    p.t("qg", g.dma_start(
                        out=bb_sb[:, sl:sl + TCS],
                        in_=ssms[128 + n:129 + n,
                                 ch * TCS:(ch + 1) * TCS].partition_broadcast(128)))
                    bct_bb[k] = p.tok("qg")

                def issue_cc(k):
                    n = k % 16
                    sl = (k % 2) * TCS
                    p.t("qg", g.dma_start(
                        out=cc_sb[:, sl:sl + TCS],
                        in_=ssms[144 + n:145 + n,
                                 ch * TCS:(ch + 1) * TCS].partition_broadcast(128)))
                    bct_cc[k] = p.tok("qg")

                if ch > 0:
                    p.w(g, "qg", [("dve", v0)])
                for k in (0, 1):
                    issue_bb(k)
                    issue_cc(k)
                for k in range(64):
                    mt, n = divmod(k, 16)
                    if k + 2 < 64:
                        p.w(g, "qg", [("dve", bmul_tok[k])])
                        issue_bb(k + 2)
                    if n == 15:
                        if k + 2 < 64:
                            p.w(g, "qg", [("dve", y15_tok[mt])])
                            issue_cc(k + 2)
                        continue
                    p.w(g, "gp", [("dve", scan_tok[k]), bct_cc[k]])
                    sl = (k % 2) * TCS
                    if n == 0:
                        p.t("gp", g.tensor_mul(y_sb[:], h_sb[:, sl:sl + TCS],
                                               cc_sb[:, sl:sl + TCS]), 1)
                    else:
                        p.t("gp", g.tensor_mul(hy_sb[:], h_sb[:, sl:sl + TCS],
                                               cc_sb[:, sl:sl + TCS]), 1)
                        p.w(g, "gp", [p.tok("gp")])
                        p.t("gp", g.tensor_add(y_sb[:], y_sb[:], hy_sb[:]), 1)
                    if k + 2 < 64:
                        issue_cc(k + 2)
            blk.gpsimd(f_gp)
            assert p.cnt["gp"] == gc, (p.cnt["gp"], gc)

            a0 = p.cnt["act"]
            def f_dm(te, ch=ch, a0=a0):
                p.w(te, "pe", [cast_done])
                for mt in range(4):
                    ps = sp1 if mt % 2 == 0 else sp2
                    if mt >= 2:
                        p.w(te, "pe", [("act", a0 + 4 * (mt - 1))])
                    elif ch > 0:
                        p.w(te, "pe", [("act", a0)])
                    p.t("pe", MM(p, te, ps[:], wdt[:, mt * 128:(mt + 1) * 128],
                                 dtrb[:, ch * TCS:(ch + 1) * TCS], True, True))
            blk.tensor(f_dm)
            pe0 = p.cnt["pe"]

            def f_sc(sc, ch=ch, v0=v0, scan_tok=scan_tok):
                for mt in range(4):
                    ps = sp1 if mt % 2 == 0 else sp2
                    o = mt * TCS
                    p.w(sc, "act", [("pe", pe0 - 4 + mt + 1)])
                    if ch > 0:
                        p.w(sc, "act", [("dve", v0)])
                    p.t("act", sc.activation(delta[:, o:o + TCS], ps[:],
                                             AF.Abs, bias=dtb_sb[:, mt:mt + 1]))
                    p.w(sc, "act", [p.tok("act")])
                    p.t("act", sc.activation(delta[:, o:o + TCS],
                                             delta[:, o:o + TCS], AF.Exp,
                                             scale=-1.0))
                    p.w(sc, "act", [p.tok("act")])
                    p.t("act", sc.activation(delta[:, o:o + TCS],
                                             delta[:, o:o + TCS], AF.Ln,
                                             bias=1.0))
                    p.t("act", sc.activation(spr[:, o:o + TCS], ps[:], AF.Relu,
                                             bias=dtb_sb[:, mt:mt + 1]))
                p.w(sc, "act", [p.tok("act")])
                for mt in range(4):
                    for n in range(16):
                        k = mt * 16 + n
                        sl = ((k % 2)) * TCS
                        w = v0 + mt + 1
                        if k >= 2:
                            w = max(w, scan_tok[k - 2])
                        p.w(sc, "act", [("dve", w)])
                        p.t("act", sc.activation(
                            a_sb[:, sl:sl + TCS], delta[:, mt * TCS:(mt + 1) * TCS],
                            AF.Exp, scale=acol_sb[:, mt * 16 + n:mt * 16 + n + 1]))
            blk.scalar(f_sc)
            act_after_sp = a0 + 4 * 4
            def exp_tok(mt, n):
                return ("act", act_after_sp + mt * 16 + n + 1)

            def f_v(v, ch=ch, v0=v0, lu_done=lu_done, reset=reset,
                    prev_ymul62=prev_ymul62):
                for mt in range(4):
                    o = mt * TCS
                    p.w(v, "dve", [("act", a0 + 4 * (mt + 1))])
                    p.t("dve", v.tensor_add(delta[:, o:o + TCS],
                                            delta[:, o:o + TCS],
                                            spr[:, o:o + TCS]))
                if reset:
                    p.w(v, "dve", [p.tok("dve")])
                    p.t("dve", v.memset(state[:], 0.0))
                for mt in range(4):
                    o = mt * TCS
                    p.w(v, "dve", [lu_done, p.tok("dve")])
                    p.t("dve", v.tensor_mul(wmt[:], delta[:, o:o + TCS],
                                            u_sb[:, o:o + TCS]))
                    for n in range(16):
                        k = mt * 16 + n
                        col = mt * 16 + n
                        sl = (k % 2) * TCS
                        p.w(v, "dve", [bct_bb[k], exp_tok(mt, n),
                                       p.tok("dve")])
                        p.t("dve", v.tensor_mul(b_sb[:], wmt[:],
                                                bb_sb[:, sl:sl + TCS]))
                        assert p.cnt["dve"] == bmul_tok[k]
                        w2 = [p.tok("dve")]
                        if k >= 2 and (k - 2) % 16 != 15:
                            w2.append(("gp", ymul_tok[k - 2]))
                        elif k < 2 and ch > 0:
                            w2.append(("gp", prev_ymul62))
                        p.w(v, "dve", w2)
                        p.t("dve", v.tensor_tensor_scan(
                            h_sb[:, sl:sl + TCS], a_sb[:, sl:sl + TCS], b_sb[:],
                            state[:, col:col + 1], ALU.mult, ALU.add))
                        assert p.cnt["dve"] == scan_tok[k]
                        p.w(v, "dve", [p.tok("dve")])
                        p.t("dve", v.tensor_copy(state[:, col:col + 1],
                                                 h_sb[:, sl + TCS - 1:sl + TCS]))
                        if n == 15:
                            p.w(v, "dve", [bct_cc[k], p.tok("dve")])
                            p.t("dve", v.tensor_mul(y15_sb[:],
                                                    h_sb[:, sl:sl + TCS],
                                                    cc_sb[:, sl:sl + TCS]))
                    ysl = ((ch * 4 + mt) % 2) * TCS
                    if mt >= 2:
                        p.w(v, "dve", [("qs", qs_after_lu + 16 * (mt - 1))])
                    p.w(v, "dve", [("qs", qs_after_lu),
                                   ("gp", yadd_tok[mt * 16 + 14]),
                                   p.tok("dve")])
                    p.t("dve", v.scalar_tensor_tensor(
                        t1_sb[:], u_sb[:, o:o + TCS], dcol_sb[:, mt:mt + 1],
                        y_sb[:], ALU.mult, ALU.add))
                    p.w(v, "dve", [p.tok("dve")])
                    p.t("dve", v.tensor_add(t1_sb[:], t1_sb[:], y15_sb[:]))
                    p.w(v, "dve", [p.tok("dve")])
                    p.t("dve", v.tensor_mul(yz_sb[:, ysl:ysl + TCS], t1_sb[:],
                                            zs_sb[:, o:o + TCS]))
                    assert p.cnt["dve"] == fin_tok[mt]
            blk.vector(f_v)
            assert p.cnt["dve"] == vc, (p.cnt["dve"], vc)
            prev_ymul62 = ymul_tok[62]

            def f_ys(sy, ch=ch):
                for mt in range(4):
                    ysl = ((ch * 4 + mt) % 2) * TCS
                    p.w(sy, "qs", [("dve", fin_tok[mt])])
                    p.t("qs", sy.dma_start(
                        out=yszo[ch * 512 + mt * 128:ch * 512 + (mt + 1) * 128, :],
                        in_=yz_sb[:, ysl:ysl + TCS]))
            blk.sync(f_ys)

        def f_end(sy):
            sy.wait_ge(p.sem["qs"], p.cnt["qs"])
        blk.sync(f_end)

def emit_post1(nc, p, blk, IO):
    TP = TPC
    yszg = IO["yszg"]
    woutT = IO["woutT"]
    xT = IO["xT"]
    selW = IO["selW"]
    mdT = IO["mdT"]
    mdrep = IO["mdrep"]
    lupF = IO["lupF"]
    ldgF = IO["ldgF"]
    biasw = IO["biasw"]
    dbwT = IO["dbwT"]
    slope_i = IO["slope_c"]
    center_i = IO["center_c"]
    wq_i = IO["wq"]
    wk_i = IO["wk"]
    wv_i = IO["wv"]
    ones_bf_i = IO["ones_h"]
    ones_b16_i = IO["ones_bf"]
    ones_row_i = IO["ones_row"]
    eps_i = IO["eps_c"]
    identf_i = IO["identf"]
    e8bf_i = IO["e8h"]
    e8f_i = IO["e8f"]
    e8t_i = IO["e8t"]
    iota9_i = IO["iota9"]
    iotan_i = IO["iotan"]
    x2To = IO["x2T"]
    qO = IO["qO"]
    kvpO = IO["kvp"]

    with contextlib.ExitStack() as st_:
        E = st_.enter_context
        ones_bf = E(nc.sbuf_tensor("c_ones_h", [128, 1], F16))
        ones_b16 = E(nc.sbuf_tensor("c_ones_b16", [128, 1], BF16))
        ones_row = E(nc.sbuf_tensor("c_ones_row", [1, 128], F32))
        eps_c = E(nc.sbuf_tensor("c_eps", [128, 1], F32))
        identf = E(nc.sbuf_tensor("c_identf", [128, 128], F32))
        e8bf = E(nc.sbuf_tensor("c_e8h", [8, 64], F16))
        e8f = E(nc.sbuf_tensor("c_e8f", [8, 64], F32))
        e8t = E(nc.sbuf_tensor("c_e8t", [64, 8], BF16))
        iota9 = E(nc.sbuf_tensor("c_iota9", [128, 8], F32))
        iotan = E(nc.sbuf_tensor("c_iotan", [128, 8], F32))
        slo_sb = E(nc.sbuf_tensor("slo_sb", [8, 1], F32))
        cen_sb = E(nc.sbuf_tensor("cen_sb", [8, 1], F32))
        xar = E(nc.sbuf_tensor("xar", [128, 16 * TP], F32))
        A1 = E(nc.sbuf_tensor("A1", [128, 16 * TP], F32))
        xh = E(nc.sbuf_tensor("xh", [128, 16 * TP], F16))
        wslot = E(nc.sbuf_tensor("wslot", [128, 2048], F32))
        selw_bf = E(nc.sbuf_tensor("selw_bf", [128, 16 * 64], F16))
        mdt_bf = E(nc.sbuf_tensor("mdt_bf", [128, 16 * 8], F16))
        mdr_bf = E(nc.sbuf_tensor("mdr_bf", [128, 16 * 64], F16))
        lup_bf = E(nc.sbuf_tensor("lup_bf", [128, 16 * 64], F16))
        dbwt_bf = E(nc.sbuf_tensor("dbwt_bf", [128, 16 * 8], F16))
        bw_bf = E(nc.sbuf_tensor("bw_bf", [8, D], F16))
        ldg_bf = E(nc.sbuf_tensor("ldg_bf", [64, D], F16))
        c1n_sb = E(nc.sbuf_tensor("c1n_sb", [64, 1], F32))
        md2_sb = E(nc.sbuf_tensor("md2_sb", [8, 1], F32))

        xm8_sb = E(nc.sbuf_tensor("xm8_sb", [8, TP], F32))
        nrm_sb = E(nc.sbuf_tensor("nrm_sb", [8, TP], F32))
        t64_sb = E(nc.sbuf_tensor("t64_sb", [64, TP], F32))
        cont_bf = E(nc.sbuf_tensor("cont_bf", [64, TP], BF16))
        negs_sb = E(nc.sbuf_tensor("negs_sb", [8, 1], F32))
        sbias_sb = E(nc.sbuf_tensor("sbias_sb", [8, 1], F32))
        mxc = E(nc.sbuf_tensor("mxc", [128, 8], F32))
        eq_sb = E(nc.sbuf_tensor("eq_sb", [128, 8], F32))
        idxc = E(nc.sbuf_tensor("idxc", [128, 1], F32))
        oneh_t = E(nc.sbuf_tensor("oneh_t", [128, 8], F32))
        onehot8 = E(nc.sbuf_tensor("onehot8", [8, TP], F16))

        k2b_bf = E(nc.sbuf_tensor("k2b_bf", [8, 64], F16))
        mx8_bf = E(nc.sbuf_tensor("mx8_bf", [64, TP], F16))
        d_sb = E(nc.sbuf_tensor("d_sb", [128, TP], F32))
        kvp_sb = E(nc.sbuf_tensor("kvp_sb", [128, 65], F32))
        psA = E(nc.psum_tensor("psA", [128, TP], F32))
        psB = E(nc.psum_tensor("psB", [128, TP], F32))
        psC = E(nc.psum_tensor("psC", [128, TP], F32))
        psD = E(nc.psum_tensor("psD", [64, TP], F32))

        rows = A1[0:1, 12 * TP:16 * TP]
        bc_sb = A1[:, 8 * TP:10 * TP]
        ssbc_sb = A1[:, 10 * TP:11 * TP]
        rrow = bc_sb[0:1, 0:TP]
        ss_sb = bc_sb[0:1, TP:2 * TP]
        rslot = xh.bitcast(F16)[:, 0:2 * TP]
        scrT = d_sb[:, 0:TP]
        smalls32 = wslot[:, 0:1024]
        e_sb = nrm_sb
        inv8_sb = xm8_sb
        a1bf = A1.bitcast(F16)
        a1h = A1.bitcast(F16)
        qkv_ev = d_sb.bitcast(F16)

        def xt(i):
            return xar[:, i * TP:(i + 1) * TP]

        def xht(i):
            return xh[:, i * TP:(i + 1) * TP]

        def f_ld(sy):
            for sb, ap in [(ones_bf, ones_bf_i), (ones_b16, ones_b16_i),
                           (ones_row, ones_row_i),
                           (eps_c, eps_i), (identf, identf_i), (e8bf, e8bf_i),
                           (e8f, e8f_i), (e8t, e8t_i), (iota9, iota9_i),
                           (iotan, iotan_i), (slo_sb, slope_i),
                           (cen_sb, center_i)]:
                p.t("qs", sy.dma_start(out=sb[:], in_=ap))
            for i in range(16):
                p.t("qs", sy.dma_start(out=xt(i), in_=xT[i * 128:(i + 1) * 128, :]))
        blk.sync(f_ld)
        ld0 = p.tok("qs")

        yar = xh.bitcast(F16)
        for half in range(2):
            ctoks = []
            for kk in range(16):
                kt = half * 16 + kk

                def f_lw(sy, kt=kt, kk=kk):
                    if kk >= 1:
                        p.w(sy, "qs", [ctoks[kk - 1]])
                    elif half > 0:
                        p.w(sy, "qs", [("act", p.cnt["act"])])
                    p.t("qs", sy.dma_start(
                        out=wslot[:, 0:2048],
                        in_=woutT[kt * 128:(kt + 1) * 128, :]))
                blk.sync(f_lw)
                lwt = p.tok("qs")

                def f_cw(sc, kk=kk, lwt=lwt, half=half):
                    p.w(sc, "act", [lwt])
                    if half > 0 and kk == 0:
                        p.w(sc, "act", [("pe", p.cnt["pe"])])
                    ctoks.append(p.t("act", sc.copy(
                        a1h[:, kk * 2048:(kk + 1) * 2048], wslot[:, 0:2048])))
                blk.scalar(f_cw)
            wc = p.tok("act")

            def f_lys(sy, half=half):
                p.w(sy, "qg", [("pe", p.cnt["pe"])])
                for kk in range(16):
                    kt = half * 16 + kk
                    p.t("qg", sy.dma_start(
                        out=yar[:, kk * TP:(kk + 1) * TP],
                        in_=yszg[kt * 128:(kt + 1) * 128, :]))
            blk.gpsimd(f_lys)
            ys_done = p.tok("qg")

            pe_b = p.cnt["pe"]
            dv_b = p.cnt["dve"]

            def f_mm(te, half=half, pe_b=pe_b, dv_b=dv_b, wc=wc,
                     ys_done=ys_done):
                p.w(te, "pe", [wc, ys_done])
                for mt in range(16):
                    ps = psA if mt % 2 == 0 else psB
                    if mt >= 2:
                        p.w(te, "pe", [("dve", dv_b + mt - 1)])
                    elif half > 0:
                        p.w(te, "pe", [("dve", dv_b)])
                    for kk in range(16):
                        i1 = MM(p, te, ps[:], a1h[:, kk * 2048 + mt * 128:
                                                  kk * 2048 + (mt + 1) * 128],
                                yar[:, kk * TP:(kk + 1) * TP],
                                kk == 0, kk == 15)
                    p.t("pe", i1)
            blk.tensor(f_mm)

            def f_ev(v, pe_b=pe_b):
                for mt in range(16):
                    ps = psA if mt % 2 == 0 else psB
                    p.w(v, "dve", [("pe", pe_b + mt + 1), ld0])
                    p.t("dve", v.tensor_add(xt(mt), xt(mt), ps[:]))
            blk.vector(f_ev)
        x1_done = p.tok("dve")


        ln2 = emit_ln(p, blk, "", xar, TP, xh, rows, psA, psB, psC,
                      a1bf[:, 0:16 * TP], bc_sb[:, 0:TP], bc_sb[:, TP:2 * TP],
                      ones_bf, ones_row, eps_c[:, :], [x1_done], scrT=scrT)

        def small_cast(src_ap, dst_bf, w):
            def f_l(sy):
                p.w(sy, "qs", [("act", p.cnt["act"])])
                p.t("qs", sy.dma_start(out=smalls32[:, 0:16 * w],
                                       in_=src_ap.rearrange("(a p) m -> p a m",
                                                            p=128)))
            blk.sync(f_l)
            lt = p.tok("qs")

            def f_c(sc, lt=lt):
                p.w(sc, "act", [lt])
                p.t("act", sc.copy(dst_bf[:, 0:16 * w], smalls32[:, 0:16 * w]))
            blk.scalar(f_c)

        small_cast(selW, selw_bf, 64)
        small_cast(mdT, mdt_bf, 8)
        small_cast(mdrep, mdr_bf, 64)
        small_cast(lupF, lup_bf, 64)
        small_cast(dbwT, dbwt_bf, 8)

        def f_lbw(sy):
            p.w(sy, "qs", [("act", p.cnt["act"])])
            p.t("qs", sy.dma_start(out=wslot[0:8, :], in_=biasw))
        blk.sync(f_lbw)
        t_ = p.tok("qs")

        def f_cbw(sc, t_=t_):
            p.w(sc, "act", [t_])
            p.t("act", sc.copy(bw_bf[:], wslot[0:8, :]))
        blk.scalar(f_cbw)

        def f_lldg(sy):
            p.w(sy, "qs", [("act", p.cnt["act"])])
            p.t("qs", sy.dma_start(out=wslot[0:64, :], in_=ldgF))
        blk.sync(f_lldg)
        t_ = p.tok("qs")

        def f_cldg(sc, t_=t_):
            p.w(sc, "act", [t_])
            p.t("act", sc.copy(ldg_bf[:], wslot[0:64, :]))
        blk.scalar(f_cldg)
        smalls_done = p.tok("act")

        def f_u64(te):
            p.w(te, "pe", [smalls_done] + ln2)
            for kt in range(16):
                i1 = MM(p, te, psD[:], selw_bf[:, kt * 64:(kt + 1) * 64],
                        xht(kt), kt == 0, kt == 15)
            p.t("pe", i1)
            for kt in range(16):
                i2 = MM(p, te, psB[0:8, :], mdt_bf[:, kt * 8:(kt + 1) * 8],
                        xht(kt), kt == 0, kt == 15)
            p.t("pe", i2)
        blk.tensor(f_u64)
        u64_done = p.tok("pe")

        def f_sq(sc):
            p.w(sc, "act", [u64_done] + ln2)
            for kt in range(16):
                p.t("act", sc.activation(a1bf[:, kt * TP:(kt + 1) * TP],
                                         xht(kt), AF.Square))
        blk.scalar(f_sq)
        sq_done = p.tok("act")

        def f_r2(te):
            p.w(te, "pe", [sq_done])
            for kt in range(16):
                i1 = MM(p, te, psA[0:1, :], ones_bf[:],
                        a1bf[:, kt * TP:(kt + 1) * TP], kt == 0, kt == 15)
            p.t("pe", i1)
        blk.tensor(f_r2)
        r2_done = p.tok("pe")

        def f_prod(v):
            p.w(v, "dve", [smalls_done])
            p.t("dve", v.tensor_mul(smalls32.bitcast(BF16)[:, 0:16 * 64],
                                    selw_bf[:], mdr_bf[:]))
        blk.vector(f_prod)
        prod_done = p.tok("dve")

        def f_mdsq(sc):
            p.w(sc, "act", [prod_done])
            p.t("act", sc.activation(smalls32.bitcast(BF16)[:, 16 * 64:16 * 72],
                                     mdt_bf[:], AF.Square))
        blk.scalar(f_mdsq)
        mdsq_done = p.tok("act")

        def f_c1(te):
            p.w(te, "pe", [prod_done, mdsq_done, r2_done])
            pbf = smalls32.bitcast(BF16)
            for kt in range(16):
                i1 = te.matmul(psC[0:64, 0:1], pbf[:, kt * 64:(kt + 1) * 64],
                               ones_b16[:], start=(kt == 0), stop=(kt == 15))
            p.t("pe", i1)
            for kt in range(16):
                i2 = te.matmul(psC[0:8, 4:5], pbf[:, 16 * 64 + kt * 8:
                                                  16 * 64 + (kt + 1) * 8],
                               ones_b16[:], start=(kt == 0), stop=(kt == 15))
            p.t("pe", i2)
        blk.tensor(f_c1)
        c1_done = p.tok("pe")

        def f_ev1(v):
            p.w(v, "dve", [c1_done])
            p.t("dve", v.tensor_scalar_mul(c1n_sb[:], psC[0:64, 0:1], -1.0))
            p.t("dve", v.tensor_copy(md2_sb[:], psC[0:8, 4:5]))
            p.t("dve", v.tensor_copy(rrow[:], psA[0:1, :]))
            p.t("dve", v.tensor_copy(xm8_sb[:], psB[0:8, :]))
        blk.vector(f_ev1)
        ev1_done = p.tok("dve")

        def f_r2b(te):
            p.w(te, "pe", [ev1_done])
            p.t("pe", MM(p, te, psB[0:8, :], ones_row[0:1, 0:8], rrow[:],
                         True, True))
        blk.tensor(f_r2b)
        r2b_done = p.tok("pe")

        def f_nrm(v):
            p.w(v, "dve", [r2b_done])
            p.t("dve", v.scalar_tensor_tensor(nrm_sb[:], xm8_sb[:], -2.0,
                                              psB[0:8, :], ALU.mult, ALU.add))
            p.w(v, "dve", [p.tok("dve")])
            p.t("dve", v.tensor_scalar_add(nrm_sb[:], nrm_sb[:],
                                           md2_sb[0:8, 0:1]))
            p.t("dve", v.tensor_mul(sbias_sb[:], slo_sb[:], cen_sb[:]))
            p.t("dve", v.tensor_scalar_mul(negs_sb[:], slo_sb[:], -1.0))
        blk.vector(f_nrm)
        nrm_done = p.tok("dve")

        def f_inv(sc):
            p.w(sc, "act", [nrm_done])
            p.t("act", sc.activation(inv8_sb[:], nrm_sb[:], AF.Ln))
            p.w(sc, "act", [p.tok("act")])
            p.t("act", sc.activation(inv8_sb[:], inv8_sb[:], AF.Exp, scale=-0.5))
        blk.scalar(f_inv)
        inv_done = p.tok("act")

        def f_invbc(te):
            p.w(te, "pe", [inv_done])
            p.t("pe", MM(p, te, psC[0:64, :], e8f[:], inv8_sb[:], True, True))
        blk.tensor(f_invbc)
        invbc_done = p.tok("pe")

        def f_cont1(sc):
            p.w(sc, "act", [invbc_done, ("dve", p.cnt["dve"])])
            p.t("act", sc.activation(t64_sb[:], psD[:], AF.Identity,
                                     bias=c1n_sb[0:64, 0:1]))
        blk.scalar(f_cont1)
        cont1 = p.tok("act")

        def f_cont2(v):
            p.w(v, "dve", [cont1])
            p.t("dve", v.tensor_mul(t64_sb[:], t64_sb[:], psC[0:64, :]))
        blk.vector(f_cont2)
        cont2 = p.tok("dve")

        def f_cont3(sc):
            p.w(sc, "act", [cont2])
            p.t("act", sc.activation(cont_bf[:], t64_sb[:], AF.Square))
        blk.scalar(f_cont3)
        cont3 = p.tok("act")

        def f_score(te):
            p.w(te, "pe", [cont3, ("dve", p.cnt["dve"])])
            p.t("pe", MM(p, te, psB[0:8, :], e8t[:], cont_bf[:], True, True))
        blk.tensor(f_score)
        score_done = p.tok("pe")

        def f_gate1(sc):
            p.w(sc, "act", [score_done, ("dve", p.cnt["dve"])])
            p.t("act", sc.activation(e_sb[:], psB[0:8, :], AF.Exp,
                                     bias=sbias_sb[0:8, 0:1],
                                     scale=negs_sb[0:8, 0:1]))
        blk.scalar(f_gate1)
        gate1 = p.tok("act")

        def f_gate2(v):
            p.w(v, "dve", [gate1])
            p.t("dve", v.tensor_scalar_add(e_sb[:], e_sb[:], 1.0))
            p.w(v, "dve", [p.tok("dve")])
            p.t("dve", v.reciprocal(e_sb[:], e_sb[:]))
        blk.vector(f_gate2)
        gate_done = p.tok("dve")

        for j in range(8):
            def f_tr(te, j=j):
                p.w(te, "pe", [gate_done, ("dve", p.cnt["dve"]),
                               ("act", p.cnt["act"])])
                p.t("pe", te.transpose(psA[:, 0:8],
                                       e_sb[0:8, j * 128:(j + 1) * 128],
                                       identf[0:8, 0:8]))
            blk.tensor(f_tr)
            trt = p.tok("pe")

            def f_sel(v, j=j, trt=trt):
                p.w(v, "dve", [trt])
                p.t("dve", v.reduce_max(mxc[:, j:j + 1], psA[:, 0:8], AX.X))
                p.w(v, "dve", [p.tok("dve")])
                p.t("dve", v.tensor_single_scalar(eq_sb[:], psA[:, 0:8],
                                                  mxc[:, j:j + 1], ALU.is_equal))
                p.w(v, "dve", [p.tok("dve")])
                p.t("dve", v.scalar_tensor_tensor(eq_sb[:], eq_sb[:], -9.0,
                                                  iota9[:], ALU.mult, ALU.add))
                p.w(v, "dve", [p.tok("dve")])
                p.t("dve", v.tensor_reduce(idxc[:], eq_sb[:], AX.X, ALU.min))
                p.w(v, "dve", [p.tok("dve")])
                p.t("dve", v.tensor_single_scalar(oneh_t[:], iotan[:],
                                                  idxc[:, 0:1], ALU.is_equal))
            blk.vector(f_sel)
            selt = p.tok("dve")

            def f_trb(te, j=j, selt=selt):
                p.w(te, "pe", [selt, ("act", p.cnt["act"])])
                p.t("pe", te.transpose(psD[0:8, 0:128], oneh_t[:], identf[:]))
                p.t("pe", te.transpose(psD[0:1, 128:256], mxc[:, j:j + 1],
                                       identf[:]))
            blk.tensor(f_trb)
            trb = p.tok("pe")

            def f_cp(sc, j=j, trb=trb):
                p.w(sc, "act", [trb])
                p.t("act", sc.copy(onehot8[:, j * 128:(j + 1) * 128],
                                   psD[0:8, 0:128]))
                p.t("act", sc.copy(ss_sb[:, j * 128:(j + 1) * 128],
                                   psD[0:1, 128:256]))
            blk.scalar(f_cp)
        oneh_done = p.tok("act")

        def f_xl(te):
            p.w(te, "pe", [oneh_done, ("dve", p.cnt["dve"])])
            for kt in range(16):
                i1 = MM(p, te, psC[0:64, :], lup_bf[:, kt * 64:(kt + 1) * 64],
                        xht(kt), kt == 0, kt == 15)
            p.t("pe", i1)
            for kt in range(16):
                i2 = te.matmul(psB[0:8, 0:64], dbwt_bf[:, kt * 8:(kt + 1) * 8],
                               lup_bf[:, kt * 64:(kt + 1) * 64],
                               start=(kt == 0), stop=(kt == 15))
            p.t("pe", i2)
        blk.tensor(f_xl)
        xl_done = p.tok("pe")

        def f_evxl(sc):
            p.w(sc, "act", [xl_done])
            p.t("act", sc.copy(t64_sb[:], psC[0:64, :]))
            p.t("act", sc.copy(k2b_bf[:], psB[0:8, 0:64]))
        blk.scalar(f_evxl)
        evxl = p.tok("act")

        def f_t2(te):
            p.w(te, "pe", [evxl])
            p.t("pe", MM(p, te, psD[:], k2b_bf[:], onehot8[:], True, True))
            p.t("pe", MM(p, te, psB[0:64, :], e8bf[:], onehot8[:], True, True))
            p.t("pe", MM(p, te, psC[:], ones_row[:], ss_sb[:], True, True))
        blk.tensor(f_t2)
        t2_done = p.tok("pe")

        def f_mx8(v):
            p.w(v, "dve", [t2_done])
            p.t("dve", v.tensor_sub(t64_sb[:], t64_sb[:], psD[:]))
            p.w(v, "dve", [p.tok("dve")])
            p.t("dve", v.tensor_mul(mx8_bf[:], t64_sb[:], psB[0:64, :]))
            p.t("dve", v.tensor_copy(ssbc_sb[:], psC[:]))
        blk.vector(f_mx8)
        mx8_done = p.tok("dve")

        for mt in range(16):
            ps = psA if mt % 2 == 0 else psB

            def f_bd(te, mt=mt, ps=ps):
                p.w(te, "pe", [mx8_done, ("dve", p.cnt["dve"])])
                MM(p, te, ps[:], bw_bf[:, mt * 128:(mt + 1) * 128], onehot8[:],
                   True, False)
                i1 = MM(p, te, ps[:], ldg_bf[:, mt * 128:(mt + 1) * 128],
                        mx8_bf[:], False, True)
                p.t("pe", i1)
            blk.tensor(f_bd)
            bd = p.tok("pe")

            def f_x2(v, mt=mt, ps=ps, bd=bd):
                p.w(v, "dve", [bd])
                p.t("dve", v.tensor_sub(d_sb[:], ps[:], xht(mt)))
                p.w(v, "dve", [p.tok("dve")])
                p.t("dve", v.tensor_mul(d_sb[:], d_sb[:], ssbc_sb[:]))
                p.w(v, "dve", [p.tok("dve")])
                p.t("dve", v.tensor_add(d_sb[:], d_sb[:], xht(mt)))
                p.w(v, "dve", [p.tok("dve")])
                p.t("dve", v.tensor_add(xt(mt), xt(mt), d_sb[:]))
            blk.vector(f_x2)
        x2_done = p.tok("dve")

        def f_stx2(sy):
            p.w(sy, "qs", [x2_done])
            for i in range(16):
                p.t("qs", sy.dma_start(out=x2To[i * 128:(i + 1) * 128, :],
                                       in_=xt(i)))
        blk.sync(f_stx2)

        ln3 = emit_ln(p, blk, "3", xar, TP, xh, rows, psA, psB, psC,
                      a1bf[:, 0:16 * TP], bc_sb[:, 0:TP], bc_sb[:, TP:2 * TP],
                      ones_bf, ones_row, eps_c[:, :],
                      [x2_done, ("qs", p.cnt["qs"])], scrT=scrT)

        war = xar.bitcast(F16)
        kvar = A1.bitcast(F16)

        def wload(w_ap):
            guard = [("qs", p.cnt["qs"]), ("dve", p.cnt["dve"]),
                     ("pe", p.cnt["pe"])]
            for kt in range(16):
                def f_l(sy, kt=kt):
                    p.w(sy, "qs", [("act", p.cnt["act"]), ("pe", p.cnt["pe"])])
                    p.t("qs", sy.dma_start(out=wslot[:],
                                           in_=w_ap[kt * 128:(kt + 1) * 128, :]))
                blk.sync(f_l)
                lt = p.tok("qs")

                def f_c(sc, kt=kt, lt=lt, guard=guard):
                    p.w(sc, "act", [lt] + guard)
                    p.t("act", sc.copy(war[:, kt * 2048:(kt + 1) * 2048],
                                       wslot[:]))
                blk.scalar(f_c)
            return p.tok("act")

        wq_done = wload(wq_i)
        for mt in range(16):
            ps = psA if mt % 2 == 0 else psB

            def f_qm(te, mt=mt, ps=ps, wq_done=wq_done):
                p.w(te, "pe", [wq_done] + ln3 + [("act", p.cnt["act"])])
                for kt in range(16):
                    i1 = MM(p, te, ps[:], war[:, kt * 2048 + mt * 128:
                                              kt * 2048 + (mt + 1) * 128],
                            xht(kt), kt == 0, kt == 15)
                p.t("pe", i1)
            blk.tensor(f_qm)
            qm = p.tok("pe")

            def f_qe(sc, mt=mt, ps=ps, qm=qm):
                p.w(sc, "act", [qm, ("qs", p.cnt["qs"])])
                p.t("act", sc.activation(qkv_ev[:, 0:TP], ps[:], AF.Relu))
            blk.scalar(f_qe)
            qe = p.tok("act")

            def f_qs(sy, mt=mt, qe=qe):
                p.w(sy, "qs", [qe])
                p.t("qs", sy.dma_start(out=qO[mt * 128:(mt + 1) * 128, :],
                                       in_=qkv_ev[:, 0:TP]))
            blk.sync(f_qs)

        for which, w_ap, act in [(0, wk_i, AF.Relu), (1, wv_i, AF.Identity)]:
            wdone = wload(w_ap)
            for ms in range(8):
                for nh in range(2):
                    ps = psA if (ms * 2 + nh) % 2 == 0 else psB

                    def f_km(te, ms=ms, nh=nh, ps=ps, wdone=wdone):
                        p.w(te, "pe", [wdone, ("act", p.cnt["act"])] + ln3)
                        for kt in range(16):
                            i1 = MM(p, te, ps[:],
                                    xh[:, kt * TP + ms * 128:
                                       kt * TP + (ms + 1) * 128],
                                    war[:, kt * 2048 + nh * 1024:
                                        kt * 2048 + (nh + 1) * 1024],
                                    kt == 0, kt == 15)
                        p.t("pe", i1)
                    blk.tensor(f_km)
                    km = p.tok("pe")

                    def f_ke(sc, which=which, ms=ms, nh=nh, ps=ps, km=km,
                             act=act):
                        p.w(sc, "act", [km])
                        dst = kvar[:, which * 16 * TP + ms * 2048 + nh * 1024:
                                   which * 16 * TP + ms * 2048 +
                                   (nh + 1) * 1024]
                        p.t("act", sc.activation(dst, ps[:], act))
                    blk.scalar(f_ke)
        kv_in_done = p.tok("act")

        def kcol(which, ms, inner):
            return which * 16 * TP + ms * 2048 + (inner // 1024) * 1024 + \
                (inner % 1024)

        for w in range(16):
            def f_kv(te, w=w):
                p.w(te, "pe", [kv_in_done, ("dve", p.cnt["dve"])])
                for h2 in range(2):
                    h = 2 * w + h2
                    for ms in range(8):
                        c0 = kcol(0, ms, h * 64)
                        c1 = kcol(1, ms, h * 64)
                        te.matmul(psC[h2 * 64:h2 * 64 + 64, 0:64],
                                  kvar[:, c0:c0 + 64], kvar[:, c1:c1 + 64],
                                  start=(ms == 0), stop=(ms == 7))
                for ms in range(8):
                    c0 = kcol(0, ms, w * 128)
                    i2 = te.matmul(psC[0:128, 64:65], kvar[:, c0:c0 + 128],
                                   ones_bf[:], start=(ms == 0), stop=(ms == 7))
                p.t("pe", i2)
            blk.tensor(f_kv)
            kvt = p.tok("pe")

            def f_kve(v, kvt=kvt):
                p.w(v, "dve", [kvt, ("qs", p.cnt["qs"])])
                p.t("dve", v.tensor_copy(kvp_sb[:], psC[:, 0:65]))
            blk.vector(f_kve)
            kve = p.tok("dve")

            def f_kvs(sy, w=w, kve=kve):
                p.w(sy, "qs", [kve])
                p.t("qs", sy.dma_start(out=kvpO[w * 128:(w + 1) * 128, :],
                                       in_=kvp_sb[:]))
            blk.sync(f_kvs)

        def f_fin(sy):
            sy.wait_ge(p.sem["qs"], p.cnt["qs"])
        blk.sync(f_fin)



def emit_post2(nc, p, blk, IO):
    TP = TPC
    x2T = IO["x2T"]
    qI = IO["qO"]
    kvs = IO["kvs"]
    wo_i = IO["wo"]
    bo_i = IO["bo_c"]
    w1q_i = IO["w1qf"]
    w2q_i = IO["w2qf"]
    wtot = IO["wtot"]
    ones_bf_i = IO["ones_h"]
    ones_row_i = IO["ones_row"]
    eps_i = IO["eps_c"]
    identf_i = IO["identf"]
    ehw_i = IO["ehw"]
    fhw_i = IO["fhw"]
    outT = IO["outT"]
    x3d = IO["x3d"]

    with contextlib.ExitStack() as st_:
        E = st_.enter_context
        ones_bf = E(nc.sbuf_tensor("c_ones_h", [128, 1], F16))
        ones_row = E(nc.sbuf_tensor("c_ones_row", [1, 128], F32))
        eps_c = E(nc.sbuf_tensor("c_eps", [128, 1], F32))
        identf = E(nc.sbuf_tensor("c_identf", [128, 128], F32))
        ehw = E(nc.sbuf_tensor("c_ehw", [128, 16 * 32], F32))

        bo_sb = E(nc.sbuf_tensor("bo_sb", [128, 16], F32))
        wab = E(nc.sbuf_tensor("wab", [1, 2], F32))
        H1 = E(nc.sbuf_tensor("H1", [128, 16 * TP], F32))
        H2 = E(nc.sbuf_tensor("H2", [128, 16 * TP], F32))
        q_sb = E(nc.sbuf_tensor("q_sb", [128, 16 * TP], F16))

        qs_sb = E(nc.sbuf_tensor("qs_sb", [128, TP], F32))
        evslot = E(nc.sbuf_tensor("evslot", [128, TP], F32))
        x2slot = E(nc.sbuf_tensor("x2slot", [128, TP], F32))
        rowblk = E(nc.sbuf_tensor("rowblk", [1, 6 * TP], F32))
        srows = E(nc.sbuf_tensor("srows", [1, 8], F32))
        rs1bc = E(nc.sbuf_tensor("rs1bc", [128, TP], F32))
        schbc = E(nc.sbuf_tensor("schbc", [128, TP], F32))

        psA = E(nc.psum_tensor("psA", [128, TP], F32))
        psB = E(nc.psum_tensor("psB", [128, TP], F32))
        psC = E(nc.psum_tensor("psC", [128, TP], F32))
        psD = E(nc.psum_tensor("psD", [64, TP], F32))

        woar = H1.bitcast(F16)
        nrm_sb = rs1bc[0:32, :]
        rs2bc = qs_sb
        wtile = ehw.bitcast(BF16)[:, 0:4 * 128]
        kv_sb = H1[:, 0:16 * 65]
        kvt_sb = H1[:, 16 * 65:32 * 65]
        kvbf = H1.bitcast(F16)[:, 64 * 65:80 * 65]
        fhw = H2[0:32, 8 * TP:8 * TP + 16 * 128]
        qkvn = H2.bitcast(F16)
        hf16_lo = H2.bitcast(F16)
        hf16_hi = H1.bitcast(F16)
        rows = rowblk[0:1, 0:4 * TP]
        amaxr = rowblk[0:1, 4 * TP:5 * TP]
        sclr = rowblk[0:1, 5 * TP:6 * TP]

        def qt(i):
            return q_sb[:, i * TP:(i + 1) * TP]

        def f_ld(sy):
            for sb, ap in [(ones_bf, ones_bf_i), (ones_row, ones_row_i),
                           (eps_c, eps_i), (identf, identf_i), (ehw, ehw_i),
                           (bo_sb, bo_i)]:
                p.t("qs", sy.dma_start(out=sb[:], in_=ap))
            p.t("qs", sy.dma_start(out=wab[:, 0:1], in_=wtot[0:1, :]))
            p.t("qs", sy.dma_start(out=wab[:, 1:2], in_=wtot[1:2, :]))
            for i in range(16):
                p.t("qs", sy.dma_start(out=qt(i), in_=qI[i * 128:(i + 1) * 128, :]))
            p.t("qs", sy.dma_start(out=fhw[:], in_=fhw_i))
            p.t("qs", sy.dma_start(out=kv_sb[:], in_=kvs.rearrange(
                "(a p) m -> p a m", p=128)))
        blk.sync(f_ld)
        ld0 = p.tok("qs")
        kvsum = ld0

        def f_ckv(sc):
            p.w(sc, "act", [kvsum])
            p.t("act", sc.copy(kvbf[:], kv_sb[:]))
        blk.scalar(f_ckv)
        kvc = p.tok("act")

        for w in range(16):
            def f_qs(v, w=w):
                p.w(v, "dve", [kvsum, ld0, ("pe", p.cnt["pe"])])
                p.t("dve", v.tensor_scalar_mul(qs_sb[:], qt(w),
                                               kv_sb[:, w * 65 + 64:w * 65 + 65]))
            blk.vector(f_qs)
            qst = p.tok("dve")

            def f_nm(te, w=w, qst=qst):
                p.w(te, "pe", [qst])
                i1 = MM(p, te, psB[0:32, :], ehw[:, w * 32:(w + 1) * 32],
                        qs_sb[:], w == 0, w == 15)
                p.t("pe", i1)
            blk.tensor(f_nm)
        nm_done = p.tok("pe")

        def f_rn(v):
            p.w(v, "dve", [nm_done])
            p.t("dve", v.tensor_single_scalar(nrm_sb[:], psB[0:32, :], 1e-6,
                                              ALU.max))
            p.w(v, "dve", [p.tok("dve")])
            p.t("dve", v.reciprocal(nrm_sb[:], nrm_sb[:]))
        blk.vector(f_rn)
        rn_done = p.tok("dve")

        for w in range(16):
            def f_qkv(te, w=w):
                p.w(te, "pe", [kvc, rn_done, ("act", p.cnt["act"]),
                               ("dve", p.cnt["dve"])])
                MM(p, te, psA[0:64, :], kvbf[0:64, w * 65:w * 65 + 64],
                   q_sb[0:64, w * TP:(w + 1) * TP], True, True)
                i1 = MM(p, te, psA[64:128, :],
                        kvbf[64:128, w * 65:w * 65 + 64],
                        q_sb[64:128, w * TP:(w + 1) * TP], True, True)
                p.t("pe", i1)
                i2 = MM(p, te, psC[:], fhw[:, w * 128:(w + 1) * 128], nrm_sb[:],
                        True, True)
                p.t("pe", i2)
            blk.tensor(f_qkv)
            qkvt = p.tok("pe")

            def f_qe(sc, w=w, qkvt=qkvt):
                p.w(sc, "act", [qkvt])
                p.t("act", sc.copy(evslot[:], psA[:]))
            blk.scalar(f_qe)
            qev = p.tok("act")

            def f_qn(v, w=w, qev=qev):
                p.w(v, "dve", [qev])
                p.t("dve", v.tensor_mul(qkvn[:, w * TP:(w + 1) * TP],
                                        evslot[:], psC[:]))
            blk.vector(f_qn)
        qkvn_done = p.tok("dve")

        def wload16(w_ap, dst, slot32):
            for kt in range(16):
                def f_l(sy, kt=kt):
                    p.w(sy, "qs", [("act", p.cnt["act"]), ("pe", p.cnt["pe"]),
                                   ("dve", p.cnt["dve"])])
                    p.t("qs", sy.dma_start(out=slot32,
                                           in_=w_ap[kt * 128:(kt + 1) * 128, :]))
                blk.sync(f_l)
                lt = p.tok("qs")

                def f_c(sc, kt=kt, lt=lt):
                    p.w(sc, "act", [lt])
                    p.t("act", sc.copy(dst[:, kt * 2048:(kt + 1) * 2048],
                                       slot32))
                blk.scalar(f_c)
            return p.tok("act")

        wslot2 = H2[:, 12 * TP:14 * TP]
        wo_done = wload16(wo_i, woar, wslot2)

        for mt in range(16):
            ps = psA if mt % 2 == 0 else psB

            def f_lx2(sy, mt=mt):
                p.w(sy, "qs", [("dve", p.cnt["dve"]), ("qs", p.cnt["qs"])])
                p.t("qs", sy.dma_start(out=x2slot[:],
                                       in_=x2T[mt * 128:(mt + 1) * 128, :]))
            blk.sync(f_lx2)
            lx2 = p.tok("qs")

            def f_wo(te, mt=mt, ps=ps):
                p.w(te, "pe", [wo_done, qkvn_done, ("dve", p.cnt["dve"])])
                for kt in range(16):
                    i1 = MM(p, te, ps[:], woar[:, kt * 2048 + mt * 128:
                                               kt * 2048 + (mt + 1) * 128],
                            qkvn[:, kt * TP:(kt + 1) * TP], kt == 0, kt == 15)
                p.t("pe", i1)
            blk.tensor(f_wo)
            wot = p.tok("pe")

            def f_x3(v, mt=mt, ps=ps, wot=wot, lx2=lx2):
                p.w(v, "dve", [wot, lx2])
                p.t("dve", v.scalar_tensor_tensor(x2slot[:], ps[:],
                                                  bo_sb[:, mt:mt + 1],
                                                  x2slot[:], ALU.add, ALU.add))
            blk.vector(f_x3)
            x3t = p.tok("dve")

            def f_sx3(sy, mt=mt, x3t=x3t):
                p.w(sy, "qs", [x3t])
                p.t("qs", sy.dma_start(out=x3d[mt * 128:(mt + 1) * 128, :],
                                       in_=x2slot[:]))
            blk.sync(f_sx3)
        x3_done = p.tok("qs")

        def f_lx3(sy):
            p.w(sy, "qs", [x3_done, ("pe", p.cnt["pe"]),
                           ("dve", p.cnt["dve"]), ("act", p.cnt["act"])])
            for i in range(16):
                p.t("qs", sy.dma_start(out=H2[:, i * TP:(i + 1) * TP],
                                       in_=x3d[i * 128:(i + 1) * 128, :]))
        blk.sync(f_lx3)
        x3l = p.tok("qs")

        ln4 = emit_ln(p, blk, "4", H2, TP, q_sb, rows, psA, psB, psC,
                      H1.bitcast(F16)[:, 0:16 * TP], rs1bc[:, 0:TP],
                      schbc[:, 0:TP], ones_bf, ones_row, eps_c[:, :],
                      [x3l], xh32_sb=H1, scrT=evslot[:, 0:TP])

        def f_amax1(v):
            p.w(v, "dve", ln4)
            p.t("dve", v.tensor_reduce(
                evslot[:], H1[:].rearrange("p (a t) -> p t a", a=16),
                AX.X, ALU.max, apply_absolute_value=True))
        blk.vector(f_amax1)
        am1 = p.tok("dve")

        for j in range(8):
            def f_amt(te, j=j, am1=am1):
                p.w(te, "pe", [am1, ("dve", p.cnt["dve"]),
                               ("act", p.cnt["act"])])
                p.t("pe", te.transpose(psA[:, 0:128],
                                       evslot[:, j * 128:(j + 1) * 128],
                                       identf[:]))
            blk.tensor(f_amt)
            amt = p.tok("pe")

            def f_amr(v, j=j, amt=amt):
                p.w(v, "dve", [amt])
                p.t("dve", v.reduce_max(qs_sb[:, j:j + 1], psA[:, 0:128], AX.X))
            blk.vector(f_amr)
            amr = p.tok("dve")

            def f_amt2(te, j=j, amr=amr):
                p.w(te, "pe", [amr])
                p.t("pe", te.transpose(psB[0:1, 0:128], qs_sb[:, j:j + 1],
                                       identf[:]))
            blk.tensor(f_amt2)
            amt2 = p.tok("pe")

            def f_amc(sc, j=j, amt2=amt2):
                p.w(sc, "act", [amt2])
                p.t("act", sc.copy(amaxr[0:1, j * 128:(j + 1) * 128],
                                   psB[0:1, 0:128]))
            blk.scalar(f_amc)
        amax_done = p.tok("act")

        def f_srow(v):
            p.w(v, "dve", [amax_done, ld0])
            p.t("dve", v.tensor_scalar_mul(srows[0:1, 0:2], wab[0:1, 0:2],
                                           1.0 / (DFF * D)))
            p.w(v, "dve", [p.tok("dve")])
            p.t("dve", v.tensor_single_scalar(amaxr[:], amaxr[:], 1e-5,
                                              ALU.max))
            p.w(v, "dve", [p.tok("dve")])
            p.t("dve", v.reciprocal(sclr[:], amaxr[:]))
            p.w(v, "dve", [p.tok("dve")])
            p.t("dve", v.tensor_scalar_mul(sclr[:], sclr[:], 127.0))
            p.t("dve", v.tensor_scalar_mul(rows[0:1, 0:TP], amaxr[:],
                                           1.0 / 127.0))
            p.w(v, "dve", [p.tok("dve")])
            p.t("dve", v.tensor_scalar_mul(rows[0:1, 0:TP], rows[0:1, 0:TP],
                                           srows[0:1, 0:1]))
        blk.vector(f_srow)
        srow_done = p.tok("dve")

        def f_bc1(te):
            p.w(te, "pe", [srow_done])
            p.t("pe", MM(p, te, psC[:], ones_row[:], sclr[:], True, True))
        blk.tensor(f_bc1)
        b1 = p.tok("pe")

        def f_bc1e(sc):
            p.w(sc, "act", [b1])
            p.t("act", sc.copy(schbc[:], psC[:]))
        blk.scalar(f_bc1e)

        def f_bc2(te):
            p.w(te, "pe", [("act", p.cnt["act"])])
            p.t("pe", MM(p, te, psC[:], ones_row[:], rows[0:1, 0:TP], True, True))
        blk.tensor(f_bc2)
        b2 = p.tok("pe")

        def f_bc2e(sc):
            p.w(sc, "act", [b2])
            p.t("act", sc.copy(rs1bc[:], psC[:]))
        blk.scalar(f_bc2e)
        bc_done = p.tok("act")

        for i in range(16):
            def f_xq(v, i=i):
                p.w(v, "dve", [bc_done])
                p.t("dve", v.tensor_mul(evslot[:], H1[:, i * TP:(i + 1) * TP],
                                        schbc[:]))
                p.w(v, "dve", [p.tok("dve")])
                p.t("dve", v.tensor_scalar(q_sb.bitcast(BF16)[:, i * TP:(i + 1) * TP],
                                           evslot[:], MAGIC, MAGIC,
                                           ALU.add, ALU.subtract))
            blk.vector(f_xq)
        xq_done = p.tok("dve")

        def htile(k):
            if k < 32:
                return hf16_lo[:, k * TP:(k + 1) * TP]
            return hf16_hi[:, (k - 32) * TP:(k - 31) * TP]

        for mt2 in range(64):
            ps = psA if mt2 % 2 == 0 else psB

            def f_lw1(sy, mt2=mt2):
                if mt2 == 0:
                    sy.wait_ge(IO["ccs3"], 1)
                p.w(sy, "qs", [("pe", p.cnt["pe"])])
                for kt in range(4):
                    p.t("qs", sy.dma_start(
                        out=wtile[:, kt * 128:(kt + 1) * 128],
                        in_=w1q_i[kt * 128:(kt + 1) * 128,
                                  mt2 * 128:(mt2 + 1) * 128]))
            blk.sync(f_lw1)

            def f_w1(te, mt2=mt2, ps=ps):
                p.w(te, "pe", [xq_done, ("act", p.cnt["act"]),
                               ("qs", p.cnt["qs"])])
                for kt in range(4):
                    i1 = MM(p, te, ps[:], wtile[:, kt * 128:(kt + 1) * 128],
                            q_sb.bitcast(BF16)[:, kt * TP:(kt + 1) * TP],
                            kt == 0, False)
                p.t("pe", i1)
            blk.tensor(f_w1)

            for g in range(1, 4):
                def f_lw1b(sy, mt2=mt2, g=g):
                    p.w(sy, "qs", [("pe", p.cnt["pe"])])
                    for kk in range(4):
                        kt = g * 4 + kk
                        p.t("qs", sy.dma_start(
                            out=wtile[:, kk * 128:(kk + 1) * 128],
                            in_=w1q_i[kt * 128:(kt + 1) * 128,
                                      mt2 * 128:(mt2 + 1) * 128]))
                blk.sync(f_lw1b)
                lwb = p.tok("qs")

                def f_w1b(te, mt2=mt2, g=g, ps=ps, lwb=lwb):
                    p.w(te, "pe", [lwb])
                    for kk in range(4):
                        kt = g * 4 + kk
                        i1 = MM(p, te, ps[:], wtile[:, kk * 128:(kk + 1) * 128],
                                q_sb.bitcast(BF16)[:, kt * TP:(kt + 1) * TP],
                                False, kt == 15)
                    p.t("pe", i1)
                blk.tensor(f_w1b)
            w1t = p.tok("pe")

            def f_hev(v, mt2=mt2, ps=ps, w1t=w1t):
                p.w(v, "dve", [w1t, bc_done])
                p.t("dve", v.tensor_mul(evslot[:], ps[:], rs1bc[:]))
            blk.vector(f_hev)
            hev = p.tok("dve")

            def f_hsil(sc, mt2=mt2, hev=hev):
                p.w(sc, "act", [hev])
                p.t("act", sc.activation(htile(mt2), evslot[:], AF.Silu))
            blk.scalar(f_hsil)
        h_done = p.tok("act")

        hm16 = qs_sb.bitcast(F16)

        def f_ham(v):
            p.w(v, "dve", [h_done])
            p.t("dve", v.tensor_reduce(
                evslot[:], hf16_lo[:, 0:32 * TP].rearrange(
                    "p (a t) -> p t a", a=32),
                AX.X, ALU.max, apply_absolute_value=True))
            p.t("dve", v.tensor_reduce(
                x2slot[:], hf16_hi[:, 0:32 * TP].rearrange(
                    "p (a t) -> p t a", a=32),
                AX.X, ALU.max, apply_absolute_value=True))
            p.w(v, "dve", [p.tok("dve")])
            p.t("dve", v.tensor_max(evslot[:], evslot[:], x2slot[:]))
        blk.vector(f_ham)
        ham = p.tok("dve")

        ham32 = ham

        for j in range(8):
            def f_hmt(te, j=j, ham32=ham32):
                p.w(te, "pe", [ham32, ("dve", p.cnt["dve"]),
                               ("act", p.cnt["act"])])
                p.t("pe", te.transpose(psA[:, 0:128],
                                       evslot[:, j * 128:(j + 1) * 128],
                                       identf[:]))
            blk.tensor(f_hmt)
            hmt = p.tok("pe")

            def f_hmr(v, j=j, hmt=hmt):
                p.w(v, "dve", [hmt])
                p.t("dve", v.reduce_max(qs_sb[:, 8 + j:9 + j], psA[:, 0:128],
                                        AX.X))
            blk.vector(f_hmr)
            hmr = p.tok("dve")

            def f_hmt2(te, j=j, hmr=hmr):
                p.w(te, "pe", [hmr])
                p.t("pe", te.transpose(psB[0:1, 0:128], qs_sb[:, 8 + j:9 + j],
                                       identf[:]))
            blk.tensor(f_hmt2)
            hmt2 = p.tok("pe")

            def f_hmc(sc, j=j, hmt2=hmt2):
                p.w(sc, "act", [hmt2])
                p.t("act", sc.copy(amaxr[0:1, j * 128:(j + 1) * 128],
                                   psB[0:1, 0:128]))
            blk.scalar(f_hmc)
        hamax_done = p.tok("act")

        def f_srow2(v):
            p.w(v, "dve", [hamax_done])
            p.t("dve", v.tensor_single_scalar(amaxr[:], amaxr[:], 1e-5,
                                              ALU.max))
            p.w(v, "dve", [p.tok("dve")])
            p.t("dve", v.reciprocal(sclr[:], amaxr[:]))
            p.w(v, "dve", [p.tok("dve")])
            p.t("dve", v.tensor_scalar_mul(sclr[:], sclr[:], 127.0))
            p.t("dve", v.tensor_scalar_mul(rows[0:1, 0:TP], amaxr[:],
                                           1.0 / 127.0))
            p.w(v, "dve", [p.tok("dve")])
            p.t("dve", v.tensor_scalar_mul(rows[0:1, 0:TP], rows[0:1, 0:TP],
                                           srows[0:1, 1:2]))
        blk.vector(f_srow2)
        srow2 = p.tok("dve")

        def f_bch(te):
            p.w(te, "pe", [srow2])
            p.t("pe", MM(p, te, psC[:], ones_row[:], sclr[:], True, True))
        blk.tensor(f_bch)
        bh1 = p.tok("pe")

        def f_bche(sc):
            p.w(sc, "act", [bh1])
            p.t("act", sc.copy(schbc[:], psC[:]))
        blk.scalar(f_bche)
        bch1 = p.tok("act")

        def f_bch2(te):
            p.w(te, "pe", [bch1])
            p.t("pe", MM(p, te, psC[:], ones_row[:], rows[0:1, 0:TP], True, True))
        blk.tensor(f_bch2)
        bh2 = p.tok("pe")

        def f_bch2e(sc):
            p.w(sc, "act", [bh2])
            p.t("act", sc.copy(rs2bc[:], psC[:]))
        blk.scalar(f_bch2e)
        bch_done = p.tok("act")

        for k in range(64):
            def f_hq(v, k=k):
                p.w(v, "dve", [bch_done, p.tok("dve")])
                p.t("dve", v.tensor_mul(evslot[:], htile(k), schbc[:]))
                p.w(v, "dve", [p.tok("dve")])
                dst = (hf16_lo if k < 32 else hf16_hi).bitcast(BF16)
                kk = k if k < 32 else k - 32
                p.t("dve", v.tensor_scalar(dst[:, kk * TP:(kk + 1) * TP],
                                           evslot[:], MAGIC, MAGIC, ALU.add,
                                           ALU.subtract))
            blk.vector(f_hq)
        hq_done = p.tok("dve")

        def hqtile(k):
            if k < 32:
                return hf16_lo.bitcast(BF16)[:, k * TP:(k + 1) * TP]
            return hf16_hi.bitcast(BF16)[:, (k - 32) * TP:(k - 31) * TP]

        for mt in range(16):
            ps = psA if mt % 2 == 0 else psB

            def f_lx3b(sy, mt=mt):
                p.w(sy, "qs", [("dve", p.cnt["dve"]), ("qs", p.cnt["qs"])])
                p.t("qs", sy.dma_start(out=x2slot[:],
                                       in_=x3d[mt * 128:(mt + 1) * 128, :]))
            blk.sync(f_lx3b)
            lx3 = p.tok("qs")

            for g in range(16):
                def f_lw2(sy, mt=mt, g=g):
                    if mt == 0 and g == 0:
                        sy.wait_ge(IO["ccs4"], 1)
                    p.w(sy, "qs", [("pe", p.cnt["pe"])])
                    for kk in range(4):
                        kt = g * 4 + kk
                        p.t("qs", sy.dma_start(
                            out=wtile[:, kk * 128:(kk + 1) * 128],
                            in_=w2q_i[kt * 128:(kt + 1) * 128,
                                      mt * 128:(mt + 1) * 128]))
                blk.sync(f_lw2)
                lw2 = p.tok("qs")

                def f_w2(te, mt=mt, g=g, ps=ps, lw2=lw2):
                    p.w(te, "pe", [lw2, hq_done, ("dve", p.cnt["dve"])])
                    for kk in range(4):
                        kt = g * 4 + kk
                        i1 = MM(p, te, ps[:], wtile[:, kk * 128:(kk + 1) * 128],
                                hqtile(kt), kt == 0, kt == 63)
                    p.t("pe", i1)
                blk.tensor(f_w2)
            w2t = p.tok("pe")

            def f_out(v, mt=mt, ps=ps, w2t=w2t, lx3=lx3):
                p.w(v, "dve", [w2t, lx3, bch_done])
                p.t("dve", v.tensor_mul(evslot[:], ps[:], rs2bc[:]))
                p.w(v, "dve", [p.tok("dve")])
                p.t("dve", v.tensor_add(x2slot[:], x2slot[:], evslot[:]))
            blk.vector(f_out)
            outt = p.tok("dve")

            def f_so(sy, mt=mt, outt=outt):
                p.w(sy, "qs", [outt])
                p.t("qs", sy.dma_start(out=outT[mt * 128:(mt + 1) * 128, :],
                                       in_=x2slot[:]))
            blk.sync(f_so)

        def f_fin(sy):
            sy.wait_ge(p.sem["qs"], p.cnt["qs"])
        blk.sync(f_fin)



from concourse.aot_env import allow_ici_all_to_all

G_BATCH = [[0, 1, 2, 3], [4, 5, 6, 7]]
G_ALL = [[0, 1, 2, 3, 4, 5, 6, 7]]


def _sbar(p, blk, tag):
    toks = p.bar()
    for name, opener in (("Esy", blk.sync), ("Egp", blk.gpsimd),
                         ("Edv", blk.vector), ("Eac", blk.scalar),
                         ("Ete", blk.tensor)):
        def f(eng, name=name, toks=toks):
            p.w(eng, name + tag, toks)
        opener(f)


class _NcU:

    def __init__(self, nc, tag):
        self._nc = nc
        self._tag = tag

    def __getattr__(self, a):
        return getattr(self._nc, a)

    def sbuf_tensor(self, name, *a, **k):
        return self._nc.sbuf_tensor(self._tag + name, *a, **k)

    def psum_tensor(self, name, *a, **k):
        return self._nc.psum_tensor(self._tag + name, *a, **k)


def build_fused():
    nc = bass.Bass("TRN2", target_bir_lowering=False, debug=False)
    D = {}
    def ein(n, s, dt):
        D[n] = _ein(nc, n, s, dt)
    ein("xT", [2048, TPC], F32)
    ein("w1T", [256, DFF], F32)
    ein("w2T", [1024, 2048], F32)
    ein("ones_h", [128, 1], F16)
    ein("ones_bf", [128, 1], BF16)
    ein("ones_row", [1, 128], F32)
    ein("ones_f", [128, 1], F32)
    ein("eps_c", [128, 1], F32)
    ein("identf", [128, 128], F32)
    ein("winT", [2048, 1024], F32)
    ein("convw", [128, 16], F32)
    ein("convb", [128, 4], F32)
    ein("wxT", [128, 4 * 160], F32)
    ein("wdtT", [128, 512], F32)
    ein("dtb", [128, 4], F32)
    ein("acol", [128, 64], F32)
    ein("dcol", [128, 4], F32)
    ein("woutT", [DIN, 2048], F32)
    ein("selW", [2048, 64], F32)
    ein("mdT", [2048, 8], F32)
    ein("mdrep", [2048, 64], F32)
    ein("lupF", [2048, 64], F32)
    ein("ldgF", [64, 2048], F32)
    ein("biasw", [8, 2048], F32)
    ein("dbwT", [2048, 8], F32)
    ein("slope_c", [8, 1], F32)
    ein("center_c", [8, 1], F32)
    ein("wq", [2048, INNER], F32)
    ein("wk", [2048, INNER], F32)
    ein("wv", [2048, INNER], F32)
    ein("e8h", [8, 64], F16)
    ein("e8f", [8, 64], F32)
    ein("e8t", [64, 8], BF16)
    ein("iota9", [128, 8], F32)
    ein("iotan", [128, 8], F32)
    ein("wo", [INNER, 2048], F32)
    ein("bo_c", [128, 16], F32)
    ein("ehw", [128, 16 * 32], F32)
    ein("fhw", [32, 16 * 128], F32)
    def itn(n, s, dt):
        D[n] = nc.dram_tensor("i_" + n, s, dt).ap()
    itn("xhT", [2048, TPC], F16)
    itn("xhTb", [8 * 2048, TPC], F16)
    itn("wabs", [2, 1], F32)
    itn("wtot", [2, 1], F32)
    itn("u16", [512, NTOK], F16)
    itn("zs16", [512, NTOK], F16)
    itn("ssmp", [160, NTOK], F32)
    itn("ssms", [160, NTOK], F32)
    itn("w1q", [256, DFF], BF16)
    itn("w2q", [1024, 2048], BF16)
    itn("w1qf", [2048, DFF], BF16)
    itn("w2qf", [DFF, 2048], BF16)
    itn("ysz_a2a", [DIN, TPC], F16)
    itn("yszg", [DIN, TPC], F16)
    itn("x2T", [2048, TPC], F32)
    itn("qO", [INNER, TPC], F16)
    itn("kvp", [INNER, 65], F32)
    itn("kvs", [INNER, 65], F32)
    itn("x3d", [2048, TPC], F32)
    D["outT"] = _eout(nc, "outT", [2048, TPC], F32)

    with contextlib.ExitStack() as outer:
        E = outer.enter_context
        p = Px(nc, E)
        ccsems = [E(nc.semaphore("ccs%d" % k)) for k in range(8)]
        blk = E(nc.Block())

        def cc(k, kind, op, groups, in_ap, out_ap, wait=True):
            def f(g):
                p.w(g, "Egp_cc%d" % k, p.bar())
                with allow_ici_all_to_all():
                    inst = g.collective_compute(
                        kind, op, replica_groups=groups,
                        ins=[in_ap.opt()], outs=[out_ap.opt()])
                inst.then_inc(ccsems[k])
                if wait:
                    g.wait_ge(ccsems[k], 1)
                    p.t("gpc", g.nop(), 1)
            blk.gpsimd(f)

        emit_ln1(_NcU(nc, 's0_'), p, blk, D)
        _sbar(p, blk, "c1")
        cc(0, "AllGather", ALU.bypass, G_ALL, D["xhT"], D["xhTb"])
        cc(1, "AllReduce", ALU.add, G_ALL, D["wabs"], D["wtot"])
        _sbar(p, blk, "c2")
        emit_front(_NcU(nc, 's1_'), p, blk, D)
        _sbar(p, blk, "c3")
        cc(2, "AllReduce", ALU.add, G_ALL, D["ssmp"], D["ssms"])
        cc(3, "AllGather", ALU.bypass, G_ALL, D["w1q"], D["w1qf"], wait=True)
        cc(4, "AllGather", ALU.bypass, G_ALL, D["w2q"], D["w2qf"], wait=True)
        D["ccs3"] = ccsems[3]
        D["ccs4"] = ccsems[4]
        _sbar(p, blk, "c4")
        emit_scan(_NcU(nc, 's2_'), p, blk, D)
        _sbar(p, blk, "c5")
        cc(5, "AllToAll", ALU.bypass, G_ALL, D["ysz_a2a"], D["yszg"])
        _sbar(p, blk, "c6")
        emit_post1(_NcU(nc, 's3_'), p, blk, D)
        _sbar(p, blk, "c7")
        cc(6, "AllReduce", ALU.add, G_BATCH, D["kvp"], D["kvs"])
        _sbar(p, blk, "c8")
        emit_post2(_NcU(nc, 's4_'), p, blk, D)

        def f_end(sy):
            p.w(sy, "Esy_end", p.bar())
        blk.sync(f_end)
    return nc


def _prep(inputs):
    g = {}
    x = np.ascontiguousarray(inputs["L0_attn_hidden_states"], dtype=np.float32)
    xT = np.ascontiguousarray(x.reshape(NTOK, D).T)
    g["xT"] = [np.ascontiguousarray(xT[:, c * TPC:(c + 1) * TPC]) for c in range(NC)]
    w1T = np.ascontiguousarray(inputs["L1_ffn_W1"].T)
    w2T = np.ascontiguousarray(inputs["L1_ffn_W2"].T)
    g["w1T"] = [np.ascontiguousarray(w1T[c * 256:(c + 1) * 256, :]) for c in range(NC)]
    g["w2T"] = [np.ascontiguousarray(w2T[c * 1024:(c + 1) * 1024, :]) for c in range(NC)]
    g["ones_bf"] = [np.ones((128, 1), ml_dtypes.bfloat16)] * NC
    g["ones_h"] = [np.ones((128, 1), np.float16)] * NC
    g["ones_row"] = [np.ones((1, 128), np.float32)] * NC
    g["ones_f"] = [np.ones((128, 1), np.float32)] * NC
    g["eps_c"] = [np.full((128, 1), LN_EPS, np.float32)] * NC

    w_in = inputs["L0_attn_W_in"]
    SL = DIN // NC
    g["winT"], g["convw"], g["convb"], g["wxT"] = [], [], [], []
    g["wdtT"], g["dtb"], g["acol"], g["dcol"] = [], [], [], []
    for c in range(NC):
        sl = slice(c * SL, (c + 1) * SL)
        rows = np.r_[np.arange(c * SL, (c + 1) * SL),
                     np.arange(DIN + c * SL, DIN + (c + 1) * SL)]
        g["winT"].append(np.ascontiguousarray(w_in[rows, :].T))
        g["convw"].append(np.ascontiguousarray(
            inputs["L0_attn_conv_w"][sl, :].reshape(4, 128, 4)
            .transpose(1, 0, 2).reshape(128, 16)))
        g["convb"].append(np.ascontiguousarray(
            inputs["L0_attn_conv_b"][sl].reshape(4, 128).T))
        g["wxT"].append(np.ascontiguousarray(
            inputs["L0_attn_W_x"][:, sl].T.reshape(4, 128, 160)
            .transpose(1, 0, 2).reshape(128, 4 * 160)))
        g["wdtT"].append(np.ascontiguousarray(inputs["L0_attn_W_dt"][sl, :].T))
        dtb = inputs["L0_attn_dt_bias"][sl].reshape(4, 128).T
        g["dtb"].append(np.ascontiguousarray(dtb))
        A = inputs["L0_attn_A"][sl, :].reshape(4, 128, 16)
        acol = np.ascontiguousarray(A.transpose(1, 0, 2).reshape(128, 64))
        g["acol"].append(acol)
        dvec = inputs["L0_attn_D"][sl].reshape(4, 128).T
        g["dcol"].append(np.ascontiguousarray(dvec))

    f32 = np.float32
    g["woutT"] = np.ascontiguousarray(inputs["L0_attn_W_out"].T.astype(f32))
    g["selW"] = np.ascontiguousarray(
        inputs["L0_ffn_select_weight"].transpose(1, 0, 2).reshape(D, 64))
    g["mdT"] = np.ascontiguousarray(inputs["L0_ffn_select_mean_diff"].T)
    g["mdrep"] = np.ascontiguousarray(np.repeat(g["mdT"], 8, axis=1))
    g["lupF"] = np.ascontiguousarray(
        inputs["L0_ffn_lora_update"].transpose(1, 0, 2).reshape(D, 64))
    g["ldgF"] = np.ascontiguousarray(
        inputs["L0_ffn_lora_degen"].transpose(0, 2, 1).reshape(64, D))
    g["biasw"] = np.ascontiguousarray(inputs["L0_ffn_bias_w"].astype(f32))
    g["dbwT"] = np.ascontiguousarray(inputs["L0_ffn_debias_w"].T)
    g["slope_c"] = np.ascontiguousarray(
        inputs["L0_ffn_imp_slope"].reshape(8, 1).astype(f32))
    g["center_c"] = np.ascontiguousarray(
        inputs["L0_ffn_imp_center"].reshape(8, 1).astype(f32))
    g["wq"] = np.ascontiguousarray(inputs["L1_attn_Wq"].astype(f32))
    g["wk"] = np.ascontiguousarray(inputs["L1_attn_Wk"].astype(f32))
    g["wv"] = np.ascontiguousarray(inputs["L1_attn_Wv"].astype(f32))
    g["wo"] = np.ascontiguousarray(inputs["L1_attn_Wo"].astype(f32))
    g["bo_c"] = np.ascontiguousarray(
        inputs["L1_attn_bo"].reshape(16, 128).T.astype(f32))
    g["identf"] = np.eye(128, dtype=f32)
    e8 = np.zeros((8, 64), f32)
    for n in range(8):
        e8[n, n * 8:(n + 1) * 8] = 1.0
    g["e8f"] = e8
    g["e8bf"] = e8.astype(ml_dtypes.bfloat16)
    g["e8h"] = e8.astype(np.float16)
    g["e8t"] = np.ascontiguousarray(e8.T).astype(ml_dtypes.bfloat16)
    g["iota9"] = np.ascontiguousarray(
        np.broadcast_to(np.arange(8, dtype=f32) + 9.0, (128, 8)))
    g["iotan"] = np.ascontiguousarray(
        np.broadcast_to(np.arange(8, dtype=f32), (128, 8)))
    ehw = np.zeros((128, 16 * 32), f32)
    fhw = np.zeros((32, 16 * 128), f32)
    for w in range(16):
        for pp in range(128):
            h = (w * 128 + pp) // 64
            ehw[pp, w * 32 + h] = 1.0
            fhw[h, w * 128 + pp] = 1.0
    g["ehw"] = ehw
    g["fhw"] = np.ascontiguousarray(fhw)
    return g


def kernel(**inputs):
    g = _prep(inputs)
    nc = build_fused()
    names_shared = ["woutT", "selW", "mdT", "mdrep", "lupF", "ldgF", "biasw",
                    "dbwT", "slope_c", "center_c", "wq", "wk", "wv", "wo",
                    "bo_c", "identf", "e8h", "e8f", "e8t", "iota9", "iotan",
                    "ehw", "fhw"]
    names_per = ["xT", "w1T", "w2T", "ones_h", "ones_bf", "ones_row", "ones_f",
                 "eps_c", "winT", "convw", "convb", "wxT", "wdtT", "dtb",
                 "acol", "dcol"]
    maps = []
    for c in range(NC):
        m = {}
        for k in names_shared:
            m[k] = g[k]
        for k in names_per:
            m[k] = g[k][c]
        maps.append(m)
    res = run_bass_kernel_spmd(nc, maps, core_ids=list(range(NC)))
    out = np.concatenate(
        [np.ascontiguousarray(res.results[c]["outT"]).T for c in range(NC)],
        axis=0)
    return np.ascontiguousarray(out.reshape(B, L, D)).astype(np.float32)


# version 12 (final)
# speedup vs baseline: 1.2238x; 1.2238x over previous
import contextlib
import numpy as np
import ml_dtypes
import concourse.bass as bass
import concourse.mybir as mybir
from concourse.bass_utils import run_bass_kernel_spmd

F32 = mybir.dt.float32
BF16 = mybir.dt.bfloat16
F16 = mybir.dt.float16
AF = mybir.ActivationFunctionType
ALU = mybir.AluOpType
AX = mybir.AxisListType

MAGIC = float(np.float32(1.5 * 2 ** 23))
LN_EPS = 1e-5

B, L, D = 2, 4096, 2048
NTOK = B * L
NC = 8
TPC = NTOK // NC
DIN = 4096
DPC = DIN // 4
DST = 16
DTR = 128
HEADS, HD = 32, 64
INNER = 2048
DFF = 8192
NT_D = D // 128
NT_DPC = DPC // 128


class Px:

    def __init__(self, nc, E):
        self.nc = nc
        self.sem = {}
        self.cnt = {}
        self.hw = {}
        for s in ("qs", "qg", "pe", "act", "dve", "gpc", "gp"):
            self.sem[s] = E(nc.semaphore("S_" + s))
            self.cnt[s] = 0

    def tok(self, s):
        return (s, self.cnt[s])

    def t(self, s, inst, amt=None):
        if amt is None:
            amt = 16 if s in ("qs", "qg") else 1
        self.cnt[s] += amt
        inst.then_inc(self.sem[s], amt)
        return (s, self.cnt[s])

    def w(self, eng, engname, toks):
        for (s, v) in toks:
            if v <= 0:
                continue
            key = (engname, s)
            if self.hw.get(key, -1) >= v:
                continue
            eng.wait_ge(self.sem[s], v)
            self.hw[key] = v

    def bar(self):
        return [(s, self.cnt[s]) for s in self.cnt]



def MM(p, te, out, lhsT, rhs, start, stop, nmax=512):
    n = rhs.shape[-1]
    inst = None
    for o in range(0, n, nmax):
        e = min(n, o + nmax)
        inst = te.matmul(out[:, o:e], lhsT, rhs[:, o:e], start=start, stop=stop)
    return inst


def _ein(nc, name, shape, dtype):
    return nc.dram_tensor(name, shape, dtype, kind="ExternalInput").ap()


def _eout(nc, name, shape, dtype):
    return nc.dram_tensor(name, shape, dtype, kind="ExternalOutput").ap()


def emit_ln(p, blk, tag, x_sb, tc, xh_sb, rows, ps_r1, ps_r2, ps_bc,
            scr_bf, bci, bcn, ones_bf, ones_row, eps_col, deps, xh32_sb=None,
            nt=16, scrT=None):

    def tile(t, i):
        return t[:, i * tc:(i + 1) * tc]

    def f_scalar(sc):
        p.w(sc, "act" + tag, deps)
        for i in range(nt):
            p.t("act", sc.copy(tile(scr_bf, i), tile(x_sb, i)))
    blk.scalar(f_scalar)
    cast_done = p.tok("act")

    def f_pe1(te):
        p.w(te, "pe" + tag, [cast_done])
        for i in range(nt):
            inst = MM(p, te, ps_r1[0:1, 0:tc], ones_bf[:], tile(scr_bf, i),
                      (i == 0), (i == nt - 1))
        p.t("pe", inst)
    blk.tensor(f_pe1)
    sum_done = p.tok("pe")

    def f_s2(sc):
        p.w(sc, "act" + tag, [sum_done])
        for i in range(nt):
            p.t("act", sc.activation(tile(scr_bf, i), tile(x_sb, i), AF.Square))
    blk.scalar(f_s2)
    sq_done = p.tok("act")

    def f_pe2(te):
        p.w(te, "pe" + tag, [sq_done])
        for i in range(nt):
            inst = MM(p, te, ps_r2[0:1, 0:tc], ones_bf[:], tile(scr_bf, i),
                      (i == 0), (i == nt - 1))
        p.t("pe", inst)
    blk.tensor(f_pe2)
    sq_sum = p.tok("pe")

    Dn = nt * 128

    def f_v2(v):
        p.w(v, "dve" + tag, [sum_done, sq_sum])
        p.t("dve", v.tensor_scalar_mul(rows[:, 0:tc], ps_r1[0:1, 0:tc], 1.0 / Dn))
        p.t("dve", v.tensor_scalar_mul(rows[:, tc:2 * tc], ps_r2[0:1, 0:tc],
                                       1.0 / Dn))
        p.w(v, "dve" + tag, [p.tok("dve")])
        p.t("dve", v.tensor_mul(rows[:, 2 * tc:3 * tc], rows[:, 0:tc],
                                rows[:, 0:tc]))
        p.w(v, "dve" + tag, [p.tok("dve")])
        p.t("dve", v.tensor_sub(rows[:, tc:2 * tc], rows[:, tc:2 * tc],
                                rows[:, 2 * tc:3 * tc]))
    blk.vector(f_v2)
    var_done = p.tok("dve")

    def f_s3(sc):
        p.w(sc, "act" + tag, [var_done])
        p.t("act", sc.activation(rows[:, 2 * tc:3 * tc], rows[:, tc:2 * tc],
                                 AF.Ln, bias=eps_col[0:1, 0:1]))
        p.w(sc, "act" + tag, [p.tok("act")])
        p.t("act", sc.activation(rows[:, 2 * tc:3 * tc], rows[:, 2 * tc:3 * tc],
                                 AF.Exp, scale=-0.5))
    blk.scalar(f_s3)
    inv_done = p.tok("act")

    def f_v3(v):
        p.w(v, "dve" + tag, [inv_done])
        p.t("dve", v.tensor_mul(rows[:, 3 * tc:4 * tc], rows[:, 0:tc],
                                rows[:, 2 * tc:3 * tc]))
        p.w(v, "dve" + tag, [p.tok("dve")])
        p.t("dve", v.tensor_scalar_mul(rows[:, 3 * tc:4 * tc],
                                       rows[:, 3 * tc:4 * tc], -1.0))
    blk.vector(f_v3)
    nmi_done = p.tok("dve")

    def f_pe3(te):
        p.w(te, "pe" + tag, [inv_done])
        p.t("pe", MM(p, te, ps_bc[:, 0:tc], ones_row[:],
                     rows[0:1, 2 * tc:3 * tc], True, True))
    blk.tensor(f_pe3)
    bc1 = p.tok("pe")

    def f_sc1(sc):
        p.w(sc, "act" + tag, [bc1])
        p.t("act", sc.copy(bci, ps_bc[:, 0:tc]))
    blk.scalar(f_sc1)
    ev1 = p.tok("act")

    def f_pe4(te):
        p.w(te, "pe" + tag, [nmi_done, ev1])
        p.t("pe", MM(p, te, ps_bc[:, 0:tc], ones_row[:],
                     rows[0:1, 3 * tc:4 * tc], True, True))
    blk.tensor(f_pe4)
    bc2 = p.tok("pe")

    def f_sc2(sc):
        p.w(sc, "act" + tag, [bc2])
        p.t("act", sc.copy(bcn, ps_bc[:, 0:tc]))
    blk.scalar(f_sc2)
    ev2 = p.tok("act")

    def f_v5(v):
        p.w(v, "dve" + tag, [ev1, ev2])
        for i in range(nt):
            if scrT is None:
                p.t("dve", v.tensor_mul(tile(x_sb, i), tile(x_sb, i), bci))
                p.w(v, "dve" + tag, [p.tok("dve")])
                p.t("dve", v.tensor_add(tile(xh_sb, i), tile(x_sb, i), bcn))
                if xh32_sb is not None:
                    p.t("dve", v.tensor_add(tile(xh32_sb, i), tile(x_sb, i),
                                            bcn))
            else:
                p.t("dve", v.tensor_mul(scrT[:, 0:tc], tile(x_sb, i), bci))
                p.w(v, "dve" + tag, [p.tok("dve")])
                p.t("dve", v.tensor_add(tile(xh_sb, i), scrT[:, 0:tc], bcn))
                if xh32_sb is not None:
                    p.t("dve", v.tensor_add(tile(xh32_sb, i), scrT[:, 0:tc],
                                            bcn))
                p.w(v, "dve" + tag, [p.tok("dve")])
    blk.vector(f_v5)
    return [p.tok("dve")]


def emit_ln1(nc, p, blk, IO):
    xT = IO["xT"]
    w1T = IO["w1T"]
    w2T = IO["w2T"]
    ones_bf_i = IO["ones_h"]
    ones_row_i = IO["ones_row"]
    ones_f_i = IO["ones_f"]
    eps_i = IO["eps_c"]
    xhT = IO["xhT"]
    wabs = IO["wabs"]

    with contextlib.ExitStack() as st_:
        E = st_.enter_context
        ones_bf = E(nc.sbuf_tensor("c_ones_h", [128, 1], F16))
        ones_row = E(nc.sbuf_tensor("c_ones_row", [1, 128], F32))
        ones_f = E(nc.sbuf_tensor("c_ones_f", [128, 1], F32))
        eps_c = E(nc.sbuf_tensor("c_eps", [128, 1], F32))
        x_sb = E(nc.sbuf_tensor("x_sb", [128, NT_D * TPC], F32))
        xh_sb = E(nc.sbuf_tensor("xh_sb", [128, NT_D * TPC], F16))
        scr_bf = E(nc.sbuf_tensor("scr_bf", [128, NT_D * TPC], F16))
        bc_sb = E(nc.sbuf_tensor("bc_sb", [128, 2 * TPC], F32))
        rows = E(nc.sbuf_tensor("rows", [1, 4 * TPC], F32))
        wbuf = E(nc.sbuf_tensor("wbuf", [128, DFF], F32))
        red = E(nc.sbuf_tensor("red", [128, 16], F32))
        red2 = E(nc.sbuf_tensor("red2", [128, 64], F32))
        wsum = E(nc.sbuf_tensor("wsum", [2, 1], F32))
        ps_r1 = E(nc.psum_tensor("ps_r1", [1, TPC], F32))
        ps_r2 = E(nc.psum_tensor("ps_r2", [1, TPC], F32))
        ps_bc = E(nc.psum_tensor("ps_bc", [128, TPC], F32))
        ps_w = E(nc.psum_tensor("ps_w", [2, 1], F32))

        def f_ld(sy):
            for sb, ap in [(ones_bf, ones_bf_i), (ones_row, ones_row_i),
                           (ones_f, ones_f_i), (eps_c, eps_i)]:
                p.t("qs", sy.dma_start(out=sb[:], in_=ap))
            for i in range(NT_D):
                p.t("qs", sy.dma_start(out=x_sb[:, i * TPC:(i + 1) * TPC],
                                       in_=xT[i * 128:(i + 1) * 128, :]))
        blk.sync(f_ld)
        ld_done = p.tok("qs")

        def wpass(src, nrow, width, colbase):
            for i in range(nrow):
                def f_g(g, i=i):
                    p.w(g, "qg", [p.tok("dve")])
                    p.t("qg", g.dma_start(out=wbuf[:, 0:width],
                                          in_=src[i * 128:(i + 1) * 128, :]))
                blk.gpsimd(f_g)
                ldt = p.tok("qg")

                def f_v(v, i=i):
                    p.w(v, "dve", [ldt])
                    p.t("dve", v.tensor_reduce(
                        red2[:, 0:width // 128],
                        wbuf[:, 0:width].rearrange("p (a b) -> p a b", b=128),
                        AX.X, ALU.add, apply_absolute_value=True))
                    p.w(v, "dve", [p.tok("dve")])
                    p.t("dve", v.tensor_reduce(
                        red[:, colbase + i:colbase + i + 1],
                        red2[:, 0:width // 128].rearrange("p (o a) -> p o a", o=1),
                        AX.X, ALU.add))
                blk.vector(f_v)

        wpass(w1T, 2, DFF, 0)
        wpass(w2T, 8, D, 2)

        def f_vsum(v):
            p.w(v, "dve", [p.tok("dve")])
            p.t("dve", v.tensor_reduce(
                red[:, 10:11], red[:, 0:2].rearrange("p (a b) -> p a b", a=1),
                AX.X, ALU.add))
            p.t("dve", v.tensor_reduce(
                red[:, 11:12], red[:, 2:10].rearrange("p (a b) -> p a b", a=1),
                AX.X, ALU.add))
        blk.vector(f_vsum)
        red_done = p.tok("dve")

        def f_pw(te):
            p.w(te, "pe", [red_done, ld_done])
            p.t("pe", te.matmul(ps_w[:], red[:, 10:12], ones_f[:],
                                start=True, stop=True))
        blk.tensor(f_pw)
        pw_done = p.tok("pe")

        def f_vw(v):
            p.w(v, "dve", [pw_done])
            p.t("dve", v.tensor_copy(wsum[:], ps_w[:]))
        blk.vector(f_vw)
        wsum_done = p.tok("dve")

        ln_done = emit_ln(p, blk, "", x_sb, TPC, xh_sb, rows, ps_r1, ps_r2,
                          ps_bc, scr_bf, bc_sb[:, 0:TPC], bc_sb[:, TPC:2 * TPC],
                          ones_bf, ones_row, eps_c[:, :], [ld_done])

        def f_st(sy):
            p.w(sy, "qs", ln_done + [wsum_done])
            for i in range(NT_D):
                p.t("qs", sy.dma_start(out=xhT[i * 128:(i + 1) * 128, :],
                                       in_=xh_sb[:, i * TPC:(i + 1) * TPC]))
            p.t("qs", sy.dma_start(out=wabs, in_=wsum[:]))
            sy.wait_ge(p.sem["qs"], p.cnt["qs"])
        blk.sync(f_st)



def emit_front(nc, p, blk, IO):
    TC = 512
    NCH = NTOK // TC
    xhTb = IO["xhTb"]
    winT = IO["winT"]
    convw = IO["convw"]
    convb = IO["convb"]
    wxT = IO["wxT"]
    w1T = IO["w1T"]
    w2T = IO["w2T"]
    wtot = IO["wtot"]
    ones_row_i = IO["ones_row"]
    u16o = IO["u16"]
    zs16o = IO["zs16"]
    ssmpo = IO["ssmp"]
    w1qo = IO["w1q"]
    w2qo = IO["w2q"]

    with contextlib.ExitStack() as st_:
        E = st_.enter_context
        ones_row = E(nc.sbuf_tensor("c_ones_row", [1, 128], F32))
        winbf = E(nc.sbuf_tensor("winbf", [128, 16 * 1024], F16))
        wx16 = E(nc.sbuf_tensor("wx16", [128, 4 * 160], F16))
        wld = E(nc.sbuf_tensor("wld", [128, 2048], F32))
        wx32 = E(nc.sbuf_tensor("wx32", [128, 4 * 160], F32))
        cw = E(nc.sbuf_tensor("cw", [128, 4 * 4], F32))
        cb = E(nc.sbuf_tensor("cb", [128, 4], F32))
        xh2 = E(nc.sbuf_tensor("xh2", [128, 2 * 16 * TC], F16))
        xzx = E(nc.sbuf_tensor("xzx", [128, 4 * (TC + 3)], F32))
        acc = E(nc.sbuf_tensor("acc", [128, 2 * TC], F32))
        ucv = E(nc.sbuf_tensor("ucv", [128, 8 * TC], F16))
        zst = E(nc.sbuf_tensor("zst", [128, 4 * TC], F16))
        s_out = E(nc.sbuf_tensor("s_out", [128, TC], F32))
        s_out2 = E(nc.sbuf_tensor("s_out2", [32, TC], F32))
        ps_x = E(nc.psum_tensor("ps_x", [128, TC], F32))
        ps_x2 = E(nc.psum_tensor("ps_x2", [128, TC], F32))
        ps_s1 = E(nc.psum_tensor("ps_s1", [128, TC], F32))
        ps_s2 = E(nc.psum_tensor("ps_s2", [32, TC], F32))
        ps_b = E(nc.psum_tensor("ps_b", [128, 2], F32))
        wqi = E(nc.sbuf_tensor("wqi", [128, DFF], F32))
        wqt = E(nc.sbuf_tensor("wqt", [128, DFF], F32))
        wqo_sb = E(nc.sbuf_tensor("wqo_sb", [128, DFF], BF16))
        srow = E(nc.sbuf_tensor("srow", [1, 4], F32))
        sbc = E(nc.sbuf_tensor("sbc", [128, 2], F32))
        wab = E(nc.sbuf_tensor("wab", [1, 2], F32))

        def f_ld(sy):
            p.t("qs", sy.dma_start(out=ones_row[:], in_=ones_row_i))
            p.t("qs", sy.dma_start(out=cw[:], in_=convw))
            p.t("qs", sy.dma_start(out=cb[:], in_=convb))
            p.t("qs", sy.dma_start(out=wx32[:], in_=wxT))
            p.t("qs", sy.dma_start(out=wab[:, 0:1], in_=wtot[0:1, :]))
            p.t("qs", sy.dma_start(out=wab[:, 1:2], in_=wtot[1:2, :]))
        blk.sync(f_ld)
        ld0 = p.tok("qs")

        def f_cast0(sc):
            p.w(sc, "act", [ld0])
            p.t("act", sc.copy(wx16[:], wx32[:]))
        blk.scalar(f_cast0)

        for i in range(16):
            def f_ldi(sy, i=i):
                p.w(sy, "qs", [p.tok("act")])
                p.t("qs", sy.dma_start(out=wld[:, 0:1024],
                                       in_=winT[i * 128:(i + 1) * 128, :]))
            blk.sync(f_ldi)
            ldt = p.tok("qs")

            def f_ci(sc, i=i, ldt=ldt):
                p.w(sc, "act", [ldt])
                p.t("act", sc.copy(winbf[:, i * 1024:(i + 1) * 1024],
                                   wld[:, 0:1024]))
            blk.scalar(f_ci)
        wcast = p.tok("act")

        def f_vq0(v):
            p.w(v, "dve", [ld0])
            p.t("dve", v.tensor_scalar_mul(srow[:, 0:2], wab[0:1, 0:2],
                                           1.0 / (DFF * D)))
            p.w(v, "dve", [p.tok("dve")])
            p.t("dve", v.reciprocal(srow[:, 2:4], srow[:, 0:2]))
        blk.vector(f_vq0)
        sc_done = p.tok("dve")

        def f_pq(te):
            p.w(te, "pe", [sc_done])
            p.t("pe", te.matmul(ps_b[:], ones_row[:], srow[0:1, 2:4],
                                start=True, stop=True))
        blk.tensor(f_pq)
        pb_done = p.tok("pe")

        def f_vq1(v):
            p.w(v, "dve", [pb_done])
            p.t("dve", v.tensor_copy(sbc[:], ps_b[:]))
        blk.vector(f_vq1)
        sbc_done = p.tok("dve")

        def quantpass(src, dst, nrow, width, scol):
            for i in range(nrow):
                def f_g(g, i=i):
                    p.w(g, "qg", [p.tok("dve"), p.tok("qg")])
                    p.t("qg", g.dma_start(out=wqi[:, 0:width],
                                          in_=src[i * 128:(i + 1) * 128, :]))
                blk.gpsimd(f_g)
                ldt = p.tok("qg")

                def f_v(v, i=i, ldt=ldt):
                    p.w(v, "dve", [ldt, sbc_done])
                    p.t("dve", v.tensor_scalar_mul(wqt[:, 0:width],
                                                   wqi[:, 0:width],
                                                   sbc[:, scol:scol + 1]))
                    p.w(v, "dve", [p.tok("dve")])
                    p.t("dve", v.tensor_scalar(wqt[:, 0:width], wqt[:, 0:width],
                                               MAGIC, MAGIC, ALU.add,
                                               ALU.subtract))
                    p.w(v, "dve", [p.tok("dve")])
                    p.t("dve", v.tensor_scalar(wqo_sb[:, 0:width],
                                               wqt[:, 0:width], 1.0, -1.0,
                                               ALU.min, ALU.max))
                blk.vector(f_v)
                qt = p.tok("dve")

                def f_s(sy, i=i, qt=qt):
                    p.w(sy, "qg", [qt])
                    p.t("qg", sy.dma_start(out=dst[i * 128:(i + 1) * 128, :],
                                           in_=wqo_sb[:, 0:width]))
                blk.gpsimd(f_s)

        quantpass(w1T, w1qo, 2, DFF, 0)
        quantpass(w2T, w2qo, 8, D, 1)
        quant_done = p.tok("qg")

        def tl(t, i, w=TC):
            return t[:, i * w:(i + 1) * w]

        def f_hist0(v):
            for mt in range(4):
                p.t("dve", v.memset(xzx[:, mt * (TC + 3):mt * (TC + 3) + 3], 0.0))
        blk.vector(f_hist0)

        for ch in range(NCH):
            xsl = (ch % 2) * 16 * TC
            if ch == NCH // 2:
                def f_histr(v):
                    p.w(v, "dve", [("act", p.cnt["act"]), p.tok("dve")])
                    for mt in range(4):
                        p.t("dve", v.memset(
                            xzx[:, mt * (TC + 3):mt * (TC + 3) + 3], 0.0))
                blk.vector(f_histr)

            pe_c = p.cnt["pe"]
            a_base = p.cnt["act"]
            v_base = p.cnt["dve"]
            qs_c = p.cnt["qs"]

            def f_ldc(sy, ch=ch, xsl=xsl, pe_c=pe_c):
                if ch >= 2:
                    p.w(sy, "qs", [("pe", pe_c - 9)])
                for i in range(16):
                    p.t("qs", sy.dma_start(
                        out=xh2[:, xsl + i * TC:xsl + (i + 1) * TC],
                        in_=xhTb[((ch * TC) // 1024) * 2048 + i * 128:
                                 ((ch * TC) // 1024) * 2048 + (i + 1) * 128,
                                 (ch * TC) % 1024:(ch * TC) % 1024 + TC]))
            blk.sync(f_ldc)
            ldc = p.tok("qs")

            def f_pe(te, ch=ch, xsl=xsl, a_base=a_base, ldc=ldc):
                p.w(te, "pe", [ldc, wcast])
                for mt in range(8):
                    ps = ps_x if mt % 2 == 0 else ps_x2
                    if mt >= 2:
                        p.w(te, "pe", [("act", a_base + mt - 1)])
                    elif ch > 0:
                        p.w(te, "pe", [("act", a_base)])
                    for kt in range(16):
                        inst = te.matmul(
                            ps[:],
                            winbf[:, kt * 1024 + mt * 128:kt * 1024 + (mt + 1) * 128],
                            xh2[:, xsl + kt * TC:xsl + (kt + 1) * TC],
                            start=(kt == 0), stop=(kt == 15))
                    p.t("pe", inst)
                p.w(te, "pe", [("act", a_base + 12)])
                for kt in range(4):
                    i1 = te.matmul(ps_s1[:], wx16[:, kt * 160:kt * 160 + 128],
                                   tl(ucv, kt), start=(kt == 0), stop=(kt == 3))
                for kt in range(4):
                    i2 = te.matmul(ps_s2[:], wx16[:, kt * 160 + 128:kt * 160 + 160],
                                   tl(ucv, kt), start=(kt == 0), stop=(kt == 3))
                p.t("pe", i2)
            blk.tensor(f_pe)

            def f_sc(sc, ch=ch, pe_c=pe_c, v_base=v_base, qs_c=qs_c):
                for mt in range(8):
                    ps = ps_x if mt % 2 == 0 else ps_x2
                    p.w(sc, "act", [("pe", pe_c + mt + 1)])
                    if mt < 4:
                        xb = mt * (TC + 3)
                        if ch > 0:
                            p.w(sc, "act", [("dve", v_base)])
                        p.t("act", sc.copy(xzx[:, xb + 3:xb + 3 + TC], ps[:]))
                    else:
                        if ch >= 1:
                            p.w(sc, "act", [("qs", qs_c - 16 * 10 + 16 * (mt - 3))])
                        p.t("act", sc.activation(
                            zst[:, (mt - 4) * TC:(mt - 3) * TC], ps[:], AF.Silu))
                for mt in range(4):
                    p.w(sc, "act", [("dve", v_base + 5 * (mt + 1))])
                    if ch >= 1:
                        p.w(sc, "act", [("qs", qs_c - 16 * 6 + 16 * (mt + 1))])
                    p.w(sc, "act", [("pe", pe_c)])
                    p.t("act", sc.activation(tl(ucv, mt),
                                             acc[:, (mt % 2) * TC:(mt % 2 + 1) * TC],
                                             AF.Silu, bias=cb[:, mt:mt + 1]))
            blk.scalar(f_sc)

            def f_cv(v, ch=ch, a_base=a_base):
                for mt in range(4):
                    xb = mt * (TC + 3)
                    asl = (mt % 2) * TC
                    p.w(v, "dve", [("act", a_base + mt + 1), p.tok("dve")])
                    if mt >= 2:
                        p.w(v, "dve", [("act", a_base + 8 + mt - 1)])
                    elif ch > 0:
                        p.w(v, "dve", [("act", a_base)])
                    p.t("dve", v.tensor_scalar_mul(acc[:, asl:asl + TC],
                                                   xzx[:, xb:xb + TC],
                                                   cw[:, mt * 4:mt * 4 + 1]))
                    for k in range(1, 4):
                        p.w(v, "dve", [p.tok("dve")])
                        p.t("dve", v.scalar_tensor_tensor(
                            acc[:, asl:asl + TC], xzx[:, xb + k:xb + k + TC],
                            cw[:, mt * 4 + k:mt * 4 + k + 1],
                            acc[:, asl:asl + TC], ALU.mult, ALU.add))
                    p.w(v, "dve", [p.tok("dve"),
                                   ("act", a_base + 8 + mt)])
                    p.t("dve", v.tensor_copy(xzx[:, xb:xb + 3],
                                             xzx[:, xb + TC:xb + TC + 3]))
            blk.vector(f_cv)

            def f_sev(v, pe_c=pe_c):
                p.w(v, "dve", [("pe", pe_c + 9), ("qs", p.cnt["qs"])])
                p.t("dve", v.tensor_copy(s_out[:], ps_s1[:]))
                p.w(v, "dve", [p.tok("dve")])
                p.t("dve", v.tensor_copy(s_out2[:], ps_s2[:]))
            blk.vector(f_sev)

            def f_st(sy, ch=ch, a_base=a_base):
                for mt in range(4):
                    p.w(sy, "qs", [("act", a_base + 4 + mt + 1)])
                    p.t("qs", sy.dma_start(
                        out=zs16o[mt * 128:(mt + 1) * 128, ch * TC:(ch + 1) * TC],
                        in_=zst[:, mt * TC:(mt + 1) * TC]))
                for mt in range(4):
                    p.w(sy, "qs", [("act", a_base + 8 + mt + 1)])
                    p.t("qs", sy.dma_start(
                        out=u16o[mt * 128:(mt + 1) * 128, ch * TC:(ch + 1) * TC],
                        in_=tl(ucv, mt)))
                p.w(sy, "qs", [("dve", p.cnt["dve"])])
                p.t("qs", sy.dma_start(out=ssmpo[0:128, ch * TC:(ch + 1) * TC],
                                       in_=s_out[:]))
                p.t("qs", sy.dma_start(out=ssmpo[128:160, ch * TC:(ch + 1) * TC],
                                       in_=s_out2[:]))
            blk.sync(f_st)

        def f_fin(sy):
            p.w(sy, "qs", [quant_done])
            sy.wait_ge(p.sem["qs"], p.cnt["qs"])
            sy.wait_ge(p.sem["qg"], p.cnt["qg"])
        blk.sync(f_fin)



def emit_scan(nc, p, blk, IO):
    TCS = 1024
    NCH = NTOK // TCS
    ssms = IO["ssms"]
    u16 = IO["u16"]
    zs16 = IO["zs16"]
    wdtT = IO["wdtT"]
    dtb = IO["dtb"]
    acol = IO["acol"]
    dcol = IO["dcol"]
    yszo = IO["ysz_a2a"]

    with contextlib.ExitStack() as st_:
        E = st_.enter_context
        dtr = E(nc.sbuf_tensor("dtr", [128, NTOK], F32))
        dtrb = E(nc.sbuf_tensor("dtrb", [128, NTOK], F16))
        wdt = E(nc.sbuf_tensor("wdt", [128, 512], F16))
        wdt32 = E(nc.sbuf_tensor("wdt32", [128, 512], F32))
        dtb_sb = E(nc.sbuf_tensor("dtb_sb", [128, 4], F32))
        acol_sb = E(nc.sbuf_tensor("acol_sb", [128, 64], F32))
        dcol_sb = E(nc.sbuf_tensor("dcol_sb", [128, 4], F32))
        delta = E(nc.sbuf_tensor("delta", [128, 4 * TCS], F32))
        spr = E(nc.sbuf_tensor("spr", [128, 4 * TCS], F32))
        u_sb = E(nc.sbuf_tensor("u_sb", [128, 4 * TCS], F16))
        zs_sb = E(nc.sbuf_tensor("zs_sb", [128, 4 * TCS], F16))
        wmt = E(nc.sbuf_tensor("wmt", [128, TCS], F32))
        a_sb = E(nc.sbuf_tensor("a_sb", [128, 2 * TCS], F32))
        b_sb = E(nc.sbuf_tensor("b_sb", [128, TCS], F32))
        h_sb = E(nc.sbuf_tensor("h_sb", [128, 2 * TCS], F32))
        hy_sb = E(nc.sbuf_tensor("hy_sb", [128, TCS], F32))
        y15_sb = E(nc.sbuf_tensor("y15_sb", [128, TCS], F32))
        y_sb = E(nc.sbuf_tensor("y_sb", [128, TCS], F32))
        bb_sb = E(nc.sbuf_tensor("bb_sb", [128, 2 * TCS], F32))
        cc_sb = E(nc.sbuf_tensor("cc_sb", [128, 2 * TCS], F32))
        t1_sb = E(nc.sbuf_tensor("t1_sb", [128, TCS], F32))
        yz_sb = E(nc.sbuf_tensor("yz_sb", [128, 2 * TCS], F16))
        state = E(nc.sbuf_tensor("state", [128, 64], F32))
        sp1 = E(nc.psum_tensor("sp1", [128, 1024], F32))
        sp2 = E(nc.psum_tensor("sp2", [128, 1024], F32))

        def f_ld(sy):
            for sb, ap in [(wdt32, wdtT), (dtb_sb, dtb), (acol_sb, acol),
                           (dcol_sb, dcol)]:
                p.t("qs", sy.dma_start(out=sb[:], in_=ap))
            p.t("qs", sy.dma_start(out=dtr[:], in_=ssms[0:128, :]))
        blk.sync(f_ld)
        ld0 = p.tok("qs")

        def f_cast(sc):
            p.w(sc, "act", [ld0])
            p.t("act", sc.copy(dtrb[:], dtr[:]))
            p.t("act", sc.copy(wdt[:], wdt32[:]))
        blk.scalar(f_cast)
        cast_done = p.tok("act")

        def f_st0(v):
            p.t("dve", v.memset(state[:], 0.0))
        blk.vector(f_st0)

        prev_ymul62 = 0
        for ch in range(NCH):
            reset = (ch == NCH // 2)
            v0 = p.cnt["dve"]
            g0 = p.cnt["gp"]
            vc = v0 + 4 + (1 if reset else 0)
            bmul_tok = {}
            scan_tok = {}
            y15_tok = {}
            fin_tok = {}
            for mt in range(4):
                vc += 1
                for n in range(16):
                    k = mt * 16 + n
                    vc += 1
                    bmul_tok[k] = vc
                    vc += 1
                    scan_tok[k] = vc
                    vc += 1
                    if n == 15:
                        vc += 1
                        y15_tok[mt] = vc
                vc += 3
                fin_tok[mt] = vc
            gc = g0
            ymul_tok = {}
            yadd_tok = {}
            for mt in range(4):
                for n in range(15):
                    k = mt * 16 + n
                    gc += 1
                    ymul_tok[k] = gc
                    if n > 0:
                        gc += 1
                        yadd_tok[k] = gc

            def f_lu(sy, ch=ch, v0=v0):
                p.w(sy, "qs", [("dve", v0)])
                for mt in range(4):
                    p.t("qs", sy.dma_start(
                        out=u_sb[:, mt * TCS:(mt + 1) * TCS],
                        in_=u16[mt * 128:(mt + 1) * 128, ch * TCS:(ch + 1) * TCS]))
                    p.t("qs", sy.dma_start(
                        out=zs_sb[:, mt * TCS:(mt + 1) * TCS],
                        in_=zs16[mt * 128:(mt + 1) * 128, ch * TCS:(ch + 1) * TCS]))
            blk.sync(f_lu)
            lu_done = p.tok("qs")
            qs_after_lu = p.cnt["qs"]

            bct_bb = {}
            bct_cc = {}

            def f_gp(g, ch=ch, v0=v0, prev_ymul62=prev_ymul62):
                def issue_bb(k):
                    n = k % 16
                    sl = (k % 2) * TCS
                    p.t("qg", g.dma_start(
                        out=bb_sb[:, sl:sl + TCS],
                        in_=ssms[128 + n:129 + n,
                                 ch * TCS:(ch + 1) * TCS].partition_broadcast(128)))
                    bct_bb[k] = p.tok("qg")

                def issue_cc(k):
                    n = k % 16
                    sl = (k % 2) * TCS
                    p.t("qg", g.dma_start(
                        out=cc_sb[:, sl:sl + TCS],
                        in_=ssms[144 + n:145 + n,
                                 ch * TCS:(ch + 1) * TCS].partition_broadcast(128)))
                    bct_cc[k] = p.tok("qg")

                if ch > 0:
                    p.w(g, "qg", [("dve", v0)])
                for k in (0, 1):
                    issue_bb(k)
                    issue_cc(k)
                for k in range(64):
                    mt, n = divmod(k, 16)
                    if k + 2 < 64:
                        p.w(g, "qg", [("dve", bmul_tok[k])])
                        issue_bb(k + 2)
                    if n == 15:
                        if k + 2 < 64:
                            p.w(g, "qg", [("dve", y15_tok[mt])])
                            issue_cc(k + 2)
                        continue
                    p.w(g, "gp", [("dve", scan_tok[k]), bct_cc[k]])
                    sl = (k % 2) * TCS
                    if n == 0:
                        p.t("gp", g.tensor_mul(y_sb[:], h_sb[:, sl:sl + TCS],
                                               cc_sb[:, sl:sl + TCS]), 1)
                    else:
                        p.t("gp", g.tensor_mul(hy_sb[:], h_sb[:, sl:sl + TCS],
                                               cc_sb[:, sl:sl + TCS]), 1)
                        p.w(g, "gp", [p.tok("gp")])
                        p.t("gp", g.tensor_add(y_sb[:], y_sb[:], hy_sb[:]), 1)
                    if k + 2 < 64:
                        issue_cc(k + 2)
            blk.gpsimd(f_gp)
            assert p.cnt["gp"] == gc, (p.cnt["gp"], gc)

            a0 = p.cnt["act"]
            def f_dm(te, ch=ch, a0=a0):
                p.w(te, "pe", [cast_done])
                for mt in range(4):
                    ps = sp1 if mt % 2 == 0 else sp2
                    if mt >= 2:
                        p.w(te, "pe", [("act", a0 + 4 * (mt - 1))])
                    elif ch > 0:
                        p.w(te, "pe", [("act", a0)])
                    p.t("pe", MM(p, te, ps[:], wdt[:, mt * 128:(mt + 1) * 128],
                                 dtrb[:, ch * TCS:(ch + 1) * TCS], True, True))
            blk.tensor(f_dm)
            pe0 = p.cnt["pe"]

            def f_sc(sc, ch=ch, v0=v0, scan_tok=scan_tok):
                for mt in range(4):
                    ps = sp1 if mt % 2 == 0 else sp2
                    o = mt * TCS
                    p.w(sc, "act", [("pe", pe0 - 4 + mt + 1)])
                    if ch > 0:
                        p.w(sc, "act", [("dve", v0)])
                    p.t("act", sc.activation(delta[:, o:o + TCS], ps[:],
                                             AF.Abs, bias=dtb_sb[:, mt:mt + 1]))
                    p.w(sc, "act", [p.tok("act")])
                    p.t("act", sc.activation(delta[:, o:o + TCS],
                                             delta[:, o:o + TCS], AF.Exp,
                                             scale=-1.0))
                    p.w(sc, "act", [p.tok("act")])
                    p.t("act", sc.activation(delta[:, o:o + TCS],
                                             delta[:, o:o + TCS], AF.Ln,
                                             bias=1.0))
                    p.t("act", sc.activation(spr[:, o:o + TCS], ps[:], AF.Relu,
                                             bias=dtb_sb[:, mt:mt + 1]))
                p.w(sc, "act", [p.tok("act")])
                for mt in range(4):
                    for n in range(16):
                        k = mt * 16 + n
                        sl = ((k % 2)) * TCS
                        w = v0 + mt + 1
                        if k >= 2:
                            w = max(w, scan_tok[k - 2])
                        p.w(sc, "act", [("dve", w)])
                        p.t("act", sc.activation(
                            a_sb[:, sl:sl + TCS], delta[:, mt * TCS:(mt + 1) * TCS],
                            AF.Exp, scale=acol_sb[:, mt * 16 + n:mt * 16 + n + 1]))
            blk.scalar(f_sc)
            act_after_sp = a0 + 4 * 4
            def exp_tok(mt, n):
                return ("act", act_after_sp + mt * 16 + n + 1)

            def f_v(v, ch=ch, v0=v0, lu_done=lu_done, reset=reset,
                    prev_ymul62=prev_ymul62):
                for mt in range(4):
                    o = mt * TCS
                    p.w(v, "dve", [("act", a0 + 4 * (mt + 1))])
                    p.t("dve", v.tensor_add(delta[:, o:o + TCS],
                                            delta[:, o:o + TCS],
                                            spr[:, o:o + TCS]))
                if reset:
                    p.w(v, "dve", [p.tok("dve")])
                    p.t("dve", v.memset(state[:], 0.0))
                for mt in range(4):
                    o = mt * TCS
                    p.w(v, "dve", [lu_done, p.tok("dve")])
                    p.t("dve", v.tensor_mul(wmt[:], delta[:, o:o + TCS],
                                            u_sb[:, o:o + TCS]))
                    for n in range(16):
                        k = mt * 16 + n
                        col = mt * 16 + n
                        sl = (k % 2) * TCS
                        p.w(v, "dve", [bct_bb[k], exp_tok(mt, n),
                                       p.tok("dve")])
                        p.t("dve", v.tensor_mul(b_sb[:], wmt[:],
                                                bb_sb[:, sl:sl + TCS]))
                        assert p.cnt["dve"] == bmul_tok[k]
                        w2 = [p.tok("dve")]
                        if k >= 2 and (k - 2) % 16 != 15:
                            w2.append(("gp", ymul_tok[k - 2]))
                        elif k < 2 and ch > 0:
                            w2.append(("gp", prev_ymul62))
                        p.w(v, "dve", w2)
                        p.t("dve", v.tensor_tensor_scan(
                            h_sb[:, sl:sl + TCS], a_sb[:, sl:sl + TCS], b_sb[:],
                            state[:, col:col + 1], ALU.mult, ALU.add))
                        assert p.cnt["dve"] == scan_tok[k]
                        p.w(v, "dve", [p.tok("dve")])
                        p.t("dve", v.tensor_copy(state[:, col:col + 1],
                                                 h_sb[:, sl + TCS - 1:sl + TCS]))
                        if n == 15:
                            p.w(v, "dve", [bct_cc[k], p.tok("dve")])
                            p.t("dve", v.tensor_mul(y15_sb[:],
                                                    h_sb[:, sl:sl + TCS],
                                                    cc_sb[:, sl:sl + TCS]))
                    ysl = ((ch * 4 + mt) % 2) * TCS
                    if mt >= 2:
                        p.w(v, "dve", [("qs", qs_after_lu + 16 * (mt - 1))])
                    p.w(v, "dve", [("qs", qs_after_lu),
                                   ("gp", yadd_tok[mt * 16 + 14]),
                                   p.tok("dve")])
                    p.t("dve", v.scalar_tensor_tensor(
                        t1_sb[:], u_sb[:, o:o + TCS], dcol_sb[:, mt:mt + 1],
                        y_sb[:], ALU.mult, ALU.add))
                    p.w(v, "dve", [p.tok("dve")])
                    p.t("dve", v.tensor_add(t1_sb[:], t1_sb[:], y15_sb[:]))
                    p.w(v, "dve", [p.tok("dve")])
                    p.t("dve", v.tensor_mul(yz_sb[:, ysl:ysl + TCS], t1_sb[:],
                                            zs_sb[:, o:o + TCS]))
                    assert p.cnt["dve"] == fin_tok[mt]
            blk.vector(f_v)
            assert p.cnt["dve"] == vc, (p.cnt["dve"], vc)
            prev_ymul62 = ymul_tok[62]

            def f_ys(sy, ch=ch):
                for mt in range(4):
                    ysl = ((ch * 4 + mt) % 2) * TCS
                    p.w(sy, "qs", [("dve", fin_tok[mt])])
                    p.t("qs", sy.dma_start(
                        out=yszo[ch * 512 + mt * 128:ch * 512 + (mt + 1) * 128, :],
                        in_=yz_sb[:, ysl:ysl + TCS]))
            blk.sync(f_ys)

        def f_end(sy):
            sy.wait_ge(p.sem["qs"], p.cnt["qs"])
        blk.sync(f_end)

def emit_post1(nc, p, blk, IO):
    TP = TPC
    yszg = IO["yszg"]
    woutT = IO["woutT"]
    xT = IO["xT"]
    selW = IO["selW"]
    mdT = IO["mdT"]
    mdrep = IO["mdrep"]
    lupF = IO["lupF"]
    ldgF = IO["ldgF"]
    biasw = IO["biasw"]
    dbwT = IO["dbwT"]
    slope_i = IO["slope_c"]
    center_i = IO["center_c"]
    wq_i = IO["wq"]
    wk_i = IO["wk"]
    wv_i = IO["wv"]
    ones_bf_i = IO["ones_h"]
    ones_b16_i = IO["ones_bf"]
    ones_row_i = IO["ones_row"]
    eps_i = IO["eps_c"]
    identf_i = IO["identf"]
    e8bf_i = IO["e8h"]
    e8f_i = IO["e8f"]
    e8t_i = IO["e8t"]
    iota9_i = IO["iota9"]
    iotan_i = IO["iotan"]
    x2To = IO["x2T"]
    qO = IO["qO"]
    kvpO = IO["kvp"]

    with contextlib.ExitStack() as st_:
        E = st_.enter_context
        ones_bf = E(nc.sbuf_tensor("c_ones_h", [128, 1], F16))
        ones_b16 = E(nc.sbuf_tensor("c_ones_b16", [128, 1], BF16))
        ones_row = E(nc.sbuf_tensor("c_ones_row", [1, 128], F32))
        eps_c = E(nc.sbuf_tensor("c_eps", [128, 1], F32))
        identf = E(nc.sbuf_tensor("c_identf", [128, 128], F32))
        e8bf = E(nc.sbuf_tensor("c_e8h", [8, 64], F16))
        e8f = E(nc.sbuf_tensor("c_e8f", [8, 64], F32))
        e8t = E(nc.sbuf_tensor("c_e8t", [64, 8], BF16))
        iota9 = E(nc.sbuf_tensor("c_iota9", [128, 8], F32))
        iotan = E(nc.sbuf_tensor("c_iotan", [128, 8], F32))
        slo_sb = E(nc.sbuf_tensor("slo_sb", [8, 1], F32))
        cen_sb = E(nc.sbuf_tensor("cen_sb", [8, 1], F32))
        xar = E(nc.sbuf_tensor("xar", [128, 16 * TP], F32))
        A1 = E(nc.sbuf_tensor("A1", [128, 16 * TP], F32))
        xh = E(nc.sbuf_tensor("xh", [128, 16 * TP], F16))
        wslot = E(nc.sbuf_tensor("wslot", [128, 2048], F32))
        selw_bf = E(nc.sbuf_tensor("selw_bf", [128, 16 * 64], F16))
        mdt_bf = E(nc.sbuf_tensor("mdt_bf", [128, 16 * 8], F16))
        mdr_bf = E(nc.sbuf_tensor("mdr_bf", [128, 16 * 64], F16))
        lup_bf = E(nc.sbuf_tensor("lup_bf", [128, 16 * 64], F16))
        dbwt_bf = E(nc.sbuf_tensor("dbwt_bf", [128, 16 * 8], F16))
        bw_bf = E(nc.sbuf_tensor("bw_bf", [8, D], F16))
        ldg_bf = E(nc.sbuf_tensor("ldg_bf", [64, D], F16))
        c1n_sb = E(nc.sbuf_tensor("c1n_sb", [64, 1], F32))
        md2_sb = E(nc.sbuf_tensor("md2_sb", [8, 1], F32))

        xm8_sb = E(nc.sbuf_tensor("xm8_sb", [8, TP], F32))
        nrm_sb = E(nc.sbuf_tensor("nrm_sb", [8, TP], F32))
        t64_sb = E(nc.sbuf_tensor("t64_sb", [64, TP], F32))
        cont_bf = E(nc.sbuf_tensor("cont_bf", [64, TP], BF16))
        negs_sb = E(nc.sbuf_tensor("negs_sb", [8, 1], F32))
        sbias_sb = E(nc.sbuf_tensor("sbias_sb", [8, 1], F32))
        mxc = E(nc.sbuf_tensor("mxc", [128, 8], F32))
        eq_sb = E(nc.sbuf_tensor("eq_sb", [128, 8], F32))
        idxc = E(nc.sbuf_tensor("idxc", [128, 1], F32))
        oneh_t = E(nc.sbuf_tensor("oneh_t", [128, 8], F32))
        onehot8 = E(nc.sbuf_tensor("onehot8", [8, TP], F16))

        k2b_bf = E(nc.sbuf_tensor("k2b_bf", [8, 64], F16))
        mx8_bf = E(nc.sbuf_tensor("mx8_bf", [64, TP], F16))
        d_sb = E(nc.sbuf_tensor("d_sb", [128, TP], F32))
        kvp_sb = E(nc.sbuf_tensor("kvp_sb", [128, 65], F32))
        psA = E(nc.psum_tensor("psA", [128, TP], F32))
        psB = E(nc.psum_tensor("psB", [128, TP], F32))
        psC = E(nc.psum_tensor("psC", [128, TP], F32))
        psD = E(nc.psum_tensor("psD", [64, TP], F32))

        rows = A1[0:1, 12 * TP:16 * TP]
        bc_sb = A1[:, 8 * TP:10 * TP]
        ssbc_sb = A1[:, 10 * TP:11 * TP]
        rrow = bc_sb[0:1, 0:TP]
        ss_sb = bc_sb[0:1, TP:2 * TP]
        rslot = xh.bitcast(F16)[:, 0:2 * TP]
        scrT = d_sb[:, 0:TP]
        smalls32 = wslot[:, 0:1024]
        e_sb = nrm_sb
        inv8_sb = xm8_sb
        a1bf = A1.bitcast(F16)
        a1h = A1.bitcast(F16)
        qkv_ev = d_sb.bitcast(F16)

        def xt(i):
            return xar[:, i * TP:(i + 1) * TP]

        def xht(i):
            return xh[:, i * TP:(i + 1) * TP]

        def f_ld(sy):
            for sb, ap in [(ones_bf, ones_bf_i), (ones_b16, ones_b16_i),
                           (ones_row, ones_row_i),
                           (eps_c, eps_i), (identf, identf_i), (e8bf, e8bf_i),
                           (e8f, e8f_i), (e8t, e8t_i), (iota9, iota9_i),
                           (iotan, iotan_i), (slo_sb, slope_i),
                           (cen_sb, center_i)]:
                p.t("qs", sy.dma_start(out=sb[:], in_=ap))
            for i in range(16):
                p.t("qs", sy.dma_start(out=xt(i), in_=xT[i * 128:(i + 1) * 128, :]))
        blk.sync(f_ld)
        ld0 = p.tok("qs")

        yar = xh.bitcast(F16)
        for half in range(2):
            ctoks = []
            for kk in range(16):
                kt = half * 16 + kk

                def f_lw(sy, kt=kt, kk=kk):
                    if kk >= 1:
                        p.w(sy, "qs", [ctoks[kk - 1]])
                    elif half > 0:
                        p.w(sy, "qs", [("act", p.cnt["act"])])
                    p.t("qs", sy.dma_start(
                        out=wslot[:, 0:2048],
                        in_=woutT[kt * 128:(kt + 1) * 128, :]))
                blk.sync(f_lw)
                lwt = p.tok("qs")

                def f_cw(sc, kk=kk, lwt=lwt, half=half):
                    p.w(sc, "act", [lwt])
                    if half > 0 and kk == 0:
                        p.w(sc, "act", [("pe", p.cnt["pe"])])
                    ctoks.append(p.t("act", sc.copy(
                        a1h[:, kk * 2048:(kk + 1) * 2048], wslot[:, 0:2048])))
                blk.scalar(f_cw)
            wc = p.tok("act")

            def f_lys(sy, half=half):
                p.w(sy, "qg", [("pe", p.cnt["pe"])])
                for kk in range(16):
                    kt = half * 16 + kk
                    p.t("qg", sy.dma_start(
                        out=yar[:, kk * TP:(kk + 1) * TP],
                        in_=yszg[kt * 128:(kt + 1) * 128, :]))
            blk.gpsimd(f_lys)
            ys_done = p.tok("qg")

            pe_b = p.cnt["pe"]
            dv_b = p.cnt["dve"]

            def f_mm(te, half=half, pe_b=pe_b, dv_b=dv_b, wc=wc,
                     ys_done=ys_done):
                p.w(te, "pe", [wc, ys_done])
                for mt in range(16):
                    ps = psA if mt % 2 == 0 else psB
                    if mt >= 2:
                        p.w(te, "pe", [("dve", dv_b + mt - 1)])
                    elif half > 0:
                        p.w(te, "pe", [("dve", dv_b)])
                    for kk in range(16):
                        i1 = MM(p, te, ps[:], a1h[:, kk * 2048 + mt * 128:
                                                  kk * 2048 + (mt + 1) * 128],
                                yar[:, kk * TP:(kk + 1) * TP],
                                kk == 0, kk == 15)
                    p.t("pe", i1)
            blk.tensor(f_mm)

            def f_ev(v, pe_b=pe_b):
                for mt in range(16):
                    ps = psA if mt % 2 == 0 else psB
                    p.w(v, "dve", [("pe", pe_b + mt + 1), ld0])
                    p.t("dve", v.tensor_add(xt(mt), xt(mt), ps[:]))
            blk.vector(f_ev)
        x1_done = p.tok("dve")


        ln2 = emit_ln(p, blk, "", xar, TP, xh, rows, psA, psB, psC,
                      a1bf[:, 0:16 * TP], bc_sb[:, 0:TP], bc_sb[:, TP:2 * TP],
                      ones_bf, ones_row, eps_c[:, :], [x1_done], scrT=scrT)

        def small_cast(src_ap, dst_bf, w):
            def f_l(sy):
                p.w(sy, "qs", [("act", p.cnt["act"])])
                p.t("qs", sy.dma_start(out=smalls32[:, 0:16 * w],
                                       in_=src_ap.rearrange("(a p) m -> p a m",
                                                            p=128)))
            blk.sync(f_l)
            lt = p.tok("qs")

            def f_c(sc, lt=lt):
                p.w(sc, "act", [lt])
                p.t("act", sc.copy(dst_bf[:, 0:16 * w], smalls32[:, 0:16 * w]))
            blk.scalar(f_c)

        small_cast(selW, selw_bf, 64)
        small_cast(mdT, mdt_bf, 8)
        small_cast(mdrep, mdr_bf, 64)
        small_cast(lupF, lup_bf, 64)
        small_cast(dbwT, dbwt_bf, 8)

        def f_lbw(sy):
            p.w(sy, "qs", [("act", p.cnt["act"])])
            p.t("qs", sy.dma_start(out=wslot[0:8, :], in_=biasw))
        blk.sync(f_lbw)
        t_ = p.tok("qs")

        def f_cbw(sc, t_=t_):
            p.w(sc, "act", [t_])
            p.t("act", sc.copy(bw_bf[:], wslot[0:8, :]))
        blk.scalar(f_cbw)

        def f_lldg(sy):
            p.w(sy, "qs", [("act", p.cnt["act"])])
            p.t("qs", sy.dma_start(out=wslot[0:64, :], in_=ldgF))
        blk.sync(f_lldg)
        t_ = p.tok("qs")

        def f_cldg(sc, t_=t_):
            p.w(sc, "act", [t_])
            p.t("act", sc.copy(ldg_bf[:], wslot[0:64, :]))
        blk.scalar(f_cldg)
        smalls_done = p.tok("act")

        def f_u64(te):
            p.w(te, "pe", [smalls_done] + ln2)
            for kt in range(16):
                i1 = MM(p, te, psD[:], selw_bf[:, kt * 64:(kt + 1) * 64],
                        xht(kt), kt == 0, kt == 15)
            p.t("pe", i1)
            for kt in range(16):
                i2 = MM(p, te, psB[0:8, :], mdt_bf[:, kt * 8:(kt + 1) * 8],
                        xht(kt), kt == 0, kt == 15)
            p.t("pe", i2)
        blk.tensor(f_u64)
        u64_done = p.tok("pe")

        def f_sq(sc):
            p.w(sc, "act", [u64_done] + ln2)
            for kt in range(16):
                p.t("act", sc.activation(a1bf[:, kt * TP:(kt + 1) * TP],
                                         xht(kt), AF.Square))
        blk.scalar(f_sq)
        sq_done = p.tok("act")

        def f_r2(te):
            p.w(te, "pe", [sq_done])
            for kt in range(16):
                i1 = MM(p, te, psA[0:1, :], ones_bf[:],
                        a1bf[:, kt * TP:(kt + 1) * TP], kt == 0, kt == 15)
            p.t("pe", i1)
        blk.tensor(f_r2)
        r2_done = p.tok("pe")

        def f_prod(v):
            p.w(v, "dve", [smalls_done])
            p.t("dve", v.tensor_mul(smalls32.bitcast(BF16)[:, 0:16 * 64],
                                    selw_bf[:], mdr_bf[:]))
        blk.vector(f_prod)
        prod_done = p.tok("dve")

        def f_mdsq(sc):
            p.w(sc, "act", [prod_done])
            p.t("act", sc.activation(smalls32.bitcast(BF16)[:, 16 * 64:16 * 72],
                                     mdt_bf[:], AF.Square))
        blk.scalar(f_mdsq)
        mdsq_done = p.tok("act")

        def f_c1(te):
            p.w(te, "pe", [prod_done, mdsq_done, r2_done])
            pbf = smalls32.bitcast(BF16)
            for kt in range(16):
                i1 = te.matmul(psC[0:64, 0:1], pbf[:, kt * 64:(kt + 1) * 64],
                               ones_b16[:], start=(kt == 0), stop=(kt == 15))
            p.t("pe", i1)
            for kt in range(16):
                i2 = te.matmul(psC[0:8, 4:5], pbf[:, 16 * 64 + kt * 8:
                                                  16 * 64 + (kt + 1) * 8],
                               ones_b16[:], start=(kt == 0), stop=(kt == 15))
            p.t("pe", i2)
        blk.tensor(f_c1)
        c1_done = p.tok("pe")

        def f_ev1(v):
            p.w(v, "dve", [c1_done])
            p.t("dve", v.tensor_scalar_mul(c1n_sb[:], psC[0:64, 0:1], -1.0))
            p.t("dve", v.tensor_copy(md2_sb[:], psC[0:8, 4:5]))
            p.t("dve", v.tensor_copy(rrow[:], psA[0:1, :]))
            p.t("dve", v.tensor_copy(xm8_sb[:], psB[0:8, :]))
        blk.vector(f_ev1)
        ev1_done = p.tok("dve")

        def f_r2b(te):
            p.w(te, "pe", [ev1_done])
            p.t("pe", MM(p, te, psB[0:8, :], ones_row[0:1, 0:8], rrow[:],
                         True, True))
        blk.tensor(f_r2b)
        r2b_done = p.tok("pe")

        def f_nrm(v):
            p.w(v, "dve", [r2b_done])
            p.t("dve", v.scalar_tensor_tensor(nrm_sb[:], xm8_sb[:], -2.0,
                                              psB[0:8, :], ALU.mult, ALU.add))
            p.w(v, "dve", [p.tok("dve")])
            p.t("dve", v.tensor_scalar_add(nrm_sb[:], nrm_sb[:],
                                           md2_sb[0:8, 0:1]))
            p.t("dve", v.tensor_mul(sbias_sb[:], slo_sb[:], cen_sb[:]))
            p.t("dve", v.tensor_scalar_mul(negs_sb[:], slo_sb[:], -1.0))
        blk.vector(f_nrm)
        nrm_done = p.tok("dve")

        def f_inv(sc):
            p.w(sc, "act", [nrm_done])
            p.t("act", sc.activation(inv8_sb[:], nrm_sb[:], AF.Ln))
            p.w(sc, "act", [p.tok("act")])
            p.t("act", sc.activation(inv8_sb[:], inv8_sb[:], AF.Exp, scale=-0.5))
        blk.scalar(f_inv)
        inv_done = p.tok("act")

        def f_invbc(te):
            p.w(te, "pe", [inv_done])
            p.t("pe", MM(p, te, psC[0:64, :], e8f[:], inv8_sb[:], True, True))
        blk.tensor(f_invbc)
        invbc_done = p.tok("pe")

        def f_cont1(sc):
            p.w(sc, "act", [invbc_done, ("dve", p.cnt["dve"])])
            p.t("act", sc.activation(t64_sb[:], psD[:], AF.Identity,
                                     bias=c1n_sb[0:64, 0:1]))
        blk.scalar(f_cont1)
        cont1 = p.tok("act")

        def f_cont2(v):
            p.w(v, "dve", [cont1])
            p.t("dve", v.tensor_mul(t64_sb[:], t64_sb[:], psC[0:64, :]))
        blk.vector(f_cont2)
        cont2 = p.tok("dve")

        def f_cont3(sc):
            p.w(sc, "act", [cont2])
            p.t("act", sc.activation(cont_bf[:], t64_sb[:], AF.Square))
        blk.scalar(f_cont3)
        cont3 = p.tok("act")

        def f_score(te):
            p.w(te, "pe", [cont3, ("dve", p.cnt["dve"])])
            p.t("pe", MM(p, te, psB[0:8, :], e8t[:], cont_bf[:], True, True))
        blk.tensor(f_score)
        score_done = p.tok("pe")

        def f_gate1(sc):
            p.w(sc, "act", [score_done, ("dve", p.cnt["dve"])])
            p.t("act", sc.activation(e_sb[:], psB[0:8, :], AF.Exp,
                                     bias=sbias_sb[0:8, 0:1],
                                     scale=negs_sb[0:8, 0:1]))
        blk.scalar(f_gate1)
        gate1 = p.tok("act")

        def f_gate2(v):
            p.w(v, "dve", [gate1])
            p.t("dve", v.tensor_scalar_add(e_sb[:], e_sb[:], 1.0))
            p.w(v, "dve", [p.tok("dve")])
            p.t("dve", v.reciprocal(e_sb[:], e_sb[:]))
        blk.vector(f_gate2)
        gate_done = p.tok("dve")

        for j in range(8):
            def f_tr(te, j=j):
                p.w(te, "pe", [gate_done, ("dve", p.cnt["dve"]),
                               ("act", p.cnt["act"])])
                p.t("pe", te.transpose(psA[:, 0:8],
                                       e_sb[0:8, j * 128:(j + 1) * 128],
                                       identf[0:8, 0:8]))
            blk.tensor(f_tr)
            trt = p.tok("pe")

            def f_sel(v, j=j, trt=trt):
                p.w(v, "dve", [trt])
                p.t("dve", v.reduce_max(mxc[:, j:j + 1], psA[:, 0:8], AX.X))
                p.w(v, "dve", [p.tok("dve")])
                p.t("dve", v.tensor_single_scalar(eq_sb[:], psA[:, 0:8],
                                                  mxc[:, j:j + 1], ALU.is_equal))
                p.w(v, "dve", [p.tok("dve")])
                p.t("dve", v.scalar_tensor_tensor(eq_sb[:], eq_sb[:], -9.0,
                                                  iota9[:], ALU.mult, ALU.add))
                p.w(v, "dve", [p.tok("dve")])
                p.t("dve", v.tensor_reduce(idxc[:], eq_sb[:], AX.X, ALU.min))
                p.w(v, "dve", [p.tok("dve")])
                p.t("dve", v.tensor_single_scalar(oneh_t[:], iotan[:],
                                                  idxc[:, 0:1], ALU.is_equal))
            blk.vector(f_sel)
            selt = p.tok("dve")

            def f_trb(te, j=j, selt=selt):
                p.w(te, "pe", [selt, ("act", p.cnt["act"])])
                p.t("pe", te.transpose(psD[0:8, 0:128], oneh_t[:], identf[:]))
                p.t("pe", te.transpose(psD[0:1, 128:256], mxc[:, j:j + 1],
                                       identf[:]))
            blk.tensor(f_trb)
            trb = p.tok("pe")

            def f_cp(sc, j=j, trb=trb):
                p.w(sc, "act", [trb])
                p.t("act", sc.copy(onehot8[:, j * 128:(j + 1) * 128],
                                   psD[0:8, 0:128]))
                p.t("act", sc.copy(ss_sb[:, j * 128:(j + 1) * 128],
                                   psD[0:1, 128:256]))
            blk.scalar(f_cp)
        oneh_done = p.tok("act")

        def f_xl(te):
            p.w(te, "pe", [oneh_done, ("dve", p.cnt["dve"])])
            for kt in range(16):
                i1 = MM(p, te, psC[0:64, :], lup_bf[:, kt * 64:(kt + 1) * 64],
                        xht(kt), kt == 0, kt == 15)
            p.t("pe", i1)
            for kt in range(16):
                i2 = te.matmul(psB[0:8, 0:64], dbwt_bf[:, kt * 8:(kt + 1) * 8],
                               lup_bf[:, kt * 64:(kt + 1) * 64],
                               start=(kt == 0), stop=(kt == 15))
            p.t("pe", i2)
        blk.tensor(f_xl)
        xl_done = p.tok("pe")

        def f_evxl(sc):
            p.w(sc, "act", [xl_done])
            p.t("act", sc.copy(t64_sb[:], psC[0:64, :]))
            p.t("act", sc.copy(k2b_bf[:], psB[0:8, 0:64]))
        blk.scalar(f_evxl)
        evxl = p.tok("act")

        def f_t2(te):
            p.w(te, "pe", [evxl])
            p.t("pe", MM(p, te, psD[:], k2b_bf[:], onehot8[:], True, True))
            p.t("pe", MM(p, te, psB[0:64, :], e8bf[:], onehot8[:], True, True))
            p.t("pe", MM(p, te, psC[:], ones_row[:], ss_sb[:], True, True))
        blk.tensor(f_t2)
        t2_done = p.tok("pe")

        def f_mx8(v):
            p.w(v, "dve", [t2_done])
            p.t("dve", v.tensor_sub(t64_sb[:], t64_sb[:], psD[:]))
            p.w(v, "dve", [p.tok("dve")])
            p.t("dve", v.tensor_mul(mx8_bf[:], t64_sb[:], psB[0:64, :]))
            p.t("dve", v.tensor_copy(ssbc_sb[:], psC[:]))
        blk.vector(f_mx8)
        mx8_done = p.tok("dve")

        for mt in range(16):
            ps = psA if mt % 2 == 0 else psB

            def f_bd(te, mt=mt, ps=ps):
                p.w(te, "pe", [mx8_done, ("dve", p.cnt["dve"])])
                MM(p, te, ps[:], bw_bf[:, mt * 128:(mt + 1) * 128], onehot8[:],
                   True, False)
                i1 = MM(p, te, ps[:], ldg_bf[:, mt * 128:(mt + 1) * 128],
                        mx8_bf[:], False, True)
                p.t("pe", i1)
            blk.tensor(f_bd)
            bd = p.tok("pe")

            def f_x2(v, mt=mt, ps=ps, bd=bd):
                p.w(v, "dve", [bd])
                p.t("dve", v.tensor_sub(d_sb[:], ps[:], xht(mt)))
                p.w(v, "dve", [p.tok("dve")])
                p.t("dve", v.tensor_mul(d_sb[:], d_sb[:], ssbc_sb[:]))
                p.w(v, "dve", [p.tok("dve")])
                p.t("dve", v.tensor_add(d_sb[:], d_sb[:], xht(mt)))
                p.w(v, "dve", [p.tok("dve")])
                p.t("dve", v.tensor_add(xt(mt), xt(mt), d_sb[:]))
            blk.vector(f_x2)
        x2_done = p.tok("dve")

        def f_stx2(sy):
            p.w(sy, "qs", [x2_done])
            for i in range(16):
                p.t("qs", sy.dma_start(out=x2To[i * 128:(i + 1) * 128, :],
                                       in_=xt(i)))
        blk.sync(f_stx2)

        ln3 = emit_ln(p, blk, "3", xar, TP, xh, rows, psA, psB, psC,
                      a1bf[:, 0:16 * TP], bc_sb[:, 0:TP], bc_sb[:, TP:2 * TP],
                      ones_bf, ones_row, eps_c[:, :],
                      [x2_done, ("qs", p.cnt["qs"])], scrT=scrT)

        war = xar.bitcast(F16)
        kvar = A1.bitcast(F16)

        def wload(w_ap):
            guard = [("qs", p.cnt["qs"]), ("dve", p.cnt["dve"]),
                     ("pe", p.cnt["pe"])]
            for kt in range(16):
                def f_l(sy, kt=kt):
                    p.w(sy, "qs", [("act", p.cnt["act"]), ("pe", p.cnt["pe"])])
                    p.t("qs", sy.dma_start(out=wslot[:],
                                           in_=w_ap[kt * 128:(kt + 1) * 128, :]))
                blk.sync(f_l)
                lt = p.tok("qs")

                def f_c(sc, kt=kt, lt=lt, guard=guard):
                    p.w(sc, "act", [lt] + guard)
                    p.t("act", sc.copy(war[:, kt * 2048:(kt + 1) * 2048],
                                       wslot[:]))
                blk.scalar(f_c)
            return p.tok("act")

        wq_done = wload(wq_i)
        for mt in range(16):
            ps = psA if mt % 2 == 0 else psB

            def f_qm(te, mt=mt, ps=ps, wq_done=wq_done):
                p.w(te, "pe", [wq_done] + ln3 + [("act", p.cnt["act"])])
                for kt in range(16):
                    i1 = MM(p, te, ps[:], war[:, kt * 2048 + mt * 128:
                                              kt * 2048 + (mt + 1) * 128],
                            xht(kt), kt == 0, kt == 15)
                p.t("pe", i1)
            blk.tensor(f_qm)
            qm = p.tok("pe")

            def f_qe(sc, mt=mt, ps=ps, qm=qm):
                p.w(sc, "act", [qm, ("qs", p.cnt["qs"])])
                p.t("act", sc.activation(qkv_ev[:, 0:TP], ps[:], AF.Relu))
            blk.scalar(f_qe)
            qe = p.tok("act")

            def f_qs(sy, mt=mt, qe=qe):
                p.w(sy, "qs", [qe])
                p.t("qs", sy.dma_start(out=qO[mt * 128:(mt + 1) * 128, :],
                                       in_=qkv_ev[:, 0:TP]))
            blk.sync(f_qs)

        for which, w_ap, act in [(0, wk_i, AF.Relu), (1, wv_i, AF.Identity)]:
            wdone = wload(w_ap)
            for ms in range(8):
                for nh in range(2):
                    ps = psA if (ms * 2 + nh) % 2 == 0 else psB

                    def f_km(te, ms=ms, nh=nh, ps=ps, wdone=wdone):
                        p.w(te, "pe", [wdone, ("act", p.cnt["act"])] + ln3)
                        for kt in range(16):
                            i1 = MM(p, te, ps[:],
                                    xh[:, kt * TP + ms * 128:
                                       kt * TP + (ms + 1) * 128],
                                    war[:, kt * 2048 + nh * 1024:
                                        kt * 2048 + (nh + 1) * 1024],
                                    kt == 0, kt == 15)
                        p.t("pe", i1)
                    blk.tensor(f_km)
                    km = p.tok("pe")

                    def f_ke(sc, which=which, ms=ms, nh=nh, ps=ps, km=km,
                             act=act):
                        p.w(sc, "act", [km])
                        dst = kvar[:, which * 16 * TP + ms * 2048 + nh * 1024:
                                   which * 16 * TP + ms * 2048 +
                                   (nh + 1) * 1024]
                        p.t("act", sc.activation(dst, ps[:], act))
                    blk.scalar(f_ke)
        kv_in_done = p.tok("act")

        def kcol(which, ms, inner):
            return which * 16 * TP + ms * 2048 + (inner // 1024) * 1024 + \
                (inner % 1024)

        for w in range(16):
            def f_kv(te, w=w):
                p.w(te, "pe", [kv_in_done, ("dve", p.cnt["dve"])])
                for h2 in range(2):
                    h = 2 * w + h2
                    for ms in range(8):
                        c0 = kcol(0, ms, h * 64)
                        c1 = kcol(1, ms, h * 64)
                        te.matmul(psC[h2 * 64:h2 * 64 + 64, 0:64],
                                  kvar[:, c0:c0 + 64], kvar[:, c1:c1 + 64],
                                  start=(ms == 0), stop=(ms == 7))
                for ms in range(8):
                    c0 = kcol(0, ms, w * 128)
                    i2 = te.matmul(psC[0:128, 64:65], kvar[:, c0:c0 + 128],
                                   ones_bf[:], start=(ms == 0), stop=(ms == 7))
                p.t("pe", i2)
            blk.tensor(f_kv)
            kvt = p.tok("pe")

            def f_kve(v, kvt=kvt):
                p.w(v, "dve", [kvt, ("qs", p.cnt["qs"])])
                p.t("dve", v.tensor_copy(kvp_sb[:], psC[:, 0:65]))
            blk.vector(f_kve)
            kve = p.tok("dve")

            def f_kvs(sy, w=w, kve=kve):
                p.w(sy, "qs", [kve])
                p.t("qs", sy.dma_start(out=kvpO[w * 128:(w + 1) * 128, :],
                                       in_=kvp_sb[:]))
            blk.sync(f_kvs)

        def f_fin(sy):
            sy.wait_ge(p.sem["qs"], p.cnt["qs"])
        blk.sync(f_fin)



def emit_post2(nc, p, blk, IO):
    TP = TPC
    x2T = IO["x2T"]
    qI = IO["qO"]
    kvs = IO["kvs"]
    wo_i = IO["wo"]
    bo_i = IO["bo_c"]
    w1q_i = IO["w1qf"]
    w2q_i = IO["w2qf"]
    wtot = IO["wtot"]
    ones_bf_i = IO["ones_h"]
    ones_row_i = IO["ones_row"]
    eps_i = IO["eps_c"]
    identf_i = IO["identf"]
    ehw_i = IO["ehw"]
    fhw_i = IO["fhw"]
    outT = IO["outT"]
    x3d = IO["x3d"]

    with contextlib.ExitStack() as st_:
        E = st_.enter_context
        ones_bf = E(nc.sbuf_tensor("c_ones_h", [128, 1], F16))
        ones_row = E(nc.sbuf_tensor("c_ones_row", [1, 128], F32))
        eps_c = E(nc.sbuf_tensor("c_eps", [128, 1], F32))
        identf = E(nc.sbuf_tensor("c_identf", [128, 128], F32))
        ehw = E(nc.sbuf_tensor("c_ehw", [128, 16 * 32], F32))

        bo_sb = E(nc.sbuf_tensor("bo_sb", [128, 16], F32))
        wab = E(nc.sbuf_tensor("wab", [1, 2], F32))
        H1 = E(nc.sbuf_tensor("H1", [128, 16 * TP], F32))
        H2 = E(nc.sbuf_tensor("H2", [128, 16 * TP], F32))
        q_sb = E(nc.sbuf_tensor("q_sb", [128, 16 * TP], F16))

        qs_sb = E(nc.sbuf_tensor("qs_sb", [128, TP], F32))
        evslot = E(nc.sbuf_tensor("evslot", [128, TP], F32))
        x2slot = E(nc.sbuf_tensor("x2slot", [128, TP], F32))
        rowblk = E(nc.sbuf_tensor("rowblk", [1, 6 * TP], F32))
        srows = E(nc.sbuf_tensor("srows", [1, 8], F32))
        rs1bc = E(nc.sbuf_tensor("rs1bc", [128, TP], F32))
        schbc = E(nc.sbuf_tensor("schbc", [128, TP], F32))

        psA = E(nc.psum_tensor("psA", [128, TP], F32))
        psB = E(nc.psum_tensor("psB", [128, TP], F32))
        psC = E(nc.psum_tensor("psC", [128, TP], F32))
        psD = E(nc.psum_tensor("psD", [128, TP], F32))

        woar = H1.bitcast(F16)
        wslab = ehw.bitcast(BF16)
        nrm_sb = rs1bc[0:32, :]
        rs2bc = qs_sb
        wtile = ehw.bitcast(BF16)[:, 0:4 * 128]
        kv_sb = H1[:, 0:16 * 65]
        kvt_sb = H1[:, 16 * 65:32 * 65]
        kvbf = H1.bitcast(F16)[:, 64 * 65:80 * 65]
        fhw = H2[0:32, 8 * TP:8 * TP + 16 * 128]
        qkvn = H2.bitcast(F16)
        hf16_lo = H2.bitcast(F16)
        hf16_hi = H1.bitcast(F16)
        rows = rowblk[0:1, 0:4 * TP]
        amaxr = rowblk[0:1, 4 * TP:5 * TP]
        sclr = rowblk[0:1, 5 * TP:6 * TP]

        def qt(i):
            return q_sb[:, i * TP:(i + 1) * TP]

        def f_ld(sy):
            for sb, ap in [(ones_bf, ones_bf_i), (ones_row, ones_row_i),
                           (eps_c, eps_i), (identf, identf_i), (ehw, ehw_i),
                           (bo_sb, bo_i)]:
                p.t("qs", sy.dma_start(out=sb[:], in_=ap))
            p.t("qs", sy.dma_start(out=wab[:, 0:1], in_=wtot[0:1, :]))
            p.t("qs", sy.dma_start(out=wab[:, 1:2], in_=wtot[1:2, :]))
            for i in range(16):
                p.t("qs", sy.dma_start(out=qt(i), in_=qI[i * 128:(i + 1) * 128, :]))
            p.t("qs", sy.dma_start(out=fhw[:], in_=fhw_i))
            p.t("qs", sy.dma_start(out=kv_sb[:], in_=kvs.rearrange(
                "(a p) m -> p a m", p=128)))
        blk.sync(f_ld)
        ld0 = p.tok("qs")
        kvsum = ld0

        def f_ckv(sc):
            p.w(sc, "act", [kvsum])
            p.t("act", sc.copy(kvbf[:], kv_sb[:]))
        blk.scalar(f_ckv)
        kvc = p.tok("act")

        for w in range(16):
            def f_qs(v, w=w):
                p.w(v, "dve", [kvsum, ld0, ("pe", p.cnt["pe"])])
                p.t("dve", v.tensor_scalar_mul(qs_sb[:], qt(w),
                                               kv_sb[:, w * 65 + 64:w * 65 + 65]))
            blk.vector(f_qs)
            qst = p.tok("dve")

            def f_nm(te, w=w, qst=qst):
                p.w(te, "pe", [qst])
                i1 = MM(p, te, psB[0:32, :], ehw[:, w * 32:(w + 1) * 32],
                        qs_sb[:], w == 0, w == 15)
                p.t("pe", i1)
            blk.tensor(f_nm)
        nm_done = p.tok("pe")

        def f_rn(v):
            p.w(v, "dve", [nm_done])
            p.t("dve", v.tensor_single_scalar(nrm_sb[:], psB[0:32, :], 1e-6,
                                              ALU.max))
            p.w(v, "dve", [p.tok("dve")])
            p.t("dve", v.reciprocal(nrm_sb[:], nrm_sb[:]))
        blk.vector(f_rn)
        rn_done = p.tok("dve")

        for w in range(16):
            def f_qkv(te, w=w):
                p.w(te, "pe", [kvc, rn_done, ("act", p.cnt["act"]),
                               ("dve", p.cnt["dve"])])
                MM(p, te, psA[0:64, :], kvbf[0:64, w * 65:w * 65 + 64],
                   q_sb[0:64, w * TP:(w + 1) * TP], True, True)
                i1 = MM(p, te, psA[64:128, :],
                        kvbf[64:128, w * 65:w * 65 + 64],
                        q_sb[64:128, w * TP:(w + 1) * TP], True, True)
                p.t("pe", i1)
                i2 = MM(p, te, psC[:], fhw[:, w * 128:(w + 1) * 128], nrm_sb[:],
                        True, True)
                p.t("pe", i2)
            blk.tensor(f_qkv)
            qkvt = p.tok("pe")

            def f_qe(sc, w=w, qkvt=qkvt):
                p.w(sc, "act", [qkvt])
                p.t("act", sc.copy(evslot[:], psA[:]))
            blk.scalar(f_qe)
            qev = p.tok("act")

            def f_qn(v, w=w, qev=qev):
                p.w(v, "dve", [qev])
                p.t("dve", v.tensor_mul(qkvn[:, w * TP:(w + 1) * TP],
                                        evslot[:], psC[:]))
            blk.vector(f_qn)
        qkvn_done = p.tok("dve")

        def wload16(w_ap, dst, slot32):
            for kt in range(16):
                def f_l(sy, kt=kt):
                    p.w(sy, "qs", [("act", p.cnt["act"]), ("pe", p.cnt["pe"]),
                                   ("dve", p.cnt["dve"])])
                    p.t("qs", sy.dma_start(out=slot32,
                                           in_=w_ap[kt * 128:(kt + 1) * 128, :]))
                blk.sync(f_l)
                lt = p.tok("qs")

                def f_c(sc, kt=kt, lt=lt):
                    p.w(sc, "act", [lt])
                    p.t("act", sc.copy(dst[:, kt * 2048:(kt + 1) * 2048],
                                       slot32))
                blk.scalar(f_c)
            return p.tok("act")

        wslot2 = H2[:, 12 * TP:14 * TP]
        wo_done = wload16(wo_i, woar, wslot2)

        for mt in range(16):
            ps = psA if mt % 2 == 0 else psB

            def f_lx2(sy, mt=mt):
                p.w(sy, "qs", [("dve", p.cnt["dve"]), ("qs", p.cnt["qs"])])
                p.t("qs", sy.dma_start(out=x2slot[:],
                                       in_=x2T[mt * 128:(mt + 1) * 128, :]))
            blk.sync(f_lx2)
            lx2 = p.tok("qs")

            def f_wo(te, mt=mt, ps=ps):
                p.w(te, "pe", [wo_done, qkvn_done, ("dve", p.cnt["dve"])])
                for kt in range(16):
                    i1 = MM(p, te, ps[:], woar[:, kt * 2048 + mt * 128:
                                               kt * 2048 + (mt + 1) * 128],
                            qkvn[:, kt * TP:(kt + 1) * TP], kt == 0, kt == 15)
                p.t("pe", i1)
            blk.tensor(f_wo)
            wot = p.tok("pe")

            def f_x3(v, mt=mt, ps=ps, wot=wot, lx2=lx2):
                p.w(v, "dve", [wot, lx2])
                p.t("dve", v.scalar_tensor_tensor(x2slot[:], ps[:],
                                                  bo_sb[:, mt:mt + 1],
                                                  x2slot[:], ALU.add, ALU.add))
            blk.vector(f_x3)
            x3t = p.tok("dve")

            def f_sx3(sy, mt=mt, x3t=x3t):
                p.w(sy, "qs", [x3t])
                p.t("qs", sy.dma_start(out=x3d[mt * 128:(mt + 1) * 128, :],
                                       in_=x2slot[:]))
            blk.sync(f_sx3)
        x3_done = p.tok("qs")

        def f_lx3(sy):
            p.w(sy, "qs", [x3_done, ("pe", p.cnt["pe"]),
                           ("dve", p.cnt["dve"]), ("act", p.cnt["act"])])
            for i in range(16):
                p.t("qs", sy.dma_start(out=H2[:, i * TP:(i + 1) * TP],
                                       in_=x3d[i * 128:(i + 1) * 128, :]))
        blk.sync(f_lx3)
        x3l = p.tok("qs")

        ln4 = emit_ln(p, blk, "4", H2, TP, q_sb, rows, psA, psB, psC,
                      H1.bitcast(F16)[:, 0:16 * TP], rs1bc[:, 0:TP],
                      schbc[:, 0:TP], ones_bf, ones_row, eps_c[:, :],
                      [x3l], xh32_sb=H1, scrT=evslot[:, 0:TP])

        def f_amax1(v):
            p.w(v, "dve", ln4)
            p.t("dve", v.tensor_reduce(
                evslot[:], H1[:].rearrange("p (a t) -> p t a", a=16),
                AX.X, ALU.max, apply_absolute_value=True))
        blk.vector(f_amax1)
        am1 = p.tok("dve")

        for j in range(8):
            def f_amt(te, j=j, am1=am1):
                p.w(te, "pe", [am1, ("dve", p.cnt["dve"]),
                               ("act", p.cnt["act"])])
                p.t("pe", te.transpose(psA[:, 0:128],
                                       evslot[:, j * 128:(j + 1) * 128],
                                       identf[:]))
            blk.tensor(f_amt)
            amt = p.tok("pe")

            def f_amr(v, j=j, amt=amt):
                p.w(v, "dve", [amt])
                p.t("dve", v.reduce_max(qs_sb[:, j:j + 1], psA[:, 0:128], AX.X))
            blk.vector(f_amr)
            amr = p.tok("dve")

            def f_amt2(te, j=j, amr=amr):
                p.w(te, "pe", [amr])
                p.t("pe", te.transpose(psB[0:1, 0:128], qs_sb[:, j:j + 1],
                                       identf[:]))
            blk.tensor(f_amt2)
            amt2 = p.tok("pe")

            def f_amc(sc, j=j, amt2=amt2):
                p.w(sc, "act", [amt2])
                p.t("act", sc.copy(amaxr[0:1, j * 128:(j + 1) * 128],
                                   psB[0:1, 0:128]))
            blk.scalar(f_amc)
        amax_done = p.tok("act")

        def f_srow(v):
            p.w(v, "dve", [amax_done, ld0])
            p.t("dve", v.tensor_scalar_mul(srows[0:1, 0:2], wab[0:1, 0:2],
                                           1.0 / (DFF * D)))
            p.w(v, "dve", [p.tok("dve")])
            p.t("dve", v.tensor_single_scalar(amaxr[:], amaxr[:], 1e-5,
                                              ALU.max))
            p.w(v, "dve", [p.tok("dve")])
            p.t("dve", v.reciprocal(sclr[:], amaxr[:]))
            p.w(v, "dve", [p.tok("dve")])
            p.t("dve", v.tensor_scalar_mul(sclr[:], sclr[:], 127.0))
            p.t("dve", v.tensor_scalar_mul(rows[0:1, 0:TP], amaxr[:],
                                           1.0 / 127.0))
            p.w(v, "dve", [p.tok("dve")])
            p.t("dve", v.tensor_scalar_mul(rows[0:1, 0:TP], rows[0:1, 0:TP],
                                           srows[0:1, 0:1]))
        blk.vector(f_srow)
        srow_done = p.tok("dve")

        def f_bc1(te):
            p.w(te, "pe", [srow_done])
            p.t("pe", MM(p, te, psC[:], ones_row[:], sclr[:], True, True))
        blk.tensor(f_bc1)
        b1 = p.tok("pe")

        def f_bc1e(sc):
            p.w(sc, "act", [b1])
            p.t("act", sc.copy(schbc[:], psC[:]))
        blk.scalar(f_bc1e)

        def f_bc2(te):
            p.w(te, "pe", [("act", p.cnt["act"])])
            p.t("pe", MM(p, te, psC[:], ones_row[:], rows[0:1, 0:TP], True, True))
        blk.tensor(f_bc2)
        b2 = p.tok("pe")

        def f_bc2e(sc):
            p.w(sc, "act", [b2])
            p.t("act", sc.copy(rs1bc[:], psC[:]))
        blk.scalar(f_bc2e)
        bc_done = p.tok("act")

        for i in range(16):
            def f_xq(v, i=i):
                p.w(v, "dve", [bc_done])
                p.t("dve", v.tensor_mul(evslot[:], H1[:, i * TP:(i + 1) * TP],
                                        schbc[:]))
                p.w(v, "dve", [p.tok("dve")])
                p.t("dve", v.tensor_scalar(q_sb.bitcast(BF16)[:, i * TP:(i + 1) * TP],
                                           evslot[:], MAGIC, MAGIC,
                                           ALU.add, ALU.subtract))
            blk.vector(f_xq)
        xq_done = p.tok("dve")

        def ps_slot(s):
            t = [psA, psB, psC, psD][s // 2]
            o = (s % 2) * 512
            return t[:, o:o + 512]

        qbf = q_sb.bitcast(BF16)

        def hcol(mt2, h):
            if mt2 < 32:
                return hf16_lo[:, mt2 * TP + h * 512:mt2 * TP + h * 512 + 512]
            return hf16_hi[:, (mt2 - 32) * TP + h * 512:
                           (mt2 - 32) * TP + h * 512 + 512]

        ev_last1 = 0
        for mg in range(16):
            pe_c1 = p.cnt["pe"]
            st1 = {}

            def f_lsl(sy, mg=mg, pe_c1=pe_c1):
                for kt in range(16):
                    if kt == 0 and mg == 0:
                        p.w(sy, "qs", [("pe", pe_c1)])
                    p.w(sy, "qs", [("pe", pe_c1 + kt - 1)])
                    p.t("qs", sy.dma_start(
                        out=wslab[:, (kt % 2) * 512:(kt % 2 + 1) * 512],
                        in_=w1q_i[kt * 128:(kt + 1) * 128,
                                  mg * 512:(mg + 1) * 512]))
                    st1[kt] = p.tok("qs")
            blk.sync(f_lsl)

            def f_mm1(te, mg=mg, ev_last1=ev_last1):
                if mg == 0:
                    p.w(te, "pe", [xq_done, ("act", p.cnt["act"]),
                                   ("dve", p.cnt["dve"])])
                for kt in range(16):
                    p.w(te, "pe", [st1[kt]])
                    if kt == 0 and mg > 0:
                        p.w(te, "pe", [("dve", ev_last1)])
                    for s in range(8):
                        j, h = divmod(s, 2)
                        i1 = te.matmul(
                            ps_slot(s),
                            wslab[:, (kt % 2) * 512 + j * 128:
                                  (kt % 2) * 512 + (j + 1) * 128],
                            qbf[:, kt * TP + h * 512:kt * TP + h * 512 + 512],
                            start=(kt == 0), stop=(kt == 15))
                    p.t("pe", i1)
            blk.tensor(f_mm1)
            w1done = p.tok("pe")

            v2 = p.cnt["dve"]
            a2 = p.cnt["act"]

            def f_ev1(v, mg=mg, w1done=w1done, v2=v2, a2=a2):
                for s in range(8):
                    j, h = divmod(s, 2)
                    p.w(v, "dve", [w1done, bc_done, p.tok("dve")])
                    if s >= 2:
                        p.w(v, "dve", [("act", a2 + s - 1)])
                    p.t("dve", v.tensor_mul(
                        evslot[:, (s % 2) * 512:(s % 2 + 1) * 512],
                        ps_slot(s), rs1bc[:, h * 512:h * 512 + 512]))
            blk.vector(f_ev1)
            ev_last1 = p.cnt["dve"]

            def f_sil1(sc, mg=mg, v2=v2):
                for s in range(8):
                    j, h = divmod(s, 2)
                    mt2 = mg * 4 + j
                    p.w(sc, "act", [("dve", v2 + s + 1)])
                    p.t("act", sc.activation(
                        hcol(mt2, h), evslot[:, (s % 2) * 512:(s % 2 + 1) * 512],
                        AF.Silu))
            blk.scalar(f_sil1)
        h_done = p.tok("act")

        hm16 = qs_sb.bitcast(F16)

        def f_ham(v):
            p.w(v, "dve", [h_done])
            p.t("dve", v.tensor_reduce(
                evslot[:], hf16_lo[:, 0:32 * TP].rearrange(
                    "p (a t) -> p t a", a=32),
                AX.X, ALU.max, apply_absolute_value=True))
            p.t("dve", v.tensor_reduce(
                x2slot[:], hf16_hi[:, 0:32 * TP].rearrange(
                    "p (a t) -> p t a", a=32),
                AX.X, ALU.max, apply_absolute_value=True))
            p.w(v, "dve", [p.tok("dve")])
            p.t("dve", v.tensor_max(evslot[:], evslot[:], x2slot[:]))
        blk.vector(f_ham)
        ham = p.tok("dve")

        ham32 = ham

        for j in range(8):
            def f_hmt(te, j=j, ham32=ham32):
                p.w(te, "pe", [ham32, ("dve", p.cnt["dve"]),
                               ("act", p.cnt["act"])])
                p.t("pe", te.transpose(psA[:, 0:128],
                                       evslot[:, j * 128:(j + 1) * 128],
                                       identf[:]))
            blk.tensor(f_hmt)
            hmt = p.tok("pe")

            def f_hmr(v, j=j, hmt=hmt):
                p.w(v, "dve", [hmt])
                p.t("dve", v.reduce_max(qs_sb[:, 8 + j:9 + j], psA[:, 0:128],
                                        AX.X))
            blk.vector(f_hmr)
            hmr = p.tok("dve")

            def f_hmt2(te, j=j, hmr=hmr):
                p.w(te, "pe", [hmr])
                p.t("pe", te.transpose(psB[0:1, 0:128], qs_sb[:, 8 + j:9 + j],
                                       identf[:]))
            blk.tensor(f_hmt2)
            hmt2 = p.tok("pe")

            def f_hmc(sc, j=j, hmt2=hmt2):
                p.w(sc, "act", [hmt2])
                p.t("act", sc.copy(amaxr[0:1, j * 128:(j + 1) * 128],
                                   psB[0:1, 0:128]))
            blk.scalar(f_hmc)
        hamax_done = p.tok("act")

        def f_srow2(v):
            p.w(v, "dve", [hamax_done])
            p.t("dve", v.tensor_single_scalar(amaxr[:], amaxr[:], 1e-5,
                                              ALU.max))
            p.w(v, "dve", [p.tok("dve")])
            p.t("dve", v.reciprocal(sclr[:], amaxr[:]))
            p.w(v, "dve", [p.tok("dve")])
            p.t("dve", v.tensor_scalar_mul(sclr[:], sclr[:], 127.0))
            p.t("dve", v.tensor_scalar_mul(rows[0:1, 0:TP], amaxr[:],
                                           1.0 / 127.0))
            p.w(v, "dve", [p.tok("dve")])
            p.t("dve", v.tensor_scalar_mul(rows[0:1, 0:TP], rows[0:1, 0:TP],
                                           srows[0:1, 1:2]))
        blk.vector(f_srow2)
        srow2 = p.tok("dve")

        def f_bch(te):
            p.w(te, "pe", [srow2])
            p.t("pe", MM(p, te, psC[:], ones_row[:], sclr[:], True, True))
        blk.tensor(f_bch)
        bh1 = p.tok("pe")

        def f_bche(sc):
            p.w(sc, "act", [bh1])
            p.t("act", sc.copy(schbc[:], psC[:]))
        blk.scalar(f_bche)
        bch1 = p.tok("act")

        def f_bch2(te):
            p.w(te, "pe", [bch1])
            p.t("pe", MM(p, te, psC[:], ones_row[:], rows[0:1, 0:TP], True, True))
        blk.tensor(f_bch2)
        bh2 = p.tok("pe")

        def f_bch2e(sc):
            p.w(sc, "act", [bh2])
            p.t("act", sc.copy(rs2bc[:], psC[:]))
        blk.scalar(f_bch2e)
        bch_done = p.tok("act")

        def htile(k):
            if k < 32:
                return hf16_lo[:, k * TP:(k + 1) * TP]
            return hf16_hi[:, (k - 32) * TP:(k - 31) * TP]

        for k in range(64):
            def f_hq(v, k=k):
                p.w(v, "dve", [bch_done, p.tok("dve")])
                p.t("dve", v.tensor_mul(evslot[:], htile(k), schbc[:]))
                p.w(v, "dve", [p.tok("dve")])
                dst = (hf16_lo if k < 32 else hf16_hi).bitcast(BF16)
                kk = k if k < 32 else k - 32
                p.t("dve", v.tensor_scalar(dst[:, kk * TP:(kk + 1) * TP],
                                           evslot[:], MAGIC, MAGIC, ALU.add,
                                           ALU.subtract))
            blk.vector(f_hq)
        hq_done = p.tok("dve")

        def hqtile(k):
            if k < 32:
                return hf16_lo.bitcast(BF16)[:, k * TP:(k + 1) * TP]
            return hf16_hi.bitcast(BF16)[:, (k - 32) * TP:(k - 31) * TP]

        def hqcol(kt, h):
            if kt < 32:
                return hf16_lo.bitcast(BF16)[
                    :, kt * TP + h * 512:kt * TP + h * 512 + 512]
            return hf16_hi.bitcast(BF16)[
                :, (kt - 32) * TP + h * 512:(kt - 32) * TP + h * 512 + 512]

        ev_last2 = 0
        for mg in range(4):
            pe_c2 = p.cnt["pe"]
            st2 = {}

            def f_lsl2(sy, mg=mg, pe_c2=pe_c2):
                for kt in range(64):
                    p.w(sy, "qs", [("pe", pe_c2 + kt - 1)])
                    p.t("qs", sy.dma_start(
                        out=wslab[:, (kt % 2) * 512:(kt % 2 + 1) * 512],
                        in_=w2q_i[kt * 128:(kt + 1) * 128,
                                  mg * 512:(mg + 1) * 512]))
                    st2[kt] = p.tok("qs")
            blk.sync(f_lsl2)

            def f_mm2(te, mg=mg, ev_last2=ev_last2):
                if mg == 0:
                    p.w(te, "pe", [hq_done, ("act", p.cnt["act"]),
                                   ("dve", p.cnt["dve"])])
                for kt in range(64):
                    p.w(te, "pe", [st2[kt]])
                    if kt == 0 and mg > 0:
                        p.w(te, "pe", [("dve", ev_last2)])
                    for s in range(8):
                        j, h = divmod(s, 2)
                        i1 = te.matmul(
                            ps_slot(s),
                            wslab[:, (kt % 2) * 512 + j * 128:
                                  (kt % 2) * 512 + (j + 1) * 128],
                            hqcol(kt, h),
                            start=(kt == 0), stop=(kt == 63))
                    p.t("pe", i1)
            blk.tensor(f_mm2)
            w2done = p.tok("pe")

            for j in range(4):
                mt = mg * 4 + j

                def f_lx3b(sy, mt=mt):
                    p.w(sy, "qs", [("dve", p.cnt["dve"]), ("qs", p.cnt["qs"])])
                    p.t("qs", sy.dma_start(out=x2slot[:],
                                           in_=x3d[mt * 128:(mt + 1) * 128, :]))
                blk.sync(f_lx3b)
                lx3 = p.tok("qs")

                def f_out(v, j=j, w2done=w2done, lx3=lx3):
                    for h in range(2):
                        p.w(v, "dve", [w2done, lx3, bch_done, p.tok("dve")])
                        p.t("dve", v.tensor_mul(
                            evslot[:, h * 512:h * 512 + 512],
                            ps_slot(j * 2 + h), rs2bc[:, h * 512:h * 512 + 512]))
                        p.w(v, "dve", [p.tok("dve")])
                        p.t("dve", v.tensor_add(
                            x2slot[:, h * 512:h * 512 + 512],
                            x2slot[:, h * 512:h * 512 + 512],
                            evslot[:, h * 512:h * 512 + 512]))
                blk.vector(f_out)
                outt = p.tok("dve")
                if j == 3:
                    ev_last2 = p.cnt["dve"]

                def f_so(sy, mt=mt, outt=outt):
                    p.w(sy, "qs", [outt])
                    p.t("qs", sy.dma_start(out=outT[mt * 128:(mt + 1) * 128, :],
                                           in_=x2slot[:]))
                blk.sync(f_so)

        def f_fin(sy):
            sy.wait_ge(p.sem["qs"], p.cnt["qs"])
        blk.sync(f_fin)



from concourse.aot_env import allow_ici_all_to_all

G_BATCH = [[0, 1, 2, 3], [4, 5, 6, 7]]
G_ALL = [[0, 1, 2, 3, 4, 5, 6, 7]]


def _sbar(p, blk, tag):
    toks = p.bar()
    for name, opener in (("Esy", blk.sync), ("Egp", blk.gpsimd),
                         ("Edv", blk.vector), ("Eac", blk.scalar),
                         ("Ete", blk.tensor)):
        def f(eng, name=name, toks=toks):
            p.w(eng, name + tag, toks)
        opener(f)


class _NcU:

    def __init__(self, nc, tag):
        self._nc = nc
        self._tag = tag

    def __getattr__(self, a):
        return getattr(self._nc, a)

    def sbuf_tensor(self, name, *a, **k):
        return self._nc.sbuf_tensor(self._tag + name, *a, **k)

    def psum_tensor(self, name, *a, **k):
        return self._nc.psum_tensor(self._tag + name, *a, **k)


def build_fused():
    nc = bass.Bass("TRN2", target_bir_lowering=False, debug=False)
    D = {}
    def ein(n, s, dt):
        D[n] = _ein(nc, n, s, dt)
    ein("xT", [2048, TPC], F32)
    ein("w1T", [256, DFF], F32)
    ein("w2T", [1024, 2048], F32)
    ein("ones_h", [128, 1], F16)
    ein("ones_bf", [128, 1], BF16)
    ein("ones_row", [1, 128], F32)
    ein("ones_f", [128, 1], F32)
    ein("eps_c", [128, 1], F32)
    ein("identf", [128, 128], F32)
    ein("winT", [2048, 1024], F32)
    ein("convw", [128, 16], F32)
    ein("convb", [128, 4], F32)
    ein("wxT", [128, 4 * 160], F32)
    ein("wdtT", [128, 512], F32)
    ein("dtb", [128, 4], F32)
    ein("acol", [128, 64], F32)
    ein("dcol", [128, 4], F32)
    ein("woutT", [DIN, 2048], F32)
    ein("selW", [2048, 64], F32)
    ein("mdT", [2048, 8], F32)
    ein("mdrep", [2048, 64], F32)
    ein("lupF", [2048, 64], F32)
    ein("ldgF", [64, 2048], F32)
    ein("biasw", [8, 2048], F32)
    ein("dbwT", [2048, 8], F32)
    ein("slope_c", [8, 1], F32)
    ein("center_c", [8, 1], F32)
    ein("wq", [2048, INNER], F32)
    ein("wk", [2048, INNER], F32)
    ein("wv", [2048, INNER], F32)
    ein("e8h", [8, 64], F16)
    ein("e8f", [8, 64], F32)
    ein("e8t", [64, 8], BF16)
    ein("iota9", [128, 8], F32)
    ein("iotan", [128, 8], F32)
    ein("wo", [INNER, 2048], F32)
    ein("bo_c", [128, 16], F32)
    ein("ehw", [128, 16 * 32], F32)
    ein("fhw", [32, 16 * 128], F32)
    def itn(n, s, dt):
        D[n] = nc.dram_tensor("i_" + n, s, dt).ap()
    itn("xhT", [2048, TPC], F16)
    itn("xhTb", [8 * 2048, TPC], F16)
    itn("wabs", [2, 1], F32)
    itn("wtot", [2, 1], F32)
    itn("u16", [512, NTOK], F16)
    itn("zs16", [512, NTOK], F16)
    itn("ssmp", [160, NTOK], F32)
    itn("ssms", [160, NTOK], F32)
    itn("w1q", [256, DFF], BF16)
    itn("w2q", [1024, 2048], BF16)
    itn("w1qf", [2048, DFF], BF16)
    itn("w2qf", [DFF, 2048], BF16)
    itn("ysz_a2a", [DIN, TPC], F16)
    itn("yszg", [DIN, TPC], F16)
    itn("x2T", [2048, TPC], F32)
    itn("qO", [INNER, TPC], F16)
    itn("kvp", [INNER, 65], F32)
    itn("kvs", [INNER, 65], F32)
    itn("x3d", [2048, TPC], F32)
    D["outT"] = _eout(nc, "outT", [2048, TPC], F32)

    with contextlib.ExitStack() as outer:
        E = outer.enter_context
        p = Px(nc, E)
        ccsems = [E(nc.semaphore("ccs%d" % k)) for k in range(8)]
        blk = E(nc.Block())

        def cc(k, kind, op, groups, in_ap, out_ap, wait=True):
            def f(g):
                p.w(g, "Egp_cc%d" % k, p.bar())
                with allow_ici_all_to_all():
                    inst = g.collective_compute(
                        kind, op, replica_groups=groups,
                        ins=[in_ap.opt()], outs=[out_ap.opt()])
                inst.then_inc(ccsems[k])
                if wait:
                    g.wait_ge(ccsems[k], 1)
                    p.t("gpc", g.nop(), 1)
            blk.gpsimd(f)

        emit_ln1(_NcU(nc, 's0_'), p, blk, D)
        _sbar(p, blk, "c1")
        cc(0, "AllGather", ALU.bypass, G_ALL, D["xhT"], D["xhTb"])
        cc(1, "AllReduce", ALU.add, G_ALL, D["wabs"], D["wtot"])
        _sbar(p, blk, "c2")
        emit_front(_NcU(nc, 's1_'), p, blk, D)
        _sbar(p, blk, "c3")
        cc(2, "AllReduce", ALU.add, G_ALL, D["ssmp"], D["ssms"])
        cc(3, "AllGather", ALU.bypass, G_ALL, D["w1q"], D["w1qf"], wait=True)
        cc(4, "AllGather", ALU.bypass, G_ALL, D["w2q"], D["w2qf"], wait=True)
        D["ccs3"] = ccsems[3]
        D["ccs4"] = ccsems[4]
        _sbar(p, blk, "c4")
        emit_scan(_NcU(nc, 's2_'), p, blk, D)
        _sbar(p, blk, "c5")
        cc(5, "AllToAll", ALU.bypass, G_ALL, D["ysz_a2a"], D["yszg"])
        _sbar(p, blk, "c6")
        emit_post1(_NcU(nc, 's3_'), p, blk, D)
        _sbar(p, blk, "c7")
        cc(6, "AllReduce", ALU.add, G_BATCH, D["kvp"], D["kvs"])
        _sbar(p, blk, "c8")
        emit_post2(_NcU(nc, 's4_'), p, blk, D)

        def f_end(sy):
            p.w(sy, "Esy_end", p.bar())
        blk.sync(f_end)
    return nc


def _prep(inputs):
    g = {}
    x = np.ascontiguousarray(inputs["L0_attn_hidden_states"], dtype=np.float32)
    xT = np.ascontiguousarray(x.reshape(NTOK, D).T)
    g["xT"] = [np.ascontiguousarray(xT[:, c * TPC:(c + 1) * TPC]) for c in range(NC)]
    w1T = np.ascontiguousarray(inputs["L1_ffn_W1"].T)
    w2T = np.ascontiguousarray(inputs["L1_ffn_W2"].T)
    g["w1T"] = [np.ascontiguousarray(w1T[c * 256:(c + 1) * 256, :]) for c in range(NC)]
    g["w2T"] = [np.ascontiguousarray(w2T[c * 1024:(c + 1) * 1024, :]) for c in range(NC)]
    g["ones_bf"] = [np.ones((128, 1), ml_dtypes.bfloat16)] * NC
    g["ones_h"] = [np.ones((128, 1), np.float16)] * NC
    g["ones_row"] = [np.ones((1, 128), np.float32)] * NC
    g["ones_f"] = [np.ones((128, 1), np.float32)] * NC
    g["eps_c"] = [np.full((128, 1), LN_EPS, np.float32)] * NC

    w_in = inputs["L0_attn_W_in"]
    SL = DIN // NC
    g["winT"], g["convw"], g["convb"], g["wxT"] = [], [], [], []
    g["wdtT"], g["dtb"], g["acol"], g["dcol"] = [], [], [], []
    for c in range(NC):
        sl = slice(c * SL, (c + 1) * SL)
        rows = np.r_[np.arange(c * SL, (c + 1) * SL),
                     np.arange(DIN + c * SL, DIN + (c + 1) * SL)]
        g["winT"].append(np.ascontiguousarray(w_in[rows, :].T))
        g["convw"].append(np.ascontiguousarray(
            inputs["L0_attn_conv_w"][sl, :].reshape(4, 128, 4)
            .transpose(1, 0, 2).reshape(128, 16)))
        g["convb"].append(np.ascontiguousarray(
            inputs["L0_attn_conv_b"][sl].reshape(4, 128).T))
        g["wxT"].append(np.ascontiguousarray(
            inputs["L0_attn_W_x"][:, sl].T.reshape(4, 128, 160)
            .transpose(1, 0, 2).reshape(128, 4 * 160)))
        g["wdtT"].append(np.ascontiguousarray(inputs["L0_attn_W_dt"][sl, :].T))
        dtb = inputs["L0_attn_dt_bias"][sl].reshape(4, 128).T
        g["dtb"].append(np.ascontiguousarray(dtb))
        A = inputs["L0_attn_A"][sl, :].reshape(4, 128, 16)
        acol = np.ascontiguousarray(A.transpose(1, 0, 2).reshape(128, 64))
        g["acol"].append(acol)
        dvec = inputs["L0_attn_D"][sl].reshape(4, 128).T
        g["dcol"].append(np.ascontiguousarray(dvec))

    f32 = np.float32
    g["woutT"] = np.ascontiguousarray(inputs["L0_attn_W_out"].T.astype(f32))
    g["selW"] = np.ascontiguousarray(
        inputs["L0_ffn_select_weight"].transpose(1, 0, 2).reshape(D, 64))
    g["mdT"] = np.ascontiguousarray(inputs["L0_ffn_select_mean_diff"].T)
    g["mdrep"] = np.ascontiguousarray(np.repeat(g["mdT"], 8, axis=1))
    g["lupF"] = np.ascontiguousarray(
        inputs["L0_ffn_lora_update"].transpose(1, 0, 2).reshape(D, 64))
    g["ldgF"] = np.ascontiguousarray(
        inputs["L0_ffn_lora_degen"].transpose(0, 2, 1).reshape(64, D))
    g["biasw"] = np.ascontiguousarray(inputs["L0_ffn_bias_w"].astype(f32))
    g["dbwT"] = np.ascontiguousarray(inputs["L0_ffn_debias_w"].T)
    g["slope_c"] = np.ascontiguousarray(
        inputs["L0_ffn_imp_slope"].reshape(8, 1).astype(f32))
    g["center_c"] = np.ascontiguousarray(
        inputs["L0_ffn_imp_center"].reshape(8, 1).astype(f32))
    g["wq"] = np.ascontiguousarray(inputs["L1_attn_Wq"].astype(f32))
    g["wk"] = np.ascontiguousarray(inputs["L1_attn_Wk"].astype(f32))
    g["wv"] = np.ascontiguousarray(inputs["L1_attn_Wv"].astype(f32))
    g["wo"] = np.ascontiguousarray(inputs["L1_attn_Wo"].astype(f32))
    g["bo_c"] = np.ascontiguousarray(
        inputs["L1_attn_bo"].reshape(16, 128).T.astype(f32))
    g["identf"] = np.eye(128, dtype=f32)
    e8 = np.zeros((8, 64), f32)
    for n in range(8):
        e8[n, n * 8:(n + 1) * 8] = 1.0
    g["e8f"] = e8
    g["e8bf"] = e8.astype(ml_dtypes.bfloat16)
    g["e8h"] = e8.astype(np.float16)
    g["e8t"] = np.ascontiguousarray(e8.T).astype(ml_dtypes.bfloat16)
    g["iota9"] = np.ascontiguousarray(
        np.broadcast_to(np.arange(8, dtype=f32) + 9.0, (128, 8)))
    g["iotan"] = np.ascontiguousarray(
        np.broadcast_to(np.arange(8, dtype=f32), (128, 8)))
    ehw = np.zeros((128, 16 * 32), f32)
    fhw = np.zeros((32, 16 * 128), f32)
    for w in range(16):
        for pp in range(128):
            h = (w * 128 + pp) // 64
            ehw[pp, w * 32 + h] = 1.0
            fhw[h, w * 128 + pp] = 1.0
    g["ehw"] = ehw
    g["fhw"] = np.ascontiguousarray(fhw)
    return g


def kernel(**inputs):
    g = _prep(inputs)
    nc = build_fused()
    names_shared = ["woutT", "selW", "mdT", "mdrep", "lupF", "ldgF", "biasw",
                    "dbwT", "slope_c", "center_c", "wq", "wk", "wv", "wo",
                    "bo_c", "identf", "e8h", "e8f", "e8t", "iota9", "iotan",
                    "ehw", "fhw"]
    names_per = ["xT", "w1T", "w2T", "ones_h", "ones_bf", "ones_row", "ones_f",
                 "eps_c", "winT", "convw", "convb", "wxT", "wdtT", "dtb",
                 "acol", "dcol"]
    maps = []
    for c in range(NC):
        m = {}
        for k in names_shared:
            m[k] = g[k]
        for k in names_per:
            m[k] = g[k][c]
        maps.append(m)
    res = run_bass_kernel_spmd(nc, maps, core_ids=list(range(NC)))
    out = np.concatenate(
        [np.ascontiguousarray(res.results[c]["outT"]).T for c in range(NC)],
        axis=0)
    return np.ascontiguousarray(out.reshape(B, L, D)).astype(np.float32)


# version 13
# speedup vs baseline: 1.2328x; 1.0074x over previous
import contextlib
import numpy as np
import ml_dtypes
import concourse.bass as bass
import concourse.mybir as mybir
from concourse.bass_utils import run_bass_kernel_spmd

F32 = mybir.dt.float32
BF16 = mybir.dt.bfloat16
F16 = mybir.dt.float16
AF = mybir.ActivationFunctionType
ALU = mybir.AluOpType
AX = mybir.AxisListType

MAGIC = float(np.float32(1.5 * 2 ** 23))
LN_EPS = 1e-5

B, L, D = 2, 4096, 2048
NTOK = B * L
NC = 8
TPC = NTOK // NC
DIN = 4096
DPC = DIN // 4
DST = 16
DTR = 128
HEADS, HD = 32, 64
INNER = 2048
DFF = 8192
NT_D = D // 128
NT_DPC = DPC // 128


class Px:

    def __init__(self, nc, E):
        self.nc = nc
        self.sem = {}
        self.cnt = {}
        self.hw = {}
        for s in ("qs", "qg", "pe", "act", "dve", "gpc", "gp"):
            self.sem[s] = E(nc.semaphore("S_" + s))
            self.cnt[s] = 0

    def tok(self, s):
        return (s, self.cnt[s])

    def t(self, s, inst, amt=None):
        if amt is None:
            amt = 16 if s in ("qs", "qg") else 1
        self.cnt[s] += amt
        inst.then_inc(self.sem[s], amt)
        return (s, self.cnt[s])

    def w(self, eng, engname, toks):
        for (s, v) in toks:
            if v <= 0:
                continue
            key = (engname, s)
            if self.hw.get(key, -1) >= v:
                continue
            eng.wait_ge(self.sem[s], v)
            self.hw[key] = v

    def bar(self):
        return [(s, self.cnt[s]) for s in self.cnt]



def MM(p, te, out, lhsT, rhs, start, stop, nmax=512):
    n = rhs.shape[-1]
    inst = None
    for o in range(0, n, nmax):
        e = min(n, o + nmax)
        inst = te.matmul(out[:, o:e], lhsT, rhs[:, o:e], start=start, stop=stop)
    return inst


def _ein(nc, name, shape, dtype):
    return nc.dram_tensor(name, shape, dtype, kind="ExternalInput").ap()


def _eout(nc, name, shape, dtype):
    return nc.dram_tensor(name, shape, dtype, kind="ExternalOutput").ap()


def emit_ln(p, blk, tag, x_sb, tc, xh_sb, rows, ps_r1, ps_r2, ps_bc,
            scr_bf, bci, bcn, ones_bf, ones_row, eps_col, deps, xh32_sb=None,
            nt=16, scrT=None):

    def tile(t, i):
        return t[:, i * tc:(i + 1) * tc]

    def f_scalar(sc):
        p.w(sc, "act" + tag, deps)
        for i in range(nt):
            p.t("act", sc.copy(tile(scr_bf, i), tile(x_sb, i)))
    blk.scalar(f_scalar)
    cast_done = p.tok("act")

    def f_pe1(te):
        p.w(te, "pe" + tag, [cast_done])
        for i in range(nt):
            inst = MM(p, te, ps_r1[0:1, 0:tc], ones_bf[:], tile(scr_bf, i),
                      (i == 0), (i == nt - 1))
        p.t("pe", inst)
    blk.tensor(f_pe1)
    sum_done = p.tok("pe")

    def f_s2(sc):
        p.w(sc, "act" + tag, [sum_done])
        for i in range(nt):
            p.t("act", sc.activation(tile(scr_bf, i), tile(x_sb, i), AF.Square))
    blk.scalar(f_s2)
    sq_done = p.tok("act")

    def f_pe2(te):
        p.w(te, "pe" + tag, [sq_done])
        for i in range(nt):
            inst = MM(p, te, ps_r2[0:1, 0:tc], ones_bf[:], tile(scr_bf, i),
                      (i == 0), (i == nt - 1))
        p.t("pe", inst)
    blk.tensor(f_pe2)
    sq_sum = p.tok("pe")

    Dn = nt * 128

    def f_v2(v):
        p.w(v, "dve" + tag, [sum_done, sq_sum])
        p.t("dve", v.tensor_scalar_mul(rows[:, 0:tc], ps_r1[0:1, 0:tc], 1.0 / Dn))
        p.t("dve", v.tensor_scalar_mul(rows[:, tc:2 * tc], ps_r2[0:1, 0:tc],
                                       1.0 / Dn))
        p.w(v, "dve" + tag, [p.tok("dve")])
        p.t("dve", v.tensor_mul(rows[:, 2 * tc:3 * tc], rows[:, 0:tc],
                                rows[:, 0:tc]))
        p.w(v, "dve" + tag, [p.tok("dve")])
        p.t("dve", v.tensor_sub(rows[:, tc:2 * tc], rows[:, tc:2 * tc],
                                rows[:, 2 * tc:3 * tc]))
    blk.vector(f_v2)
    var_done = p.tok("dve")

    def f_s3(sc):
        p.w(sc, "act" + tag, [var_done])
        p.t("act", sc.activation(rows[:, 2 * tc:3 * tc], rows[:, tc:2 * tc],
                                 AF.Ln, bias=eps_col[0:1, 0:1]))
        p.w(sc, "act" + tag, [p.tok("act")])
        p.t("act", sc.activation(rows[:, 2 * tc:3 * tc], rows[:, 2 * tc:3 * tc],
                                 AF.Exp, scale=-0.5))
    blk.scalar(f_s3)
    inv_done = p.tok("act")

    def f_v3(v):
        p.w(v, "dve" + tag, [inv_done])
        p.t("dve", v.tensor_mul(rows[:, 3 * tc:4 * tc], rows[:, 0:tc],
                                rows[:, 2 * tc:3 * tc]))
        p.w(v, "dve" + tag, [p.tok("dve")])
        p.t("dve", v.tensor_scalar_mul(rows[:, 3 * tc:4 * tc],
                                       rows[:, 3 * tc:4 * tc], -1.0))
    blk.vector(f_v3)
    nmi_done = p.tok("dve")

    def f_pe3(te):
        p.w(te, "pe" + tag, [inv_done])
        p.t("pe", MM(p, te, ps_bc[:, 0:tc], ones_row[:],
                     rows[0:1, 2 * tc:3 * tc], True, True))
    blk.tensor(f_pe3)
    bc1 = p.tok("pe")

    def f_sc1(sc):
        p.w(sc, "act" + tag, [bc1])
        p.t("act", sc.copy(bci, ps_bc[:, 0:tc]))
    blk.scalar(f_sc1)
    ev1 = p.tok("act")

    def f_pe4(te):
        p.w(te, "pe" + tag, [nmi_done, ev1])
        p.t("pe", MM(p, te, ps_bc[:, 0:tc], ones_row[:],
                     rows[0:1, 3 * tc:4 * tc], True, True))
    blk.tensor(f_pe4)
    bc2 = p.tok("pe")

    def f_sc2(sc):
        p.w(sc, "act" + tag, [bc2])
        p.t("act", sc.copy(bcn, ps_bc[:, 0:tc]))
    blk.scalar(f_sc2)
    ev2 = p.tok("act")

    def f_v5(v):
        p.w(v, "dve" + tag, [ev1, ev2])
        for i in range(nt):
            if scrT is None:
                p.t("dve", v.tensor_mul(tile(x_sb, i), tile(x_sb, i), bci))
                p.w(v, "dve" + tag, [p.tok("dve")])
                p.t("dve", v.tensor_add(tile(xh_sb, i), tile(x_sb, i), bcn))
                if xh32_sb is not None:
                    p.t("dve", v.tensor_add(tile(xh32_sb, i), tile(x_sb, i),
                                            bcn))
            else:
                p.t("dve", v.tensor_mul(scrT[:, 0:tc], tile(x_sb, i), bci))
                p.w(v, "dve" + tag, [p.tok("dve")])
                p.t("dve", v.tensor_add(tile(xh_sb, i), scrT[:, 0:tc], bcn))
                if xh32_sb is not None:
                    p.t("dve", v.tensor_add(tile(xh32_sb, i), scrT[:, 0:tc],
                                            bcn))
                p.w(v, "dve" + tag, [p.tok("dve")])
    blk.vector(f_v5)
    return [p.tok("dve")]


def emit_ln1(nc, p, blk, IO):
    xT = IO["xT"]
    w1T = IO["w1T"]
    w2T = IO["w2T"]
    ones_bf_i = IO["ones_h"]
    ones_row_i = IO["ones_row"]
    ones_f_i = IO["ones_f"]
    eps_i = IO["eps_c"]
    xhT = IO["xhT"]
    wabs = IO["wabs"]

    with contextlib.ExitStack() as st_:
        E = st_.enter_context
        ones_bf = E(nc.sbuf_tensor("c_ones_h", [128, 1], F16))
        ones_row = E(nc.sbuf_tensor("c_ones_row", [1, 128], F32))
        ones_f = E(nc.sbuf_tensor("c_ones_f", [128, 1], F32))
        eps_c = E(nc.sbuf_tensor("c_eps", [128, 1], F32))
        x_sb = E(nc.sbuf_tensor("x_sb", [128, NT_D * TPC], F32))
        xh_sb = E(nc.sbuf_tensor("xh_sb", [128, NT_D * TPC], F16))
        scr_bf = E(nc.sbuf_tensor("scr_bf", [128, NT_D * TPC], F16))
        bc_sb = E(nc.sbuf_tensor("bc_sb", [128, 2 * TPC], F32))
        rows = E(nc.sbuf_tensor("rows", [1, 4 * TPC], F32))
        wbuf = E(nc.sbuf_tensor("wbuf", [128, DFF], F32))
        red = E(nc.sbuf_tensor("red", [128, 16], F32))
        red2 = E(nc.sbuf_tensor("red2", [128, 64], F32))
        wsum = E(nc.sbuf_tensor("wsum", [2, 1], F32))
        ps_r1 = E(nc.psum_tensor("ps_r1", [1, TPC], F32))
        ps_r2 = E(nc.psum_tensor("ps_r2", [1, TPC], F32))
        ps_bc = E(nc.psum_tensor("ps_bc", [128, TPC], F32))
        ps_w = E(nc.psum_tensor("ps_w", [2, 1], F32))

        def f_ld(sy):
            for sb, ap in [(ones_bf, ones_bf_i), (ones_row, ones_row_i),
                           (ones_f, ones_f_i), (eps_c, eps_i)]:
                p.t("qs", sy.dma_start(out=sb[:], in_=ap))
            for i in range(NT_D):
                p.t("qs", sy.dma_start(out=x_sb[:, i * TPC:(i + 1) * TPC],
                                       in_=xT[i * 128:(i + 1) * 128, :]))
        blk.sync(f_ld)
        ld_done = p.tok("qs")

        def wpass(src, nrow, width, colbase):
            for i in range(nrow):
                def f_g(g, i=i):
                    p.w(g, "qg", [p.tok("dve")])
                    p.t("qg", g.dma_start(out=wbuf[:, 0:width],
                                          in_=src[i * 128:(i + 1) * 128, :]))
                blk.gpsimd(f_g)
                ldt = p.tok("qg")

                def f_v(v, i=i):
                    p.w(v, "dve", [ldt])
                    p.t("dve", v.tensor_reduce(
                        red2[:, 0:width // 128],
                        wbuf[:, 0:width].rearrange("p (a b) -> p a b", b=128),
                        AX.X, ALU.add, apply_absolute_value=True))
                    p.w(v, "dve", [p.tok("dve")])
                    p.t("dve", v.tensor_reduce(
                        red[:, colbase + i:colbase + i + 1],
                        red2[:, 0:width // 128].rearrange("p (o a) -> p o a", o=1),
                        AX.X, ALU.add))
                blk.vector(f_v)

        wpass(w1T, 2, DFF, 0)
        wpass(w2T, 8, D, 2)

        def f_vsum(v):
            p.w(v, "dve", [p.tok("dve")])
            p.t("dve", v.tensor_reduce(
                red[:, 10:11], red[:, 0:2].rearrange("p (a b) -> p a b", a=1),
                AX.X, ALU.add))
            p.t("dve", v.tensor_reduce(
                red[:, 11:12], red[:, 2:10].rearrange("p (a b) -> p a b", a=1),
                AX.X, ALU.add))
        blk.vector(f_vsum)
        red_done = p.tok("dve")

        def f_pw(te):
            p.w(te, "pe", [red_done, ld_done])
            p.t("pe", te.matmul(ps_w[:], red[:, 10:12], ones_f[:],
                                start=True, stop=True))
        blk.tensor(f_pw)
        pw_done = p.tok("pe")

        def f_vw(v):
            p.w(v, "dve", [pw_done])
            p.t("dve", v.tensor_copy(wsum[:], ps_w[:]))
        blk.vector(f_vw)
        wsum_done = p.tok("dve")

        ln_done = emit_ln(p, blk, "", x_sb, TPC, xh_sb, rows, ps_r1, ps_r2,
                          ps_bc, scr_bf, bc_sb[:, 0:TPC], bc_sb[:, TPC:2 * TPC],
                          ones_bf, ones_row, eps_c[:, :], [ld_done])

        def f_st(sy):
            p.w(sy, "qs", ln_done + [wsum_done])
            for i in range(NT_D):
                p.t("qs", sy.dma_start(out=xhT[i * 128:(i + 1) * 128, :],
                                       in_=xh_sb[:, i * TPC:(i + 1) * TPC]))
            p.t("qs", sy.dma_start(out=wabs, in_=wsum[:]))
            sy.wait_ge(p.sem["qs"], p.cnt["qs"])
        blk.sync(f_st)



def emit_front(nc, p, blk, IO):
    TC = 512
    NCH = NTOK // TC
    xhTb = IO["xhTb"]
    winT = IO["winT"]
    convw = IO["convw"]
    convb = IO["convb"]
    wxT = IO["wxT"]
    w1T = IO["w1T"]
    w2T = IO["w2T"]
    wtot = IO["wtot"]
    ones_row_i = IO["ones_row"]
    u16o = IO["u16"]
    zs16o = IO["zs16"]
    ssmpo = IO["ssmp"]
    w1qo = IO["w1q"]
    w2qo = IO["w2q"]

    with contextlib.ExitStack() as st_:
        E = st_.enter_context
        ones_row = E(nc.sbuf_tensor("c_ones_row", [1, 128], F32))
        winbf = E(nc.sbuf_tensor("winbf", [128, 16 * 1024], F16))
        wx16 = E(nc.sbuf_tensor("wx16", [128, 4 * 160], F16))
        wld = E(nc.sbuf_tensor("wld", [128, 2048], F32))
        wx32 = E(nc.sbuf_tensor("wx32", [128, 4 * 160], F32))
        cw = E(nc.sbuf_tensor("cw", [128, 4 * 4], F32))
        cb = E(nc.sbuf_tensor("cb", [128, 4], F32))
        xh2 = E(nc.sbuf_tensor("xh2", [128, 2 * 16 * TC], F16))
        xzx = E(nc.sbuf_tensor("xzx", [128, 4 * (TC + 3)], F32))
        acc = E(nc.sbuf_tensor("acc", [128, 2 * TC], F32))
        ucv = E(nc.sbuf_tensor("ucv", [128, 8 * TC], F16))
        zst = E(nc.sbuf_tensor("zst", [128, 4 * TC], F16))
        s_out = E(nc.sbuf_tensor("s_out", [128, TC], F32))
        s_out2 = E(nc.sbuf_tensor("s_out2", [32, TC], F32))
        ps_x = E(nc.psum_tensor("ps_x", [128, TC], F32))
        ps_x2 = E(nc.psum_tensor("ps_x2", [128, TC], F32))
        ps_s1 = E(nc.psum_tensor("ps_s1", [128, TC], F32))
        ps_s2 = E(nc.psum_tensor("ps_s2", [32, TC], F32))
        ps_b = E(nc.psum_tensor("ps_b", [128, 2], F32))
        wqi = E(nc.sbuf_tensor("wqi", [128, DFF], F32))
        wqt = E(nc.sbuf_tensor("wqt", [128, DFF], F32))
        wqo_sb = E(nc.sbuf_tensor("wqo_sb", [128, DFF], BF16))
        srow = E(nc.sbuf_tensor("srow", [1, 4], F32))
        sbc = E(nc.sbuf_tensor("sbc", [128, 2], F32))
        wab = E(nc.sbuf_tensor("wab", [1, 2], F32))

        def f_ld(sy):
            p.t("qs", sy.dma_start(out=ones_row[:], in_=ones_row_i))
            p.t("qs", sy.dma_start(out=cw[:], in_=convw))
            p.t("qs", sy.dma_start(out=cb[:], in_=convb))
            p.t("qs", sy.dma_start(out=wx32[:], in_=wxT))
            p.t("qs", sy.dma_start(out=wab[:, 0:1], in_=wtot[0:1, :]))
            p.t("qs", sy.dma_start(out=wab[:, 1:2], in_=wtot[1:2, :]))
        blk.sync(f_ld)
        ld0 = p.tok("qs")

        def f_cast0(sc):
            p.w(sc, "act", [ld0])
            p.t("act", sc.copy(wx16[:], wx32[:]))
        blk.scalar(f_cast0)

        for i in range(16):
            def f_ldi(sy, i=i):
                p.w(sy, "qs", [p.tok("act")])
                p.t("qs", sy.dma_start(out=wld[:, 0:1024],
                                       in_=winT[i * 128:(i + 1) * 128, :]))
            blk.sync(f_ldi)
            ldt = p.tok("qs")

            def f_ci(sc, i=i, ldt=ldt):
                p.w(sc, "act", [ldt])
                p.t("act", sc.copy(winbf[:, i * 1024:(i + 1) * 1024],
                                   wld[:, 0:1024]))
            blk.scalar(f_ci)
        wcast = p.tok("act")

        def f_vq0(v):
            p.w(v, "dve", [ld0])
            p.t("dve", v.tensor_scalar_mul(srow[:, 0:2], wab[0:1, 0:2],
                                           1.0 / (DFF * D)))
            p.w(v, "dve", [p.tok("dve")])
            p.t("dve", v.reciprocal(srow[:, 2:4], srow[:, 0:2]))
        blk.vector(f_vq0)
        sc_done = p.tok("dve")

        def f_pq(te):
            p.w(te, "pe", [sc_done])
            p.t("pe", te.matmul(ps_b[:], ones_row[:], srow[0:1, 2:4],
                                start=True, stop=True))
        blk.tensor(f_pq)
        pb_done = p.tok("pe")

        def f_vq1(v):
            p.w(v, "dve", [pb_done])
            p.t("dve", v.tensor_copy(sbc[:], ps_b[:]))
        blk.vector(f_vq1)
        sbc_done = p.tok("dve")

        def quantpass(src, dst, nrow, width, scol):
            for i in range(nrow):
                def f_g(g, i=i):
                    p.w(g, "qg", [p.tok("dve"), p.tok("qg")])
                    p.t("qg", g.dma_start(out=wqi[:, 0:width],
                                          in_=src[i * 128:(i + 1) * 128, :]))
                blk.gpsimd(f_g)
                ldt = p.tok("qg")

                def f_v(v, i=i, ldt=ldt):
                    p.w(v, "dve", [ldt, sbc_done])
                    p.t("dve", v.tensor_scalar_mul(wqt[:, 0:width],
                                                   wqi[:, 0:width],
                                                   sbc[:, scol:scol + 1]))
                    p.w(v, "dve", [p.tok("dve")])
                    p.t("dve", v.tensor_scalar(wqt[:, 0:width], wqt[:, 0:width],
                                               MAGIC, MAGIC, ALU.add,
                                               ALU.subtract))
                    p.w(v, "dve", [p.tok("dve")])
                    p.t("dve", v.tensor_scalar(wqo_sb[:, 0:width],
                                               wqt[:, 0:width], 1.0, -1.0,
                                               ALU.min, ALU.max))
                blk.vector(f_v)
                qt = p.tok("dve")

                def f_s(sy, i=i, qt=qt):
                    p.w(sy, "qg", [qt])
                    p.t("qg", sy.dma_start(out=dst[i * 128:(i + 1) * 128, :],
                                           in_=wqo_sb[:, 0:width]))
                blk.gpsimd(f_s)

        quantpass(w1T, w1qo, 2, DFF, 0)
        quantpass(w2T, w2qo, 8, D, 1)
        quant_done = p.tok("qg")

        def tl(t, i, w=TC):
            return t[:, i * w:(i + 1) * w]

        def f_hist0(v):
            for mt in range(4):
                p.t("dve", v.memset(xzx[:, mt * (TC + 3):mt * (TC + 3) + 3], 0.0))
        blk.vector(f_hist0)

        for ch in range(NCH):
            xsl = (ch % 2) * 16 * TC
            if ch == NCH // 2:
                def f_histr(v):
                    p.w(v, "dve", [("act", p.cnt["act"]), p.tok("dve")])
                    for mt in range(4):
                        p.t("dve", v.memset(
                            xzx[:, mt * (TC + 3):mt * (TC + 3) + 3], 0.0))
                blk.vector(f_histr)

            pe_c = p.cnt["pe"]
            a_base = p.cnt["act"]
            v_base = p.cnt["dve"]
            qs_c = p.cnt["qs"]

            def f_ldc(sy, ch=ch, xsl=xsl, pe_c=pe_c):
                if ch >= 2:
                    p.w(sy, "qs", [("pe", pe_c - 9)])
                for i in range(16):
                    p.t("qs", sy.dma_start(
                        out=xh2[:, xsl + i * TC:xsl + (i + 1) * TC],
                        in_=xhTb[((ch * TC) // 1024) * 2048 + i * 128:
                                 ((ch * TC) // 1024) * 2048 + (i + 1) * 128,
                                 (ch * TC) % 1024:(ch * TC) % 1024 + TC]))
            blk.sync(f_ldc)
            ldc = p.tok("qs")

            def f_pe(te, ch=ch, xsl=xsl, a_base=a_base, ldc=ldc):
                p.w(te, "pe", [ldc, wcast])
                for mt in range(8):
                    ps = ps_x if mt % 2 == 0 else ps_x2
                    if mt >= 2:
                        p.w(te, "pe", [("act", a_base + mt - 1)])
                    elif ch > 0:
                        p.w(te, "pe", [("act", a_base)])
                    for kt in range(16):
                        inst = te.matmul(
                            ps[:],
                            winbf[:, kt * 1024 + mt * 128:kt * 1024 + (mt + 1) * 128],
                            xh2[:, xsl + kt * TC:xsl + (kt + 1) * TC],
                            start=(kt == 0), stop=(kt == 15))
                    p.t("pe", inst)
                p.w(te, "pe", [("act", a_base + 12)])
                for kt in range(4):
                    i1 = te.matmul(ps_s1[:], wx16[:, kt * 160:kt * 160 + 128],
                                   tl(ucv, kt), start=(kt == 0), stop=(kt == 3))
                for kt in range(4):
                    i2 = te.matmul(ps_s2[:], wx16[:, kt * 160 + 128:kt * 160 + 160],
                                   tl(ucv, kt), start=(kt == 0), stop=(kt == 3))
                p.t("pe", i2)
            blk.tensor(f_pe)

            def f_sc(sc, ch=ch, pe_c=pe_c, v_base=v_base, qs_c=qs_c):
                for mt in range(8):
                    ps = ps_x if mt % 2 == 0 else ps_x2
                    p.w(sc, "act", [("pe", pe_c + mt + 1)])
                    if mt < 4:
                        xb = mt * (TC + 3)
                        if ch > 0:
                            p.w(sc, "act", [("dve", v_base)])
                        p.t("act", sc.copy(xzx[:, xb + 3:xb + 3 + TC], ps[:]))
                    else:
                        if ch >= 1:
                            p.w(sc, "act", [("qs", qs_c - 16 * 10 + 16 * (mt - 3))])
                        p.t("act", sc.activation(
                            zst[:, (mt - 4) * TC:(mt - 3) * TC], ps[:], AF.Silu))
                for mt in range(4):
                    p.w(sc, "act", [("dve", v_base + 5 * (mt + 1))])
                    if ch >= 1:
                        p.w(sc, "act", [("qs", qs_c - 16 * 6 + 16 * (mt + 1))])
                    p.w(sc, "act", [("pe", pe_c)])
                    p.t("act", sc.activation(tl(ucv, mt),
                                             acc[:, (mt % 2) * TC:(mt % 2 + 1) * TC],
                                             AF.Silu, bias=cb[:, mt:mt + 1]))
            blk.scalar(f_sc)

            def f_cv(v, ch=ch, a_base=a_base):
                for mt in range(4):
                    xb = mt * (TC + 3)
                    asl = (mt % 2) * TC
                    p.w(v, "dve", [("act", a_base + mt + 1), p.tok("dve")])
                    if mt >= 2:
                        p.w(v, "dve", [("act", a_base + 8 + mt - 1)])
                    elif ch > 0:
                        p.w(v, "dve", [("act", a_base)])
                    p.t("dve", v.tensor_scalar_mul(acc[:, asl:asl + TC],
                                                   xzx[:, xb:xb + TC],
                                                   cw[:, mt * 4:mt * 4 + 1]))
                    for k in range(1, 4):
                        p.w(v, "dve", [p.tok("dve")])
                        p.t("dve", v.scalar_tensor_tensor(
                            acc[:, asl:asl + TC], xzx[:, xb + k:xb + k + TC],
                            cw[:, mt * 4 + k:mt * 4 + k + 1],
                            acc[:, asl:asl + TC], ALU.mult, ALU.add))
                    p.w(v, "dve", [p.tok("dve"),
                                   ("act", a_base + 8 + mt)])
                    p.t("dve", v.tensor_copy(xzx[:, xb:xb + 3],
                                             xzx[:, xb + TC:xb + TC + 3]))
            blk.vector(f_cv)

            def f_sev(v, pe_c=pe_c):
                p.w(v, "dve", [("pe", pe_c + 9), ("qs", p.cnt["qs"])])
                p.t("dve", v.tensor_copy(s_out[:], ps_s1[:]))
                p.w(v, "dve", [p.tok("dve")])
                p.t("dve", v.tensor_copy(s_out2[:], ps_s2[:]))
            blk.vector(f_sev)

            def f_st(sy, ch=ch, a_base=a_base):
                for mt in range(4):
                    p.w(sy, "qs", [("act", a_base + 4 + mt + 1)])
                    p.t("qs", sy.dma_start(
                        out=zs16o[mt * 128:(mt + 1) * 128, ch * TC:(ch + 1) * TC],
                        in_=zst[:, mt * TC:(mt + 1) * TC]))
                for mt in range(4):
                    p.w(sy, "qs", [("act", a_base + 8 + mt + 1)])
                    p.t("qs", sy.dma_start(
                        out=u16o[mt * 128:(mt + 1) * 128, ch * TC:(ch + 1) * TC],
                        in_=tl(ucv, mt)))
                p.w(sy, "qs", [("dve", p.cnt["dve"])])
                p.t("qs", sy.dma_start(out=ssmpo[0:128, ch * TC:(ch + 1) * TC],
                                       in_=s_out[:]))
                p.t("qs", sy.dma_start(out=ssmpo[128:160, ch * TC:(ch + 1) * TC],
                                       in_=s_out2[:]))
            blk.sync(f_st)

        def f_fin(sy):
            p.w(sy, "qs", [quant_done])
            sy.wait_ge(p.sem["qs"], p.cnt["qs"])
            sy.wait_ge(p.sem["qg"], p.cnt["qg"])
        blk.sync(f_fin)



def emit_scan(nc, p, blk, IO):
    TCS = 1024
    NCH = NTOK // TCS
    ssms = IO["ssms"]
    u16 = IO["u16"]
    zs16 = IO["zs16"]
    wdtT = IO["wdtT"]
    dtb = IO["dtb"]
    acol = IO["acol"]
    dcol = IO["dcol"]
    yszo = IO["ysz_a2a"]

    with contextlib.ExitStack() as st_:
        E = st_.enter_context
        dtr = E(nc.sbuf_tensor("dtr", [128, NTOK], F32))
        dtrb = E(nc.sbuf_tensor("dtrb", [128, NTOK], F16))
        wdt = E(nc.sbuf_tensor("wdt", [128, 512], F16))
        wdt32 = E(nc.sbuf_tensor("wdt32", [128, 512], F32))
        dtb_sb = E(nc.sbuf_tensor("dtb_sb", [128, 4], F32))
        acol_sb = E(nc.sbuf_tensor("acol_sb", [128, 64], F32))
        dcol_sb = E(nc.sbuf_tensor("dcol_sb", [128, 4], F32))
        delta = E(nc.sbuf_tensor("delta", [128, 4 * TCS], F32))
        spr = E(nc.sbuf_tensor("spr", [128, 4 * TCS], F32))
        u_sb = E(nc.sbuf_tensor("u_sb", [128, 4 * TCS], F16))
        zs_sb = E(nc.sbuf_tensor("zs_sb", [128, 4 * TCS], F16))
        wmt = E(nc.sbuf_tensor("wmt", [128, TCS], F32))
        a_sb = E(nc.sbuf_tensor("a_sb", [128, 2 * TCS], F32))
        b_sb = E(nc.sbuf_tensor("b_sb", [128, TCS], F32))
        h_sb = E(nc.sbuf_tensor("h_sb", [128, 2 * TCS], F32))
        hy_sb = E(nc.sbuf_tensor("hy_sb", [128, TCS], F32))
        y15_sb = E(nc.sbuf_tensor("y15_sb", [128, TCS], F32))
        y_sb = E(nc.sbuf_tensor("y_sb", [128, TCS], F32))
        bb_sb = E(nc.sbuf_tensor("bb_sb", [128, 2 * TCS], F16))
        cc_sb = E(nc.sbuf_tensor("cc_sb", [128, 2 * TCS], F16))
        t1_sb = E(nc.sbuf_tensor("t1_sb", [128, TCS], F32))
        yz_sb = E(nc.sbuf_tensor("yz_sb", [128, 2 * TCS], F16))
        state = E(nc.sbuf_tensor("state", [128, 64], F32))
        sp1 = E(nc.psum_tensor("sp1", [128, 1024], F32))
        sp2 = E(nc.psum_tensor("sp2", [128, 1024], F32))

        def f_ld(sy):
            for sb, ap in [(wdt32, wdtT), (dtb_sb, dtb), (acol_sb, acol),
                           (dcol_sb, dcol)]:
                p.t("qs", sy.dma_start(out=sb[:], in_=ap))
            p.t("qs", sy.dma_start(out=dtr[:], in_=ssms[0:128, :]))
        blk.sync(f_ld)
        ld0 = p.tok("qs")

        def f_cast(sc):
            p.w(sc, "act", [ld0])
            p.t("act", sc.copy(dtrb[:], dtr[:]))
            p.t("act", sc.copy(wdt[:], wdt32[:]))
        blk.scalar(f_cast)
        cast_done = p.tok("act")

        def f_st0(v):
            p.t("dve", v.memset(state[:], 0.0))
        blk.vector(f_st0)

        ssbc = IO["ssbc"]
        cv32 = delta[0:32, 0:2048]
        cv16 = spr.bitcast(F16)[0:32, 0:2048]
        for cq in range(4):
            def f_cvl(sy, cq=cq):
                p.w(sy, "qs", [p.tok("act")])
                p.t("qs", sy.dma_start(
                    out=cv32, in_=ssms[128:160, cq * 2048:(cq + 1) * 2048]))
            blk.sync(f_cvl)
            cvl = p.tok("qs")

            def f_cvc(sc, cq=cq, cvl=cvl):
                p.w(sc, "act", [cvl, p.tok("qs")])
                p.t("act", sc.copy(cv16, cv32))
            blk.scalar(f_cvc)
            cvc = p.tok("act")

            def f_cvs(sy, cq=cq, cvc=cvc):
                p.w(sy, "qs", [cvc])
                p.t("qs", sy.dma_start(
                    out=ssbc[:, cq * 2048:(cq + 1) * 2048], in_=cv16))
            blk.sync(f_cvs)
        bc_ready = p.tok("qs")

        prev_ymul62 = 0
        for ch in range(NCH):
            reset = (ch == NCH // 2)
            v0 = p.cnt["dve"]
            g0 = p.cnt["gp"]
            vc = v0 + 4 + (1 if reset else 0)
            bmul_tok = {}
            scan_tok = {}
            y15_tok = {}
            fin_tok = {}
            for mt in range(4):
                vc += 1
                for n in range(16):
                    k = mt * 16 + n
                    vc += 1
                    bmul_tok[k] = vc
                    vc += 1
                    scan_tok[k] = vc
                    vc += 1
                    if n == 15:
                        vc += 1
                        y15_tok[mt] = vc
                vc += 3
                fin_tok[mt] = vc
            gc = g0
            ymul_tok = {}
            yadd_tok = {}
            for mt in range(4):
                for n in range(15):
                    k = mt * 16 + n
                    gc += 1
                    ymul_tok[k] = gc
                    if n > 0:
                        gc += 1
                        yadd_tok[k] = gc

            def f_lu(sy, ch=ch, v0=v0):
                p.w(sy, "qs", [("dve", v0)])
                for mt in range(4):
                    p.t("qs", sy.dma_start(
                        out=u_sb[:, mt * TCS:(mt + 1) * TCS],
                        in_=u16[mt * 128:(mt + 1) * 128, ch * TCS:(ch + 1) * TCS]))
                    p.t("qs", sy.dma_start(
                        out=zs_sb[:, mt * TCS:(mt + 1) * TCS],
                        in_=zs16[mt * 128:(mt + 1) * 128, ch * TCS:(ch + 1) * TCS]))
            blk.sync(f_lu)
            lu_done = p.tok("qs")
            qs_after_lu = p.cnt["qs"]

            bct_bb = {}
            bct_cc = {}

            def f_gp(g, ch=ch, v0=v0, prev_ymul62=prev_ymul62):
                def issue_bb(k):
                    n = k % 16
                    sl = (k % 2) * TCS
                    p.t("qg", g.dma_start(
                        out=bb_sb[:, sl:sl + TCS],
                        in_=ssbc[n:n + 1,
                                 ch * TCS:(ch + 1) * TCS].partition_broadcast(128)))
                    bct_bb[k] = p.tok("qg")

                def issue_cc(k):
                    n = k % 16
                    sl = (k % 2) * TCS
                    p.t("qg", g.dma_start(
                        out=cc_sb[:, sl:sl + TCS],
                        in_=ssbc[16 + n:17 + n,
                                 ch * TCS:(ch + 1) * TCS].partition_broadcast(128)))
                    bct_cc[k] = p.tok("qg")

                if ch > 0:
                    p.w(g, "qg", [("dve", v0)])
                else:
                    p.w(g, "qg", [bc_ready])
                for k in (0, 1):
                    issue_bb(k)
                    issue_cc(k)
                for k in range(64):
                    mt, n = divmod(k, 16)
                    if k + 2 < 64:
                        p.w(g, "qg", [("dve", bmul_tok[k])])
                        issue_bb(k + 2)
                    if n == 15:
                        if k + 2 < 64:
                            p.w(g, "qg", [("dve", y15_tok[mt])])
                            issue_cc(k + 2)
                        continue
                    p.w(g, "gp", [("dve", scan_tok[k]), bct_cc[k]])
                    sl = (k % 2) * TCS
                    if n == 0:
                        p.t("gp", g.tensor_mul(y_sb[:], h_sb[:, sl:sl + TCS],
                                               cc_sb[:, sl:sl + TCS]), 1)
                    else:
                        p.t("gp", g.tensor_mul(hy_sb[:], h_sb[:, sl:sl + TCS],
                                               cc_sb[:, sl:sl + TCS]), 1)
                        p.w(g, "gp", [p.tok("gp")])
                        p.t("gp", g.tensor_add(y_sb[:], y_sb[:], hy_sb[:]), 1)
                    if k + 2 < 64:
                        issue_cc(k + 2)
            blk.gpsimd(f_gp)
            assert p.cnt["gp"] == gc, (p.cnt["gp"], gc)

            a0 = p.cnt["act"]
            def f_dm(te, ch=ch, a0=a0):
                p.w(te, "pe", [cast_done])
                for mt in range(4):
                    ps = sp1 if mt % 2 == 0 else sp2
                    if mt >= 2:
                        p.w(te, "pe", [("act", a0 + 4 * (mt - 1))])
                    elif ch > 0:
                        p.w(te, "pe", [("act", a0)])
                    p.t("pe", MM(p, te, ps[:], wdt[:, mt * 128:(mt + 1) * 128],
                                 dtrb[:, ch * TCS:(ch + 1) * TCS], True, True))
            blk.tensor(f_dm)
            pe0 = p.cnt["pe"]

            def f_sc(sc, ch=ch, v0=v0, scan_tok=scan_tok):
                if ch == 0:
                    p.w(sc, "act", [bc_ready])
                for mt in range(4):
                    ps = sp1 if mt % 2 == 0 else sp2
                    o = mt * TCS
                    p.w(sc, "act", [("pe", pe0 - 4 + mt + 1)])
                    if ch > 0:
                        p.w(sc, "act", [("dve", v0)])
                    p.t("act", sc.activation(delta[:, o:o + TCS], ps[:],
                                             AF.Abs, bias=dtb_sb[:, mt:mt + 1]))
                    p.w(sc, "act", [p.tok("act")])
                    p.t("act", sc.activation(delta[:, o:o + TCS],
                                             delta[:, o:o + TCS], AF.Exp,
                                             scale=-1.0))
                    p.w(sc, "act", [p.tok("act")])
                    p.t("act", sc.activation(delta[:, o:o + TCS],
                                             delta[:, o:o + TCS], AF.Ln,
                                             bias=1.0))
                    p.t("act", sc.activation(spr[:, o:o + TCS], ps[:], AF.Relu,
                                             bias=dtb_sb[:, mt:mt + 1]))
                p.w(sc, "act", [p.tok("act")])
                for mt in range(4):
                    for n in range(16):
                        k = mt * 16 + n
                        sl = ((k % 2)) * TCS
                        w = v0 + mt + 1
                        if k >= 2:
                            w = max(w, scan_tok[k - 2])
                        p.w(sc, "act", [("dve", w)])
                        p.t("act", sc.activation(
                            a_sb[:, sl:sl + TCS], delta[:, mt * TCS:(mt + 1) * TCS],
                            AF.Exp, scale=acol_sb[:, mt * 16 + n:mt * 16 + n + 1]))
            blk.scalar(f_sc)
            act_after_sp = a0 + 4 * 4
            def exp_tok(mt, n):
                return ("act", act_after_sp + mt * 16 + n + 1)

            def f_v(v, ch=ch, v0=v0, lu_done=lu_done, reset=reset,
                    prev_ymul62=prev_ymul62):
                for mt in range(4):
                    o = mt * TCS
                    p.w(v, "dve", [("act", a0 + 4 * (mt + 1))])
                    p.t("dve", v.tensor_add(delta[:, o:o + TCS],
                                            delta[:, o:o + TCS],
                                            spr[:, o:o + TCS]))
                if reset:
                    p.w(v, "dve", [p.tok("dve")])
                    p.t("dve", v.memset(state[:], 0.0))
                for mt in range(4):
                    o = mt * TCS
                    p.w(v, "dve", [lu_done, p.tok("dve")])
                    p.t("dve", v.tensor_mul(wmt[:], delta[:, o:o + TCS],
                                            u_sb[:, o:o + TCS]))
                    for n in range(16):
                        k = mt * 16 + n
                        col = mt * 16 + n
                        sl = (k % 2) * TCS
                        p.w(v, "dve", [bct_bb[k], exp_tok(mt, n),
                                       p.tok("dve")])
                        p.t("dve", v.tensor_mul(b_sb[:], wmt[:],
                                                bb_sb[:, sl:sl + TCS]))
                        assert p.cnt["dve"] == bmul_tok[k]
                        w2 = [p.tok("dve")]
                        if k >= 2 and (k - 2) % 16 != 15:
                            w2.append(("gp", ymul_tok[k - 2]))
                        elif k < 2 and ch > 0:
                            w2.append(("gp", prev_ymul62))
                        p.w(v, "dve", w2)
                        p.t("dve", v.tensor_tensor_scan(
                            h_sb[:, sl:sl + TCS], a_sb[:, sl:sl + TCS], b_sb[:],
                            state[:, col:col + 1], ALU.mult, ALU.add))
                        assert p.cnt["dve"] == scan_tok[k]
                        p.w(v, "dve", [p.tok("dve")])
                        p.t("dve", v.tensor_copy(state[:, col:col + 1],
                                                 h_sb[:, sl + TCS - 1:sl + TCS]))
                        if n == 15:
                            p.w(v, "dve", [bct_cc[k], p.tok("dve")])
                            p.t("dve", v.tensor_mul(y15_sb[:],
                                                    h_sb[:, sl:sl + TCS],
                                                    cc_sb[:, sl:sl + TCS]))
                    ysl = ((ch * 4 + mt) % 2) * TCS
                    if mt >= 2:
                        p.w(v, "dve", [("qs", qs_after_lu + 16 * (mt - 1))])
                    p.w(v, "dve", [("qs", qs_after_lu),
                                   ("gp", yadd_tok[mt * 16 + 14]),
                                   p.tok("dve")])
                    p.t("dve", v.scalar_tensor_tensor(
                        t1_sb[:], u_sb[:, o:o + TCS], dcol_sb[:, mt:mt + 1],
                        y_sb[:], ALU.mult, ALU.add))
                    p.w(v, "dve", [p.tok("dve")])
                    p.t("dve", v.tensor_add(t1_sb[:], t1_sb[:], y15_sb[:]))
                    p.w(v, "dve", [p.tok("dve")])
                    p.t("dve", v.tensor_mul(yz_sb[:, ysl:ysl + TCS], t1_sb[:],
                                            zs_sb[:, o:o + TCS]))
                    assert p.cnt["dve"] == fin_tok[mt]
            blk.vector(f_v)
            assert p.cnt["dve"] == vc, (p.cnt["dve"], vc)
            prev_ymul62 = ymul_tok[62]

            def f_ys(sy, ch=ch):
                for mt in range(4):
                    ysl = ((ch * 4 + mt) % 2) * TCS
                    p.w(sy, "qs", [("dve", fin_tok[mt])])
                    p.t("qs", sy.dma_start(
                        out=yszo[ch * 512 + mt * 128:ch * 512 + (mt + 1) * 128, :],
                        in_=yz_sb[:, ysl:ysl + TCS]))
            blk.sync(f_ys)

        def f_end(sy):
            sy.wait_ge(p.sem["qs"], p.cnt["qs"])
        blk.sync(f_end)

def emit_post1(nc, p, blk, IO):
    TP = TPC
    yszg = IO["yszg"]
    woutT = IO["woutT"]
    xT = IO["xT"]
    selW = IO["selW"]
    mdT = IO["mdT"]
    mdrep = IO["mdrep"]
    lupF = IO["lupF"]
    ldgF = IO["ldgF"]
    biasw = IO["biasw"]
    dbwT = IO["dbwT"]
    slope_i = IO["slope_c"]
    center_i = IO["center_c"]
    wq_i = IO["wq"]
    wk_i = IO["wk"]
    wv_i = IO["wv"]
    ones_bf_i = IO["ones_h"]
    ones_b16_i = IO["ones_bf"]
    ones_row_i = IO["ones_row"]
    eps_i = IO["eps_c"]
    identf_i = IO["identf"]
    e8bf_i = IO["e8h"]
    e8f_i = IO["e8f"]
    e8t_i = IO["e8t"]
    iota9_i = IO["iota9"]
    iotan_i = IO["iotan"]
    x2To = IO["x2T"]
    qO = IO["qO"]
    kvpO = IO["kvp"]

    with contextlib.ExitStack() as st_:
        E = st_.enter_context
        ones_bf = E(nc.sbuf_tensor("c_ones_h", [128, 1], F16))
        ones_b16 = E(nc.sbuf_tensor("c_ones_b16", [128, 1], BF16))
        ones_row = E(nc.sbuf_tensor("c_ones_row", [1, 128], F32))
        eps_c = E(nc.sbuf_tensor("c_eps", [128, 1], F32))
        identf = E(nc.sbuf_tensor("c_identf", [128, 128], F32))
        e8bf = E(nc.sbuf_tensor("c_e8h", [8, 64], F16))
        e8f = E(nc.sbuf_tensor("c_e8f", [8, 64], F32))
        e8t = E(nc.sbuf_tensor("c_e8t", [64, 8], BF16))
        iota9 = E(nc.sbuf_tensor("c_iota9", [128, 8], F32))
        iotan = E(nc.sbuf_tensor("c_iotan", [128, 8], F32))
        slo_sb = E(nc.sbuf_tensor("slo_sb", [8, 1], F32))
        cen_sb = E(nc.sbuf_tensor("cen_sb", [8, 1], F32))
        xar = E(nc.sbuf_tensor("xar", [128, 16 * TP], F32))
        A1 = E(nc.sbuf_tensor("A1", [128, 16 * TP], F32))
        xh = E(nc.sbuf_tensor("xh", [128, 16 * TP], F16))
        wslot = E(nc.sbuf_tensor("wslot", [128, 2048], F32))
        selw_bf = E(nc.sbuf_tensor("selw_bf", [128, 16 * 64], F16))
        mdt_bf = E(nc.sbuf_tensor("mdt_bf", [128, 16 * 8], F16))
        mdr_bf = E(nc.sbuf_tensor("mdr_bf", [128, 16 * 64], F16))
        lup_bf = E(nc.sbuf_tensor("lup_bf", [128, 16 * 64], F16))
        dbwt_bf = E(nc.sbuf_tensor("dbwt_bf", [128, 16 * 8], F16))
        bw_bf = E(nc.sbuf_tensor("bw_bf", [8, D], F16))
        ldg_bf = E(nc.sbuf_tensor("ldg_bf", [64, D], F16))
        c1n_sb = E(nc.sbuf_tensor("c1n_sb", [64, 1], F32))
        md2_sb = E(nc.sbuf_tensor("md2_sb", [8, 1], F32))

        xm8_sb = E(nc.sbuf_tensor("xm8_sb", [8, TP], F32))
        nrm_sb = E(nc.sbuf_tensor("nrm_sb", [8, TP], F32))
        t64_sb = E(nc.sbuf_tensor("t64_sb", [64, TP], F32))
        cont_bf = E(nc.sbuf_tensor("cont_bf", [64, TP], BF16))
        negs_sb = E(nc.sbuf_tensor("negs_sb", [8, 1], F32))
        sbias_sb = E(nc.sbuf_tensor("sbias_sb", [8, 1], F32))
        mxc = E(nc.sbuf_tensor("mxc", [128, 8], F32))
        eq_sb = E(nc.sbuf_tensor("eq_sb", [128, 8], F32))
        idxc = E(nc.sbuf_tensor("idxc", [128, 1], F32))
        oneh_t = E(nc.sbuf_tensor("oneh_t", [128, 8], F32))
        onehot8 = E(nc.sbuf_tensor("onehot8", [8, TP], F16))

        k2b_bf = E(nc.sbuf_tensor("k2b_bf", [8, 64], F16))
        mx8_bf = E(nc.sbuf_tensor("mx8_bf", [64, TP], F16))
        d_sb = E(nc.sbuf_tensor("d_sb", [128, TP], F32))
        kvp_sb = E(nc.sbuf_tensor("kvp_sb", [128, 65], F32))
        psA = E(nc.psum_tensor("psA", [128, TP], F32))
        psB = E(nc.psum_tensor("psB", [128, TP], F32))
        psC = E(nc.psum_tensor("psC", [128, TP], F32))
        psD = E(nc.psum_tensor("psD", [64, TP], F32))

        rows = A1[0:1, 12 * TP:16 * TP]
        bc_sb = A1[:, 8 * TP:10 * TP]
        ssbc_sb = A1[:, 10 * TP:11 * TP]
        rrow = bc_sb[0:1, 0:TP]
        ss_sb = bc_sb[0:1, TP:2 * TP]
        rslot = xh.bitcast(F16)[:, 0:2 * TP]
        scrT = d_sb[:, 0:TP]
        smalls32 = wslot[:, 0:1024]
        e_sb = nrm_sb
        inv8_sb = xm8_sb
        a1bf = A1.bitcast(F16)
        a1h = A1.bitcast(F16)
        qkv_ev = d_sb.bitcast(F16)

        def xt(i):
            return xar[:, i * TP:(i + 1) * TP]

        def xht(i):
            return xh[:, i * TP:(i + 1) * TP]

        def f_ld(sy):
            for sb, ap in [(ones_bf, ones_bf_i), (ones_b16, ones_b16_i),
                           (ones_row, ones_row_i),
                           (eps_c, eps_i), (identf, identf_i), (e8bf, e8bf_i),
                           (e8f, e8f_i), (e8t, e8t_i), (iota9, iota9_i),
                           (iotan, iotan_i), (slo_sb, slope_i),
                           (cen_sb, center_i)]:
                p.t("qs", sy.dma_start(out=sb[:], in_=ap))
            for i in range(16):
                p.t("qs", sy.dma_start(out=xt(i), in_=xT[i * 128:(i + 1) * 128, :]))
        blk.sync(f_ld)
        ld0 = p.tok("qs")

        yar = xh.bitcast(F16)
        for half in range(2):
            ctoks = []
            for kk in range(16):
                kt = half * 16 + kk

                def f_lw(sy, kt=kt, kk=kk):
                    if kk >= 1:
                        p.w(sy, "qs", [ctoks[kk - 1]])
                    elif half > 0:
                        p.w(sy, "qs", [("act", p.cnt["act"])])
                    p.t("qs", sy.dma_start(
                        out=wslot[:, 0:2048],
                        in_=woutT[kt * 128:(kt + 1) * 128, :]))
                blk.sync(f_lw)
                lwt = p.tok("qs")

                def f_cw(sc, kk=kk, lwt=lwt, half=half):
                    p.w(sc, "act", [lwt])
                    if half > 0 and kk == 0:
                        p.w(sc, "act", [("pe", p.cnt["pe"])])
                    ctoks.append(p.t("act", sc.copy(
                        a1h[:, kk * 2048:(kk + 1) * 2048], wslot[:, 0:2048])))
                blk.scalar(f_cw)
            wc = p.tok("act")

            def f_lys(sy, half=half):
                p.w(sy, "qg", [("pe", p.cnt["pe"])])
                for kk in range(16):
                    kt = half * 16 + kk
                    p.t("qg", sy.dma_start(
                        out=yar[:, kk * TP:(kk + 1) * TP],
                        in_=yszg[kt * 128:(kt + 1) * 128, :]))
            blk.gpsimd(f_lys)
            ys_done = p.tok("qg")

            pe_b = p.cnt["pe"]
            dv_b = p.cnt["dve"]

            def f_mm(te, half=half, pe_b=pe_b, dv_b=dv_b, wc=wc,
                     ys_done=ys_done):
                p.w(te, "pe", [wc, ys_done])
                for mt in range(16):
                    ps = psA if mt % 2 == 0 else psB
                    if mt >= 2:
                        p.w(te, "pe", [("dve", dv_b + mt - 1)])
                    elif half > 0:
                        p.w(te, "pe", [("dve", dv_b)])
                    for kk in range(16):
                        i1 = MM(p, te, ps[:], a1h[:, kk * 2048 + mt * 128:
                                                  kk * 2048 + (mt + 1) * 128],
                                yar[:, kk * TP:(kk + 1) * TP],
                                kk == 0, kk == 15)
                    p.t("pe", i1)
            blk.tensor(f_mm)

            def f_ev(v, pe_b=pe_b):
                for mt in range(16):
                    ps = psA if mt % 2 == 0 else psB
                    p.w(v, "dve", [("pe", pe_b + mt + 1), ld0])
                    p.t("dve", v.tensor_add(xt(mt), xt(mt), ps[:]))
            blk.vector(f_ev)
        x1_done = p.tok("dve")


        ln2 = emit_ln(p, blk, "", xar, TP, xh, rows, psA, psB, psC,
                      a1bf[:, 0:16 * TP], bc_sb[:, 0:TP], bc_sb[:, TP:2 * TP],
                      ones_bf, ones_row, eps_c[:, :], [x1_done], scrT=scrT)

        def small_cast(src_ap, dst_bf, w):
            def f_l(sy):
                p.w(sy, "qs", [("act", p.cnt["act"])])
                p.t("qs", sy.dma_start(out=smalls32[:, 0:16 * w],
                                       in_=src_ap.rearrange("(a p) m -> p a m",
                                                            p=128)))
            blk.sync(f_l)
            lt = p.tok("qs")

            def f_c(sc, lt=lt):
                p.w(sc, "act", [lt])
                p.t("act", sc.copy(dst_bf[:, 0:16 * w], smalls32[:, 0:16 * w]))
            blk.scalar(f_c)

        small_cast(selW, selw_bf, 64)
        small_cast(mdT, mdt_bf, 8)
        small_cast(mdrep, mdr_bf, 64)
        small_cast(lupF, lup_bf, 64)
        small_cast(dbwT, dbwt_bf, 8)

        def f_lbw(sy):
            p.w(sy, "qs", [("act", p.cnt["act"])])
            p.t("qs", sy.dma_start(out=wslot[0:8, :], in_=biasw))
        blk.sync(f_lbw)
        t_ = p.tok("qs")

        def f_cbw(sc, t_=t_):
            p.w(sc, "act", [t_])
            p.t("act", sc.copy(bw_bf[:], wslot[0:8, :]))
        blk.scalar(f_cbw)

        def f_lldg(sy):
            p.w(sy, "qs", [("act", p.cnt["act"])])
            p.t("qs", sy.dma_start(out=wslot[0:64, :], in_=ldgF))
        blk.sync(f_lldg)
        t_ = p.tok("qs")

        def f_cldg(sc, t_=t_):
            p.w(sc, "act", [t_])
            p.t("act", sc.copy(ldg_bf[:], wslot[0:64, :]))
        blk.scalar(f_cldg)
        smalls_done = p.tok("act")

        def f_u64(te):
            p.w(te, "pe", [smalls_done] + ln2)
            for kt in range(16):
                i1 = MM(p, te, psD[:], selw_bf[:, kt * 64:(kt + 1) * 64],
                        xht(kt), kt == 0, kt == 15)
            p.t("pe", i1)
            for kt in range(16):
                i2 = MM(p, te, psB[0:8, :], mdt_bf[:, kt * 8:(kt + 1) * 8],
                        xht(kt), kt == 0, kt == 15)
            p.t("pe", i2)
        blk.tensor(f_u64)
        u64_done = p.tok("pe")

        def f_sq(sc):
            p.w(sc, "act", [u64_done] + ln2)
            for kt in range(16):
                p.t("act", sc.activation(a1bf[:, kt * TP:(kt + 1) * TP],
                                         xht(kt), AF.Square))
        blk.scalar(f_sq)
        sq_done = p.tok("act")

        def f_r2(te):
            p.w(te, "pe", [sq_done])
            for kt in range(16):
                i1 = MM(p, te, psA[0:1, :], ones_bf[:],
                        a1bf[:, kt * TP:(kt + 1) * TP], kt == 0, kt == 15)
            p.t("pe", i1)
        blk.tensor(f_r2)
        r2_done = p.tok("pe")

        def f_prod(v):
            p.w(v, "dve", [smalls_done])
            p.t("dve", v.tensor_mul(smalls32.bitcast(BF16)[:, 0:16 * 64],
                                    selw_bf[:], mdr_bf[:]))
        blk.vector(f_prod)
        prod_done = p.tok("dve")

        def f_mdsq(sc):
            p.w(sc, "act", [prod_done])
            p.t("act", sc.activation(smalls32.bitcast(BF16)[:, 16 * 64:16 * 72],
                                     mdt_bf[:], AF.Square))
        blk.scalar(f_mdsq)
        mdsq_done = p.tok("act")

        def f_c1(te):
            p.w(te, "pe", [prod_done, mdsq_done, r2_done])
            pbf = smalls32.bitcast(BF16)
            for kt in range(16):
                i1 = te.matmul(psC[0:64, 0:1], pbf[:, kt * 64:(kt + 1) * 64],
                               ones_b16[:], start=(kt == 0), stop=(kt == 15))
            p.t("pe", i1)
            for kt in range(16):
                i2 = te.matmul(psC[0:8, 4:5], pbf[:, 16 * 64 + kt * 8:
                                                  16 * 64 + (kt + 1) * 8],
                               ones_b16[:], start=(kt == 0), stop=(kt == 15))
            p.t("pe", i2)
        blk.tensor(f_c1)
        c1_done = p.tok("pe")

        def f_ev1(v):
            p.w(v, "dve", [c1_done])
            p.t("dve", v.tensor_scalar_mul(c1n_sb[:], psC[0:64, 0:1], -1.0))
            p.t("dve", v.tensor_copy(md2_sb[:], psC[0:8, 4:5]))
            p.t("dve", v.tensor_copy(rrow[:], psA[0:1, :]))
            p.t("dve", v.tensor_copy(xm8_sb[:], psB[0:8, :]))
        blk.vector(f_ev1)
        ev1_done = p.tok("dve")

        def f_r2b(te):
            p.w(te, "pe", [ev1_done])
            p.t("pe", MM(p, te, psB[0:8, :], ones_row[0:1, 0:8], rrow[:],
                         True, True))
        blk.tensor(f_r2b)
        r2b_done = p.tok("pe")

        def f_nrm(v):
            p.w(v, "dve", [r2b_done])
            p.t("dve", v.scalar_tensor_tensor(nrm_sb[:], xm8_sb[:], -2.0,
                                              psB[0:8, :], ALU.mult, ALU.add))
            p.w(v, "dve", [p.tok("dve")])
            p.t("dve", v.tensor_scalar_add(nrm_sb[:], nrm_sb[:],
                                           md2_sb[0:8, 0:1]))
            p.t("dve", v.tensor_mul(sbias_sb[:], slo_sb[:], cen_sb[:]))
            p.t("dve", v.tensor_scalar_mul(negs_sb[:], slo_sb[:], -1.0))
        blk.vector(f_nrm)
        nrm_done = p.tok("dve")

        def f_inv(sc):
            p.w(sc, "act", [nrm_done])
            p.t("act", sc.activation(inv8_sb[:], nrm_sb[:], AF.Ln))
            p.w(sc, "act", [p.tok("act")])
            p.t("act", sc.activation(inv8_sb[:], inv8_sb[:], AF.Exp, scale=-0.5))
        blk.scalar(f_inv)
        inv_done = p.tok("act")

        def f_invbc(te):
            p.w(te, "pe", [inv_done])
            p.t("pe", MM(p, te, psC[0:64, :], e8f[:], inv8_sb[:], True, True))
        blk.tensor(f_invbc)
        invbc_done = p.tok("pe")

        def f_cont1(sc):
            p.w(sc, "act", [invbc_done, ("dve", p.cnt["dve"])])
            p.t("act", sc.activation(t64_sb[:], psD[:], AF.Identity,
                                     bias=c1n_sb[0:64, 0:1]))
        blk.scalar(f_cont1)
        cont1 = p.tok("act")

        def f_cont2(v):
            p.w(v, "dve", [cont1])
            p.t("dve", v.tensor_mul(t64_sb[:], t64_sb[:], psC[0:64, :]))
        blk.vector(f_cont2)
        cont2 = p.tok("dve")

        def f_cont3(sc):
            p.w(sc, "act", [cont2])
            p.t("act", sc.activation(cont_bf[:], t64_sb[:], AF.Square))
        blk.scalar(f_cont3)
        cont3 = p.tok("act")

        def f_score(te):
            p.w(te, "pe", [cont3, ("dve", p.cnt["dve"])])
            p.t("pe", MM(p, te, psB[0:8, :], e8t[:], cont_bf[:], True, True))
        blk.tensor(f_score)
        score_done = p.tok("pe")

        def f_gate1(sc):
            p.w(sc, "act", [score_done, ("dve", p.cnt["dve"])])
            p.t("act", sc.activation(e_sb[:], psB[0:8, :], AF.Exp,
                                     bias=sbias_sb[0:8, 0:1],
                                     scale=negs_sb[0:8, 0:1]))
        blk.scalar(f_gate1)
        gate1 = p.tok("act")

        def f_gate2(v):
            p.w(v, "dve", [gate1])
            p.t("dve", v.tensor_scalar_add(e_sb[:], e_sb[:], 1.0))
            p.w(v, "dve", [p.tok("dve")])
            p.t("dve", v.reciprocal(e_sb[:], e_sb[:]))
        blk.vector(f_gate2)
        gate_done = p.tok("dve")

        for j in range(8):
            def f_tr(te, j=j):
                p.w(te, "pe", [gate_done, ("dve", p.cnt["dve"]),
                               ("act", p.cnt["act"])])
                p.t("pe", te.transpose(psA[:, 0:8],
                                       e_sb[0:8, j * 128:(j + 1) * 128],
                                       identf[0:8, 0:8]))
            blk.tensor(f_tr)
            trt = p.tok("pe")

            def f_sel(v, j=j, trt=trt):
                p.w(v, "dve", [trt])
                p.t("dve", v.reduce_max(mxc[:, j:j + 1], psA[:, 0:8], AX.X))
                p.w(v, "dve", [p.tok("dve")])
                p.t("dve", v.tensor_single_scalar(eq_sb[:], psA[:, 0:8],
                                                  mxc[:, j:j + 1], ALU.is_equal))
                p.w(v, "dve", [p.tok("dve")])
                p.t("dve", v.scalar_tensor_tensor(eq_sb[:], eq_sb[:], -9.0,
                                                  iota9[:], ALU.mult, ALU.add))
                p.w(v, "dve", [p.tok("dve")])
                p.t("dve", v.tensor_reduce(idxc[:], eq_sb[:], AX.X, ALU.min))
                p.w(v, "dve", [p.tok("dve")])
                p.t("dve", v.tensor_single_scalar(oneh_t[:], iotan[:],
                                                  idxc[:, 0:1], ALU.is_equal))
            blk.vector(f_sel)
            selt = p.tok("dve")

            def f_trb(te, j=j, selt=selt):
                p.w(te, "pe", [selt, ("act", p.cnt["act"])])
                p.t("pe", te.transpose(psD[0:8, 0:128], oneh_t[:], identf[:]))
                p.t("pe", te.transpose(psD[0:1, 128:256], mxc[:, j:j + 1],
                                       identf[:]))
            blk.tensor(f_trb)
            trb = p.tok("pe")

            def f_cp(sc, j=j, trb=trb):
                p.w(sc, "act", [trb])
                p.t("act", sc.copy(onehot8[:, j * 128:(j + 1) * 128],
                                   psD[0:8, 0:128]))
                p.t("act", sc.copy(ss_sb[:, j * 128:(j + 1) * 128],
                                   psD[0:1, 128:256]))
            blk.scalar(f_cp)
        oneh_done = p.tok("act")

        def f_xl(te):
            p.w(te, "pe", [oneh_done, ("dve", p.cnt["dve"])])
            for kt in range(16):
                i1 = MM(p, te, psC[0:64, :], lup_bf[:, kt * 64:(kt + 1) * 64],
                        xht(kt), kt == 0, kt == 15)
            p.t("pe", i1)
            for kt in range(16):
                i2 = te.matmul(psB[0:8, 0:64], dbwt_bf[:, kt * 8:(kt + 1) * 8],
                               lup_bf[:, kt * 64:(kt + 1) * 64],
                               start=(kt == 0), stop=(kt == 15))
            p.t("pe", i2)
        blk.tensor(f_xl)
        xl_done = p.tok("pe")

        def f_evxl(sc):
            p.w(sc, "act", [xl_done])
            p.t("act", sc.copy(t64_sb[:], psC[0:64, :]))
            p.t("act", sc.copy(k2b_bf[:], psB[0:8, 0:64]))
        blk.scalar(f_evxl)
        evxl = p.tok("act")

        def f_t2(te):
            p.w(te, "pe", [evxl])
            p.t("pe", MM(p, te, psD[:], k2b_bf[:], onehot8[:], True, True))
            p.t("pe", MM(p, te, psB[0:64, :], e8bf[:], onehot8[:], True, True))
            p.t("pe", MM(p, te, psC[:], ones_row[:], ss_sb[:], True, True))
        blk.tensor(f_t2)
        t2_done = p.tok("pe")

        def f_mx8(v):
            p.w(v, "dve", [t2_done])
            p.t("dve", v.tensor_sub(t64_sb[:], t64_sb[:], psD[:]))
            p.w(v, "dve", [p.tok("dve")])
            p.t("dve", v.tensor_mul(mx8_bf[:], t64_sb[:], psB[0:64, :]))
            p.t("dve", v.tensor_copy(ssbc_sb[:], psC[:]))
        blk.vector(f_mx8)
        mx8_done = p.tok("dve")

        for mt in range(16):
            ps = psA if mt % 2 == 0 else psB

            def f_bd(te, mt=mt, ps=ps):
                p.w(te, "pe", [mx8_done, ("dve", p.cnt["dve"])])
                MM(p, te, ps[:], bw_bf[:, mt * 128:(mt + 1) * 128], onehot8[:],
                   True, False)
                i1 = MM(p, te, ps[:], ldg_bf[:, mt * 128:(mt + 1) * 128],
                        mx8_bf[:], False, True)
                p.t("pe", i1)
            blk.tensor(f_bd)
            bd = p.tok("pe")

            def f_x2(v, mt=mt, ps=ps, bd=bd):
                p.w(v, "dve", [bd])
                p.t("dve", v.tensor_sub(d_sb[:], ps[:], xht(mt)))
                p.w(v, "dve", [p.tok("dve")])
                p.t("dve", v.tensor_mul(d_sb[:], d_sb[:], ssbc_sb[:]))
                p.w(v, "dve", [p.tok("dve")])
                p.t("dve", v.tensor_add(d_sb[:], d_sb[:], xht(mt)))
                p.w(v, "dve", [p.tok("dve")])
                p.t("dve", v.tensor_add(xt(mt), xt(mt), d_sb[:]))
            blk.vector(f_x2)
        x2_done = p.tok("dve")

        def f_stx2(sy):
            p.w(sy, "qs", [x2_done])
            for i in range(16):
                p.t("qs", sy.dma_start(out=x2To[i * 128:(i + 1) * 128, :],
                                       in_=xt(i)))
        blk.sync(f_stx2)

        ln3 = emit_ln(p, blk, "3", xar, TP, xh, rows, psA, psB, psC,
                      a1bf[:, 0:16 * TP], bc_sb[:, 0:TP], bc_sb[:, TP:2 * TP],
                      ones_bf, ones_row, eps_c[:, :],
                      [x2_done, ("qs", p.cnt["qs"])], scrT=scrT)

        war = xar.bitcast(F16)
        kvar = A1.bitcast(F16)

        def wload(w_ap):
            guard = [("qs", p.cnt["qs"]), ("dve", p.cnt["dve"]),
                     ("pe", p.cnt["pe"])]
            for kt in range(16):
                def f_l(sy, kt=kt):
                    p.w(sy, "qs", [("act", p.cnt["act"]), ("pe", p.cnt["pe"])])
                    p.t("qs", sy.dma_start(out=wslot[:],
                                           in_=w_ap[kt * 128:(kt + 1) * 128, :]))
                blk.sync(f_l)
                lt = p.tok("qs")

                def f_c(sc, kt=kt, lt=lt, guard=guard):
                    p.w(sc, "act", [lt] + guard)
                    p.t("act", sc.copy(war[:, kt * 2048:(kt + 1) * 2048],
                                       wslot[:]))
                blk.scalar(f_c)
            return p.tok("act")

        wq_done = wload(wq_i)
        for mt in range(16):
            ps = psA if mt % 2 == 0 else psB

            def f_qm(te, mt=mt, ps=ps, wq_done=wq_done):
                p.w(te, "pe", [wq_done] + ln3 + [("act", p.cnt["act"])])
                for kt in range(16):
                    i1 = MM(p, te, ps[:], war[:, kt * 2048 + mt * 128:
                                              kt * 2048 + (mt + 1) * 128],
                            xht(kt), kt == 0, kt == 15)
                p.t("pe", i1)
            blk.tensor(f_qm)
            qm = p.tok("pe")

            def f_qe(sc, mt=mt, ps=ps, qm=qm):
                p.w(sc, "act", [qm, ("qs", p.cnt["qs"])])
                p.t("act", sc.activation(qkv_ev[:, 0:TP], ps[:], AF.Relu))
            blk.scalar(f_qe)
            qe = p.tok("act")

            def f_qs(sy, mt=mt, qe=qe):
                p.w(sy, "qs", [qe])
                p.t("qs", sy.dma_start(out=qO[mt * 128:(mt + 1) * 128, :],
                                       in_=qkv_ev[:, 0:TP]))
            blk.sync(f_qs)

        for which, w_ap, act in [(0, wk_i, AF.Relu), (1, wv_i, AF.Identity)]:
            wdone = wload(w_ap)
            for ms in range(8):
                for nh in range(2):
                    ps = psA if (ms * 2 + nh) % 2 == 0 else psB

                    def f_km(te, ms=ms, nh=nh, ps=ps, wdone=wdone):
                        p.w(te, "pe", [wdone, ("act", p.cnt["act"])] + ln3)
                        for kt in range(16):
                            i1 = MM(p, te, ps[:],
                                    xh[:, kt * TP + ms * 128:
                                       kt * TP + (ms + 1) * 128],
                                    war[:, kt * 2048 + nh * 1024:
                                        kt * 2048 + (nh + 1) * 1024],
                                    kt == 0, kt == 15)
                        p.t("pe", i1)
                    blk.tensor(f_km)
                    km = p.tok("pe")

                    def f_ke(sc, which=which, ms=ms, nh=nh, ps=ps, km=km,
                             act=act):
                        p.w(sc, "act", [km])
                        dst = kvar[:, which * 16 * TP + ms * 2048 + nh * 1024:
                                   which * 16 * TP + ms * 2048 +
                                   (nh + 1) * 1024]
                        p.t("act", sc.activation(dst, ps[:], act))
                    blk.scalar(f_ke)
        kv_in_done = p.tok("act")

        def kcol(which, ms, inner):
            return which * 16 * TP + ms * 2048 + (inner // 1024) * 1024 + \
                (inner % 1024)

        for w in range(16):
            def f_kv(te, w=w):
                p.w(te, "pe", [kv_in_done, ("dve", p.cnt["dve"])])
                for h2 in range(2):
                    h = 2 * w + h2
                    for ms in range(8):
                        c0 = kcol(0, ms, h * 64)
                        c1 = kcol(1, ms, h * 64)
                        te.matmul(psC[h2 * 64:h2 * 64 + 64, 0:64],
                                  kvar[:, c0:c0 + 64], kvar[:, c1:c1 + 64],
                                  start=(ms == 0), stop=(ms == 7))
                for ms in range(8):
                    c0 = kcol(0, ms, w * 128)
                    i2 = te.matmul(psC[0:128, 64:65], kvar[:, c0:c0 + 128],
                                   ones_bf[:], start=(ms == 0), stop=(ms == 7))
                p.t("pe", i2)
            blk.tensor(f_kv)
            kvt = p.tok("pe")

            def f_kve(v, kvt=kvt):
                p.w(v, "dve", [kvt, ("qs", p.cnt["qs"])])
                p.t("dve", v.tensor_copy(kvp_sb[:], psC[:, 0:65]))
            blk.vector(f_kve)
            kve = p.tok("dve")

            def f_kvs(sy, w=w, kve=kve):
                p.w(sy, "qs", [kve])
                p.t("qs", sy.dma_start(out=kvpO[w * 128:(w + 1) * 128, :],
                                       in_=kvp_sb[:]))
            blk.sync(f_kvs)

        def f_fin(sy):
            sy.wait_ge(p.sem["qs"], p.cnt["qs"])
        blk.sync(f_fin)



def emit_post2(nc, p, blk, IO):
    TP = TPC
    x2T = IO["x2T"]
    qI = IO["qO"]
    kvs = IO["kvs"]
    wo_i = IO["wo"]
    bo_i = IO["bo_c"]
    w1q_i = IO["w1qf"]
    w2q_i = IO["w2qf"]
    wtot = IO["wtot"]
    ones_bf_i = IO["ones_h"]
    ones_row_i = IO["ones_row"]
    eps_i = IO["eps_c"]
    identf_i = IO["identf"]
    ehw_i = IO["ehw"]
    fhw_i = IO["fhw"]
    outT = IO["outT"]
    x3d = IO["x3d"]

    with contextlib.ExitStack() as st_:
        E = st_.enter_context
        ones_bf = E(nc.sbuf_tensor("c_ones_h", [128, 1], F16))
        ones_row = E(nc.sbuf_tensor("c_ones_row", [1, 128], F32))
        eps_c = E(nc.sbuf_tensor("c_eps", [128, 1], F32))
        identf = E(nc.sbuf_tensor("c_identf", [128, 128], F32))
        ehw = E(nc.sbuf_tensor("c_ehw", [128, 16 * 32], F32))

        bo_sb = E(nc.sbuf_tensor("bo_sb", [128, 16], F32))
        wab = E(nc.sbuf_tensor("wab", [1, 2], F32))
        H1 = E(nc.sbuf_tensor("H1", [128, 16 * TP], F32))
        H2 = E(nc.sbuf_tensor("H2", [128, 16 * TP], F32))
        q_sb = E(nc.sbuf_tensor("q_sb", [128, 16 * TP], F16))

        qs_sb = E(nc.sbuf_tensor("qs_sb", [128, TP], F32))
        evslot = E(nc.sbuf_tensor("evslot", [128, TP], F32))
        x2slot = E(nc.sbuf_tensor("x2slot", [128, TP], F32))
        rowblk = E(nc.sbuf_tensor("rowblk", [1, 6 * TP], F32))
        srows = E(nc.sbuf_tensor("srows", [1, 8], F32))
        rs1bc = E(nc.sbuf_tensor("rs1bc", [128, TP], F32))
        schbc = E(nc.sbuf_tensor("schbc", [128, TP], F32))

        psA = E(nc.psum_tensor("psA", [128, TP], F32))
        psB = E(nc.psum_tensor("psB", [128, TP], F32))
        psC = E(nc.psum_tensor("psC", [128, TP], F32))
        psD = E(nc.psum_tensor("psD", [128, TP], F32))

        woar = H1.bitcast(F16)
        wslab = ehw.bitcast(BF16)
        nrm_sb = rs1bc[0:32, :]
        rs2bc = qs_sb
        wtile = ehw.bitcast(BF16)[:, 0:4 * 128]
        kv_sb = H1[:, 0:16 * 65]
        kvt_sb = H1[:, 16 * 65:32 * 65]
        kvbf = H1.bitcast(F16)[:, 64 * 65:80 * 65]
        fhw = H2[0:32, 8 * TP:8 * TP + 16 * 128]
        qkvn = H2.bitcast(F16)
        hf16_lo = H2.bitcast(F16)
        hf16_hi = H1.bitcast(F16)
        rows = rowblk[0:1, 0:4 * TP]
        amaxr = rowblk[0:1, 4 * TP:5 * TP]
        sclr = rowblk[0:1, 5 * TP:6 * TP]

        def qt(i):
            return q_sb[:, i * TP:(i + 1) * TP]

        def f_ld(sy):
            for sb, ap in [(ones_bf, ones_bf_i), (ones_row, ones_row_i),
                           (eps_c, eps_i), (identf, identf_i), (ehw, ehw_i),
                           (bo_sb, bo_i)]:
                p.t("qs", sy.dma_start(out=sb[:], in_=ap))
            p.t("qs", sy.dma_start(out=wab[:, 0:1], in_=wtot[0:1, :]))
            p.t("qs", sy.dma_start(out=wab[:, 1:2], in_=wtot[1:2, :]))
            for i in range(16):
                p.t("qs", sy.dma_start(out=qt(i), in_=qI[i * 128:(i + 1) * 128, :]))
            p.t("qs", sy.dma_start(out=fhw[:], in_=fhw_i))
            p.t("qs", sy.dma_start(out=kv_sb[:], in_=kvs.rearrange(
                "(a p) m -> p a m", p=128)))
        blk.sync(f_ld)
        ld0 = p.tok("qs")
        kvsum = ld0

        def f_ckv(sc):
            p.w(sc, "act", [kvsum])
            p.t("act", sc.copy(kvbf[:], kv_sb[:]))
        blk.scalar(f_ckv)
        kvc = p.tok("act")

        for w in range(16):
            def f_qs(v, w=w):
                p.w(v, "dve", [kvsum, ld0, ("pe", p.cnt["pe"])])
                p.t("dve", v.tensor_scalar_mul(qs_sb[:], qt(w),
                                               kv_sb[:, w * 65 + 64:w * 65 + 65]))
            blk.vector(f_qs)
            qst = p.tok("dve")

            def f_nm(te, w=w, qst=qst):
                p.w(te, "pe", [qst])
                i1 = MM(p, te, psB[0:32, :], ehw[:, w * 32:(w + 1) * 32],
                        qs_sb[:], w == 0, w == 15)
                p.t("pe", i1)
            blk.tensor(f_nm)
        nm_done = p.tok("pe")

        def f_rn(v):
            p.w(v, "dve", [nm_done])
            p.t("dve", v.tensor_single_scalar(nrm_sb[:], psB[0:32, :], 1e-6,
                                              ALU.max))
            p.w(v, "dve", [p.tok("dve")])
            p.t("dve", v.reciprocal(nrm_sb[:], nrm_sb[:]))
        blk.vector(f_rn)
        rn_done = p.tok("dve")

        for w in range(16):
            def f_qkv(te, w=w):
                p.w(te, "pe", [kvc, rn_done, ("act", p.cnt["act"]),
                               ("dve", p.cnt["dve"])])
                MM(p, te, psA[0:64, :], kvbf[0:64, w * 65:w * 65 + 64],
                   q_sb[0:64, w * TP:(w + 1) * TP], True, True)
                i1 = MM(p, te, psA[64:128, :],
                        kvbf[64:128, w * 65:w * 65 + 64],
                        q_sb[64:128, w * TP:(w + 1) * TP], True, True)
                p.t("pe", i1)
                i2 = MM(p, te, psC[:], fhw[:, w * 128:(w + 1) * 128], nrm_sb[:],
                        True, True)
                p.t("pe", i2)
            blk.tensor(f_qkv)
            qkvt = p.tok("pe")

            def f_qe(sc, w=w, qkvt=qkvt):
                p.w(sc, "act", [qkvt])
                p.t("act", sc.copy(evslot[:], psA[:]))
            blk.scalar(f_qe)
            qev = p.tok("act")

            def f_qn(v, w=w, qev=qev):
                p.w(v, "dve", [qev])
                p.t("dve", v.tensor_mul(qkvn[:, w * TP:(w + 1) * TP],
                                        evslot[:], psC[:]))
            blk.vector(f_qn)
        qkvn_done = p.tok("dve")

        def wload16(w_ap, dst, slot32):
            for kt in range(16):
                def f_l(sy, kt=kt):
                    p.w(sy, "qs", [("act", p.cnt["act"]), ("pe", p.cnt["pe"]),
                                   ("dve", p.cnt["dve"])])
                    p.t("qs", sy.dma_start(out=slot32,
                                           in_=w_ap[kt * 128:(kt + 1) * 128, :]))
                blk.sync(f_l)
                lt = p.tok("qs")

                def f_c(sc, kt=kt, lt=lt):
                    p.w(sc, "act", [lt])
                    p.t("act", sc.copy(dst[:, kt * 2048:(kt + 1) * 2048],
                                       slot32))
                blk.scalar(f_c)
            return p.tok("act")

        wslot2 = H2[:, 12 * TP:14 * TP]
        wo_done = wload16(wo_i, woar, wslot2)

        for mt in range(16):
            ps = psA if mt % 2 == 0 else psB

            def f_lx2(sy, mt=mt):
                p.w(sy, "qs", [("dve", p.cnt["dve"]), ("qs", p.cnt["qs"])])
                p.t("qs", sy.dma_start(out=x2slot[:],
                                       in_=x2T[mt * 128:(mt + 1) * 128, :]))
            blk.sync(f_lx2)
            lx2 = p.tok("qs")

            def f_wo(te, mt=mt, ps=ps):
                p.w(te, "pe", [wo_done, qkvn_done, ("dve", p.cnt["dve"])])
                for kt in range(16):
                    i1 = MM(p, te, ps[:], woar[:, kt * 2048 + mt * 128:
                                               kt * 2048 + (mt + 1) * 128],
                            qkvn[:, kt * TP:(kt + 1) * TP], kt == 0, kt == 15)
                p.t("pe", i1)
            blk.tensor(f_wo)
            wot = p.tok("pe")

            def f_x3(v, mt=mt, ps=ps, wot=wot, lx2=lx2):
                p.w(v, "dve", [wot, lx2])
                p.t("dve", v.scalar_tensor_tensor(x2slot[:], ps[:],
                                                  bo_sb[:, mt:mt + 1],
                                                  x2slot[:], ALU.add, ALU.add))
            blk.vector(f_x3)
            x3t = p.tok("dve")

            def f_sx3(sy, mt=mt, x3t=x3t):
                p.w(sy, "qs", [x3t])
                p.t("qs", sy.dma_start(out=x3d[mt * 128:(mt + 1) * 128, :],
                                       in_=x2slot[:]))
            blk.sync(f_sx3)
        x3_done = p.tok("qs")

        def f_lx3(sy):
            p.w(sy, "qs", [x3_done, ("pe", p.cnt["pe"]),
                           ("dve", p.cnt["dve"]), ("act", p.cnt["act"])])
            for i in range(16):
                p.t("qs", sy.dma_start(out=H2[:, i * TP:(i + 1) * TP],
                                       in_=x3d[i * 128:(i + 1) * 128, :]))
        blk.sync(f_lx3)
        x3l = p.tok("qs")

        ln4 = emit_ln(p, blk, "4", H2, TP, q_sb, rows, psA, psB, psC,
                      H1.bitcast(F16)[:, 0:16 * TP], rs1bc[:, 0:TP],
                      schbc[:, 0:TP], ones_bf, ones_row, eps_c[:, :],
                      [x3l], xh32_sb=H1, scrT=evslot[:, 0:TP])

        def f_amax1(v):
            p.w(v, "dve", ln4)
            p.t("dve", v.tensor_reduce(
                evslot[:], H1[:].rearrange("p (a t) -> p t a", a=16),
                AX.X, ALU.max, apply_absolute_value=True))
        blk.vector(f_amax1)
        am1 = p.tok("dve")

        for j in range(8):
            def f_amt(te, j=j, am1=am1):
                p.w(te, "pe", [am1, ("dve", p.cnt["dve"]),
                               ("act", p.cnt["act"])])
                p.t("pe", te.transpose(psA[:, 0:128],
                                       evslot[:, j * 128:(j + 1) * 128],
                                       identf[:]))
            blk.tensor(f_amt)
            amt = p.tok("pe")

            def f_amr(v, j=j, amt=amt):
                p.w(v, "dve", [amt])
                p.t("dve", v.reduce_max(qs_sb[:, j:j + 1], psA[:, 0:128], AX.X))
            blk.vector(f_amr)
            amr = p.tok("dve")

            def f_amt2(te, j=j, amr=amr):
                p.w(te, "pe", [amr])
                p.t("pe", te.transpose(psB[0:1, 0:128], qs_sb[:, j:j + 1],
                                       identf[:]))
            blk.tensor(f_amt2)
            amt2 = p.tok("pe")

            def f_amc(sc, j=j, amt2=amt2):
                p.w(sc, "act", [amt2])
                p.t("act", sc.copy(amaxr[0:1, j * 128:(j + 1) * 128],
                                   psB[0:1, 0:128]))
            blk.scalar(f_amc)
        amax_done = p.tok("act")

        def f_srow(v):
            p.w(v, "dve", [amax_done, ld0])
            p.t("dve", v.tensor_scalar_mul(srows[0:1, 0:2], wab[0:1, 0:2],
                                           1.0 / (DFF * D)))
            p.w(v, "dve", [p.tok("dve")])
            p.t("dve", v.tensor_single_scalar(amaxr[:], amaxr[:], 1e-5,
                                              ALU.max))
            p.w(v, "dve", [p.tok("dve")])
            p.t("dve", v.reciprocal(sclr[:], amaxr[:]))
            p.w(v, "dve", [p.tok("dve")])
            p.t("dve", v.tensor_scalar_mul(sclr[:], sclr[:], 127.0))
            p.t("dve", v.tensor_scalar_mul(rows[0:1, 0:TP], amaxr[:],
                                           1.0 / 127.0))
            p.w(v, "dve", [p.tok("dve")])
            p.t("dve", v.tensor_scalar_mul(rows[0:1, 0:TP], rows[0:1, 0:TP],
                                           srows[0:1, 0:1]))
        blk.vector(f_srow)
        srow_done = p.tok("dve")

        def f_bc1(te):
            p.w(te, "pe", [srow_done])
            p.t("pe", MM(p, te, psC[:], ones_row[:], sclr[:], True, True))
        blk.tensor(f_bc1)
        b1 = p.tok("pe")

        def f_bc1e(sc):
            p.w(sc, "act", [b1])
            p.t("act", sc.copy(schbc[:], psC[:]))
        blk.scalar(f_bc1e)

        def f_bc2(te):
            p.w(te, "pe", [("act", p.cnt["act"])])
            p.t("pe", MM(p, te, psC[:], ones_row[:], rows[0:1, 0:TP], True, True))
        blk.tensor(f_bc2)
        b2 = p.tok("pe")

        def f_bc2e(sc):
            p.w(sc, "act", [b2])
            p.t("act", sc.copy(rs1bc[:], psC[:]))
        blk.scalar(f_bc2e)
        bc_done = p.tok("act")

        for i in range(16):
            def f_xq(v, i=i):
                p.w(v, "dve", [bc_done])
                p.t("dve", v.tensor_mul(evslot[:], H1[:, i * TP:(i + 1) * TP],
                                        schbc[:]))
                p.w(v, "dve", [p.tok("dve")])
                p.t("dve", v.tensor_scalar(q_sb.bitcast(BF16)[:, i * TP:(i + 1) * TP],
                                           evslot[:], MAGIC, MAGIC,
                                           ALU.add, ALU.subtract))
            blk.vector(f_xq)
        xq_done = p.tok("dve")

        def ps_slot(s):
            t = [psA, psB, psC, psD][s // 2]
            o = (s % 2) * 512
            return t[:, o:o + 512]

        qbf = q_sb.bitcast(BF16)

        def hcol(mt2, h):
            if mt2 < 32:
                return hf16_lo[:, mt2 * TP + h * 512:mt2 * TP + h * 512 + 512]
            return hf16_hi[:, (mt2 - 32) * TP + h * 512:
                           (mt2 - 32) * TP + h * 512 + 512]

        ev_last1 = 0
        for mg in range(16):
            pe_c1 = p.cnt["pe"]
            st1 = {}

            def f_lsl(sy, mg=mg, pe_c1=pe_c1):
                for kt in range(16):
                    if kt == 0 and mg == 0:
                        p.w(sy, "qs", [("pe", pe_c1)])
                    p.w(sy, "qs", [("pe", pe_c1 + kt - 1)])
                    p.t("qs", sy.dma_start(
                        out=wslab[:, (kt % 2) * 512:(kt % 2 + 1) * 512],
                        in_=w1q_i[kt * 128:(kt + 1) * 128,
                                  mg * 512:(mg + 1) * 512]))
                    st1[kt] = p.tok("qs")
            blk.sync(f_lsl)

            def f_mm1(te, mg=mg, ev_last1=ev_last1):
                if mg == 0:
                    p.w(te, "pe", [xq_done, ("act", p.cnt["act"]),
                                   ("dve", p.cnt["dve"])])
                for kt in range(16):
                    p.w(te, "pe", [st1[kt]])
                    if kt == 0 and mg > 0:
                        p.w(te, "pe", [("dve", ev_last1)])
                    for s in range(8):
                        j, h = divmod(s, 2)
                        i1 = te.matmul(
                            ps_slot(s),
                            wslab[:, (kt % 2) * 512 + j * 128:
                                  (kt % 2) * 512 + (j + 1) * 128],
                            qbf[:, kt * TP + h * 512:kt * TP + h * 512 + 512],
                            start=(kt == 0), stop=(kt == 15))
                    p.t("pe", i1)
            blk.tensor(f_mm1)
            w1done = p.tok("pe")

            v2 = p.cnt["dve"]
            a2 = p.cnt["act"]

            def f_ev1(v, mg=mg, w1done=w1done, v2=v2, a2=a2):
                for s in range(8):
                    j, h = divmod(s, 2)
                    p.w(v, "dve", [w1done, bc_done, p.tok("dve")])
                    if s >= 2:
                        p.w(v, "dve", [("act", a2 + s - 1)])
                    p.t("dve", v.tensor_mul(
                        evslot[:, (s % 2) * 512:(s % 2 + 1) * 512],
                        ps_slot(s), rs1bc[:, h * 512:h * 512 + 512]))
            blk.vector(f_ev1)
            ev_last1 = p.cnt["dve"]

            def f_sil1(sc, mg=mg, v2=v2):
                for s in range(8):
                    j, h = divmod(s, 2)
                    mt2 = mg * 4 + j
                    p.w(sc, "act", [("dve", v2 + s + 1)])
                    p.t("act", sc.activation(
                        hcol(mt2, h), evslot[:, (s % 2) * 512:(s % 2 + 1) * 512],
                        AF.Silu))
            blk.scalar(f_sil1)
        h_done = p.tok("act")

        hm16 = qs_sb.bitcast(F16)

        def f_ham(v):
            p.w(v, "dve", [h_done])
            p.t("dve", v.tensor_reduce(
                evslot[:], hf16_lo[:, 0:32 * TP].rearrange(
                    "p (a t) -> p t a", a=32),
                AX.X, ALU.max, apply_absolute_value=True))
            p.t("dve", v.tensor_reduce(
                x2slot[:], hf16_hi[:, 0:32 * TP].rearrange(
                    "p (a t) -> p t a", a=32),
                AX.X, ALU.max, apply_absolute_value=True))
            p.w(v, "dve", [p.tok("dve")])
            p.t("dve", v.tensor_max(evslot[:], evslot[:], x2slot[:]))
        blk.vector(f_ham)
        ham = p.tok("dve")

        ham32 = ham

        for j in range(8):
            def f_hmt(te, j=j, ham32=ham32):
                p.w(te, "pe", [ham32, ("dve", p.cnt["dve"]),
                               ("act", p.cnt["act"])])
                p.t("pe", te.transpose(psA[:, 0:128],
                                       evslot[:, j * 128:(j + 1) * 128],
                                       identf[:]))
            blk.tensor(f_hmt)
            hmt = p.tok("pe")

            def f_hmr(v, j=j, hmt=hmt):
                p.w(v, "dve", [hmt])
                p.t("dve", v.reduce_max(qs_sb[:, 8 + j:9 + j], psA[:, 0:128],
                                        AX.X))
            blk.vector(f_hmr)
            hmr = p.tok("dve")

            def f_hmt2(te, j=j, hmr=hmr):
                p.w(te, "pe", [hmr])
                p.t("pe", te.transpose(psB[0:1, 0:128], qs_sb[:, 8 + j:9 + j],
                                       identf[:]))
            blk.tensor(f_hmt2)
            hmt2 = p.tok("pe")

            def f_hmc(sc, j=j, hmt2=hmt2):
                p.w(sc, "act", [hmt2])
                p.t("act", sc.copy(amaxr[0:1, j * 128:(j + 1) * 128],
                                   psB[0:1, 0:128]))
            blk.scalar(f_hmc)
        hamax_done = p.tok("act")

        def f_srow2(v):
            p.w(v, "dve", [hamax_done])
            p.t("dve", v.tensor_single_scalar(amaxr[:], amaxr[:], 1e-5,
                                              ALU.max))
            p.w(v, "dve", [p.tok("dve")])
            p.t("dve", v.reciprocal(sclr[:], amaxr[:]))
            p.w(v, "dve", [p.tok("dve")])
            p.t("dve", v.tensor_scalar_mul(sclr[:], sclr[:], 127.0))
            p.t("dve", v.tensor_scalar_mul(rows[0:1, 0:TP], amaxr[:],
                                           1.0 / 127.0))
            p.w(v, "dve", [p.tok("dve")])
            p.t("dve", v.tensor_scalar_mul(rows[0:1, 0:TP], rows[0:1, 0:TP],
                                           srows[0:1, 1:2]))
        blk.vector(f_srow2)
        srow2 = p.tok("dve")

        def f_bch(te):
            p.w(te, "pe", [srow2])
            p.t("pe", MM(p, te, psC[:], ones_row[:], sclr[:], True, True))
        blk.tensor(f_bch)
        bh1 = p.tok("pe")

        def f_bche(sc):
            p.w(sc, "act", [bh1])
            p.t("act", sc.copy(schbc[:], psC[:]))
        blk.scalar(f_bche)
        bch1 = p.tok("act")

        def f_bch2(te):
            p.w(te, "pe", [bch1])
            p.t("pe", MM(p, te, psC[:], ones_row[:], rows[0:1, 0:TP], True, True))
        blk.tensor(f_bch2)
        bh2 = p.tok("pe")

        def f_bch2e(sc):
            p.w(sc, "act", [bh2])
            p.t("act", sc.copy(rs2bc[:], psC[:]))
        blk.scalar(f_bch2e)
        bch_done = p.tok("act")

        def htile(k):
            if k < 32:
                return hf16_lo[:, k * TP:(k + 1) * TP]
            return hf16_hi[:, (k - 32) * TP:(k - 31) * TP]

        for k in range(64):
            def f_hq(v, k=k):
                p.w(v, "dve", [bch_done, p.tok("dve")])
                p.t("dve", v.tensor_mul(evslot[:], htile(k), schbc[:]))
                p.w(v, "dve", [p.tok("dve")])
                dst = (hf16_lo if k < 32 else hf16_hi).bitcast(BF16)
                kk = k if k < 32 else k - 32
                p.t("dve", v.tensor_scalar(dst[:, kk * TP:(kk + 1) * TP],
                                           evslot[:], MAGIC, MAGIC, ALU.add,
                                           ALU.subtract))
            blk.vector(f_hq)
        hq_done = p.tok("dve")

        def hqtile(k):
            if k < 32:
                return hf16_lo.bitcast(BF16)[:, k * TP:(k + 1) * TP]
            return hf16_hi.bitcast(BF16)[:, (k - 32) * TP:(k - 31) * TP]

        def hqcol(kt, h):
            if kt < 32:
                return hf16_lo.bitcast(BF16)[
                    :, kt * TP + h * 512:kt * TP + h * 512 + 512]
            return hf16_hi.bitcast(BF16)[
                :, (kt - 32) * TP + h * 512:(kt - 32) * TP + h * 512 + 512]

        ev_last2 = 0
        for mg in range(4):
            pe_c2 = p.cnt["pe"]
            st2 = {}

            def f_lsl2(sy, mg=mg, pe_c2=pe_c2):
                for kt in range(64):
                    p.w(sy, "qs", [("pe", pe_c2 + kt - 1)])
                    p.t("qs", sy.dma_start(
                        out=wslab[:, (kt % 2) * 512:(kt % 2 + 1) * 512],
                        in_=w2q_i[kt * 128:(kt + 1) * 128,
                                  mg * 512:(mg + 1) * 512]))
                    st2[kt] = p.tok("qs")
            blk.sync(f_lsl2)

            def f_mm2(te, mg=mg, ev_last2=ev_last2):
                if mg == 0:
                    p.w(te, "pe", [hq_done, ("act", p.cnt["act"]),
                                   ("dve", p.cnt["dve"])])
                for kt in range(64):
                    p.w(te, "pe", [st2[kt]])
                    if kt == 0 and mg > 0:
                        p.w(te, "pe", [("dve", ev_last2)])
                    for s in range(8):
                        j, h = divmod(s, 2)
                        i1 = te.matmul(
                            ps_slot(s),
                            wslab[:, (kt % 2) * 512 + j * 128:
                                  (kt % 2) * 512 + (j + 1) * 128],
                            hqcol(kt, h),
                            start=(kt == 0), stop=(kt == 63))
                    p.t("pe", i1)
            blk.tensor(f_mm2)
            w2done = p.tok("pe")

            for j in range(4):
                mt = mg * 4 + j

                def f_lx3b(sy, mt=mt):
                    p.w(sy, "qs", [("dve", p.cnt["dve"]), ("qs", p.cnt["qs"])])
                    p.t("qs", sy.dma_start(out=x2slot[:],
                                           in_=x3d[mt * 128:(mt + 1) * 128, :]))
                blk.sync(f_lx3b)
                lx3 = p.tok("qs")

                def f_out(v, j=j, w2done=w2done, lx3=lx3):
                    for h in range(2):
                        p.w(v, "dve", [w2done, lx3, bch_done, p.tok("dve")])
                        p.t("dve", v.tensor_mul(
                            evslot[:, h * 512:h * 512 + 512],
                            ps_slot(j * 2 + h), rs2bc[:, h * 512:h * 512 + 512]))
                        p.w(v, "dve", [p.tok("dve")])
                        p.t("dve", v.tensor_add(
                            x2slot[:, h * 512:h * 512 + 512],
                            x2slot[:, h * 512:h * 512 + 512],
                            evslot[:, h * 512:h * 512 + 512]))
                blk.vector(f_out)
                outt = p.tok("dve")
                if j == 3:
                    ev_last2 = p.cnt["dve"]

                def f_so(sy, mt=mt, outt=outt):
                    p.w(sy, "qs", [outt])
                    p.t("qs", sy.dma_start(out=outT[mt * 128:(mt + 1) * 128, :],
                                           in_=x2slot[:]))
                blk.sync(f_so)

        def f_fin(sy):
            sy.wait_ge(p.sem["qs"], p.cnt["qs"])
        blk.sync(f_fin)



from concourse.aot_env import allow_ici_all_to_all

G_BATCH = [[0, 1, 2, 3], [4, 5, 6, 7]]
G_ALL = [[0, 1, 2, 3, 4, 5, 6, 7]]


def _sbar(p, blk, tag):
    toks = p.bar()
    for name, opener in (("Esy", blk.sync), ("Egp", blk.gpsimd),
                         ("Edv", blk.vector), ("Eac", blk.scalar),
                         ("Ete", blk.tensor)):
        def f(eng, name=name, toks=toks):
            p.w(eng, name + tag, toks)
        opener(f)


class _NcU:

    def __init__(self, nc, tag):
        self._nc = nc
        self._tag = tag

    def __getattr__(self, a):
        return getattr(self._nc, a)

    def sbuf_tensor(self, name, *a, **k):
        return self._nc.sbuf_tensor(self._tag + name, *a, **k)

    def psum_tensor(self, name, *a, **k):
        return self._nc.psum_tensor(self._tag + name, *a, **k)


def build_fused():
    nc = bass.Bass("TRN2", target_bir_lowering=False, debug=False)
    D = {}
    def ein(n, s, dt):
        D[n] = _ein(nc, n, s, dt)
    ein("xT", [2048, TPC], F32)
    ein("w1T", [256, DFF], F32)
    ein("w2T", [1024, 2048], F32)
    ein("ones_h", [128, 1], F16)
    ein("ones_bf", [128, 1], BF16)
    ein("ones_row", [1, 128], F32)
    ein("ones_f", [128, 1], F32)
    ein("eps_c", [128, 1], F32)
    ein("identf", [128, 128], F32)
    ein("winT", [2048, 1024], F32)
    ein("convw", [128, 16], F32)
    ein("convb", [128, 4], F32)
    ein("wxT", [128, 4 * 160], F32)
    ein("wdtT", [128, 512], F32)
    ein("dtb", [128, 4], F32)
    ein("acol", [128, 64], F32)
    ein("dcol", [128, 4], F32)
    ein("woutT", [DIN, 2048], F32)
    ein("selW", [2048, 64], F32)
    ein("mdT", [2048, 8], F32)
    ein("mdrep", [2048, 64], F32)
    ein("lupF", [2048, 64], F32)
    ein("ldgF", [64, 2048], F32)
    ein("biasw", [8, 2048], F32)
    ein("dbwT", [2048, 8], F32)
    ein("slope_c", [8, 1], F32)
    ein("center_c", [8, 1], F32)
    ein("wq", [2048, INNER], F32)
    ein("wk", [2048, INNER], F32)
    ein("wv", [2048, INNER], F32)
    ein("e8h", [8, 64], F16)
    ein("e8f", [8, 64], F32)
    ein("e8t", [64, 8], BF16)
    ein("iota9", [128, 8], F32)
    ein("iotan", [128, 8], F32)
    ein("wo", [INNER, 2048], F32)
    ein("bo_c", [128, 16], F32)
    ein("ehw", [128, 16 * 32], F32)
    ein("fhw", [32, 16 * 128], F32)
    def itn(n, s, dt):
        D[n] = nc.dram_tensor("i_" + n, s, dt).ap()
    itn("xhT", [2048, TPC], F16)
    itn("xhTb", [8 * 2048, TPC], F16)
    itn("wabs", [2, 1], F32)
    itn("wtot", [2, 1], F32)
    itn("u16", [512, NTOK], F16)
    itn("zs16", [512, NTOK], F16)
    itn("ssmp", [160, NTOK], F32)
    itn("ssms", [160, NTOK], F32)
    itn("ssbc", [32, NTOK], F16)
    itn("w1q", [256, DFF], BF16)
    itn("w2q", [1024, 2048], BF16)
    itn("w1qf", [2048, DFF], BF16)
    itn("w2qf", [DFF, 2048], BF16)
    itn("ysz_a2a", [DIN, TPC], F16)
    itn("yszg", [DIN, TPC], F16)
    itn("x2T", [2048, TPC], F32)
    itn("qO", [INNER, TPC], F16)
    itn("kvp", [INNER, 65], F32)
    itn("kvs", [INNER, 65], F32)
    itn("x3d", [2048, TPC], F32)
    D["outT"] = _eout(nc, "outT", [2048, TPC], F32)

    with contextlib.ExitStack() as outer:
        E = outer.enter_context
        p = Px(nc, E)
        ccsems = [E(nc.semaphore("ccs%d" % k)) for k in range(8)]
        blk = E(nc.Block())

        def cc(k, kind, op, groups, in_ap, out_ap, wait=True):
            def f(g):
                p.w(g, "Egp_cc%d" % k, p.bar())
                with allow_ici_all_to_all():
                    inst = g.collective_compute(
                        kind, op, replica_groups=groups,
                        ins=[in_ap.opt()], outs=[out_ap.opt()])
                inst.then_inc(ccsems[k])
                if wait:
                    g.wait_ge(ccsems[k], 1)
                    p.t("gpc", g.nop(), 1)
            blk.gpsimd(f)

        emit_ln1(_NcU(nc, 's0_'), p, blk, D)
        _sbar(p, blk, "c1")
        cc(0, "AllGather", ALU.bypass, G_ALL, D["xhT"], D["xhTb"])
        cc(1, "AllReduce", ALU.add, G_ALL, D["wabs"], D["wtot"])
        _sbar(p, blk, "c2")
        emit_front(_NcU(nc, 's1_'), p, blk, D)
        _sbar(p, blk, "c3")
        cc(2, "AllReduce", ALU.add, G_ALL, D["ssmp"], D["ssms"])
        cc(3, "AllGather", ALU.bypass, G_ALL, D["w1q"], D["w1qf"], wait=True)
        cc(4, "AllGather", ALU.bypass, G_ALL, D["w2q"], D["w2qf"], wait=True)
        D["ccs3"] = ccsems[3]
        D["ccs4"] = ccsems[4]
        _sbar(p, blk, "c4")
        emit_scan(_NcU(nc, 's2_'), p, blk, D)
        _sbar(p, blk, "c5")
        cc(5, "AllToAll", ALU.bypass, G_ALL, D["ysz_a2a"], D["yszg"])
        _sbar(p, blk, "c6")
        emit_post1(_NcU(nc, 's3_'), p, blk, D)
        _sbar(p, blk, "c7")
        cc(6, "AllReduce", ALU.add, G_BATCH, D["kvp"], D["kvs"])
        _sbar(p, blk, "c8")
        emit_post2(_NcU(nc, 's4_'), p, blk, D)

        def f_end(sy):
            p.w(sy, "Esy_end", p.bar())
        blk.sync(f_end)
    return nc


def _prep(inputs):
    g = {}
    x = np.ascontiguousarray(inputs["L0_attn_hidden_states"], dtype=np.float32)
    xT = np.ascontiguousarray(x.reshape(NTOK, D).T)
    g["xT"] = [np.ascontiguousarray(xT[:, c * TPC:(c + 1) * TPC]) for c in range(NC)]
    w1T = np.ascontiguousarray(inputs["L1_ffn_W1"].T)
    w2T = np.ascontiguousarray(inputs["L1_ffn_W2"].T)
    g["w1T"] = [np.ascontiguousarray(w1T[c * 256:(c + 1) * 256, :]) for c in range(NC)]
    g["w2T"] = [np.ascontiguousarray(w2T[c * 1024:(c + 1) * 1024, :]) for c in range(NC)]
    g["ones_bf"] = [np.ones((128, 1), ml_dtypes.bfloat16)] * NC
    g["ones_h"] = [np.ones((128, 1), np.float16)] * NC
    g["ones_row"] = [np.ones((1, 128), np.float32)] * NC
    g["ones_f"] = [np.ones((128, 1), np.float32)] * NC
    g["eps_c"] = [np.full((128, 1), LN_EPS, np.float32)] * NC

    w_in = inputs["L0_attn_W_in"]
    SL = DIN // NC
    g["winT"], g["convw"], g["convb"], g["wxT"] = [], [], [], []
    g["wdtT"], g["dtb"], g["acol"], g["dcol"] = [], [], [], []
    for c in range(NC):
        sl = slice(c * SL, (c + 1) * SL)
        rows = np.r_[np.arange(c * SL, (c + 1) * SL),
                     np.arange(DIN + c * SL, DIN + (c + 1) * SL)]
        g["winT"].append(np.ascontiguousarray(w_in[rows, :].T))
        g["convw"].append(np.ascontiguousarray(
            inputs["L0_attn_conv_w"][sl, :].reshape(4, 128, 4)
            .transpose(1, 0, 2).reshape(128, 16)))
        g["convb"].append(np.ascontiguousarray(
            inputs["L0_attn_conv_b"][sl].reshape(4, 128).T))
        g["wxT"].append(np.ascontiguousarray(
            inputs["L0_attn_W_x"][:, sl].T.reshape(4, 128, 160)
            .transpose(1, 0, 2).reshape(128, 4 * 160)))
        g["wdtT"].append(np.ascontiguousarray(inputs["L0_attn_W_dt"][sl, :].T))
        dtb = inputs["L0_attn_dt_bias"][sl].reshape(4, 128).T
        g["dtb"].append(np.ascontiguousarray(dtb))
        A = inputs["L0_attn_A"][sl, :].reshape(4, 128, 16)
        acol = np.ascontiguousarray(A.transpose(1, 0, 2).reshape(128, 64))
        g["acol"].append(acol)
        dvec = inputs["L0_attn_D"][sl].reshape(4, 128).T
        g["dcol"].append(np.ascontiguousarray(dvec))

    f32 = np.float32
    g["woutT"] = np.ascontiguousarray(inputs["L0_attn_W_out"].T.astype(f32))
    g["selW"] = np.ascontiguousarray(
        inputs["L0_ffn_select_weight"].transpose(1, 0, 2).reshape(D, 64))
    g["mdT"] = np.ascontiguousarray(inputs["L0_ffn_select_mean_diff"].T)
    g["mdrep"] = np.ascontiguousarray(np.repeat(g["mdT"], 8, axis=1))
    g["lupF"] = np.ascontiguousarray(
        inputs["L0_ffn_lora_update"].transpose(1, 0, 2).reshape(D, 64))
    g["ldgF"] = np.ascontiguousarray(
        inputs["L0_ffn_lora_degen"].transpose(0, 2, 1).reshape(64, D))
    g["biasw"] = np.ascontiguousarray(inputs["L0_ffn_bias_w"].astype(f32))
    g["dbwT"] = np.ascontiguousarray(inputs["L0_ffn_debias_w"].T)
    g["slope_c"] = np.ascontiguousarray(
        inputs["L0_ffn_imp_slope"].reshape(8, 1).astype(f32))
    g["center_c"] = np.ascontiguousarray(
        inputs["L0_ffn_imp_center"].reshape(8, 1).astype(f32))
    g["wq"] = np.ascontiguousarray(inputs["L1_attn_Wq"].astype(f32))
    g["wk"] = np.ascontiguousarray(inputs["L1_attn_Wk"].astype(f32))
    g["wv"] = np.ascontiguousarray(inputs["L1_attn_Wv"].astype(f32))
    g["wo"] = np.ascontiguousarray(inputs["L1_attn_Wo"].astype(f32))
    g["bo_c"] = np.ascontiguousarray(
        inputs["L1_attn_bo"].reshape(16, 128).T.astype(f32))
    g["identf"] = np.eye(128, dtype=f32)
    e8 = np.zeros((8, 64), f32)
    for n in range(8):
        e8[n, n * 8:(n + 1) * 8] = 1.0
    g["e8f"] = e8
    g["e8bf"] = e8.astype(ml_dtypes.bfloat16)
    g["e8h"] = e8.astype(np.float16)
    g["e8t"] = np.ascontiguousarray(e8.T).astype(ml_dtypes.bfloat16)
    g["iota9"] = np.ascontiguousarray(
        np.broadcast_to(np.arange(8, dtype=f32) + 9.0, (128, 8)))
    g["iotan"] = np.ascontiguousarray(
        np.broadcast_to(np.arange(8, dtype=f32), (128, 8)))
    ehw = np.zeros((128, 16 * 32), f32)
    fhw = np.zeros((32, 16 * 128), f32)
    for w in range(16):
        for pp in range(128):
            h = (w * 128 + pp) // 64
            ehw[pp, w * 32 + h] = 1.0
            fhw[h, w * 128 + pp] = 1.0
    g["ehw"] = ehw
    g["fhw"] = np.ascontiguousarray(fhw)
    return g


def kernel(**inputs):
    g = _prep(inputs)
    nc = build_fused()
    names_shared = ["woutT", "selW", "mdT", "mdrep", "lupF", "ldgF", "biasw",
                    "dbwT", "slope_c", "center_c", "wq", "wk", "wv", "wo",
                    "bo_c", "identf", "e8h", "e8f", "e8t", "iota9", "iotan",
                    "ehw", "fhw"]
    names_per = ["xT", "w1T", "w2T", "ones_h", "ones_bf", "ones_row", "ones_f",
                 "eps_c", "winT", "convw", "convb", "wxT", "wdtT", "dtb",
                 "acol", "dcol"]
    maps = []
    for c in range(NC):
        m = {}
        for k in names_shared:
            m[k] = g[k]
        for k in names_per:
            m[k] = g[k][c]
        maps.append(m)
    res = run_bass_kernel_spmd(nc, maps, core_ids=list(range(NC)))
    out = np.concatenate(
        [np.ascontiguousarray(res.results[c]["outT"]).T for c in range(NC)],
        axis=0)
    return np.ascontiguousarray(out.reshape(B, L, D)).astype(np.float32)


# version 15
# speedup vs baseline: 1.2330x; 1.0001x over previous
import contextlib
import numpy as np
import ml_dtypes
import concourse.bass as bass
import concourse.mybir as mybir
from concourse.bass_utils import run_bass_kernel_spmd

F32 = mybir.dt.float32
BF16 = mybir.dt.bfloat16
F16 = mybir.dt.float16
AF = mybir.ActivationFunctionType
ALU = mybir.AluOpType
AX = mybir.AxisListType

MAGIC = float(np.float32(1.5 * 2 ** 23))
LN_EPS = 1e-5

B, L, D = 2, 4096, 2048
NTOK = B * L
NC = 8
TPC = NTOK // NC
DIN = 4096
DPC = DIN // 4
DST = 16
DTR = 128
HEADS, HD = 32, 64
INNER = 2048
DFF = 8192
NT_D = D // 128
NT_DPC = DPC // 128


class Px:

    def __init__(self, nc, E):
        self.nc = nc
        self.sem = {}
        self.cnt = {}
        self.hw = {}
        for s in ("qs", "qg", "pe", "act", "dve", "gpc", "gp"):
            self.sem[s] = E(nc.semaphore("S_" + s))
            self.cnt[s] = 0

    def tok(self, s):
        return (s, self.cnt[s])

    def t(self, s, inst, amt=None):
        if amt is None:
            amt = 16 if s in ("qs", "qg") else 1
        self.cnt[s] += amt
        inst.then_inc(self.sem[s], amt)
        return (s, self.cnt[s])

    def w(self, eng, engname, toks):
        for (s, v) in toks:
            if v <= 0:
                continue
            key = (engname, s)
            if self.hw.get(key, -1) >= v:
                continue
            eng.wait_ge(self.sem[s], v)
            self.hw[key] = v

    def bar(self):
        return [(s, self.cnt[s]) for s in self.cnt]



def MM(p, te, out, lhsT, rhs, start, stop, nmax=512):
    n = rhs.shape[-1]
    inst = None
    for o in range(0, n, nmax):
        e = min(n, o + nmax)
        inst = te.matmul(out[:, o:e], lhsT, rhs[:, o:e], start=start, stop=stop)
    return inst


def _ein(nc, name, shape, dtype):
    return nc.dram_tensor(name, shape, dtype, kind="ExternalInput").ap()


def _eout(nc, name, shape, dtype):
    return nc.dram_tensor(name, shape, dtype, kind="ExternalOutput").ap()


def emit_ln(p, blk, tag, x_sb, tc, xh_sb, rows, ps_r1, ps_r2, ps_bc,
            scr_bf, bci, bcn, ones_bf, ones_row, eps_col, deps, xh32_sb=None,
            nt=16, scrT=None):

    def tile(t, i):
        return t[:, i * tc:(i + 1) * tc]

    def f_scalar(sc):
        p.w(sc, "act" + tag, deps)
        for i in range(nt):
            p.t("act", sc.copy(tile(scr_bf, i), tile(x_sb, i)))
    blk.scalar(f_scalar)
    cast_done = p.tok("act")

    def f_pe1(te):
        p.w(te, "pe" + tag, [cast_done])
        for i in range(nt):
            inst = MM(p, te, ps_r1[0:1, 0:tc], ones_bf[:], tile(scr_bf, i),
                      (i == 0), (i == nt - 1))
        p.t("pe", inst)
    blk.tensor(f_pe1)
    sum_done = p.tok("pe")

    def f_s2(sc):
        p.w(sc, "act" + tag, [sum_done])
        for i in range(nt):
            p.t("act", sc.activation(tile(scr_bf, i), tile(x_sb, i), AF.Square))
    blk.scalar(f_s2)
    sq_done = p.tok("act")

    def f_pe2(te):
        p.w(te, "pe" + tag, [sq_done])
        for i in range(nt):
            inst = MM(p, te, ps_r2[0:1, 0:tc], ones_bf[:], tile(scr_bf, i),
                      (i == 0), (i == nt - 1))
        p.t("pe", inst)
    blk.tensor(f_pe2)
    sq_sum = p.tok("pe")

    Dn = nt * 128

    def f_v2(v):
        p.w(v, "dve" + tag, [sum_done, sq_sum])
        p.t("dve", v.tensor_scalar_mul(rows[:, 0:tc], ps_r1[0:1, 0:tc], 1.0 / Dn))
        p.t("dve", v.tensor_scalar_mul(rows[:, tc:2 * tc], ps_r2[0:1, 0:tc],
                                       1.0 / Dn))
        p.w(v, "dve" + tag, [p.tok("dve")])
        p.t("dve", v.tensor_mul(rows[:, 2 * tc:3 * tc], rows[:, 0:tc],
                                rows[:, 0:tc]))
        p.w(v, "dve" + tag, [p.tok("dve")])
        p.t("dve", v.tensor_sub(rows[:, tc:2 * tc], rows[:, tc:2 * tc],
                                rows[:, 2 * tc:3 * tc]))
    blk.vector(f_v2)
    var_done = p.tok("dve")

    def f_s3(sc):
        p.w(sc, "act" + tag, [var_done])
        p.t("act", sc.activation(rows[:, 2 * tc:3 * tc], rows[:, tc:2 * tc],
                                 AF.Ln, bias=eps_col[0:1, 0:1]))
        p.w(sc, "act" + tag, [p.tok("act")])
        p.t("act", sc.activation(rows[:, 2 * tc:3 * tc], rows[:, 2 * tc:3 * tc],
                                 AF.Exp, scale=-0.5))
    blk.scalar(f_s3)
    inv_done = p.tok("act")

    def f_v3(v):
        p.w(v, "dve" + tag, [inv_done])
        p.t("dve", v.tensor_mul(rows[:, 3 * tc:4 * tc], rows[:, 0:tc],
                                rows[:, 2 * tc:3 * tc]))
        p.w(v, "dve" + tag, [p.tok("dve")])
        p.t("dve", v.tensor_scalar_mul(rows[:, 3 * tc:4 * tc],
                                       rows[:, 3 * tc:4 * tc], -1.0))
    blk.vector(f_v3)
    nmi_done = p.tok("dve")

    def f_pe3(te):
        p.w(te, "pe" + tag, [inv_done])
        p.t("pe", MM(p, te, ps_bc[:, 0:tc], ones_row[:],
                     rows[0:1, 2 * tc:3 * tc], True, True))
    blk.tensor(f_pe3)
    bc1 = p.tok("pe")

    def f_sc1(sc):
        p.w(sc, "act" + tag, [bc1])
        p.t("act", sc.copy(bci, ps_bc[:, 0:tc]))
    blk.scalar(f_sc1)
    ev1 = p.tok("act")

    def f_pe4(te):
        p.w(te, "pe" + tag, [nmi_done, ev1])
        p.t("pe", MM(p, te, ps_bc[:, 0:tc], ones_row[:],
                     rows[0:1, 3 * tc:4 * tc], True, True))
    blk.tensor(f_pe4)
    bc2 = p.tok("pe")

    def f_sc2(sc):
        p.w(sc, "act" + tag, [bc2])
        p.t("act", sc.copy(bcn, ps_bc[:, 0:tc]))
    blk.scalar(f_sc2)
    ev2 = p.tok("act")

    def f_v5(v):
        p.w(v, "dve" + tag, [ev1, ev2])
        for i in range(nt):
            if scrT is None:
                p.t("dve", v.tensor_mul(tile(x_sb, i), tile(x_sb, i), bci))
                p.w(v, "dve" + tag, [p.tok("dve")])
                p.t("dve", v.tensor_add(tile(xh_sb, i), tile(x_sb, i), bcn))
                if xh32_sb is not None:
                    p.t("dve", v.tensor_add(tile(xh32_sb, i), tile(x_sb, i),
                                            bcn))
            else:
                p.t("dve", v.tensor_mul(scrT[:, 0:tc], tile(x_sb, i), bci))
                p.w(v, "dve" + tag, [p.tok("dve")])
                p.t("dve", v.tensor_add(tile(xh_sb, i), scrT[:, 0:tc], bcn))
                if xh32_sb is not None:
                    p.t("dve", v.tensor_add(tile(xh32_sb, i), scrT[:, 0:tc],
                                            bcn))
                p.w(v, "dve" + tag, [p.tok("dve")])
    blk.vector(f_v5)
    return [p.tok("dve")]


def emit_ln1(nc, p, blk, IO):
    xT = IO["xT"]
    w1T = IO["w1T"]
    w2T = IO["w2T"]
    ones_bf_i = IO["ones_h"]
    ones_row_i = IO["ones_row"]
    ones_f_i = IO["ones_f"]
    eps_i = IO["eps_c"]
    xhT = IO["xhT"]
    wabs = IO["wabs"]

    with contextlib.ExitStack() as st_:
        E = st_.enter_context
        ones_bf = E(nc.sbuf_tensor("c_ones_h", [128, 1], F16))
        ones_row = E(nc.sbuf_tensor("c_ones_row", [1, 128], F32))
        ones_f = E(nc.sbuf_tensor("c_ones_f", [128, 1], F32))
        eps_c = E(nc.sbuf_tensor("c_eps", [128, 1], F32))
        x_sb = E(nc.sbuf_tensor("x_sb", [128, NT_D * TPC], F32))
        xh_sb = E(nc.sbuf_tensor("xh_sb", [128, NT_D * TPC], F16))
        scr_bf = E(nc.sbuf_tensor("scr_bf", [128, NT_D * TPC], F16))
        bc_sb = E(nc.sbuf_tensor("bc_sb", [128, 2 * TPC], F32))
        rows = E(nc.sbuf_tensor("rows", [1, 4 * TPC], F32))
        wbuf = E(nc.sbuf_tensor("wbuf", [128, DFF], F32))
        red = E(nc.sbuf_tensor("red", [128, 16], F32))
        red2 = E(nc.sbuf_tensor("red2", [128, 64], F32))
        wsum = E(nc.sbuf_tensor("wsum", [2, 1], F32))
        ps_r1 = E(nc.psum_tensor("ps_r1", [1, TPC], F32))
        ps_r2 = E(nc.psum_tensor("ps_r2", [1, TPC], F32))
        ps_bc = E(nc.psum_tensor("ps_bc", [128, TPC], F32))
        ps_w = E(nc.psum_tensor("ps_w", [2, 1], F32))

        def f_ld(sy):
            for sb, ap in [(ones_bf, ones_bf_i), (ones_row, ones_row_i),
                           (ones_f, ones_f_i), (eps_c, eps_i)]:
                p.t("qs", sy.dma_start(out=sb[:], in_=ap))
            for i in range(NT_D):
                p.t("qs", sy.dma_start(out=x_sb[:, i * TPC:(i + 1) * TPC],
                                       in_=xT[i * 128:(i + 1) * 128, :]))
        blk.sync(f_ld)
        ld_done = p.tok("qs")

        def wpass(src, nrow, width, colbase):
            for i in range(nrow):
                def f_g(g, i=i):
                    p.w(g, "qg", [p.tok("dve")])
                    p.t("qg", g.dma_start(out=wbuf[:, 0:width],
                                          in_=src[i * 128:(i + 1) * 128, :]))
                blk.gpsimd(f_g)
                ldt = p.tok("qg")

                def f_v(v, i=i):
                    p.w(v, "dve", [ldt])
                    p.t("dve", v.tensor_reduce(
                        red2[:, 0:width // 128],
                        wbuf[:, 0:width].rearrange("p (a b) -> p a b", b=128),
                        AX.X, ALU.add, apply_absolute_value=True))
                    p.w(v, "dve", [p.tok("dve")])
                    p.t("dve", v.tensor_reduce(
                        red[:, colbase + i:colbase + i + 1],
                        red2[:, 0:width // 128].rearrange("p (o a) -> p o a", o=1),
                        AX.X, ALU.add))
                blk.vector(f_v)

        wpass(w1T, 2, DFF, 0)
        wpass(w2T, 8, D, 2)

        def f_vsum(v):
            p.w(v, "dve", [p.tok("dve")])
            p.t("dve", v.tensor_reduce(
                red[:, 10:11], red[:, 0:2].rearrange("p (a b) -> p a b", a=1),
                AX.X, ALU.add))
            p.t("dve", v.tensor_reduce(
                red[:, 11:12], red[:, 2:10].rearrange("p (a b) -> p a b", a=1),
                AX.X, ALU.add))
        blk.vector(f_vsum)
        red_done = p.tok("dve")

        def f_pw(te):
            p.w(te, "pe", [red_done, ld_done])
            p.t("pe", te.matmul(ps_w[:], red[:, 10:12], ones_f[:],
                                start=True, stop=True))
        blk.tensor(f_pw)
        pw_done = p.tok("pe")

        def f_vw(v):
            p.w(v, "dve", [pw_done])
            p.t("dve", v.tensor_copy(wsum[:], ps_w[:]))
        blk.vector(f_vw)
        wsum_done = p.tok("dve")

        ln_done = emit_ln(p, blk, "", x_sb, TPC, xh_sb, rows, ps_r1, ps_r2,
                          ps_bc, scr_bf, bc_sb[:, 0:TPC], bc_sb[:, TPC:2 * TPC],
                          ones_bf, ones_row, eps_c[:, :], [ld_done])

        def f_st(sy):
            p.w(sy, "qs", ln_done + [wsum_done])
            for i in range(NT_D):
                p.t("qs", sy.dma_start(out=xhT[i * 128:(i + 1) * 128, :],
                                       in_=xh_sb[:, i * TPC:(i + 1) * TPC]))
            p.t("qs", sy.dma_start(out=wabs, in_=wsum[:]))
            sy.wait_ge(p.sem["qs"], p.cnt["qs"])
        blk.sync(f_st)



def emit_front(nc, p, blk, IO):
    TC = 512
    NCH = NTOK // TC
    xhTb = IO["xhTb"]
    winT = IO["winT"]
    convw = IO["convw"]
    convb = IO["convb"]
    wxT = IO["wxT"]
    w1T = IO["w1T"]
    w2T = IO["w2T"]
    wtot = IO["wtot"]
    ones_row_i = IO["ones_row"]
    u16o = IO["u16"]
    zs16o = IO["zs16"]
    ssmpo = IO["ssmp"]
    w1qo = IO["w1q"]
    w2qo = IO["w2q"]

    with contextlib.ExitStack() as st_:
        E = st_.enter_context
        ones_row = E(nc.sbuf_tensor("c_ones_row", [1, 128], F32))
        winbf = E(nc.sbuf_tensor("winbf", [128, 16 * 1024], F16))
        wx16 = E(nc.sbuf_tensor("wx16", [128, 4 * 160], F16))
        wld = E(nc.sbuf_tensor("wld", [128, 2048], F32))
        wx32 = E(nc.sbuf_tensor("wx32", [128, 4 * 160], F32))
        cw = E(nc.sbuf_tensor("cw", [128, 4 * 4], F32))
        cb = E(nc.sbuf_tensor("cb", [128, 4], F32))
        xh2 = E(nc.sbuf_tensor("xh2", [128, 2 * 16 * TC], F16))
        xzx = E(nc.sbuf_tensor("xzx", [128, 4 * (TC + 3)], F32))
        acc = E(nc.sbuf_tensor("acc", [128, 2 * TC], F32))
        ucv = E(nc.sbuf_tensor("ucv", [128, 8 * TC], F16))
        zst = E(nc.sbuf_tensor("zst", [128, 4 * TC], F16))
        s_out = E(nc.sbuf_tensor("s_out", [128, TC], F32))
        s_out2 = E(nc.sbuf_tensor("s_out2", [32, TC], F32))
        ps_x = E(nc.psum_tensor("ps_x", [128, TC], F32))
        ps_x2 = E(nc.psum_tensor("ps_x2", [128, TC], F32))
        ps_s1 = E(nc.psum_tensor("ps_s1", [128, TC], F32))
        ps_s2 = E(nc.psum_tensor("ps_s2", [32, TC], F32))
        ps_b = E(nc.psum_tensor("ps_b", [128, 2], F32))
        wqi = E(nc.sbuf_tensor("wqi", [128, DFF], F32))
        wqt = E(nc.sbuf_tensor("wqt", [128, DFF], F32))
        wqo_sb = E(nc.sbuf_tensor("wqo_sb", [128, DFF], BF16))
        srow = E(nc.sbuf_tensor("srow", [1, 4], F32))
        sbc = E(nc.sbuf_tensor("sbc", [128, 2], F32))
        wab = E(nc.sbuf_tensor("wab", [1, 2], F32))

        def f_ld(sy):
            p.t("qs", sy.dma_start(out=ones_row[:], in_=ones_row_i))
            p.t("qs", sy.dma_start(out=cw[:], in_=convw))
            p.t("qs", sy.dma_start(out=cb[:], in_=convb))
            p.t("qs", sy.dma_start(out=wx32[:], in_=wxT))
            p.t("qs", sy.dma_start(out=wab[:, 0:1], in_=wtot[0:1, :]))
            p.t("qs", sy.dma_start(out=wab[:, 1:2], in_=wtot[1:2, :]))
        blk.sync(f_ld)
        ld0 = p.tok("qs")

        def f_cast0(sc):
            p.w(sc, "act", [ld0])
            p.t("act", sc.copy(wx16[:], wx32[:]))
        blk.scalar(f_cast0)

        for i in range(16):
            def f_ldi(sy, i=i):
                p.w(sy, "qs", [p.tok("act")])
                p.t("qs", sy.dma_start(out=wld[:, 0:1024],
                                       in_=winT[i * 128:(i + 1) * 128, :]))
            blk.sync(f_ldi)
            ldt = p.tok("qs")

            def f_ci(sc, i=i, ldt=ldt):
                p.w(sc, "act", [ldt])
                p.t("act", sc.copy(winbf[:, i * 1024:(i + 1) * 1024],
                                   wld[:, 0:1024]))
            blk.scalar(f_ci)
        wcast = p.tok("act")

        def f_vq0(v):
            p.w(v, "dve", [ld0])
            p.t("dve", v.tensor_scalar_mul(srow[:, 0:2], wab[0:1, 0:2],
                                           1.0 / (DFF * D)))
            p.w(v, "dve", [p.tok("dve")])
            p.t("dve", v.reciprocal(srow[:, 2:4], srow[:, 0:2]))
        blk.vector(f_vq0)
        sc_done = p.tok("dve")

        def f_pq(te):
            p.w(te, "pe", [sc_done])
            p.t("pe", te.matmul(ps_b[:], ones_row[:], srow[0:1, 2:4],
                                start=True, stop=True))
        blk.tensor(f_pq)
        pb_done = p.tok("pe")

        def f_vq1(v):
            p.w(v, "dve", [pb_done])
            p.t("dve", v.tensor_copy(sbc[:], ps_b[:]))
        blk.vector(f_vq1)
        sbc_done = p.tok("dve")

        def quantpass(src, dst, nrow, width, scol):
            for i in range(nrow):
                def f_g(g, i=i):
                    p.w(g, "qg", [p.tok("dve"), p.tok("qg")])
                    p.t("qg", g.dma_start(out=wqi[:, 0:width],
                                          in_=src[i * 128:(i + 1) * 128, :]))
                blk.gpsimd(f_g)
                ldt = p.tok("qg")

                def f_v(v, i=i, ldt=ldt):
                    p.w(v, "dve", [ldt, sbc_done])
                    p.t("dve", v.tensor_scalar_mul(wqt[:, 0:width],
                                                   wqi[:, 0:width],
                                                   sbc[:, scol:scol + 1]))
                    p.w(v, "dve", [p.tok("dve")])
                    p.t("dve", v.tensor_scalar(wqt[:, 0:width], wqt[:, 0:width],
                                               MAGIC, MAGIC, ALU.add,
                                               ALU.subtract))
                    p.w(v, "dve", [p.tok("dve")])
                    p.t("dve", v.tensor_scalar(wqo_sb[:, 0:width],
                                               wqt[:, 0:width], 1.0, -1.0,
                                               ALU.min, ALU.max))
                blk.vector(f_v)
                qt = p.tok("dve")

                def f_s(sy, i=i, qt=qt):
                    p.w(sy, "qg", [qt])
                    p.t("qg", sy.dma_start(out=dst[i * 128:(i + 1) * 128, :],
                                           in_=wqo_sb[:, 0:width]))
                blk.gpsimd(f_s)

        quantpass(w1T, w1qo, 2, DFF, 0)
        quantpass(w2T, w2qo, 8, D, 1)
        quant_done = p.tok("qg")

        def tl(t, i, w=TC):
            return t[:, i * w:(i + 1) * w]

        def f_hist0(v):
            for mt in range(4):
                p.t("dve", v.memset(xzx[:, mt * (TC + 3):mt * (TC + 3) + 3], 0.0))
        blk.vector(f_hist0)

        for ch in range(NCH):
            xsl = (ch % 2) * 16 * TC
            if ch == NCH // 2:
                def f_histr(v):
                    p.w(v, "dve", [("act", p.cnt["act"]), p.tok("dve")])
                    for mt in range(4):
                        p.t("dve", v.memset(
                            xzx[:, mt * (TC + 3):mt * (TC + 3) + 3], 0.0))
                blk.vector(f_histr)

            pe_c = p.cnt["pe"]
            a_base = p.cnt["act"]
            v_base = p.cnt["dve"]
            qs_c = p.cnt["qs"]

            def f_ldc(sy, ch=ch, xsl=xsl, pe_c=pe_c):
                if ch >= 2:
                    p.w(sy, "qs", [("pe", pe_c - 9)])
                for i in range(16):
                    p.t("qs", sy.dma_start(
                        out=xh2[:, xsl + i * TC:xsl + (i + 1) * TC],
                        in_=xhTb[((ch * TC) // 1024) * 2048 + i * 128:
                                 ((ch * TC) // 1024) * 2048 + (i + 1) * 128,
                                 (ch * TC) % 1024:(ch * TC) % 1024 + TC]))
            blk.sync(f_ldc)
            ldc = p.tok("qs")

            def f_pe(te, ch=ch, xsl=xsl, a_base=a_base, ldc=ldc):
                p.w(te, "pe", [ldc, wcast])
                for mt in range(8):
                    ps = ps_x if mt % 2 == 0 else ps_x2
                    if mt >= 2:
                        p.w(te, "pe", [("act", a_base + mt - 1)])
                    elif ch > 0:
                        p.w(te, "pe", [("act", a_base)])
                    for kt in range(16):
                        inst = te.matmul(
                            ps[:],
                            winbf[:, kt * 1024 + mt * 128:kt * 1024 + (mt + 1) * 128],
                            xh2[:, xsl + kt * TC:xsl + (kt + 1) * TC],
                            start=(kt == 0), stop=(kt == 15))
                    p.t("pe", inst)
                p.w(te, "pe", [("act", a_base + 12)])
                for kt in range(4):
                    i1 = te.matmul(ps_s1[:], wx16[:, kt * 160:kt * 160 + 128],
                                   tl(ucv, kt), start=(kt == 0), stop=(kt == 3))
                for kt in range(4):
                    i2 = te.matmul(ps_s2[:], wx16[:, kt * 160 + 128:kt * 160 + 160],
                                   tl(ucv, kt), start=(kt == 0), stop=(kt == 3))
                p.t("pe", i2)
            blk.tensor(f_pe)

            def f_sc(sc, ch=ch, pe_c=pe_c, v_base=v_base, qs_c=qs_c):
                for mt in range(8):
                    ps = ps_x if mt % 2 == 0 else ps_x2
                    p.w(sc, "act", [("pe", pe_c + mt + 1)])
                    if mt < 4:
                        xb = mt * (TC + 3)
                        if ch > 0:
                            p.w(sc, "act", [("dve", v_base)])
                        p.t("act", sc.copy(xzx[:, xb + 3:xb + 3 + TC], ps[:]))
                    else:
                        if ch >= 1:
                            p.w(sc, "act", [("qs", qs_c - 16 * 10 + 16 * (mt - 3))])
                        p.t("act", sc.activation(
                            zst[:, (mt - 4) * TC:(mt - 3) * TC], ps[:], AF.Silu))
                for mt in range(4):
                    p.w(sc, "act", [("dve", v_base + 5 * (mt + 1))])
                    if ch >= 1:
                        p.w(sc, "act", [("qs", qs_c - 16 * 6 + 16 * (mt + 1))])
                    p.w(sc, "act", [("pe", pe_c)])
                    p.t("act", sc.activation(tl(ucv, mt),
                                             acc[:, (mt % 2) * TC:(mt % 2 + 1) * TC],
                                             AF.Silu, bias=cb[:, mt:mt + 1]))
            blk.scalar(f_sc)

            def f_cv(v, ch=ch, a_base=a_base):
                for mt in range(4):
                    xb = mt * (TC + 3)
                    asl = (mt % 2) * TC
                    p.w(v, "dve", [("act", a_base + mt + 1), p.tok("dve")])
                    if mt >= 2:
                        p.w(v, "dve", [("act", a_base + 8 + mt - 1)])
                    elif ch > 0:
                        p.w(v, "dve", [("act", a_base)])
                    p.t("dve", v.tensor_scalar_mul(acc[:, asl:asl + TC],
                                                   xzx[:, xb:xb + TC],
                                                   cw[:, mt * 4:mt * 4 + 1]))
                    for k in range(1, 4):
                        p.w(v, "dve", [p.tok("dve")])
                        p.t("dve", v.scalar_tensor_tensor(
                            acc[:, asl:asl + TC], xzx[:, xb + k:xb + k + TC],
                            cw[:, mt * 4 + k:mt * 4 + k + 1],
                            acc[:, asl:asl + TC], ALU.mult, ALU.add))
                    p.w(v, "dve", [p.tok("dve"),
                                   ("act", a_base + 8 + mt)])
                    p.t("dve", v.tensor_copy(xzx[:, xb:xb + 3],
                                             xzx[:, xb + TC:xb + TC + 3]))
            blk.vector(f_cv)

            def f_sev(v, pe_c=pe_c):
                p.w(v, "dve", [("pe", pe_c + 9), ("qs", p.cnt["qs"])])
                p.t("dve", v.tensor_copy(s_out[:], ps_s1[:]))
                p.w(v, "dve", [p.tok("dve")])
                p.t("dve", v.tensor_copy(s_out2[:], ps_s2[:]))
            blk.vector(f_sev)

            def f_st(sy, ch=ch, a_base=a_base):
                for mt in range(4):
                    p.w(sy, "qs", [("act", a_base + 4 + mt + 1)])
                    p.t("qs", sy.dma_start(
                        out=zs16o[mt * 128:(mt + 1) * 128, ch * TC:(ch + 1) * TC],
                        in_=zst[:, mt * TC:(mt + 1) * TC]))
                for mt in range(4):
                    p.w(sy, "qs", [("act", a_base + 8 + mt + 1)])
                    p.t("qs", sy.dma_start(
                        out=u16o[mt * 128:(mt + 1) * 128, ch * TC:(ch + 1) * TC],
                        in_=tl(ucv, mt)))
                p.w(sy, "qs", [("dve", p.cnt["dve"])])
                p.t("qs", sy.dma_start(out=ssmpo[0:128, ch * TC:(ch + 1) * TC],
                                       in_=s_out[:]))
                p.t("qs", sy.dma_start(out=ssmpo[128:160, ch * TC:(ch + 1) * TC],
                                       in_=s_out2[:]))
            blk.sync(f_st)

        def f_fin(sy):
            p.w(sy, "qs", [quant_done])
            sy.wait_ge(p.sem["qs"], p.cnt["qs"])
            sy.wait_ge(p.sem["qg"], p.cnt["qg"])
        blk.sync(f_fin)



def emit_scan(nc, p, blk, IO):
    TCS = 1024
    NCH = NTOK // TCS
    ssms = IO["ssms"]
    u16 = IO["u16"]
    zs16 = IO["zs16"]
    wdtT = IO["wdtT"]
    dtb = IO["dtb"]
    acol = IO["acol"]
    dcol = IO["dcol"]
    yszo = IO["ysz_a2a"]

    with contextlib.ExitStack() as st_:
        E = st_.enter_context
        dtr = E(nc.sbuf_tensor("dtr", [128, NTOK], F32))
        dtrb = E(nc.sbuf_tensor("dtrb", [128, NTOK], F16))
        wdt = E(nc.sbuf_tensor("wdt", [128, 512], F16))
        wdt32 = E(nc.sbuf_tensor("wdt32", [128, 512], F32))
        dtb_sb = E(nc.sbuf_tensor("dtb_sb", [128, 4], F32))
        acol_sb = E(nc.sbuf_tensor("acol_sb", [128, 64], F32))
        dcol_sb = E(nc.sbuf_tensor("dcol_sb", [128, 4], F32))
        delta = E(nc.sbuf_tensor("delta", [128, 4 * TCS], F32))
        spr = E(nc.sbuf_tensor("spr", [128, 4 * TCS], F32))
        u_sb = E(nc.sbuf_tensor("u_sb", [128, 8 * TCS], F16))
        zs_sb = E(nc.sbuf_tensor("zs_sb", [128, 8 * TCS], F16))
        wmt = E(nc.sbuf_tensor("wmt", [128, TCS], F32))
        a_sb = E(nc.sbuf_tensor("a_sb", [128, 2 * TCS], F32))
        b_sb = E(nc.sbuf_tensor("b_sb", [128, TCS], F32))
        h_sb = E(nc.sbuf_tensor("h_sb", [128, 2 * TCS], F32))
        hy_sb = E(nc.sbuf_tensor("hy_sb", [128, TCS], F32))
        y15_sb = E(nc.sbuf_tensor("y15_sb", [128, TCS], F32))
        y_sb = E(nc.sbuf_tensor("y_sb", [128, TCS], F32))
        bb_sb = E(nc.sbuf_tensor("bb_sb", [128, 2 * TCS], F16))
        cc_sb = E(nc.sbuf_tensor("cc_sb", [128, 2 * TCS], F16))
        t1_sb = E(nc.sbuf_tensor("t1_sb", [128, TCS], F32))
        yz_sb = E(nc.sbuf_tensor("yz_sb", [128, 2 * TCS], F16))
        state = E(nc.sbuf_tensor("state", [128, 64], F32))
        sp1 = E(nc.psum_tensor("sp1", [128, 1024], F32))
        sp2 = E(nc.psum_tensor("sp2", [128, 1024], F32))

        def f_ld(sy):
            for sb, ap in [(wdt32, wdtT), (dtb_sb, dtb), (acol_sb, acol),
                           (dcol_sb, dcol)]:
                p.t("qs", sy.dma_start(out=sb[:], in_=ap))
            p.t("qs", sy.dma_start(out=dtr[:], in_=ssms[0:128, :]))
        blk.sync(f_ld)
        ld0 = p.tok("qs")

        def f_cast(sc):
            p.w(sc, "act", [ld0])
            p.t("act", sc.copy(dtrb[:], dtr[:]))
            p.t("act", sc.copy(wdt[:], wdt32[:]))
        blk.scalar(f_cast)
        cast_done = p.tok("act")

        def f_st0(v):
            p.t("dve", v.memset(state[:], 0.0))
        blk.vector(f_st0)

        ssbc = IO["ssbc"]
        cv32 = delta[0:32, 0:2048]
        cv16 = spr.bitcast(F16)[0:32, 0:2048]
        for cq in range(4):
            def f_cvl(sy, cq=cq):
                p.w(sy, "qs", [p.tok("act")])
                p.t("qs", sy.dma_start(
                    out=cv32, in_=ssms[128:160, cq * 2048:(cq + 1) * 2048]))
            blk.sync(f_cvl)
            cvl = p.tok("qs")

            def f_cvc(sc, cq=cq, cvl=cvl):
                p.w(sc, "act", [cvl, p.tok("qs")])
                p.t("act", sc.copy(cv16, cv32))
            blk.scalar(f_cvc)
            cvc = p.tok("act")

            def f_cvs(sy, cq=cq, cvc=cvc):
                p.w(sy, "qs", [cvc])
                p.t("qs", sy.dma_start(
                    out=ssbc[:, cq * 2048:(cq + 1) * 2048], in_=cv16))
            blk.sync(f_cvs)
        bc_ready = p.tok("qs")

        lu_tok = {}

        def f_lu0(sy):
            for mt in range(4):
                p.t("qs", sy.dma_start(
                    out=u_sb[:, mt * TCS:(mt + 1) * TCS],
                    in_=u16[mt * 128:(mt + 1) * 128, 0:TCS]))
                p.t("qs", sy.dma_start(
                    out=zs_sb[:, mt * TCS:(mt + 1) * TCS],
                    in_=zs16[mt * 128:(mt + 1) * 128, 0:TCS]))
        blk.sync(f_lu0)
        lu_tok[0] = p.tok("qs")

        prev_ymul62 = 0
        prev_st_tok = None
        for ch in range(NCH):
            reset = (ch == NCH // 2)
            v0 = p.cnt["dve"]
            g0 = p.cnt["gp"]
            vc = v0 + 4 + (1 if reset else 0)
            bmul_tok = {}
            scan_tok = {}
            y15_tok = {}
            fin_tok = {}
            for mt in range(4):
                vc += 1
                for n in range(16):
                    k = mt * 16 + n
                    vc += 1
                    bmul_tok[k] = vc
                    vc += 1
                    scan_tok[k] = vc
                    vc += 1
                    if n == 15:
                        vc += 1
                        y15_tok[mt] = vc
                vc += 3
                fin_tok[mt] = vc
            gc = g0
            ymul_tok = {}
            yadd_tok = {}
            for mt in range(4):
                for n in range(15):
                    k = mt * 16 + n
                    gc += 1
                    ymul_tok[k] = gc
                    if n > 0:
                        gc += 1
                        yadd_tok[k] = gc

            qs_c0 = p.cnt["qs"]
            if ch + 1 < NCH:
                def f_lu(sy, ch=ch, v0=v0):
                    p.w(sy, "qs", [("dve", v0)])
                    us2 = ((ch + 1) % 2) * 4 * TCS
                    for mt in range(4):
                        p.t("qs", sy.dma_start(
                            out=u_sb[:, us2 + mt * TCS:us2 + (mt + 1) * TCS],
                            in_=u16[mt * 128:(mt + 1) * 128,
                                    (ch + 1) * TCS:(ch + 2) * TCS]))
                        p.t("qs", sy.dma_start(
                            out=zs_sb[:, us2 + mt * TCS:us2 + (mt + 1) * TCS],
                            in_=zs16[mt * 128:(mt + 1) * 128,
                                     (ch + 1) * TCS:(ch + 2) * TCS]))
                blk.sync(f_lu)
                lu_tok[ch + 1] = p.tok("qs")
                nload = 8
            else:
                nload = 0
            lu_done = lu_tok[ch]
            us = (ch % 2) * 4 * TCS
            def st_tok(j, qs_c0=qs_c0, nload=nload):
                return qs_c0 + 16 * nload + 16 * (j + 1)

            bct_bb = {}
            bct_cc = {}

            def f_gp(g, ch=ch, v0=v0, prev_ymul62=prev_ymul62):
                def issue_bb(k):
                    n = k % 16
                    sl = (k % 2) * TCS
                    p.t("qg", g.dma_start(
                        out=bb_sb[:, sl:sl + TCS],
                        in_=ssbc[n:n + 1,
                                 ch * TCS:(ch + 1) * TCS].partition_broadcast(128)))
                    bct_bb[k] = p.tok("qg")

                def issue_cc(k):
                    n = k % 16
                    sl = (k % 2) * TCS
                    p.t("qg", g.dma_start(
                        out=cc_sb[:, sl:sl + TCS],
                        in_=ssbc[16 + n:17 + n,
                                 ch * TCS:(ch + 1) * TCS].partition_broadcast(128)))
                    bct_cc[k] = p.tok("qg")

                if ch > 0:
                    p.w(g, "qg", [("dve", v0)])
                else:
                    p.w(g, "qg", [bc_ready])
                for k in (0, 1):
                    issue_bb(k)
                    issue_cc(k)
                for k in range(64):
                    mt, n = divmod(k, 16)
                    if k + 2 < 64:
                        p.w(g, "qg", [("dve", bmul_tok[k])])
                        issue_bb(k + 2)
                    if n == 15:
                        if k + 2 < 64:
                            p.w(g, "qg", [("dve", y15_tok[mt])])
                            issue_cc(k + 2)
                        continue
                    p.w(g, "gp", [("dve", scan_tok[k]), bct_cc[k]])
                    sl = (k % 2) * TCS
                    if n == 0:
                        p.t("gp", g.tensor_mul(y_sb[:], h_sb[:, sl:sl + TCS],
                                               cc_sb[:, sl:sl + TCS]), 1)
                    else:
                        p.t("gp", g.tensor_mul(hy_sb[:], h_sb[:, sl:sl + TCS],
                                               cc_sb[:, sl:sl + TCS]), 1)
                        p.w(g, "gp", [p.tok("gp")])
                        p.t("gp", g.tensor_add(y_sb[:], y_sb[:], hy_sb[:]), 1)
                    if k + 2 < 64:
                        issue_cc(k + 2)
            blk.gpsimd(f_gp)
            assert p.cnt["gp"] == gc, (p.cnt["gp"], gc)

            a0 = p.cnt["act"]
            def f_dm(te, ch=ch, a0=a0):
                p.w(te, "pe", [cast_done])
                for mt in range(4):
                    ps = sp1 if mt % 2 == 0 else sp2
                    if mt >= 2:
                        p.w(te, "pe", [("act", a0 + 4 * (mt - 1))])
                    elif ch > 0:
                        p.w(te, "pe", [("act", a0)])
                    p.t("pe", MM(p, te, ps[:], wdt[:, mt * 128:(mt + 1) * 128],
                                 dtrb[:, ch * TCS:(ch + 1) * TCS], True, True))
            blk.tensor(f_dm)
            pe0 = p.cnt["pe"]

            def f_sc(sc, ch=ch, v0=v0, scan_tok=scan_tok):
                if ch == 0:
                    p.w(sc, "act", [bc_ready])
                for mt in range(4):
                    ps = sp1 if mt % 2 == 0 else sp2
                    o = mt * TCS
                    p.w(sc, "act", [("pe", pe0 - 4 + mt + 1)])
                    if ch > 0:
                        p.w(sc, "act", [("dve", v0)])
                    p.t("act", sc.activation(delta[:, o:o + TCS], ps[:],
                                             AF.Abs, bias=dtb_sb[:, mt:mt + 1]))
                    p.w(sc, "act", [p.tok("act")])
                    p.t("act", sc.activation(delta[:, o:o + TCS],
                                             delta[:, o:o + TCS], AF.Exp,
                                             scale=-1.0))
                    p.w(sc, "act", [p.tok("act")])
                    p.t("act", sc.activation(delta[:, o:o + TCS],
                                             delta[:, o:o + TCS], AF.Ln,
                                             bias=1.0))
                    p.t("act", sc.activation(spr[:, o:o + TCS], ps[:], AF.Relu,
                                             bias=dtb_sb[:, mt:mt + 1]))
                p.w(sc, "act", [p.tok("act")])
                for mt in range(4):
                    for n in range(16):
                        k = mt * 16 + n
                        sl = ((k % 2)) * TCS
                        w = v0 + mt + 1
                        if k >= 2:
                            w = max(w, scan_tok[k - 2])
                        p.w(sc, "act", [("dve", w)])
                        p.t("act", sc.activation(
                            a_sb[:, sl:sl + TCS], delta[:, mt * TCS:(mt + 1) * TCS],
                            AF.Exp, scale=acol_sb[:, mt * 16 + n:mt * 16 + n + 1]))
            blk.scalar(f_sc)
            act_after_sp = a0 + 4 * 4
            def exp_tok(mt, n):
                return ("act", act_after_sp + mt * 16 + n + 1)

            def f_v(v, ch=ch, v0=v0, lu_done=lu_done, reset=reset,
                    prev_ymul62=prev_ymul62, st_tok=st_tok,
                    prev_st_tok=prev_st_tok):
                for mt in range(4):
                    o = mt * TCS
                    p.w(v, "dve", [("act", a0 + 4 * (mt + 1))])
                    p.t("dve", v.tensor_add(delta[:, o:o + TCS],
                                            delta[:, o:o + TCS],
                                            spr[:, o:o + TCS]))
                if reset:
                    p.w(v, "dve", [p.tok("dve")])
                    p.t("dve", v.memset(state[:], 0.0))
                for mt in range(4):
                    o = mt * TCS
                    p.w(v, "dve", [lu_done, p.tok("dve")])
                    p.t("dve", v.tensor_mul(wmt[:], delta[:, o:o + TCS],
                                            u_sb[:, us + o:us + o + TCS]))
                    for n in range(16):
                        k = mt * 16 + n
                        col = mt * 16 + n
                        sl = (k % 2) * TCS
                        p.w(v, "dve", [bct_bb[k], exp_tok(mt, n),
                                       p.tok("dve")])
                        p.t("dve", v.tensor_mul(b_sb[:], wmt[:],
                                                bb_sb[:, sl:sl + TCS]))
                        assert p.cnt["dve"] == bmul_tok[k]
                        w2 = [p.tok("dve")]
                        if k >= 2 and (k - 2) % 16 != 15:
                            w2.append(("gp", ymul_tok[k - 2]))
                        elif k < 2 and ch > 0:
                            w2.append(("gp", prev_ymul62))
                        p.w(v, "dve", w2)
                        p.t("dve", v.tensor_tensor_scan(
                            h_sb[:, sl:sl + TCS], a_sb[:, sl:sl + TCS], b_sb[:],
                            state[:, col:col + 1], ALU.mult, ALU.add))
                        assert p.cnt["dve"] == scan_tok[k]
                        p.w(v, "dve", [p.tok("dve")])
                        p.t("dve", v.tensor_copy(state[:, col:col + 1],
                                                 h_sb[:, sl + TCS - 1:sl + TCS]))
                        if n == 15:
                            p.w(v, "dve", [bct_cc[k], p.tok("dve")])
                            p.t("dve", v.tensor_mul(y15_sb[:],
                                                    h_sb[:, sl:sl + TCS],
                                                    cc_sb[:, sl:sl + TCS]))
                    ysl = ((ch * 4 + mt) % 2) * TCS
                    if mt >= 2:
                        p.w(v, "dve", [("qs", st_tok(mt - 2))])
                    elif ch > 0:
                        p.w(v, "dve", [("qs", prev_st_tok(mt + 2))])
                    p.w(v, "dve", [("gp", yadd_tok[mt * 16 + 14]),
                                   p.tok("dve")])
                    p.t("dve", v.scalar_tensor_tensor(
                        t1_sb[:], u_sb[:, us + o:us + o + TCS],
                        dcol_sb[:, mt:mt + 1],
                        y_sb[:], ALU.mult, ALU.add))
                    p.w(v, "dve", [p.tok("dve")])
                    p.t("dve", v.tensor_add(t1_sb[:], t1_sb[:], y15_sb[:]))
                    p.w(v, "dve", [p.tok("dve")])
                    p.t("dve", v.tensor_mul(yz_sb[:, ysl:ysl + TCS], t1_sb[:],
                                            zs_sb[:, us + o:us + o + TCS]))
                    assert p.cnt["dve"] == fin_tok[mt]
            blk.vector(f_v)
            assert p.cnt["dve"] == vc, (p.cnt["dve"], vc)
            prev_ymul62 = ymul_tok[62]
            prev_st_tok = st_tok

            def f_ys(sy, ch=ch):
                for mt in range(4):
                    ysl = ((ch * 4 + mt) % 2) * TCS
                    p.w(sy, "qs", [("dve", fin_tok[mt])])
                    p.t("qs", sy.dma_start(
                        out=yszo[ch * 512 + mt * 128:ch * 512 + (mt + 1) * 128, :],
                        in_=yz_sb[:, ysl:ysl + TCS]))
            blk.sync(f_ys)

        def f_end(sy):
            sy.wait_ge(p.sem["qs"], p.cnt["qs"])
        blk.sync(f_end)

def emit_post1(nc, p, blk, IO):
    TP = TPC
    yszg = IO["yszg"]
    woutT = IO["woutT"]
    xT = IO["xT"]
    selW = IO["selW"]
    mdT = IO["mdT"]
    mdrep = IO["mdrep"]
    lupF = IO["lupF"]
    ldgF = IO["ldgF"]
    biasw = IO["biasw"]
    dbwT = IO["dbwT"]
    slope_i = IO["slope_c"]
    center_i = IO["center_c"]
    wq_i = IO["wq"]
    wk_i = IO["wk"]
    wv_i = IO["wv"]
    ones_bf_i = IO["ones_h"]
    ones_b16_i = IO["ones_bf"]
    ones_row_i = IO["ones_row"]
    eps_i = IO["eps_c"]
    identf_i = IO["identf"]
    e8bf_i = IO["e8h"]
    e8f_i = IO["e8f"]
    e8t_i = IO["e8t"]
    iota9_i = IO["iota9"]
    iotan_i = IO["iotan"]
    x2To = IO["x2T"]
    qO = IO["qO"]
    kvpO = IO["kvp"]

    with contextlib.ExitStack() as st_:
        E = st_.enter_context
        ones_bf = E(nc.sbuf_tensor("c_ones_h", [128, 1], F16))
        ones_b16 = E(nc.sbuf_tensor("c_ones_b16", [128, 1], BF16))
        ones_row = E(nc.sbuf_tensor("c_ones_row", [1, 128], F32))
        eps_c = E(nc.sbuf_tensor("c_eps", [128, 1], F32))
        identf = E(nc.sbuf_tensor("c_identf", [128, 128], F32))
        e8bf = E(nc.sbuf_tensor("c_e8h", [8, 64], F16))
        e8f = E(nc.sbuf_tensor("c_e8f", [8, 64], F32))
        e8t = E(nc.sbuf_tensor("c_e8t", [64, 8], BF16))
        iota9 = E(nc.sbuf_tensor("c_iota9", [128, 8], F32))
        iotan = E(nc.sbuf_tensor("c_iotan", [128, 8], F32))
        slo_sb = E(nc.sbuf_tensor("slo_sb", [8, 1], F32))
        cen_sb = E(nc.sbuf_tensor("cen_sb", [8, 1], F32))
        xar = E(nc.sbuf_tensor("xar", [128, 16 * TP], F32))
        A1 = E(nc.sbuf_tensor("A1", [128, 16 * TP], F32))
        xh = E(nc.sbuf_tensor("xh", [128, 16 * TP], F16))
        wslot = E(nc.sbuf_tensor("wslot", [128, 2048], F32))
        selw_bf = E(nc.sbuf_tensor("selw_bf", [128, 16 * 64], F16))
        mdt_bf = E(nc.sbuf_tensor("mdt_bf", [128, 16 * 8], F16))
        mdr_bf = E(nc.sbuf_tensor("mdr_bf", [128, 16 * 64], F16))
        lup_bf = E(nc.sbuf_tensor("lup_bf", [128, 16 * 64], F16))
        dbwt_bf = E(nc.sbuf_tensor("dbwt_bf", [128, 16 * 8], F16))
        bw_bf = E(nc.sbuf_tensor("bw_bf", [8, D], F16))
        ldg_bf = E(nc.sbuf_tensor("ldg_bf", [64, D], F16))
        c1n_sb = E(nc.sbuf_tensor("c1n_sb", [64, 1], F32))
        md2_sb = E(nc.sbuf_tensor("md2_sb", [8, 1], F32))

        xm8_sb = E(nc.sbuf_tensor("xm8_sb", [8, TP], F32))
        nrm_sb = E(nc.sbuf_tensor("nrm_sb", [8, TP], F32))
        t64_sb = E(nc.sbuf_tensor("t64_sb", [64, TP], F32))
        cont_bf = E(nc.sbuf_tensor("cont_bf", [64, TP], BF16))
        negs_sb = E(nc.sbuf_tensor("negs_sb", [8, 1], F32))
        sbias_sb = E(nc.sbuf_tensor("sbias_sb", [8, 1], F32))
        mxc = E(nc.sbuf_tensor("mxc", [128, 8], F32))
        eq_sb = E(nc.sbuf_tensor("eq_sb", [128, 8], F32))
        idxc = E(nc.sbuf_tensor("idxc", [128, 1], F32))
        oneh_t = E(nc.sbuf_tensor("oneh_t", [128, 8], F32))
        onehot8 = E(nc.sbuf_tensor("onehot8", [8, TP], F16))

        k2b_bf = E(nc.sbuf_tensor("k2b_bf", [8, 64], F16))
        mx8_bf = E(nc.sbuf_tensor("mx8_bf", [64, TP], F16))
        d_sb = E(nc.sbuf_tensor("d_sb", [128, TP], F32))
        kvp_sb = E(nc.sbuf_tensor("kvp_sb", [128, 65], F32))
        psA = E(nc.psum_tensor("psA", [128, TP], F32))
        psB = E(nc.psum_tensor("psB", [128, TP], F32))
        psC = E(nc.psum_tensor("psC", [128, TP], F32))
        psD = E(nc.psum_tensor("psD", [64, TP], F32))

        rows = A1[0:1, 12 * TP:16 * TP]
        bc_sb = A1[:, 8 * TP:10 * TP]
        ssbc_sb = A1[:, 10 * TP:11 * TP]
        rrow = bc_sb[0:1, 0:TP]
        ss_sb = bc_sb[0:1, TP:2 * TP]
        rslot = xh.bitcast(F16)[:, 0:2 * TP]
        scrT = d_sb[:, 0:TP]
        smalls32 = wslot[:, 0:1024]
        e_sb = nrm_sb
        inv8_sb = xm8_sb
        a1bf = A1.bitcast(F16)
        a1h = A1.bitcast(F16)
        qkv_ev = d_sb.bitcast(F16)

        def xt(i):
            return xar[:, i * TP:(i + 1) * TP]

        def xht(i):
            return xh[:, i * TP:(i + 1) * TP]

        def f_ld(sy):
            for sb, ap in [(ones_bf, ones_bf_i), (ones_b16, ones_b16_i),
                           (ones_row, ones_row_i),
                           (eps_c, eps_i), (identf, identf_i), (e8bf, e8bf_i),
                           (e8f, e8f_i), (e8t, e8t_i), (iota9, iota9_i),
                           (iotan, iotan_i), (slo_sb, slope_i),
                           (cen_sb, center_i)]:
                p.t("qs", sy.dma_start(out=sb[:], in_=ap))
            for i in range(16):
                p.t("qs", sy.dma_start(out=xt(i), in_=xT[i * 128:(i + 1) * 128, :]))
        blk.sync(f_ld)
        ld0 = p.tok("qs")

        yar = xh.bitcast(F16)
        for half in range(2):
            ctoks = []
            for kk in range(16):
                kt = half * 16 + kk

                def f_lw(sy, kt=kt, kk=kk):
                    if kk >= 1:
                        p.w(sy, "qs", [ctoks[kk - 1]])
                    elif half > 0:
                        p.w(sy, "qs", [("act", p.cnt["act"])])
                    p.t("qs", sy.dma_start(
                        out=wslot[:, 0:2048],
                        in_=woutT[kt * 128:(kt + 1) * 128, :]))
                blk.sync(f_lw)
                lwt = p.tok("qs")

                def f_cw(sc, kk=kk, lwt=lwt, half=half):
                    p.w(sc, "act", [lwt])
                    if half > 0 and kk == 0:
                        p.w(sc, "act", [("pe", p.cnt["pe"])])
                    ctoks.append(p.t("act", sc.copy(
                        a1h[:, kk * 2048:(kk + 1) * 2048], wslot[:, 0:2048])))
                blk.scalar(f_cw)
            wc = p.tok("act")

            def f_lys(sy, half=half):
                p.w(sy, "qg", [("pe", p.cnt["pe"])])
                for kk in range(16):
                    kt = half * 16 + kk
                    p.t("qg", sy.dma_start(
                        out=yar[:, kk * TP:(kk + 1) * TP],
                        in_=yszg[kt * 128:(kt + 1) * 128, :]))
            blk.gpsimd(f_lys)
            ys_done = p.tok("qg")

            pe_b = p.cnt["pe"]
            dv_b = p.cnt["dve"]

            def f_mm(te, half=half, pe_b=pe_b, dv_b=dv_b, wc=wc,
                     ys_done=ys_done):
                p.w(te, "pe", [wc, ys_done])
                for mt in range(16):
                    ps = psA if mt % 2 == 0 else psB
                    if mt >= 2:
                        p.w(te, "pe", [("dve", dv_b + mt - 1)])
                    elif half > 0:
                        p.w(te, "pe", [("dve", dv_b)])
                    for kk in range(16):
                        i1 = MM(p, te, ps[:], a1h[:, kk * 2048 + mt * 128:
                                                  kk * 2048 + (mt + 1) * 128],
                                yar[:, kk * TP:(kk + 1) * TP],
                                kk == 0, kk == 15)
                    p.t("pe", i1)
            blk.tensor(f_mm)

            def f_ev(v, pe_b=pe_b):
                for mt in range(16):
                    ps = psA if mt % 2 == 0 else psB
                    p.w(v, "dve", [("pe", pe_b + mt + 1), ld0])
                    p.t("dve", v.tensor_add(xt(mt), xt(mt), ps[:]))
            blk.vector(f_ev)
        x1_done = p.tok("dve")


        ln2 = emit_ln(p, blk, "", xar, TP, xh, rows, psA, psB, psC,
                      a1bf[:, 0:16 * TP], bc_sb[:, 0:TP], bc_sb[:, TP:2 * TP],
                      ones_bf, ones_row, eps_c[:, :], [x1_done], scrT=scrT)

        def small_cast(src_ap, dst_bf, w):
            def f_l(sy):
                p.w(sy, "qs", [("act", p.cnt["act"])])
                p.t("qs", sy.dma_start(out=smalls32[:, 0:16 * w],
                                       in_=src_ap.rearrange("(a p) m -> p a m",
                                                            p=128)))
            blk.sync(f_l)
            lt = p.tok("qs")

            def f_c(sc, lt=lt):
                p.w(sc, "act", [lt])
                p.t("act", sc.copy(dst_bf[:, 0:16 * w], smalls32[:, 0:16 * w]))
            blk.scalar(f_c)

        small_cast(selW, selw_bf, 64)
        small_cast(mdT, mdt_bf, 8)
        small_cast(mdrep, mdr_bf, 64)
        small_cast(lupF, lup_bf, 64)
        small_cast(dbwT, dbwt_bf, 8)

        def f_lbw(sy):
            p.w(sy, "qs", [("act", p.cnt["act"])])
            p.t("qs", sy.dma_start(out=wslot[0:8, :], in_=biasw))
        blk.sync(f_lbw)
        t_ = p.tok("qs")

        def f_cbw(sc, t_=t_):
            p.w(sc, "act", [t_])
            p.t("act", sc.copy(bw_bf[:], wslot[0:8, :]))
        blk.scalar(f_cbw)

        def f_lldg(sy):
            p.w(sy, "qs", [("act", p.cnt["act"])])
            p.t("qs", sy.dma_start(out=wslot[0:64, :], in_=ldgF))
        blk.sync(f_lldg)
        t_ = p.tok("qs")

        def f_cldg(sc, t_=t_):
            p.w(sc, "act", [t_])
            p.t("act", sc.copy(ldg_bf[:], wslot[0:64, :]))
        blk.scalar(f_cldg)
        smalls_done = p.tok("act")

        def f_u64(te):
            p.w(te, "pe", [smalls_done] + ln2)
            for kt in range(16):
                i1 = MM(p, te, psD[:], selw_bf[:, kt * 64:(kt + 1) * 64],
                        xht(kt), kt == 0, kt == 15)
            p.t("pe", i1)
            for kt in range(16):
                i2 = MM(p, te, psB[0:8, :], mdt_bf[:, kt * 8:(kt + 1) * 8],
                        xht(kt), kt == 0, kt == 15)
            p.t("pe", i2)
        blk.tensor(f_u64)
        u64_done = p.tok("pe")

        def f_sq(sc):
            p.w(sc, "act", [u64_done] + ln2)
            for kt in range(16):
                p.t("act", sc.activation(a1bf[:, kt * TP:(kt + 1) * TP],
                                         xht(kt), AF.Square))
        blk.scalar(f_sq)
        sq_done = p.tok("act")

        def f_r2(te):
            p.w(te, "pe", [sq_done])
            for kt in range(16):
                i1 = MM(p, te, psA[0:1, :], ones_bf[:],
                        a1bf[:, kt * TP:(kt + 1) * TP], kt == 0, kt == 15)
            p.t("pe", i1)
        blk.tensor(f_r2)
        r2_done = p.tok("pe")

        def f_prod(v):
            p.w(v, "dve", [smalls_done])
            p.t("dve", v.tensor_mul(smalls32.bitcast(BF16)[:, 0:16 * 64],
                                    selw_bf[:], mdr_bf[:]))
        blk.vector(f_prod)
        prod_done = p.tok("dve")

        def f_mdsq(sc):
            p.w(sc, "act", [prod_done])
            p.t("act", sc.activation(smalls32.bitcast(BF16)[:, 16 * 64:16 * 72],
                                     mdt_bf[:], AF.Square))
        blk.scalar(f_mdsq)
        mdsq_done = p.tok("act")

        def f_c1(te):
            p.w(te, "pe", [prod_done, mdsq_done, r2_done])
            pbf = smalls32.bitcast(BF16)
            for kt in range(16):
                i1 = te.matmul(psC[0:64, 0:1], pbf[:, kt * 64:(kt + 1) * 64],
                               ones_b16[:], start=(kt == 0), stop=(kt == 15))
            p.t("pe", i1)
            for kt in range(16):
                i2 = te.matmul(psC[0:8, 4:5], pbf[:, 16 * 64 + kt * 8:
                                                  16 * 64 + (kt + 1) * 8],
                               ones_b16[:], start=(kt == 0), stop=(kt == 15))
            p.t("pe", i2)
        blk.tensor(f_c1)
        c1_done = p.tok("pe")

        def f_ev1(v):
            p.w(v, "dve", [c1_done])
            p.t("dve", v.tensor_scalar_mul(c1n_sb[:], psC[0:64, 0:1], -1.0))
            p.t("dve", v.tensor_copy(md2_sb[:], psC[0:8, 4:5]))
            p.t("dve", v.tensor_copy(rrow[:], psA[0:1, :]))
            p.t("dve", v.tensor_copy(xm8_sb[:], psB[0:8, :]))
        blk.vector(f_ev1)
        ev1_done = p.tok("dve")

        def f_r2b(te):
            p.w(te, "pe", [ev1_done])
            p.t("pe", MM(p, te, psB[0:8, :], ones_row[0:1, 0:8], rrow[:],
                         True, True))
        blk.tensor(f_r2b)
        r2b_done = p.tok("pe")

        def f_nrm(v):
            p.w(v, "dve", [r2b_done])
            p.t("dve", v.scalar_tensor_tensor(nrm_sb[:], xm8_sb[:], -2.0,
                                              psB[0:8, :], ALU.mult, ALU.add))
            p.w(v, "dve", [p.tok("dve")])
            p.t("dve", v.tensor_scalar_add(nrm_sb[:], nrm_sb[:],
                                           md2_sb[0:8, 0:1]))
            p.t("dve", v.tensor_mul(sbias_sb[:], slo_sb[:], cen_sb[:]))
            p.t("dve", v.tensor_scalar_mul(negs_sb[:], slo_sb[:], -1.0))
        blk.vector(f_nrm)
        nrm_done = p.tok("dve")

        def f_inv(sc):
            p.w(sc, "act", [nrm_done])
            p.t("act", sc.activation(inv8_sb[:], nrm_sb[:], AF.Ln))
            p.w(sc, "act", [p.tok("act")])
            p.t("act", sc.activation(inv8_sb[:], inv8_sb[:], AF.Exp, scale=-0.5))
        blk.scalar(f_inv)
        inv_done = p.tok("act")

        def f_invbc(te):
            p.w(te, "pe", [inv_done])
            p.t("pe", MM(p, te, psC[0:64, :], e8f[:], inv8_sb[:], True, True))
        blk.tensor(f_invbc)
        invbc_done = p.tok("pe")

        def f_cont1(sc):
            p.w(sc, "act", [invbc_done, ("dve", p.cnt["dve"])])
            p.t("act", sc.activation(t64_sb[:], psD[:], AF.Identity,
                                     bias=c1n_sb[0:64, 0:1]))
        blk.scalar(f_cont1)
        cont1 = p.tok("act")

        def f_cont2(v):
            p.w(v, "dve", [cont1])
            p.t("dve", v.tensor_mul(t64_sb[:], t64_sb[:], psC[0:64, :]))
        blk.vector(f_cont2)
        cont2 = p.tok("dve")

        def f_cont3(sc):
            p.w(sc, "act", [cont2])
            p.t("act", sc.activation(cont_bf[:], t64_sb[:], AF.Square))
        blk.scalar(f_cont3)
        cont3 = p.tok("act")

        def f_score(te):
            p.w(te, "pe", [cont3, ("dve", p.cnt["dve"])])
            p.t("pe", MM(p, te, psB[0:8, :], e8t[:], cont_bf[:], True, True))
        blk.tensor(f_score)
        score_done = p.tok("pe")

        def f_gate1(sc):
            p.w(sc, "act", [score_done, ("dve", p.cnt["dve"])])
            p.t("act", sc.activation(e_sb[:], psB[0:8, :], AF.Exp,
                                     bias=sbias_sb[0:8, 0:1],
                                     scale=negs_sb[0:8, 0:1]))
        blk.scalar(f_gate1)
        gate1 = p.tok("act")

        def f_gate2(v):
            p.w(v, "dve", [gate1])
            p.t("dve", v.tensor_scalar_add(e_sb[:], e_sb[:], 1.0))
            p.w(v, "dve", [p.tok("dve")])
            p.t("dve", v.reciprocal(e_sb[:], e_sb[:]))
        blk.vector(f_gate2)
        gate_done = p.tok("dve")

        for j in range(8):
            def f_tr(te, j=j):
                p.w(te, "pe", [gate_done, ("dve", p.cnt["dve"]),
                               ("act", p.cnt["act"])])
                p.t("pe", te.transpose(psA[:, 0:8],
                                       e_sb[0:8, j * 128:(j + 1) * 128],
                                       identf[0:8, 0:8]))
            blk.tensor(f_tr)
            trt = p.tok("pe")

            def f_sel(v, j=j, trt=trt):
                p.w(v, "dve", [trt])
                p.t("dve", v.reduce_max(mxc[:, j:j + 1], psA[:, 0:8], AX.X))
                p.w(v, "dve", [p.tok("dve")])
                p.t("dve", v.tensor_single_scalar(eq_sb[:], psA[:, 0:8],
                                                  mxc[:, j:j + 1], ALU.is_equal))
                p.w(v, "dve", [p.tok("dve")])
                p.t("dve", v.scalar_tensor_tensor(eq_sb[:], eq_sb[:], -9.0,
                                                  iota9[:], ALU.mult, ALU.add))
                p.w(v, "dve", [p.tok("dve")])
                p.t("dve", v.tensor_reduce(idxc[:], eq_sb[:], AX.X, ALU.min))
                p.w(v, "dve", [p.tok("dve")])
                p.t("dve", v.tensor_single_scalar(oneh_t[:], iotan[:],
                                                  idxc[:, 0:1], ALU.is_equal))
            blk.vector(f_sel)
            selt = p.tok("dve")

            def f_trb(te, j=j, selt=selt):
                p.w(te, "pe", [selt, ("act", p.cnt["act"])])
                p.t("pe", te.transpose(psD[0:8, 0:128], oneh_t[:], identf[:]))
                p.t("pe", te.transpose(psD[0:1, 128:256], mxc[:, j:j + 1],
                                       identf[:]))
            blk.tensor(f_trb)
            trb = p.tok("pe")

            def f_cp(sc, j=j, trb=trb):
                p.w(sc, "act", [trb])
                p.t("act", sc.copy(onehot8[:, j * 128:(j + 1) * 128],
                                   psD[0:8, 0:128]))
                p.t("act", sc.copy(ss_sb[:, j * 128:(j + 1) * 128],
                                   psD[0:1, 128:256]))
            blk.scalar(f_cp)
        oneh_done = p.tok("act")

        def f_xl(te):
            p.w(te, "pe", [oneh_done, ("dve", p.cnt["dve"])])
            for kt in range(16):
                i1 = MM(p, te, psC[0:64, :], lup_bf[:, kt * 64:(kt + 1) * 64],
                        xht(kt), kt == 0, kt == 15)
            p.t("pe", i1)
            for kt in range(16):
                i2 = te.matmul(psB[0:8, 0:64], dbwt_bf[:, kt * 8:(kt + 1) * 8],
                               lup_bf[:, kt * 64:(kt + 1) * 64],
                               start=(kt == 0), stop=(kt == 15))
            p.t("pe", i2)
        blk.tensor(f_xl)
        xl_done = p.tok("pe")

        def f_evxl(sc):
            p.w(sc, "act", [xl_done])
            p.t("act", sc.copy(t64_sb[:], psC[0:64, :]))
            p.t("act", sc.copy(k2b_bf[:], psB[0:8, 0:64]))
        blk.scalar(f_evxl)
        evxl = p.tok("act")

        def f_t2(te):
            p.w(te, "pe", [evxl])
            p.t("pe", MM(p, te, psD[:], k2b_bf[:], onehot8[:], True, True))
            p.t("pe", MM(p, te, psB[0:64, :], e8bf[:], onehot8[:], True, True))
            p.t("pe", MM(p, te, psC[:], ones_row[:], ss_sb[:], True, True))
        blk.tensor(f_t2)
        t2_done = p.tok("pe")

        def f_mx8(v):
            p.w(v, "dve", [t2_done])
            p.t("dve", v.tensor_sub(t64_sb[:], t64_sb[:], psD[:]))
            p.w(v, "dve", [p.tok("dve")])
            p.t("dve", v.tensor_mul(mx8_bf[:], t64_sb[:], psB[0:64, :]))
            p.t("dve", v.tensor_copy(ssbc_sb[:], psC[:]))
        blk.vector(f_mx8)
        mx8_done = p.tok("dve")

        for mt in range(16):
            ps = psA if mt % 2 == 0 else psB

            def f_bd(te, mt=mt, ps=ps):
                p.w(te, "pe", [mx8_done, ("dve", p.cnt["dve"])])
                MM(p, te, ps[:], bw_bf[:, mt * 128:(mt + 1) * 128], onehot8[:],
                   True, False)
                i1 = MM(p, te, ps[:], ldg_bf[:, mt * 128:(mt + 1) * 128],
                        mx8_bf[:], False, True)
                p.t("pe", i1)
            blk.tensor(f_bd)
            bd = p.tok("pe")

            def f_x2(v, mt=mt, ps=ps, bd=bd):
                p.w(v, "dve", [bd])
                p.t("dve", v.tensor_sub(d_sb[:], ps[:], xht(mt)))
                p.w(v, "dve", [p.tok("dve")])
                p.t("dve", v.tensor_mul(d_sb[:], d_sb[:], ssbc_sb[:]))
                p.w(v, "dve", [p.tok("dve")])
                p.t("dve", v.tensor_add(d_sb[:], d_sb[:], xht(mt)))
                p.w(v, "dve", [p.tok("dve")])
                p.t("dve", v.tensor_add(xt(mt), xt(mt), d_sb[:]))
            blk.vector(f_x2)
        x2_done = p.tok("dve")

        def f_stx2(sy):
            p.w(sy, "qs", [x2_done])
            for i in range(16):
                p.t("qs", sy.dma_start(out=x2To[i * 128:(i + 1) * 128, :],
                                       in_=xt(i)))
        blk.sync(f_stx2)

        ln3 = emit_ln(p, blk, "3", xar, TP, xh, rows, psA, psB, psC,
                      a1bf[:, 0:16 * TP], bc_sb[:, 0:TP], bc_sb[:, TP:2 * TP],
                      ones_bf, ones_row, eps_c[:, :],
                      [x2_done, ("qs", p.cnt["qs"])], scrT=scrT)

        war = xar.bitcast(F16)
        kvar = A1.bitcast(F16)

        def wload(w_ap):
            guard = [("qs", p.cnt["qs"]), ("dve", p.cnt["dve"]),
                     ("pe", p.cnt["pe"])]
            for kt in range(16):
                def f_l(sy, kt=kt):
                    p.w(sy, "qs", [("act", p.cnt["act"]), ("pe", p.cnt["pe"])])
                    p.t("qs", sy.dma_start(out=wslot[:],
                                           in_=w_ap[kt * 128:(kt + 1) * 128, :]))
                blk.sync(f_l)
                lt = p.tok("qs")

                def f_c(sc, kt=kt, lt=lt, guard=guard):
                    p.w(sc, "act", [lt] + guard)
                    p.t("act", sc.copy(war[:, kt * 2048:(kt + 1) * 2048],
                                       wslot[:]))
                blk.scalar(f_c)
            return p.tok("act")

        wq_done = wload(wq_i)
        for mt in range(16):
            ps = psA if mt % 2 == 0 else psB

            def f_qm(te, mt=mt, ps=ps, wq_done=wq_done):
                p.w(te, "pe", [wq_done] + ln3 + [("act", p.cnt["act"])])
                for kt in range(16):
                    i1 = MM(p, te, ps[:], war[:, kt * 2048 + mt * 128:
                                              kt * 2048 + (mt + 1) * 128],
                            xht(kt), kt == 0, kt == 15)
                p.t("pe", i1)
            blk.tensor(f_qm)
            qm = p.tok("pe")

            def f_qe(sc, mt=mt, ps=ps, qm=qm):
                p.w(sc, "act", [qm, ("qs", p.cnt["qs"])])
                p.t("act", sc.activation(qkv_ev[:, 0:TP], ps[:], AF.Relu))
            blk.scalar(f_qe)
            qe = p.tok("act")

            def f_qs(sy, mt=mt, qe=qe):
                p.w(sy, "qs", [qe])
                p.t("qs", sy.dma_start(out=qO[mt * 128:(mt + 1) * 128, :],
                                       in_=qkv_ev[:, 0:TP]))
            blk.sync(f_qs)

        for which, w_ap, act in [(0, wk_i, AF.Relu), (1, wv_i, AF.Identity)]:
            wdone = wload(w_ap)
            for ms in range(8):
                for nh in range(2):
                    ps = psA if (ms * 2 + nh) % 2 == 0 else psB

                    def f_km(te, ms=ms, nh=nh, ps=ps, wdone=wdone):
                        p.w(te, "pe", [wdone, ("act", p.cnt["act"])] + ln3)
                        for kt in range(16):
                            i1 = MM(p, te, ps[:],
                                    xh[:, kt * TP + ms * 128:
                                       kt * TP + (ms + 1) * 128],
                                    war[:, kt * 2048 + nh * 1024:
                                        kt * 2048 + (nh + 1) * 1024],
                                    kt == 0, kt == 15)
                        p.t("pe", i1)
                    blk.tensor(f_km)
                    km = p.tok("pe")

                    def f_ke(sc, which=which, ms=ms, nh=nh, ps=ps, km=km,
                             act=act):
                        p.w(sc, "act", [km])
                        dst = kvar[:, which * 16 * TP + ms * 2048 + nh * 1024:
                                   which * 16 * TP + ms * 2048 +
                                   (nh + 1) * 1024]
                        p.t("act", sc.activation(dst, ps[:], act))
                    blk.scalar(f_ke)
        kv_in_done = p.tok("act")

        def kcol(which, ms, inner):
            return which * 16 * TP + ms * 2048 + (inner // 1024) * 1024 + \
                (inner % 1024)

        for w in range(16):
            def f_kv(te, w=w):
                p.w(te, "pe", [kv_in_done, ("dve", p.cnt["dve"])])
                for h2 in range(2):
                    h = 2 * w + h2
                    for ms in range(8):
                        c0 = kcol(0, ms, h * 64)
                        c1 = kcol(1, ms, h * 64)
                        te.matmul(psC[h2 * 64:h2 * 64 + 64, 0:64],
                                  kvar[:, c0:c0 + 64], kvar[:, c1:c1 + 64],
                                  start=(ms == 0), stop=(ms == 7))
                for ms in range(8):
                    c0 = kcol(0, ms, w * 128)
                    i2 = te.matmul(psC[0:128, 64:65], kvar[:, c0:c0 + 128],
                                   ones_bf[:], start=(ms == 0), stop=(ms == 7))
                p.t("pe", i2)
            blk.tensor(f_kv)
            kvt = p.tok("pe")

            def f_kve(v, kvt=kvt):
                p.w(v, "dve", [kvt, ("qs", p.cnt["qs"])])
                p.t("dve", v.tensor_copy(kvp_sb[:], psC[:, 0:65]))
            blk.vector(f_kve)
            kve = p.tok("dve")

            def f_kvs(sy, w=w, kve=kve):
                p.w(sy, "qs", [kve])
                p.t("qs", sy.dma_start(out=kvpO[w * 128:(w + 1) * 128, :],
                                       in_=kvp_sb[:]))
            blk.sync(f_kvs)

        def f_fin(sy):
            sy.wait_ge(p.sem["qs"], p.cnt["qs"])
        blk.sync(f_fin)



def emit_post2(nc, p, blk, IO):
    TP = TPC
    x2T = IO["x2T"]
    qI = IO["qO"]
    kvs = IO["kvs"]
    wo_i = IO["wo"]
    bo_i = IO["bo_c"]
    w1q_i = IO["w1qf"]
    w2q_i = IO["w2qf"]
    wtot = IO["wtot"]
    ones_bf_i = IO["ones_h"]
    ones_row_i = IO["ones_row"]
    eps_i = IO["eps_c"]
    identf_i = IO["identf"]
    ehw_i = IO["ehw"]
    fhw_i = IO["fhw"]
    outT = IO["outT"]
    x3d = IO["x3d"]

    with contextlib.ExitStack() as st_:
        E = st_.enter_context
        ones_bf = E(nc.sbuf_tensor("c_ones_h", [128, 1], F16))
        ones_row = E(nc.sbuf_tensor("c_ones_row", [1, 128], F32))
        eps_c = E(nc.sbuf_tensor("c_eps", [128, 1], F32))
        identf = E(nc.sbuf_tensor("c_identf", [128, 128], F32))
        ehw = E(nc.sbuf_tensor("c_ehw", [128, 16 * 32], F32))

        bo_sb = E(nc.sbuf_tensor("bo_sb", [128, 16], F32))
        wab = E(nc.sbuf_tensor("wab", [1, 2], F32))
        H1 = E(nc.sbuf_tensor("H1", [128, 16 * TP], F32))
        H2 = E(nc.sbuf_tensor("H2", [128, 16 * TP], F32))
        q_sb = E(nc.sbuf_tensor("q_sb", [128, 16 * TP], F16))

        qs_sb = E(nc.sbuf_tensor("qs_sb", [128, TP], F32))
        evslot = E(nc.sbuf_tensor("evslot", [128, TP], F32))
        x2slot = E(nc.sbuf_tensor("x2slot", [128, TP], F32))
        rowblk = E(nc.sbuf_tensor("rowblk", [1, 6 * TP], F32))
        srows = E(nc.sbuf_tensor("srows", [1, 8], F32))
        rs1bc = E(nc.sbuf_tensor("rs1bc", [128, TP], F32))
        schbc = E(nc.sbuf_tensor("schbc", [128, TP], F32))

        psA = E(nc.psum_tensor("psA", [128, TP], F32))
        psB = E(nc.psum_tensor("psB", [128, TP], F32))
        psC = E(nc.psum_tensor("psC", [128, TP], F32))
        psD = E(nc.psum_tensor("psD", [128, TP], F32))

        woar = H1.bitcast(F16)
        wslab = ehw.bitcast(BF16)
        nrm_sb = rs1bc[0:32, :]
        rs2bc = qs_sb
        wtile = ehw.bitcast(BF16)[:, 0:4 * 128]
        kv_sb = H1[:, 0:16 * 65]
        kvt_sb = H1[:, 16 * 65:32 * 65]
        kvbf = H1.bitcast(F16)[:, 64 * 65:80 * 65]
        fhw = H2[0:32, 8 * TP:8 * TP + 16 * 128]
        qkvn = H2.bitcast(F16)
        hf16_lo = H2.bitcast(F16)
        hf16_hi = H1.bitcast(F16)
        rows = rowblk[0:1, 0:4 * TP]
        amaxr = rowblk[0:1, 4 * TP:5 * TP]
        sclr = rowblk[0:1, 5 * TP:6 * TP]

        def qt(i):
            return q_sb[:, i * TP:(i + 1) * TP]

        def f_ld(sy):
            for sb, ap in [(ones_bf, ones_bf_i), (ones_row, ones_row_i),
                           (eps_c, eps_i), (identf, identf_i), (ehw, ehw_i),
                           (bo_sb, bo_i)]:
                p.t("qs", sy.dma_start(out=sb[:], in_=ap))
            p.t("qs", sy.dma_start(out=wab[:, 0:1], in_=wtot[0:1, :]))
            p.t("qs", sy.dma_start(out=wab[:, 1:2], in_=wtot[1:2, :]))
            for i in range(16):
                p.t("qs", sy.dma_start(out=qt(i), in_=qI[i * 128:(i + 1) * 128, :]))
            p.t("qs", sy.dma_start(out=fhw[:], in_=fhw_i))
            p.t("qs", sy.dma_start(out=kv_sb[:], in_=kvs.rearrange(
                "(a p) m -> p a m", p=128)))
        blk.sync(f_ld)
        ld0 = p.tok("qs")
        kvsum = ld0

        def f_ckv(sc):
            p.w(sc, "act", [kvsum])
            p.t("act", sc.copy(kvbf[:], kv_sb[:]))
        blk.scalar(f_ckv)
        kvc = p.tok("act")

        for w in range(16):
            def f_qs(v, w=w):
                p.w(v, "dve", [kvsum, ld0, ("pe", p.cnt["pe"])])
                p.t("dve", v.tensor_scalar_mul(qs_sb[:], qt(w),
                                               kv_sb[:, w * 65 + 64:w * 65 + 65]))
            blk.vector(f_qs)
            qst = p.tok("dve")

            def f_nm(te, w=w, qst=qst):
                p.w(te, "pe", [qst])
                i1 = MM(p, te, psB[0:32, :], ehw[:, w * 32:(w + 1) * 32],
                        qs_sb[:], w == 0, w == 15)
                p.t("pe", i1)
            blk.tensor(f_nm)
        nm_done = p.tok("pe")

        def f_rn(v):
            p.w(v, "dve", [nm_done])
            p.t("dve", v.tensor_single_scalar(nrm_sb[:], psB[0:32, :], 1e-6,
                                              ALU.max))
            p.w(v, "dve", [p.tok("dve")])
            p.t("dve", v.reciprocal(nrm_sb[:], nrm_sb[:]))
        blk.vector(f_rn)
        rn_done = p.tok("dve")

        for w in range(16):
            def f_qkv(te, w=w):
                p.w(te, "pe", [kvc, rn_done, ("act", p.cnt["act"]),
                               ("dve", p.cnt["dve"])])
                MM(p, te, psA[0:64, :], kvbf[0:64, w * 65:w * 65 + 64],
                   q_sb[0:64, w * TP:(w + 1) * TP], True, True)
                i1 = MM(p, te, psA[64:128, :],
                        kvbf[64:128, w * 65:w * 65 + 64],
                        q_sb[64:128, w * TP:(w + 1) * TP], True, True)
                p.t("pe", i1)
                i2 = MM(p, te, psC[:], fhw[:, w * 128:(w + 1) * 128], nrm_sb[:],
                        True, True)
                p.t("pe", i2)
            blk.tensor(f_qkv)
            qkvt = p.tok("pe")

            def f_qe(sc, w=w, qkvt=qkvt):
                p.w(sc, "act", [qkvt])
                p.t("act", sc.copy(evslot[:], psA[:]))
            blk.scalar(f_qe)
            qev = p.tok("act")

            def f_qn(v, w=w, qev=qev):
                p.w(v, "dve", [qev])
                p.t("dve", v.tensor_mul(qkvn[:, w * TP:(w + 1) * TP],
                                        evslot[:], psC[:]))
            blk.vector(f_qn)
        qkvn_done = p.tok("dve")

        def wload16(w_ap, dst, slot32):
            for kt in range(16):
                def f_l(sy, kt=kt):
                    p.w(sy, "qs", [("act", p.cnt["act"]), ("pe", p.cnt["pe"]),
                                   ("dve", p.cnt["dve"])])
                    p.t("qs", sy.dma_start(out=slot32,
                                           in_=w_ap[kt * 128:(kt + 1) * 128, :]))
                blk.sync(f_l)
                lt = p.tok("qs")

                def f_c(sc, kt=kt, lt=lt):
                    p.w(sc, "act", [lt])
                    p.t("act", sc.copy(dst[:, kt * 2048:(kt + 1) * 2048],
                                       slot32))
                blk.scalar(f_c)
            return p.tok("act")

        wslot2 = H2[:, 12 * TP:14 * TP]
        wo_done = wload16(wo_i, woar, wslot2)

        for mt in range(16):
            ps = psA if mt % 2 == 0 else psB

            def f_lx2(sy, mt=mt):
                p.w(sy, "qs", [("dve", p.cnt["dve"]), ("qs", p.cnt["qs"])])
                p.t("qs", sy.dma_start(out=x2slot[:],
                                       in_=x2T[mt * 128:(mt + 1) * 128, :]))
            blk.sync(f_lx2)
            lx2 = p.tok("qs")

            def f_wo(te, mt=mt, ps=ps):
                p.w(te, "pe", [wo_done, qkvn_done, ("dve", p.cnt["dve"])])
                for kt in range(16):
                    i1 = MM(p, te, ps[:], woar[:, kt * 2048 + mt * 128:
                                               kt * 2048 + (mt + 1) * 128],
                            qkvn[:, kt * TP:(kt + 1) * TP], kt == 0, kt == 15)
                p.t("pe", i1)
            blk.tensor(f_wo)
            wot = p.tok("pe")

            def f_x3(v, mt=mt, ps=ps, wot=wot, lx2=lx2):
                p.w(v, "dve", [wot, lx2])
                p.t("dve", v.scalar_tensor_tensor(x2slot[:], ps[:],
                                                  bo_sb[:, mt:mt + 1],
                                                  x2slot[:], ALU.add, ALU.add))
            blk.vector(f_x3)
            x3t = p.tok("dve")

            def f_sx3(sy, mt=mt, x3t=x3t):
                p.w(sy, "qs", [x3t])
                p.t("qs", sy.dma_start(out=x3d[mt * 128:(mt + 1) * 128, :],
                                       in_=x2slot[:]))
            blk.sync(f_sx3)
        x3_done = p.tok("qs")

        def f_lx3(sy):
            p.w(sy, "qs", [x3_done, ("pe", p.cnt["pe"]),
                           ("dve", p.cnt["dve"]), ("act", p.cnt["act"])])
            for i in range(16):
                p.t("qs", sy.dma_start(out=H2[:, i * TP:(i + 1) * TP],
                                       in_=x3d[i * 128:(i + 1) * 128, :]))
        blk.sync(f_lx3)
        x3l = p.tok("qs")

        ln4 = emit_ln(p, blk, "4", H2, TP, q_sb, rows, psA, psB, psC,
                      H1.bitcast(F16)[:, 0:16 * TP], rs1bc[:, 0:TP],
                      schbc[:, 0:TP], ones_bf, ones_row, eps_c[:, :],
                      [x3l], xh32_sb=H1, scrT=evslot[:, 0:TP])

        def f_amax1(v):
            p.w(v, "dve", ln4)
            p.t("dve", v.tensor_reduce(
                evslot[:], H1[:].rearrange("p (a t) -> p t a", a=16),
                AX.X, ALU.max, apply_absolute_value=True))
        blk.vector(f_amax1)
        am1 = p.tok("dve")

        for j in range(8):
            def f_amt(te, j=j, am1=am1):
                p.w(te, "pe", [am1, ("dve", p.cnt["dve"]),
                               ("act", p.cnt["act"])])
                p.t("pe", te.transpose(psA[:, 0:128],
                                       evslot[:, j * 128:(j + 1) * 128],
                                       identf[:]))
            blk.tensor(f_amt)
            amt = p.tok("pe")

            def f_amr(v, j=j, amt=amt):
                p.w(v, "dve", [amt])
                p.t("dve", v.reduce_max(qs_sb[:, j:j + 1], psA[:, 0:128], AX.X))
            blk.vector(f_amr)
            amr = p.tok("dve")

            def f_amt2(te, j=j, amr=amr):
                p.w(te, "pe", [amr])
                p.t("pe", te.transpose(psB[0:1, 0:128], qs_sb[:, j:j + 1],
                                       identf[:]))
            blk.tensor(f_amt2)
            amt2 = p.tok("pe")

            def f_amc(sc, j=j, amt2=amt2):
                p.w(sc, "act", [amt2])
                p.t("act", sc.copy(amaxr[0:1, j * 128:(j + 1) * 128],
                                   psB[0:1, 0:128]))
            blk.scalar(f_amc)
        amax_done = p.tok("act")

        def f_srow(v):
            p.w(v, "dve", [amax_done, ld0])
            p.t("dve", v.tensor_scalar_mul(srows[0:1, 0:2], wab[0:1, 0:2],
                                           1.0 / (DFF * D)))
            p.w(v, "dve", [p.tok("dve")])
            p.t("dve", v.tensor_single_scalar(amaxr[:], amaxr[:], 1e-5,
                                              ALU.max))
            p.w(v, "dve", [p.tok("dve")])
            p.t("dve", v.reciprocal(sclr[:], amaxr[:]))
            p.w(v, "dve", [p.tok("dve")])
            p.t("dve", v.tensor_scalar_mul(sclr[:], sclr[:], 127.0))
            p.t("dve", v.tensor_scalar_mul(rows[0:1, 0:TP], amaxr[:],
                                           1.0 / 127.0))
            p.w(v, "dve", [p.tok("dve")])
            p.t("dve", v.tensor_scalar_mul(rows[0:1, 0:TP], rows[0:1, 0:TP],
                                           srows[0:1, 0:1]))
        blk.vector(f_srow)
        srow_done = p.tok("dve")

        def f_bc1(te):
            p.w(te, "pe", [srow_done])
            p.t("pe", MM(p, te, psC[:], ones_row[:], sclr[:], True, True))
        blk.tensor(f_bc1)
        b1 = p.tok("pe")

        def f_bc1e(sc):
            p.w(sc, "act", [b1])
            p.t("act", sc.copy(schbc[:], psC[:]))
        blk.scalar(f_bc1e)

        def f_bc2(te):
            p.w(te, "pe", [("act", p.cnt["act"])])
            p.t("pe", MM(p, te, psC[:], ones_row[:], rows[0:1, 0:TP], True, True))
        blk.tensor(f_bc2)
        b2 = p.tok("pe")

        def f_bc2e(sc):
            p.w(sc, "act", [b2])
            p.t("act", sc.copy(rs1bc[:], psC[:]))
        blk.scalar(f_bc2e)
        bc_done = p.tok("act")

        for i in range(16):
            def f_xq(v, i=i):
                p.w(v, "dve", [bc_done])
                p.t("dve", v.tensor_mul(evslot[:], H1[:, i * TP:(i + 1) * TP],
                                        schbc[:]))
                p.w(v, "dve", [p.tok("dve")])
                p.t("dve", v.tensor_scalar(q_sb.bitcast(BF16)[:, i * TP:(i + 1) * TP],
                                           evslot[:], MAGIC, MAGIC,
                                           ALU.add, ALU.subtract))
            blk.vector(f_xq)
        xq_done = p.tok("dve")

        def ps_slot(s):
            t = [psA, psB, psC, psD][s // 2]
            o = (s % 2) * 512
            return t[:, o:o + 512]

        qbf = q_sb.bitcast(BF16)

        def hcol(mt2, h):
            if mt2 < 32:
                return hf16_lo[:, mt2 * TP + h * 512:mt2 * TP + h * 512 + 512]
            return hf16_hi[:, (mt2 - 32) * TP + h * 512:
                           (mt2 - 32) * TP + h * 512 + 512]

        ev_last1 = 0
        for mg in range(16):
            pe_c1 = p.cnt["pe"]
            st1 = {}

            def f_lsl(sy, mg=mg, pe_c1=pe_c1):
                for kt in range(16):
                    if kt == 0 and mg == 0:
                        p.w(sy, "qs", [("pe", pe_c1)])
                    p.w(sy, "qs", [("pe", pe_c1 + kt - 1)])
                    p.t("qs", sy.dma_start(
                        out=wslab[:, (kt % 2) * 512:(kt % 2 + 1) * 512],
                        in_=w1q_i[kt * 128:(kt + 1) * 128,
                                  mg * 512:(mg + 1) * 512]))
                    st1[kt] = p.tok("qs")
            blk.sync(f_lsl)

            def f_mm1(te, mg=mg, ev_last1=ev_last1):
                if mg == 0:
                    p.w(te, "pe", [xq_done, ("act", p.cnt["act"]),
                                   ("dve", p.cnt["dve"])])
                for kt in range(16):
                    p.w(te, "pe", [st1[kt]])
                    if kt == 0 and mg > 0:
                        p.w(te, "pe", [("dve", ev_last1)])
                    for s in range(8):
                        j, h = divmod(s, 2)
                        i1 = te.matmul(
                            ps_slot(s),
                            wslab[:, (kt % 2) * 512 + j * 128:
                                  (kt % 2) * 512 + (j + 1) * 128],
                            qbf[:, kt * TP + h * 512:kt * TP + h * 512 + 512],
                            start=(kt == 0), stop=(kt == 15))
                    p.t("pe", i1)
            blk.tensor(f_mm1)
            w1done = p.tok("pe")

            v2 = p.cnt["dve"]
            a2 = p.cnt["act"]

            def f_ev1(v, mg=mg, w1done=w1done, v2=v2, a2=a2):
                for s in range(8):
                    j, h = divmod(s, 2)
                    p.w(v, "dve", [w1done, bc_done, p.tok("dve")])
                    if s >= 2:
                        p.w(v, "dve", [("act", a2 + s - 1)])
                    p.t("dve", v.tensor_mul(
                        evslot[:, (s % 2) * 512:(s % 2 + 1) * 512],
                        ps_slot(s), rs1bc[:, h * 512:h * 512 + 512]))
            blk.vector(f_ev1)
            ev_last1 = p.cnt["dve"]

            def f_sil1(sc, mg=mg, v2=v2):
                for s in range(8):
                    j, h = divmod(s, 2)
                    mt2 = mg * 4 + j
                    p.w(sc, "act", [("dve", v2 + s + 1)])
                    p.t("act", sc.activation(
                        hcol(mt2, h), evslot[:, (s % 2) * 512:(s % 2 + 1) * 512],
                        AF.Silu))
            blk.scalar(f_sil1)
        h_done = p.tok("act")

        hm16 = qs_sb.bitcast(F16)

        def f_ham(v):
            p.w(v, "dve", [h_done])
            p.t("dve", v.tensor_reduce(
                evslot[:], hf16_lo[:, 0:32 * TP].rearrange(
                    "p (a t) -> p t a", a=32),
                AX.X, ALU.max, apply_absolute_value=True))
            p.t("dve", v.tensor_reduce(
                x2slot[:], hf16_hi[:, 0:32 * TP].rearrange(
                    "p (a t) -> p t a", a=32),
                AX.X, ALU.max, apply_absolute_value=True))
            p.w(v, "dve", [p.tok("dve")])
            p.t("dve", v.tensor_max(evslot[:], evslot[:], x2slot[:]))
        blk.vector(f_ham)
        ham = p.tok("dve")

        ham32 = ham

        for j in range(8):
            def f_hmt(te, j=j, ham32=ham32):
                p.w(te, "pe", [ham32, ("dve", p.cnt["dve"]),
                               ("act", p.cnt["act"])])
                p.t("pe", te.transpose(psA[:, 0:128],
                                       evslot[:, j * 128:(j + 1) * 128],
                                       identf[:]))
            blk.tensor(f_hmt)
            hmt = p.tok("pe")

            def f_hmr(v, j=j, hmt=hmt):
                p.w(v, "dve", [hmt])
                p.t("dve", v.reduce_max(qs_sb[:, 8 + j:9 + j], psA[:, 0:128],
                                        AX.X))
            blk.vector(f_hmr)
            hmr = p.tok("dve")

            def f_hmt2(te, j=j, hmr=hmr):
                p.w(te, "pe", [hmr])
                p.t("pe", te.transpose(psB[0:1, 0:128], qs_sb[:, 8 + j:9 + j],
                                       identf[:]))
            blk.tensor(f_hmt2)
            hmt2 = p.tok("pe")

            def f_hmc(sc, j=j, hmt2=hmt2):
                p.w(sc, "act", [hmt2])
                p.t("act", sc.copy(amaxr[0:1, j * 128:(j + 1) * 128],
                                   psB[0:1, 0:128]))
            blk.scalar(f_hmc)
        hamax_done = p.tok("act")

        def f_srow2(v):
            p.w(v, "dve", [hamax_done])
            p.t("dve", v.tensor_single_scalar(amaxr[:], amaxr[:], 1e-5,
                                              ALU.max))
            p.w(v, "dve", [p.tok("dve")])
            p.t("dve", v.reciprocal(sclr[:], amaxr[:]))
            p.w(v, "dve", [p.tok("dve")])
            p.t("dve", v.tensor_scalar_mul(sclr[:], sclr[:], 127.0))
            p.t("dve", v.tensor_scalar_mul(rows[0:1, 0:TP], amaxr[:],
                                           1.0 / 127.0))
            p.w(v, "dve", [p.tok("dve")])
            p.t("dve", v.tensor_scalar_mul(rows[0:1, 0:TP], rows[0:1, 0:TP],
                                           srows[0:1, 1:2]))
        blk.vector(f_srow2)
        srow2 = p.tok("dve")

        def f_bch(te):
            p.w(te, "pe", [srow2])
            p.t("pe", MM(p, te, psC[:], ones_row[:], sclr[:], True, True))
        blk.tensor(f_bch)
        bh1 = p.tok("pe")

        def f_bche(sc):
            p.w(sc, "act", [bh1])
            p.t("act", sc.copy(schbc[:], psC[:]))
        blk.scalar(f_bche)
        bch1 = p.tok("act")

        def f_bch2(te):
            p.w(te, "pe", [bch1])
            p.t("pe", MM(p, te, psC[:], ones_row[:], rows[0:1, 0:TP], True, True))
        blk.tensor(f_bch2)
        bh2 = p.tok("pe")

        def f_bch2e(sc):
            p.w(sc, "act", [bh2])
            p.t("act", sc.copy(rs2bc[:], psC[:]))
        blk.scalar(f_bch2e)
        bch_done = p.tok("act")

        def htile(k):
            if k < 32:
                return hf16_lo[:, k * TP:(k + 1) * TP]
            return hf16_hi[:, (k - 32) * TP:(k - 31) * TP]

        for k in range(64):
            def f_hq(v, k=k):
                p.w(v, "dve", [bch_done, p.tok("dve")])
                p.t("dve", v.tensor_mul(evslot[:], htile(k), schbc[:]))
                p.w(v, "dve", [p.tok("dve")])
                dst = (hf16_lo if k < 32 else hf16_hi).bitcast(BF16)
                kk = k if k < 32 else k - 32
                p.t("dve", v.tensor_scalar(dst[:, kk * TP:(kk + 1) * TP],
                                           evslot[:], MAGIC, MAGIC, ALU.add,
                                           ALU.subtract))
            blk.vector(f_hq)
        hq_done = p.tok("dve")

        def hqtile(k):
            if k < 32:
                return hf16_lo.bitcast(BF16)[:, k * TP:(k + 1) * TP]
            return hf16_hi.bitcast(BF16)[:, (k - 32) * TP:(k - 31) * TP]

        def hqcol(kt, h):
            if kt < 32:
                return hf16_lo.bitcast(BF16)[
                    :, kt * TP + h * 512:kt * TP + h * 512 + 512]
            return hf16_hi.bitcast(BF16)[
                :, (kt - 32) * TP + h * 512:(kt - 32) * TP + h * 512 + 512]

        ev_last2 = 0
        for mg in range(4):
            pe_c2 = p.cnt["pe"]
            st2 = {}

            def f_lsl2(sy, mg=mg, pe_c2=pe_c2):
                for kt in range(64):
                    p.w(sy, "qs", [("pe", pe_c2 + kt - 1)])
                    p.t("qs", sy.dma_start(
                        out=wslab[:, (kt % 2) * 512:(kt % 2 + 1) * 512],
                        in_=w2q_i[kt * 128:(kt + 1) * 128,
                                  mg * 512:(mg + 1) * 512]))
                    st2[kt] = p.tok("qs")
            blk.sync(f_lsl2)

            def f_mm2(te, mg=mg, ev_last2=ev_last2):
                if mg == 0:
                    p.w(te, "pe", [hq_done, ("act", p.cnt["act"]),
                                   ("dve", p.cnt["dve"])])
                for kt in range(64):
                    p.w(te, "pe", [st2[kt]])
                    if kt == 0 and mg > 0:
                        p.w(te, "pe", [("dve", ev_last2)])
                    for s in range(8):
                        j, h = divmod(s, 2)
                        i1 = te.matmul(
                            ps_slot(s),
                            wslab[:, (kt % 2) * 512 + j * 128:
                                  (kt % 2) * 512 + (j + 1) * 128],
                            hqcol(kt, h),
                            start=(kt == 0), stop=(kt == 63))
                    p.t("pe", i1)
            blk.tensor(f_mm2)
            w2done = p.tok("pe")

            for j in range(4):
                mt = mg * 4 + j

                def f_lx3b(sy, mt=mt):
                    p.w(sy, "qs", [("dve", p.cnt["dve"]), ("qs", p.cnt["qs"])])
                    p.t("qs", sy.dma_start(out=x2slot[:],
                                           in_=x3d[mt * 128:(mt + 1) * 128, :]))
                blk.sync(f_lx3b)
                lx3 = p.tok("qs")

                def f_out(v, j=j, w2done=w2done, lx3=lx3):
                    for h in range(2):
                        p.w(v, "dve", [w2done, lx3, bch_done, p.tok("dve")])
                        p.t("dve", v.tensor_mul(
                            evslot[:, h * 512:h * 512 + 512],
                            ps_slot(j * 2 + h), rs2bc[:, h * 512:h * 512 + 512]))
                        p.w(v, "dve", [p.tok("dve")])
                        p.t("dve", v.tensor_add(
                            x2slot[:, h * 512:h * 512 + 512],
                            x2slot[:, h * 512:h * 512 + 512],
                            evslot[:, h * 512:h * 512 + 512]))
                blk.vector(f_out)
                outt = p.tok("dve")
                if j == 3:
                    ev_last2 = p.cnt["dve"]

                def f_so(sy, mt=mt, outt=outt):
                    p.w(sy, "qs", [outt])
                    p.t("qs", sy.dma_start(out=outT[mt * 128:(mt + 1) * 128, :],
                                           in_=x2slot[:]))
                blk.sync(f_so)

        def f_fin(sy):
            sy.wait_ge(p.sem["qs"], p.cnt["qs"])
        blk.sync(f_fin)



from concourse.aot_env import allow_ici_all_to_all

G_BATCH = [[0, 1, 2, 3], [4, 5, 6, 7]]
G_ALL = [[0, 1, 2, 3, 4, 5, 6, 7]]


def _sbar(p, blk, tag):
    toks = p.bar()
    for name, opener in (("Esy", blk.sync), ("Egp", blk.gpsimd),
                         ("Edv", blk.vector), ("Eac", blk.scalar),
                         ("Ete", blk.tensor)):
        def f(eng, name=name, toks=toks):
            p.w(eng, name + tag, toks)
        opener(f)


class _NcU:

    def __init__(self, nc, tag):
        self._nc = nc
        self._tag = tag

    def __getattr__(self, a):
        return getattr(self._nc, a)

    def sbuf_tensor(self, name, *a, **k):
        return self._nc.sbuf_tensor(self._tag + name, *a, **k)

    def psum_tensor(self, name, *a, **k):
        return self._nc.psum_tensor(self._tag + name, *a, **k)


def build_fused():
    nc = bass.Bass("TRN2", target_bir_lowering=False, debug=False)
    D = {}
    def ein(n, s, dt):
        D[n] = _ein(nc, n, s, dt)
    ein("xT", [2048, TPC], F32)
    ein("w1T", [256, DFF], F32)
    ein("w2T", [1024, 2048], F32)
    ein("ones_h", [128, 1], F16)
    ein("ones_bf", [128, 1], BF16)
    ein("ones_row", [1, 128], F32)
    ein("ones_f", [128, 1], F32)
    ein("eps_c", [128, 1], F32)
    ein("identf", [128, 128], F32)
    ein("winT", [2048, 1024], F32)
    ein("convw", [128, 16], F32)
    ein("convb", [128, 4], F32)
    ein("wxT", [128, 4 * 160], F32)
    ein("wdtT", [128, 512], F32)
    ein("dtb", [128, 4], F32)
    ein("acol", [128, 64], F32)
    ein("dcol", [128, 4], F32)
    ein("woutT", [DIN, 2048], F32)
    ein("selW", [2048, 64], F32)
    ein("mdT", [2048, 8], F32)
    ein("mdrep", [2048, 64], F32)
    ein("lupF", [2048, 64], F32)
    ein("ldgF", [64, 2048], F32)
    ein("biasw", [8, 2048], F32)
    ein("dbwT", [2048, 8], F32)
    ein("slope_c", [8, 1], F32)
    ein("center_c", [8, 1], F32)
    ein("wq", [2048, INNER], F32)
    ein("wk", [2048, INNER], F32)
    ein("wv", [2048, INNER], F32)
    ein("e8h", [8, 64], F16)
    ein("e8f", [8, 64], F32)
    ein("e8t", [64, 8], BF16)
    ein("iota9", [128, 8], F32)
    ein("iotan", [128, 8], F32)
    ein("wo", [INNER, 2048], F32)
    ein("bo_c", [128, 16], F32)
    ein("ehw", [128, 16 * 32], F32)
    ein("fhw", [32, 16 * 128], F32)
    def itn(n, s, dt):
        D[n] = nc.dram_tensor("i_" + n, s, dt).ap()
    itn("xhT", [2048, TPC], F16)
    itn("xhTb", [8 * 2048, TPC], F16)
    itn("wabs", [2, 1], F32)
    itn("wtot", [2, 1], F32)
    itn("u16", [512, NTOK], F16)
    itn("zs16", [512, NTOK], F16)
    itn("ssmp", [160, NTOK], F32)
    itn("ssms", [160, NTOK], F32)
    itn("ssbc", [32, NTOK], F16)
    itn("w1q", [256, DFF], BF16)
    itn("w2q", [1024, 2048], BF16)
    itn("w1qf", [2048, DFF], BF16)
    itn("w2qf", [DFF, 2048], BF16)
    itn("ysz_a2a", [DIN, TPC], F16)
    itn("yszg", [DIN, TPC], F16)
    itn("x2T", [2048, TPC], F32)
    itn("qO", [INNER, TPC], F16)
    itn("kvp", [INNER, 65], F32)
    itn("kvs", [INNER, 65], F32)
    itn("x3d", [2048, TPC], F32)
    D["outT"] = _eout(nc, "outT", [2048, TPC], F32)

    with contextlib.ExitStack() as outer:
        E = outer.enter_context
        p = Px(nc, E)
        ccsems = [E(nc.semaphore("ccs%d" % k)) for k in range(8)]
        blk = E(nc.Block())

        def cc(k, kind, op, groups, in_ap, out_ap, wait=True):
            def f(g):
                p.w(g, "Egp_cc%d" % k, p.bar())
                with allow_ici_all_to_all():
                    inst = g.collective_compute(
                        kind, op, replica_groups=groups,
                        ins=[in_ap.opt()], outs=[out_ap.opt()])
                inst.then_inc(ccsems[k])
                if wait:
                    g.wait_ge(ccsems[k], 1)
                    p.t("gpc", g.nop(), 1)
            blk.gpsimd(f)

        emit_ln1(_NcU(nc, 's0_'), p, blk, D)
        _sbar(p, blk, "c1")
        cc(0, "AllGather", ALU.bypass, G_ALL, D["xhT"], D["xhTb"])
        cc(1, "AllReduce", ALU.add, G_ALL, D["wabs"], D["wtot"])
        _sbar(p, blk, "c2")
        emit_front(_NcU(nc, 's1_'), p, blk, D)
        _sbar(p, blk, "c3")
        cc(2, "AllReduce", ALU.add, G_ALL, D["ssmp"], D["ssms"])
        cc(3, "AllGather", ALU.bypass, G_ALL, D["w1q"], D["w1qf"], wait=True)
        cc(4, "AllGather", ALU.bypass, G_ALL, D["w2q"], D["w2qf"], wait=True)
        D["ccs3"] = ccsems[3]
        D["ccs4"] = ccsems[4]
        _sbar(p, blk, "c4")
        emit_scan(_NcU(nc, 's2_'), p, blk, D)
        _sbar(p, blk, "c5")
        cc(5, "AllToAll", ALU.bypass, G_ALL, D["ysz_a2a"], D["yszg"])
        _sbar(p, blk, "c6")
        emit_post1(_NcU(nc, 's3_'), p, blk, D)
        _sbar(p, blk, "c7")
        cc(6, "AllReduce", ALU.add, G_BATCH, D["kvp"], D["kvs"])
        _sbar(p, blk, "c8")
        emit_post2(_NcU(nc, 's4_'), p, blk, D)

        def f_end(sy):
            p.w(sy, "Esy_end", p.bar())
        blk.sync(f_end)
    return nc


def _prep(inputs):
    g = {}
    x = np.ascontiguousarray(inputs["L0_attn_hidden_states"], dtype=np.float32)
    xT = np.ascontiguousarray(x.reshape(NTOK, D).T)
    g["xT"] = [np.ascontiguousarray(xT[:, c * TPC:(c + 1) * TPC]) for c in range(NC)]
    w1T = np.ascontiguousarray(inputs["L1_ffn_W1"].T)
    w2T = np.ascontiguousarray(inputs["L1_ffn_W2"].T)
    g["w1T"] = [np.ascontiguousarray(w1T[c * 256:(c + 1) * 256, :]) for c in range(NC)]
    g["w2T"] = [np.ascontiguousarray(w2T[c * 1024:(c + 1) * 1024, :]) for c in range(NC)]
    g["ones_bf"] = [np.ones((128, 1), ml_dtypes.bfloat16)] * NC
    g["ones_h"] = [np.ones((128, 1), np.float16)] * NC
    g["ones_row"] = [np.ones((1, 128), np.float32)] * NC
    g["ones_f"] = [np.ones((128, 1), np.float32)] * NC
    g["eps_c"] = [np.full((128, 1), LN_EPS, np.float32)] * NC

    w_in = inputs["L0_attn_W_in"]
    SL = DIN // NC
    g["winT"], g["convw"], g["convb"], g["wxT"] = [], [], [], []
    g["wdtT"], g["dtb"], g["acol"], g["dcol"] = [], [], [], []
    for c in range(NC):
        sl = slice(c * SL, (c + 1) * SL)
        rows = np.r_[np.arange(c * SL, (c + 1) * SL),
                     np.arange(DIN + c * SL, DIN + (c + 1) * SL)]
        g["winT"].append(np.ascontiguousarray(w_in[rows, :].T))
        g["convw"].append(np.ascontiguousarray(
            inputs["L0_attn_conv_w"][sl, :].reshape(4, 128, 4)
            .transpose(1, 0, 2).reshape(128, 16)))
        g["convb"].append(np.ascontiguousarray(
            inputs["L0_attn_conv_b"][sl].reshape(4, 128).T))
        g["wxT"].append(np.ascontiguousarray(
            inputs["L0_attn_W_x"][:, sl].T.reshape(4, 128, 160)
            .transpose(1, 0, 2).reshape(128, 4 * 160)))
        g["wdtT"].append(np.ascontiguousarray(inputs["L0_attn_W_dt"][sl, :].T))
        dtb = inputs["L0_attn_dt_bias"][sl].reshape(4, 128).T
        g["dtb"].append(np.ascontiguousarray(dtb))
        A = inputs["L0_attn_A"][sl, :].reshape(4, 128, 16)
        acol = np.ascontiguousarray(A.transpose(1, 0, 2).reshape(128, 64))
        g["acol"].append(acol)
        dvec = inputs["L0_attn_D"][sl].reshape(4, 128).T
        g["dcol"].append(np.ascontiguousarray(dvec))

    f32 = np.float32
    g["woutT"] = np.ascontiguousarray(inputs["L0_attn_W_out"].T.astype(f32))
    g["selW"] = np.ascontiguousarray(
        inputs["L0_ffn_select_weight"].transpose(1, 0, 2).reshape(D, 64))
    g["mdT"] = np.ascontiguousarray(inputs["L0_ffn_select_mean_diff"].T)
    g["mdrep"] = np.ascontiguousarray(np.repeat(g["mdT"], 8, axis=1))
    g["lupF"] = np.ascontiguousarray(
        inputs["L0_ffn_lora_update"].transpose(1, 0, 2).reshape(D, 64))
    g["ldgF"] = np.ascontiguousarray(
        inputs["L0_ffn_lora_degen"].transpose(0, 2, 1).reshape(64, D))
    g["biasw"] = np.ascontiguousarray(inputs["L0_ffn_bias_w"].astype(f32))
    g["dbwT"] = np.ascontiguousarray(inputs["L0_ffn_debias_w"].T)
    g["slope_c"] = np.ascontiguousarray(
        inputs["L0_ffn_imp_slope"].reshape(8, 1).astype(f32))
    g["center_c"] = np.ascontiguousarray(
        inputs["L0_ffn_imp_center"].reshape(8, 1).astype(f32))
    g["wq"] = np.ascontiguousarray(inputs["L1_attn_Wq"].astype(f32))
    g["wk"] = np.ascontiguousarray(inputs["L1_attn_Wk"].astype(f32))
    g["wv"] = np.ascontiguousarray(inputs["L1_attn_Wv"].astype(f32))
    g["wo"] = np.ascontiguousarray(inputs["L1_attn_Wo"].astype(f32))
    g["bo_c"] = np.ascontiguousarray(
        inputs["L1_attn_bo"].reshape(16, 128).T.astype(f32))
    g["identf"] = np.eye(128, dtype=f32)
    e8 = np.zeros((8, 64), f32)
    for n in range(8):
        e8[n, n * 8:(n + 1) * 8] = 1.0
    g["e8f"] = e8
    g["e8bf"] = e8.astype(ml_dtypes.bfloat16)
    g["e8h"] = e8.astype(np.float16)
    g["e8t"] = np.ascontiguousarray(e8.T).astype(ml_dtypes.bfloat16)
    g["iota9"] = np.ascontiguousarray(
        np.broadcast_to(np.arange(8, dtype=f32) + 9.0, (128, 8)))
    g["iotan"] = np.ascontiguousarray(
        np.broadcast_to(np.arange(8, dtype=f32), (128, 8)))
    ehw = np.zeros((128, 16 * 32), f32)
    fhw = np.zeros((32, 16 * 128), f32)
    for w in range(16):
        for pp in range(128):
            h = (w * 128 + pp) // 64
            ehw[pp, w * 32 + h] = 1.0
            fhw[h, w * 128 + pp] = 1.0
    g["ehw"] = ehw
    g["fhw"] = np.ascontiguousarray(fhw)
    return g


def kernel(**inputs):
    g = _prep(inputs)
    nc = build_fused()
    names_shared = ["woutT", "selW", "mdT", "mdrep", "lupF", "ldgF", "biasw",
                    "dbwT", "slope_c", "center_c", "wq", "wk", "wv", "wo",
                    "bo_c", "identf", "e8h", "e8f", "e8t", "iota9", "iotan",
                    "ehw", "fhw"]
    names_per = ["xT", "w1T", "w2T", "ones_h", "ones_bf", "ones_row", "ones_f",
                 "eps_c", "winT", "convw", "convb", "wxT", "wdtT", "dtb",
                 "acol", "dcol"]
    maps = []
    for c in range(NC):
        m = {}
        for k in names_shared:
            m[k] = g[k]
        for k in names_per:
            m[k] = g[k][c]
        maps.append(m)
    res = run_bass_kernel_spmd(nc, maps, core_ids=list(range(NC)))
    out = np.concatenate(
        [np.ascontiguousarray(res.results[c]["outT"]).T for c in range(NC)],
        axis=0)
    return np.ascontiguousarray(out.reshape(B, L, D)).astype(np.float32)
